# version 03
# baseline (speedup 1.0000x reference)
import numpy as np
import ml_dtypes
import concourse.bass as bass
import concourse.mybir as mybir
from concourse.alu_op_type import AluOpType
from concourse.bass_utils import run_bass_kernel_spmd
from contextlib import ExitStack

F32 = mybir.dt.float32
BF16 = mybir.dt.bfloat16
I32 = mybir.dt.int32
AF = mybir.ActivationFunctionType
AO = AluOpType
AX = mybir.AxisListType
PI = float(np.pi)
MAGIC = 12582912.0

D = 1024
NCH = 8
T = 4096
L = 8192
FF = 4096
NE = 8
NPAIR = 8
N_ROUNDS = 4
NS = T // 512

ENGS = ("sy", "sc", "ve", "pe", "gp", "gd")
BLOCK_OF = {"sy": "sync", "sc": "scalar", "ve": "vector", "pe": "tensor",
            "gp": "gpsimd", "gd": "gpsimd"}
INC_OF = {"sy": 16, "sc": 1, "ve": 1, "pe": 1, "gp": 1, "gd": 16}
ENG_OF_BLOCK = {"sync": "sy", "scalar": "sc", "vector": "ve", "tensor": "pe",
                "gpsimd": "gp"}


class Prog:
    def __init__(self, nc, ctx):
        self.nc = nc
        self.sems = {n: ctx.enter_context(nc.semaphore("S_" + n)) for n in ENGS}
        self.count = {n: 0 for n in ENGS}
        self.blocks = {b: [] for b in ("sync", "scalar", "vector", "tensor", "gpsimd")}
        self.last_wait = {b: {} for b in self.blocks}
        self.snaps = []

    def op(self, eng, fn, deps=(), inc=None):
        blk = BLOCK_OF[eng]
        if inc is None:
            inc = INC_OF[eng]
        real = []
        for d in deps:
            if d is None:
                continue
            dsem, dval = d
            if BLOCK_OF[dsem] == blk:
                continue
            prev = self.last_wait[blk].get(dsem, 0)
            if dval > prev:
                self.last_wait[blk][dsem] = dval
                real.append((dsem, dval))
        import traceback as _tb
        fr = _tb.extract_stack(limit=2)[0]
        tag = f"{fr.filename.split('/')[-1]}:{fr.lineno}"
        if fn is not None and getattr(fn, "__closure__", None):
            snap = tuple(id(c.cell_contents) for c in fn.__closure__)
            self.snaps.append((fn, snap, tag))
        self.blocks[blk].append((eng, fn, real, inc, tag))
        self.count[eng] += inc
        return (eng, self.count[eng])

    def check_closures(self):
        bad = {}
        for fn, snap, tag in self.snaps:
            cur = tuple(id(c.cell_contents) for c in fn.__closure__)
            if cur != snap:
                names = [n for n, a, b in zip(fn.__code__.co_freevars, snap, cur) if a != b]
                bad.setdefault((tag, tuple(names)), [0])[0] += 1
        for (tag, names), cnt in sorted(bad.items()):
            print(f"CLOSURE DRIFT {tag} vars={names} x{cnt[0]}")
        return len(bad)

    def ev(self, eng):
        return (eng, self.count[eng])

    def barrier(self):
        evs = [(n, self.count[n]) for n in ENGS if self.count[n] > 0]
        for b in self.blocks:
            self.op(ENG_OF_BLOCK[b], None, deps=evs, inc=0)

    def emit(self, block):
        for blk_name, lst in self.blocks.items():
            if not lst:
                continue

            def body(engine, lst=lst):
                for eng, fn, deps, inc, tag in lst:
                    for dsem, dval in deps:
                        engine.wait_ge(self.sems[dsem], dval)
                    if fn is None:
                        continue
                    r = fn(engine)
                    if inc > 0:
                        assert r is not None, f"op on {eng} returned None, inc={inc}"
                        r.then_inc(self.sems[eng], inc)
            getattr(block, blk_name)(body)


def _bias_spec():
    spec = [("aln_w", 8), ("aln_b", 8), ("bqkv", 24), ("bproj", 8), ("rb", 1),
            ("fb1", NE * FF // 128), ("fb2", NE * D // 128),
            ("fln_w", 8), ("fln_b", 8),
            ("bpv", 8), ("bk", 8), ("bkv", 8), ("bg1", 8), ("bb1", 4),
            ("bmisc", 1), ("bln_g", 8), ("bln_b", 8), ("bo", 8)]
    rows, off = {}, 0
    for name, n in spec:
        rows[name] = off
        off += n
    return rows, ((off + 127) // 128) * 128


BROWS, BTOT = _bias_spec()

CONSTS = [1e-5, 1e-12, 0.125, 1.0, -1.0, 0.5, -PI, 0.5 * PI, PI, -2.0 * PI,
          1.0 / (2.0 * PI), MAGIC, 1.0 / D, 1.0 / 32.0, -1e30, 2.0, -0.125, 0.0]


class Kern:
    def __init__(self, nc, ctx, P, io):
        self.nc, self.ctx, self.P, self.io = nc, ctx, P, io
        self.sb = {}
        self._b3 = {}

    def sbuf(self, name, shape, dt, ctx=None):
        t = (ctx or self.ctx).enter_context(self.nc.sbuf_tensor(name, list(shape), dt))
        self.sb[name] = t
        return t

    def psum(self, name, shape, dt=F32, ctx=None):
        return (ctx or self.ctx).enter_context(self.nc.psum_tensor(name, list(shape), dt))

    def bcol(self, name, i=0):
        c = BROWS[name] + i
        return self.sb["BCOLS"][:, c:c + 1]

    def slotv(self, e, idx, mx):
        return e.value_load(self.sb["slot_t"][0:1, idx:idx + 1], min_val=0, max_val=mx)


def setup(K):
    nc, P, io = K.nc, K.P, K.io
    consts = K.sbuf("consts", [128, len(CONSTS)], F32)
    for i, v in enumerate(CONSTS):
        nc.const_aps.register(F32, v, consts[:, i:i + 1])
    for i, v in enumerate(CONSTS):
        P.op("ve", lambda e, i=i, v=v: e.memset(consts[:, i:i + 1], v))
    ones1f = K.sbuf("ones1f", [1, 128], F32)
    onescb = K.sbuf("onescb", [128, 1], BF16)
    P.op("ve", lambda e: e.memset(ones1f[:], 1.0))
    P.op("ve", lambda e: e.memset(onescb[:], 1.0))
    jnk = K.sbuf("jnk", [128, 1], F32)

    def fill(n=6):
        for _ in range(n):
            P.op("ve", lambda e: e.memset(jnk[:], 0.0))
    K.fill = fill

    idf = K.sbuf("idf", [128, 128], F32)
    idb = K.sbuf("idb", [128, 128], BF16)
    bc = K.sbuf("BCOLS", [128, BTOT], F32)
    with ExitStack() as cset:
        iota_r = K.sbuf("iota_r", [1, 128], F32, cset)
        iota_c = K.sbuf("iota_c", [128, 1], F32, cset)
        e1 = P.op("gp", lambda e: e.iota(iota_r[:], [[1, 128]], channel_multiplier=0, allow_small_or_imprecise_dtypes=True))
        e2 = P.op("gp", lambda e: e.iota(iota_c[:], [[0, 1]], channel_multiplier=1, allow_small_or_imprecise_dtypes=True))
        psb = K.psum("ps_id", [128, 128], F32, cset)
        e3 = P.op("pe", lambda e: e.matmul(psb[:], ones1f[:], iota_r[:], start=True, stop=True),
                  deps=(e1, P.ev("ve")))
        idt = K.sbuf("idt", [128, 128], F32, cset)
        P.op("ve", lambda e: e.tensor_scalar(idf[:], psb[:], iota_c[:], None, AO.is_ge),
             deps=(e3, e2))
        P.op("ve", lambda e: e.tensor_scalar(idt[:], psb[:], iota_c[:], None, AO.is_le))
        K.fill()
        e4 = P.op("ve", lambda e: e.tensor_tensor(idf[:], idf[:], idt[:], AO.mult))
        K.fill()
        P.op("ve", lambda e: e.tensor_copy(idb[:], idf[:]))

        braw = K.sbuf("braw", [128, BTOT], F32, cset)
        ld = P.op("sy", lambda e: e.dma_start(
            out=braw[:].rearrange("p (b c) -> p b c", b=BTOT // 128),
            in_=io["btab"].rearrange("(b p) c -> p b c", p=128)))
        evc = None
        for b in range(BTOT // 128):
            et = P.op("pe", lambda e, b=b: e.transpose(psb[:], braw[:, b * 128:(b + 1) * 128], idf[:]),
                      deps=(ld, e4, evc))
            evc = P.op("sc", lambda e, b=b: e.copy(out=bc[:, b * 128:(b + 1) * 128], in_=psb[:]),
                      deps=(et,))
        P.barrier()

    slot_t = K.sbuf("slot_t", [1, 8], I32)
    non_t = K.sbuf("non_t", [N_ROUNDS, 16], I32)
    mprev_t = K.sbuf("mprev_t", [128, 1], F32)
    P.op("sy", lambda e: e.dma_start(out=slot_t[:], in_=io["slot"]))
    P.op("sy", lambda e: e.dma_start(out=non_t[:], in_=io["nonces"]))
    P.op("sy", lambda e: e.dma_start(out=mprev_t[:], in_=io["mprev"]))
    P.barrier()

    def f(e):
        K.r_myfc = e.value_load(slot_t[0:1, 1:2])
        K.r_mysm = e.value_load(slot_t[0:1, 2:3])
        K.r_peersm = e.value_load(slot_t[0:1, 3:4])
        return e.engine_nop()
    P.op("gp", f)

    wg2c = K.sbuf("WG2C", [128, 8], BF16)
    wb2c = K.sbuf("WB2C", [128, 8], BF16)
    rbb = K.sbuf("RBB", [128, 8], F32)
    bb2d = K.sbuf("bb2diff", [1, 2], F32)
    K._ev_wg2 = P.op("gd", lambda e: e.dma_start(
        out=wg2c[:].rearrange("p (i c) -> p i c", i=NCH),
        in_=io["Wg2"].rearrange("(i p) c -> p i c", p=128),
        allow_slow_non_contiguous=True))
    K._ev_wb2 = P.op("gd", lambda e: e.dma_start(
        out=wb2c[:].rearrange("p (c i) -> p c i", c=2),
        in_=io["Wb2"].rearrange("(i p) c -> p c i", p=128),
        allow_slow_non_contiguous=True))
    K._ev_rbb = P.op("sy", lambda e: e.dma_start(
        out=rbb[:], in_=io["rbrow"].broadcast_to((128, 8))))
    K._ev_bb2 = P.op("sy", lambda e: e.dma_start(out=bb2d[:], in_=io["bb2d"]))
    b2b = K.sbuf("B2B", [NE, D], BF16)
    K._ev_b2m = P.op("gd", lambda e: e.dma_start(out=b2b[:], in_=io["b2m"]))
    P.barrier()


def flag_and_poll(K, rnd, deps):
    nc, P, io = K.nc, K.P, K.io
    non_t = K.sb["non_t"]
    ef = P.op("gd", lambda e: e.dma_start(
        out=io["xfl"][bass.ds(K.r_myfc, 1), :], in_=non_t[rnd:rnd + 1, :]), deps=deps)

    def poll(e):
        e0 = e.value_load(non_t[rnd:rnd + 1, 0:1])
        e1 = e.value_load(non_t[rnd:rnd + 1, 1:2])
        with e.register(f"pc{rnd}") as cond, e.register(f"pv{rnd}") as v, \
             e.register(f"p2{rnd}") as c2:
            e.reg_mov(cond, 1)
            with e.While(cond):
                e.load(v, io["xfl"][0:1, 0:1])
                e.reg_alu(cond, v, e0, AO.not_equal)
                e.load(v, io["xfl"][0:1, 1:2])
                e.reg_alu(c2, v, e1, AO.not_equal)
                e.reg_alu(cond, cond, c2, AO.add)
                e.load(v, io["xfl"][1:2, 0:1])
                e.reg_alu(c2, v, e0, AO.not_equal)
                e.reg_alu(cond, cond, c2, AO.add)
                e.load(v, io["xfl"][1:2, 1:2])
                e.reg_alu(c2, v, e1, AO.not_equal)
                e.reg_alu(cond, cond, c2, AO.add)
        return e.engine_nop()
    return P.op("gp", poll, deps=(ef,))


def ln_stats(K, ctx, tag, src, n_tiles, tok, eps, out_dram_row=None, keep_xb=False):
    nc, P, io = K.nc, K.P, K.io
    rows = K.sbuf(f"rows_{tag}", [1, 3 * tok], F32, ctx)
    XB = K.sbuf(f"XB_{tag}", [128, n_tiles * tok], BF16, ctx)
    x2 = K.sbuf(f"x2_{tag}", [128, 2 * 512], BF16, ctx)
    ps = K.psum(f"ps_{tag}", [1, 2 * 512], F32, ctx)
    onescb = K.sb["onescb"]
    ev_cast = None
    for i in range(n_tiles):
        ap, ev_src = src(i)
        ev_cast = P.op("ve", lambda e, ap=ap, i=i: e.tensor_copy(
            XB[:, i * tok:(i + 1) * tok], ap), deps=(ev_src,))
    ev1 = {}
    ev2 = {}
    nsl = tok // 512
    for s in range(nsl):
        for i in range(n_tiles):
            ev1[(s, i)] = P.op("pe", lambda e, s=s, i=i: e.matmul(
                ps[0:1, 0:512], onescb[:], XB[:, i * tok + s * 512:i * tok + (s + 1) * 512],
                start=(i == 0), stop=(i == n_tiles - 1)),
                deps=(ev_cast, ev2.get(s - 1)))
        evr1 = P.op("ve", lambda e, s=s: e.tensor_scalar(
            rows[0:1, s * 512:(s + 1) * 512], ps[0:1, 0:512], 1.0 / D, None, AO.mult),
            deps=(ev1[(s, n_tiles - 1)],))
        ev1[s] = evr1
        for i in range(n_tiles):
            b2 = i % 2
            eq = P.op("gp", lambda e, s=s, i=i, b2=b2: e.tensor_tensor(
                x2[:, b2 * 512:(b2 + 1) * 512],
                XB[:, i * tok + s * 512:i * tok + (s + 1) * 512],
                XB[:, i * tok + s * 512:i * tok + (s + 1) * 512], AO.mult),
                deps=(ev_cast, ev2.get((s, i - 2)) if i >= 2
                      else ev2.get((s - 1, n_tiles - 2 + i))))
            ev2[(s, i)] = P.op("pe", lambda e, s=s, i=i, b2=b2: e.matmul(
                ps[0:1, 512:1024], onescb[:], x2[:, b2 * 512:(b2 + 1) * 512],
                start=(i == 0), stop=(i == n_tiles - 1)),
                deps=(eq, evr1 if i == 0 else None))
        ev2[s] = P.op("ve", lambda e, s=s: e.tensor_scalar(
            rows[0:1, tok + s * 512:tok + (s + 1) * 512], ps[0:1, 512:1024],
            1.0 / D, None, AO.mult), deps=(ev2[(s, n_tiles - 1)],))
    mu, var, rstd = rows[:, 0:tok], rows[:, tok:2 * tok], rows[:, 2 * tok:3 * tok]
    edone = ev2[nsl - 1]
    P.op("ve", lambda e: e.scalar_tensor_tensor(rstd, mu, -1.0, mu, AO.mult, AO.mult),
         deps=(edone,))
    e4 = P.op("ve", lambda e: e.tensor_tensor(var, var, rstd, AO.add))
    e5 = P.op("sc", lambda e: e.activation(rstd, var, AF.Sqrt, bias=eps, scale=1.0), deps=(e4,))
    e6 = P.op("ve", lambda e: e.reciprocal(rstd, rstd), deps=(e5,))
    ev = e6
    if out_dram_row is not None:
        r = out_dram_row
        P.op("sy", lambda e: e.dma_start(out=io["ROWSD"][r:r + 1, 0:tok], in_=mu), deps=(e6,))
        ev = P.op("sy", lambda e: e.dma_start(out=io["ROWSD"][r + 1:r + 2, 0:tok], in_=rstd),
                  deps=(e6,))
    if keep_xb:
        return mu, rstd, var, ev, XB
    return mu, rstd, var, ev


def bcast_rows(K, ctx, tag, rows_evs, tok, ps=None):
    nc, P = K.nc, K.P
    n = len(rows_evs)
    out = K.sbuf(f"bc_{tag}", [128, n * tok], F32, ctx)
    if ps is None:
        ps = K.psum(f"psbc_{tag}", [128, 1024], F32, ctx)
    ones1f = K.sb["ones1f"]
    ev = None
    for j, (row, evr) in enumerate(rows_evs):
        for s in range(0, tok, 512):
            w = min(512, tok - s)
            bank = (s // 512) % 2
            em = P.op("pe", lambda e, row=row, s=s, w=w, bank=bank: e.matmul(
                ps[:, bank * 512:bank * 512 + w], ones1f[:], row[:, s:s + w],
                start=True, stop=True), deps=(evr, ev))
            ev = P.op("sc", lambda e, j=j, s=s, w=w, bank=bank: e.copy(
                out=out[:, j * tok + s:j * tok + s + w],
                in_=ps[:, bank * 512:bank * 512 + w]), deps=(em,))
    return [out[:, j * tok:(j + 1) * tok] for j in range(n)], ev


def bcast_dram_rows(K, ctx, tag, row_idx, tok, tok0, deps=(), ps=None):
    nc, P, io = K.nc, K.P, K.io
    n = len(row_idx)
    out = K.sbuf(f"bc_{tag}", [128, n * tok], F32, ctx)
    scr = K.sbuf(f"bcs_{tag}", [1, 2 * 512], F32, ctx)
    if ps is None:
        ps = K.psum(f"psbc_{tag}", [128, 1024], F32, ctx)
    ones1f = K.sb["ones1f"]
    ev = None
    evl_prev = {}
    for j, r in enumerate(row_idx):
        for s_ in range(0, tok, 512):
            w = min(512, tok - s_)
            bank = (s_ // 512) % 2
            evl = P.op("sy", lambda e, r=r, s_=s_, w=w, bank=bank: e.dma_start(
                out=scr[0:1, bank * 512:bank * 512 + w],
                in_=io["ROWSD"][r:r + 1, tok0 + s_:tok0 + s_ + w]),
                deps=deps + (evl_prev.get(bank),))
            em = P.op("pe", lambda e, s_=s_, w=w, bank=bank: e.matmul(
                ps[:, bank * 512:bank * 512 + w], ones1f[:],
                scr[0:1, bank * 512:bank * 512 + w], start=True, stop=True),
                deps=(evl, ev))
            ev = P.op("sc", lambda e, j=j, s_=s_, w=w, bank=bank: e.copy(
                out=out[:, j * tok + s_:j * tok + s_ + w],
                in_=ps[:, bank * 512:bank * 512 + w]), deps=(em,))
            evl_prev[bank] = ev
    return [out[:, j * tok:(j + 1) * tok] for j in range(n)], ev


def stream_w(K, wbuf, src_fn, n_in_tiles, deps):
    P = K.P

    def f(e):
        src = src_fn()
        return e.dma_start(
            out=wbuf.rearrange("p (i c) -> p i c", i=n_in_tiles),
            in_=src.rearrange("(i p) c -> p i c", p=128))
    return P.op("gd", f, deps=deps)


def phase_attn_a(K):
    nc, P, io = K.nc, K.P, K.io
    with ExitStack() as ctx:
        with ExitStack() as c0:
            xin = K.sbuf("A_xin", [128, T], F32, c0)
            ld = {}

            def src(i):
                ev = P.op("sy", lambda e, i=i: e.dma_start(
                    out=xin[:], in_=io["xT"][i * 128:(i + 1) * 128, :]), deps=(ld.get("p"),))
                ld["p"] = ev
                return xin[:], ev
            mu1, r1, var1, ev_st = ln_stats(K, c0, "A1", src, NCH, T, 1e-5, out_dram_row=0)
            rr = K.sbuf("A_rr", [1, 2 * T], F32, c0)
            R_row, A_row = rr[:, 0:T], rr[:, T:2 * T]
            P.op("ve", lambda e: e.tensor_tensor(R_row, r1, r1, AO.mult), deps=(ev_st,))
            e2 = P.op("ve", lambda e: e.tensor_tensor(R_row, R_row, var1, AO.mult))
            e3 = P.op("sc", lambda e: e.activation(R_row, R_row, AF.Sqrt, bias=1e-5, scale=1.0),
                      deps=(e2,))
            P.op("ve", lambda e: e.reciprocal(R_row, R_row), deps=(e3,))
            e5 = P.op("ve", lambda e: e.tensor_tensor(R_row, R_row, r1, AO.mult))
            e6 = P.op("ve", lambda e: e.tensor_tensor(A_row, mu1, R_row, AO.mult))
            P.op("sy", lambda e: e.dma_start(out=io["ROWSD"][6:7, :], in_=R_row), deps=(e5,))
            P.op("sy", lambda e: e.dma_start(out=io["ROWSD"][7:8, :], in_=A_row), deps=(e6,))
            P.barrier()

        cqa = ctx.enter_context(ExitStack())
        Q = K.sbuf("A_Q", [128, NCH * T], BF16, cqa)
        with ExitStack() as c2a:
            ZN = K.sbuf("A_ZN", [128, NCH * T], BF16, c2a)
            with ExitStack() as cz:
                (Rb, Ab), ev_bc = bcast_dram_rows(K, cz, "A", [6, 7], T, 0)
                xin2 = K.sbuf("A_xin2", [128, T], F32, cz)
                evp = None
                for i in range(NCH):
                    el = P.op("sy", lambda e, i=i: e.dma_start(
                        out=xin2[:], in_=io["xT"][i * 128:(i + 1) * 128, :]),
                        deps=(ev_bc, evp))
                    P.op("ve", lambda e: e.tensor_tensor(xin2[:], xin2[:], Rb, AO.mult),
                         deps=(el, ev_bc))
                    P.op("ve", lambda e: e.tensor_tensor(xin2[:], xin2[:], Ab, AO.subtract))
                    evp = P.op("ve", lambda e, i=i: e.tensor_scalar(
                        ZN[:, i * T:(i + 1) * T], xin2[:], K.bcol("aln_w", i),
                        K.bcol("aln_b", i), AO.mult, AO.add))
                ev_zn = evp
                P.barrier()

            with ExitStack() as cq:
                kvst = K.sbuf("A_kvst", [128, 2 * 512], BF16, cq)
                wq = K.sbuf("A_wq", [128, 2 * NCH * 128], BF16, cq)
                psq = K.psum("A_psq", [128, 2 * 512], F32, cq)
                ev_ev, kvw = {}, []
                for o in range(24):
                    buf = o % 2
                    ew = stream_w(K, wq[:, buf * NCH * 128:(buf + 1) * NCH * 128],
                                  lambda o=o: io["Wqkv"][:, o * 128:(o + 1) * 128], NCH,
                                  deps=(ev_ev.get((o - 2, NS - 1)),))
                    for s in range(NS):
                        bank = s % 2

                        def mmf(e, o=o, s=s, buf=buf, bank=bank):
                            r = None
                            for i in range(NCH):
                                r = e.matmul(
                                    psq[:, bank * 512:(bank + 1) * 512],
                                    wq[:, buf * NCH * 128 + i * 128:buf * NCH * 128 + (i + 1) * 128],
                                    ZN[:, i * T + s * 512:i * T + s * 512 + 512],
                                    start=(i == 0), stop=(i == NCH - 1))
                            return r
                        em = P.op("pe", mmf, deps=(ew, ev_zn,
                                                   ev_ev.get((o, s - 2)) if s >= 2
                                                   else ev_ev.get((o - 1, NS - 2 + s))))
                        if o < 8:
                            ee = P.op("sc", lambda e, o=o, s=s, bank=bank: e.activation(
                                Q[:, o * T + s * 512:o * T + s * 512 + 512],
                                psq[:, bank * 512:(bank + 1) * 512],
                                AF.Identity, bias=K.bcol("bqkv", o), scale=1.0), deps=(em,))
                        else:
                            kb = s % 2
                            dep_st = kvw[-2] if len(kvw) >= 2 else None
                            ee = P.op("sc", lambda e, o=o, s=s, bank=bank, kb=kb: e.activation(
                                kvst[:, kb * 512:(kb + 1) * 512],
                                psq[:, bank * 512:(bank + 1) * 512],
                                AF.Identity, bias=K.bcol("bqkv", o), scale=1.0),
                                deps=(em, dep_st))
                            ekv = P.op("gd", lambda e, o=o, s=s, kb=kb: e.dma_start(
                                out=io["KVSTG"][(o - 8) * 128:(o - 7) * 128,
                                                s * 512:(s + 1) * 512],
                                in_=kvst[:, kb * 512:(kb + 1) * 512]), deps=(ee,))
                            kvw.append(ekv)
                        ev_ev[(o, s)] = ee
                epub = P.op("gd", lambda e: e.dma_start(
                    out=io["xkv"].rearrange("(z r) t -> z r t", z=2)[
                        bass.ds(K.r_myfc, 1), :, :],
                    in_=io["KVSTG"][:, :]), deps=tuple(kvw[-2:]))
                ev_ready = flag_and_poll(K, 0, (epub,))
                P.barrier()

        with ExitStack() as ca:
            KG = K.sbuf("A_KG", [128, L], BF16, ca)
            VG = K.sbuf("A_VG", [128, L], BF16, ca)
            VT = K.sbuf("A_VT", [128, 64 * 130], BF16, ca)
            PT = K.sbuf("A_PT", [128, 2 * 1024], BF16, ca)
            mst = K.sbuf("A_mst", [64, 1024], BF16, ca)
            dent = K.sbuf("A_dent", [1, 1024], F32, ca)
            psS = K.psum("A_psS", [128, 2 * 1024], F32, ca)
            psM = K.psum("A_psM", [65, 1024], F32, ca)
            pstr = K.psum("A_pstr", [128, 128], BF16, ca)
            idb = K.sb["idb"]

            ev_pair = None
            for p in range(NPAIR):
                ekg1 = P.op("sy", lambda e, p=p: e.dma_start(
                    out=KG[:, 0:T], in_=io["xkv"][p * 128:(p + 1) * 128, :]),
                    deps=(ev_ready, ev_pair))
                P.op("sy", lambda e, p=p: e.dma_start(
                    out=KG[:, T:L], in_=io["xkv"][2048 + p * 128:2048 + (p + 1) * 128, :]))
                P.op("sy", lambda e, p=p: e.dma_start(
                    out=VG[:, 0:T], in_=io["xkv"][1024 + p * 128:1024 + (p + 1) * 128, :]))
                evg2 = P.op("sy", lambda e, p=p: e.dma_start(
                    out=VG[:, T:L],
                    in_=io["xkv"][3072 + p * 128:3072 + (p + 1) * 128, :]))
                evm = P.op("ve", lambda e: e.memset(VT[:], 1.0), deps=(ev_pair,))
                evt = None
                for k in range(64):
                    et = P.op("pe", lambda e, k=k: e.transpose(
                        pstr[:], VG[:, k * 128:(k + 1) * 128], idb[:]),
                        deps=(evg2, evm, evt, ev_pair))
                    P.op("ve", lambda e, k=k: e.tensor_copy(
                        VT[:, k * 130:k * 130 + 64], pstr[:, 0:64]), deps=(et,))
                    evt = P.op("ve", lambda e, k=k: e.tensor_copy(
                        VT[:, k * 130 + 65:k * 130 + 129], pstr[:, 64:128]))
                ev_vt = evt

                ev_q = None
                ev_evac1 = None
                ev_evac2 = None
                for h in range(2):
                    for qc in range(4):
                        q0 = qc * 1024
                        ev_exp, ev_pv = {}, {}
                        for k in range(64):
                            sb_ = k % 2

                            def smm(e, p=p, h=h, q0=q0, k=k, sb_=sb_):
                                r = None
                                for qs in range(2):
                                    r = e.matmul(
                                        psS[:, sb_ * 1024 + qs * 512:sb_ * 1024 + (qs + 1) * 512],
                                        KG[h * 64:(h + 1) * 64, k * 128:(k + 1) * 128],
                                        Q[h * 64:(h + 1) * 64,
                                          p * T + q0 + qs * 512:p * T + q0 + (qs + 1) * 512],
                                        start=True, stop=True)
                                return r
                            es = P.op("pe", smm, deps=(ekg1, evg2, ev_vt, ev_exp.get(k - 2)))
                            ee = P.op("sc", lambda e, k=k, sb_=sb_: e.activation(
                                PT[:, sb_ * 1024:(sb_ + 1) * 1024],
                                psS[:, sb_ * 1024:(sb_ + 1) * 1024],
                                AF.Exp, scale=0.125), deps=(es, ev_pv.get(k - 2)))
                            ev_exp[k] = ee

                            def pvm(e, h=h, k=k, sb_=sb_):
                                r = None
                                for qs in range(2):
                                    r = e.matmul(
                                        psM[:, qs * 512:(qs + 1) * 512],
                                        VT[:, k * 130 + h * 65:k * 130 + h * 65 + 65],
                                        PT[:, sb_ * 1024 + qs * 512:sb_ * 1024 + (qs + 1) * 512],
                                        start=(k == 0), stop=(k == 63))
                                return r
                            ev_pv[k] = P.op("pe", pvm,
                                            deps=(ee,) + ((ev_evac1, ev_evac2) if k == 0 else ()))
                        ed = P.op("ve", lambda e: e.reciprocal(dent[:], psM[64:65, :]),
                                  deps=(ev_pv[63], ev_q))
                        em = P.op("sc", lambda e: e.copy(out=mst[:], in_=psM[0:64, :]),
                                  deps=(ev_pv[63], ev_q))
                        P.op("sy", lambda e, p=p, h=h, qc=qc: e.dma_start(
                            out=io["DEN"][2 * p + h:2 * p + h + 1, qc * 1024:(qc + 1) * 1024],
                            in_=dent[:]), deps=(ed,))
                        ems = P.op("sy", lambda e, p=p, h=h, qc=qc: e.dma_start(
                            out=io["MSAR"][(2 * p + h) * 64:(2 * p + h + 1) * 64,
                                           qc * 1024:(qc + 1) * 1024],
                            in_=mst[:]), deps=(em,))
                        ev_q = ems
                        ev_evac1, ev_evac2 = ed, em
                ev_pair = ev_q
            ev_attn = ev_pair
            P.barrier()
        cqa.close()

        with ExitStack() as c3:
            MSA = K.sbuf("A_MSA", [128, NCH * T], BF16, c3)
            dro = K.sbuf("A_dro", [1, 2 * 2048], F32, c3)
            mraw = K.sbuf("A_mraw", [128, 2048], BF16, c3)
            mA = K.sbuf("A_mA", [1, 128], F32, c3)
            mB = K.sbuf("A_mB", [1, 128], F32, c3)
            psD = K.psum("A_psD", [128, 2048], F32, c3)
            P.op("ve", lambda e: e.memset(mA[:], 0.0), deps=(ev_attn,))
            P.op("ve", lambda e: e.memset(mA[:, 0:64], 1.0))
            P.op("ve", lambda e: e.memset(mB[:], 0.0))
            em4 = P.op("ve", lambda e: e.memset(mB[:, 64:128], 1.0))
            evn = None
            for p in range(NPAIR):
                for qc in range(2):
                    eda = P.op("sy", lambda e, p=p, qc=qc: e.dma_start(
                        out=dro[:, 0:2048],
                        in_=io["DEN"][2 * p:2 * p + 1, qc * 2048:(qc + 1) * 2048]),
                        deps=(evn, ev_attn))
                    edb = P.op("sy", lambda e, p=p, qc=qc: e.dma_start(
                        out=dro[:, 2048:4096],
                        in_=io["DEN"][2 * p + 1:2 * p + 2, qc * 2048:(qc + 1) * 2048]))
                    erm = P.op("sy", lambda e, p=p, qc=qc: e.dma_start(
                        out=mraw[:],
                        in_=io["MSAR"][2 * p * 64:(2 * p + 2) * 64,
                                       qc * 2048:(qc + 1) * 2048]), deps=(edb, evn))
                    eb = None
                    for half in range(4):
                        h0 = half * 512
                        P.op("pe", lambda e, h0=h0: e.matmul(
                            psD[:, h0:h0 + 512], mA[:], dro[:, h0:h0 + 512],
                            start=True, stop=False), deps=(eda, edb, em4, evn))
                        eb = P.op("pe", lambda e, h0=h0: e.matmul(
                            psD[:, h0:h0 + 512], mB[:], dro[:, 2048 + h0:2048 + h0 + 512],
                            start=False, stop=True))
                    evn = P.op("ve", lambda e, p=p, qc=qc: e.tensor_tensor(
                        MSA[:, p * T + qc * 2048:p * T + (qc + 1) * 2048], mraw[:], psD[:],
                        AO.mult), deps=(erm, eb))
            ev_msa = evn

            (m1b, r1b), ev_bc2 = bcast_dram_rows(K, c3, "A2", [0, 1], T, 0)
            wpj = K.sbuf("A_wpj", [128, 2 * NCH * 128], BF16, c3)
            xr1 = K.sbuf("A_xr1", [128, T], F32, c3)
            xr2 = K.sbuf("A_xr2", [128, T], F32, c3)
            pso = K.psum("A_pso", [128, 2 * 512], F32, c3)
            ev_o = {}
            for o in range(NCH):
                buf = o % 2
                ewp = stream_w(K, wpj[:, buf * NCH * 128:(buf + 1) * NCH * 128],
                               lambda o=o: io["Wproj"][:, o * 128:(o + 1) * 128], NCH,
                               deps=(ev_o.get(o - 2),))
                exl = P.op("sy", lambda e, o=o: e.dma_start(
                    out=xr1[:], in_=io["xT"][o * 128:(o + 1) * 128, :]),
                    deps=(ev_o.get(o - 1),))
                P.op("ve", lambda e: e.tensor_tensor(xr2[:], xr1[:], m1b, AO.subtract),
                     deps=(exl, ev_bc2))
                P.op("ve", lambda e: e.tensor_tensor(xr2[:], xr2[:], r1b, AO.mult))
                ez3 = P.op("ve", lambda e: e.tensor_tensor(xr2[:], xr2[:], xr1[:], AO.add))
                evo = None
                for s in range(NS):
                    bank = s % 2

                    def pmm(e, o=o, s=s, buf=buf, bank=bank):
                        r = None
                        for i in range(NCH):
                            r = e.matmul(
                                pso[:, bank * 512:(bank + 1) * 512],
                                wpj[:, buf * NCH * 128 + i * 128:buf * NCH * 128 + (i + 1) * 128],
                                MSA[:, i * T + s * 512:i * T + s * 512 + 512],
                                start=(i == 0), stop=(i == NCH - 1))
                        return r
                    emm = P.op("pe", pmm, deps=(ewp, ev_msa,
                                                ev_o.get((o, s - 2)) if s >= 2
                                                else ev_o.get((o - 1, NS - 2 + s))))
                    evo = P.op("ve", lambda e, o=o, s=s, bank=bank: e.scalar_tensor_tensor(
                        xr1[:, s * 512:(s + 1) * 512], pso[:, bank * 512:(bank + 1) * 512],
                        K.bcol("bproj", o), xr2[:, s * 512:(s + 1) * 512], AO.add, AO.add),
                        deps=(emm, ez3))
                    ev_o[(o, s)] = evo
                est = P.op("sy", lambda e, o=o: e.dma_start(
                    out=io["XRES"][o * 128:(o + 1) * 128, :], in_=xr1[:]), deps=(evo,))
                ev_o[o] = est
            P.barrier()


def phase_ffn(K, call):
    nc, P, io = K.nc, K.P, K.io
    last = (call == 2)
    with ExitStack() as ctx:
        with ExitStack() as c0:
            xin = K.sbuf(f"F{call}_xin", [128, T], F32, c0)
            ld = {}

            def src(i):
                ev = P.op("sy", lambda e, i=i: e.dma_start(
                    out=xin[:], in_=io["XRES"][i * 128:(i + 1) * 128, :]), deps=(ld.get("p"),))
                ld["p"] = ev
                return xin[:], ev
            mu, rstd, _v, ev_st = ln_stats(K, c0, f"F{call}", src, NCH, T, 1e-5,
                                           out_dram_row=2)
            P.barrier()

        with ExitStack() as c1:
            _ffn_routing(K, c1, call)
            P.barrier()

        for th in range(2):
            t0 = th * 2048
            with ExitStack() as c2:
                cps = c2.enter_context(ExitStack())
                ps1 = K.psum(f"F{call}t{th}_p1", [128, 2 * 1024], F32, cps)
                ps2 = K.psum(f"F{call}t{th}_p2", [128, 2 * 1024], F32, cps)
                (mub, rb), ev_bc = bcast_dram_rows(
                    K, c2, f"F{call}t{th}", [2, 3], 2048, t0, ps=ps1[:, 0:1024])
                FLAT = K.sbuf(f"F{call}t{th}_FLAT", [128, NCH * 2048], BF16, c2)
                ft = K.sbuf(f"F{call}t{th}_ft", [128, 2048], F32, c2)
                cwtb = K.sbuf(f"F{call}t{th}_cwt", [NE, 2048], BF16, c2)
                ecwt = P.op("sy", lambda e, cwtb=cwtb, t0=t0: e.dma_start(
                    out=cwtb[:], in_=io["CWD"][:, t0:t0 + 2048]),
                    deps=(K._ev_cwd[call],))
                evp = None
                for i in range(NCH):
                    el = P.op("sy", lambda e, i=i, ft=ft, t0=t0: e.dma_start(
                        out=ft[:], in_=io["XRES"][i * 128:(i + 1) * 128, t0:t0 + 2048]),
                        deps=(ev_bc, evp))
                    P.op("ve", lambda e, ft=ft, mub=mub: e.tensor_tensor(
                        ft[:], ft[:], mub, AO.subtract), deps=(el, ev_bc))
                    evp = P.op("ve", lambda e, i=i, FLAT=FLAT, ft=ft, rb=rb: e.tensor_tensor(
                        FLAT[:, i * 2048:(i + 1) * 2048], ft[:], rb, AO.mult))
                ev_flat = evp
                OUT = K.sbuf(f"F{call}t{th}_OUT", [128, NCH * 2048], F32, c2)
                emz = None
                for o in range(NCH):
                    emz = P.op("ve", lambda e, o=o, OUT=OUT: e.memset(
                        OUT[:, o * 2048:(o + 1) * 2048], 0.0))

                with ExitStack() as c3:
                    W1C = K.sbuf(f"F{call}t{th}_W1", [128, 2 * NCH * 1024], BF16, c3)
                    W2C = K.sbuf(f"F{call}t{th}_W2", [128, 8 * 1024], BF16, c3)
                    CWB = K.sbuf(f"F{call}t{th}_CWB", [128, 2048], BF16, c3)
                    h1c = K.sbuf(f"F{call}t{th}_h1", [128, 8 * 1024], BF16, c3)
                    stg = K.sbuf(f"F{call}t{th}_stg", [128, 2 * 1024], BF16, c3)
                    ev_wl = {}
                    last_mm2 = None
                    last_cw_reader = None
                    for e_i in range(NE):
                        ecb = P.op("sy", lambda e, e_i=e_i, CWB=CWB, t0=t0: e.dma_start(
                            out=CWB[:],
                            in_=io["CWD"][e_i:e_i + 1, t0:t0 + 2048].broadcast_to((128, 2048))),
                            deps=(K._ev_cwd[call], last_cw_reader))
                        for c in range(4):
                            widx = (e_i * 4 + c) % 2
                            dep_w = ev_wl.get((e_i * 4 + c - 2))
                            ew1 = stream_w(K, W1C[:, widx * NCH * 1024:(widx + 1) * NCH * 1024],
                                           lambda e_i=e_i, c=c: io["w1T"][e_i, :, c * 1024:(c + 1) * 1024],
                                           NCH, deps=(dep_w,))
                            ew2 = P.op("gd", lambda e, e_i=e_i, c=c, W2C=W2C: e.dma_start(
                                out=W2C[:].rearrange("p (i o) -> p i o", i=8),
                                in_=io["w2T"][e_i, c * 1024:(c + 1) * 1024, :].rearrange(
                                    "(i p) o -> p i o", p=128)),
                                deps=(ev_wl.get(e_i * 4 + c - 1),))
                            for ts in range(2):
                                s0 = ts * 1024
                                prev_mm2_of_ts = last_mm2
                                ev_h = {}
                                for f in range(8):
                                    pb = f % 2

                                    def m1(e, e_i=e_i, c=c, f=f, widx=widx, s0=s0, pb=pb,
                                           ps1=ps1, W1C=W1C, FLAT=FLAT):
                                        r = None
                                        for half in range(2):
                                            for i in range(NCH):
                                                r = e.matmul(
                                                    ps1[:, pb * 1024 + half * 512:
                                                        pb * 1024 + (half + 1) * 512],
                                                    W1C[:, widx * NCH * 1024 + i * 1024 + f * 128:
                                                        widx * NCH * 1024 + i * 1024 + (f + 1) * 128],
                                                    FLAT[:, i * 2048 + s0 + half * 512:
                                                         i * 2048 + s0 + (half + 1) * 512],
                                                    start=(i == 0), stop=(i == NCH - 1))
                                        return r
                                    em1 = P.op("pe", m1, deps=(ew1, ev_flat,
                                                               ev_h.get(f - 2),
                                                               prev_mm2_of_ts if f < 2 else None))
                                    eg = P.op("sc", lambda e, f=f, pb=pb, e_i=e_i, c=c,
                                              stg=stg, ps1=ps1: e.activation(
                                        stg[:, pb * 1024:(pb + 1) * 1024],
                                        ps1[:, pb * 1024:(pb + 1) * 1024],
                                        AF.Gelu, bias=K.bcol("fb1", e_i * 32 + c * 8 + f),
                                        scale=1.0), deps=(em1, ev_h.get(f - 1)))
                                    eh = P.op("ve", lambda e, f=f, pb=pb, s0=s0, e_i=e_i,
                                              h1c=h1c, stg=stg, CWB=CWB: e.tensor_tensor(
                                        h1c[:, f * 1024:(f + 1) * 1024],
                                        stg[:, pb * 1024:(pb + 1) * 1024],
                                        CWB[:, s0:s0 + 1024], AO.mult),
                                        deps=(eg, ecb, prev_mm2_of_ts if f == 0 else None))
                                    ev_h[f] = eh
                                edbg = None
                                if (os.environ.get("KDBG2") and call == 1 and th == 0
                                        and e_i == 0 and c == 0 and ts == 0):
                                    P.op("sy", lambda e, h1c=h1c: e.dma_start(
                                        out=io["KVSTG"][1024:1152, :], in_=h1c[:, 0:4096]),
                                        deps=(ev_h[7],))
                                    edbg = P.op("sy", lambda e, h1c=h1c: e.dma_start(
                                        out=io["KVSTG"][1152:1280, :], in_=h1c[:, 4096:8192]),
                                        deps=(ev_h[7],))
                                ev_m2 = {}
                                for o in range(NCH):
                                    ob = o % 2

                                    addb2 = (e_i == NE - 1 and c == 3)

                                    def m2(e, o=o, ob=ob, ps2=ps2, W2C=W2C, h1c=h1c,
                                           addb2=addb2, s0=s0, cwtb=cwtb):
                                        r = None
                                        for half in range(2):
                                            for f in range(8):
                                                r = e.matmul(
                                                    ps2[:, ob * 1024 + half * 512:
                                                        ob * 1024 + (half + 1) * 512],
                                                    W2C[:, f * 1024 + o * 128:
                                                        f * 1024 + (o + 1) * 128],
                                                    h1c[:, f * 1024 + half * 512:
                                                        f * 1024 + (half + 1) * 512],
                                                    start=(f == 0),
                                                    stop=(f == 7 and not addb2))
                                            if addb2:
                                                r = e.matmul(
                                                    ps2[:, ob * 1024 + half * 512:
                                                        ob * 1024 + (half + 1) * 512],
                                                    K.sb["B2B"][:, o * 128:(o + 1) * 128],
                                                    cwtb[:, s0 + half * 512:
                                                         s0 + (half + 1) * 512],
                                                    start=False, stop=True)
                                        return r
                                    em2 = P.op("pe", m2, deps=(ew2, ev_h[7], ev_m2.get(o - 2),
                                                               ecwt, K._ev_b2m, edbg))
                                    ea = P.op("ve", lambda e, o=o, ob=ob, s0=s0,
                                              OUT=OUT, ps2=ps2: e.tensor_tensor(
                                        OUT[:, o * 2048 + s0:o * 2048 + s0 + 1024],
                                        OUT[:, o * 2048 + s0:o * 2048 + s0 + 1024],
                                        ps2[:, ob * 1024:(ob + 1) * 1024], AO.add),
                                        deps=(em2, emz))
                                    ev_m2[o] = ea
                                last_mm2 = ev_m2[NCH - 1]
                            ev_wl[e_i * 4 + c] = last_mm2
                        last_cw_reader = last_mm2
                    ev_experts = last_mm2
                    P.barrier()
                cps.close()

                evp = None
                for i in range(NCH):
                    el = P.op("sy", lambda e, i=i, ft=ft, t0=t0: e.dma_start(
                        out=ft[:], in_=io["XRES"][i * 128:(i + 1) * 128, t0:t0 + 2048]),
                        deps=(ev_experts, evp))
                    P.op("ve", lambda e, ft=ft, mub=mub: e.tensor_tensor(
                        ft[:], ft[:], mub, AO.subtract), deps=(el,))
                    P.op("ve", lambda e, ft=ft, rb=rb: e.tensor_tensor(
                        ft[:], ft[:], rb, AO.mult))
                    evp = P.op("ve", lambda e, i=i, OUT=OUT, ft=ft: e.tensor_tensor(
                        OUT[:, i * 2048:(i + 1) * 2048], OUT[:, i * 2048:(i + 1) * 2048],
                        ft[:], AO.add))
                ev_sum = P.ev("ve")
                edmp = None
                if os.environ.get("KDBG") and call == 1:
                    for i in range(NCH):
                        P.op("sy", lambda e, i=i, OUT=OUT, t0=t0: e.dma_start(
                            out=io["MEMS"][i * 128:(i + 1) * 128, t0:t0 + 2048],
                            in_=OUT[:, i * 2048:(i + 1) * 2048]), deps=(ev_sum,))
                    for i in range(NCH):
                        edmp = P.op("sy", lambda e, i=i, FLAT=FLAT, t0=t0: e.dma_start(
                            out=io["KVSTG"][i * 128:(i + 1) * 128, t0:t0 + 2048],
                            in_=FLAT[:, i * 2048:(i + 1) * 2048]), deps=(ev_sum,))

                with ExitStack() as c4:
                    def src2(i):
                        return OUT[:, i * 2048:(i + 1) * 2048], ev_sum
                    mu2, rstd2, _v2, ev_st2 = ln_stats(K, c4, f"F{call}e{th}", src2, NCH,
                                                       2048, 1e-12)
                    (mu2b, r2b), ev_bc2 = bcast_rows(
                        K, c4, f"F{call}e{th}b", [(mu2, ev_st2), (rstd2, ev_st2)], 2048)
                    evw = None
                    for i in range(NCH):
                        el = P.op("sy", lambda e, i=i, ft=ft, t0=t0: e.dma_start(
                            out=ft[:], in_=io["XRES"][i * 128:(i + 1) * 128, t0:t0 + 2048]),
                            deps=(evw, ev_bc2))
                        P.op("ve", lambda e, i=i, OUT=OUT, mu2b=mu2b: e.tensor_tensor(
                            OUT[:, i * 2048:(i + 1) * 2048], OUT[:, i * 2048:(i + 1) * 2048],
                            mu2b, AO.subtract), deps=(ev_bc2, edmp))
                        P.op("ve", lambda e, i=i, OUT=OUT, r2b=r2b: e.tensor_tensor(
                            OUT[:, i * 2048:(i + 1) * 2048], OUT[:, i * 2048:(i + 1) * 2048],
                            r2b, AO.mult))
                        P.op("ve", lambda e, i=i, OUT=OUT: e.tensor_scalar(
                            OUT[:, i * 2048:(i + 1) * 2048], OUT[:, i * 2048:(i + 1) * 2048],
                            K.bcol("fln_w", i), K.bcol("fln_b", i), AO.mult, AO.add))
                        ex = P.op("ve", lambda e, i=i, OUT=OUT, ft=ft: e.tensor_tensor(
                            OUT[:, i * 2048:(i + 1) * 2048], OUT[:, i * 2048:(i + 1) * 2048],
                            ft[:], AO.add), deps=(el,))
                        dst = io["out"] if last else io["XRES"]
                        evw = P.op("sy", lambda e, i=i, dst=dst, OUT=OUT, t0=t0: e.dma_start(
                            out=dst[i * 128:(i + 1) * 128, t0:t0 + 2048],
                            in_=OUT[:, i * 2048:(i + 1) * 2048]), deps=(ex,))
                    P.barrier()


def _ffn_routing(K, ctx, call):
    nc, P, io = K.nc, K.P, K.io
    FLR = K.sbuf(f"R{call}_FLR", [128, NCH * T], BF16, ctx)
    rrow = K.sbuf(f"R{call}_rr", [1, 2 * T], F32, ctx)
    er1 = P.op("sy", lambda e: e.dma_start(out=rrow[:, 0:T], in_=io["ROWSD"][2:3, :]))
    er2 = P.op("sy", lambda e: e.dma_start(out=rrow[:, T:2 * T], in_=io["ROWSD"][3:4, :]))
    with ExitStack() as cb:
        (mub, rb), ev_bc = bcast_rows(K, cb, f"R{call}",
                                      [(rrow[:, 0:T], er1), (rrow[:, T:2 * T], er2)], T)
        ft = K.sbuf(f"R{call}_ft", [128, T], F32, cb)
        evp = None
        for i in range(NCH):
            el = P.op("sy", lambda e, i=i: e.dma_start(
                out=ft[:], in_=io["XRES"][i * 128:(i + 1) * 128, :]), deps=(ev_bc, evp))
            P.op("ve", lambda e: e.tensor_tensor(ft[:], ft[:], mub, AO.subtract),
                 deps=(el, ev_bc))
            evp = P.op("ve", lambda e, i=i: e.tensor_tensor(
                FLR[:, i * T:(i + 1) * T], ft[:], rb, AO.mult))
        ev_flr = evp
        P.barrier()

    rwt = K.sbuf(f"R{call}_rwt", [128, NCH * 8], BF16, ctx)
    erw = stream_w(K, rwt[:], lambda: io["rwT"], NCH, deps=())
    psl = K.psum(f"R{call}_psl", [128, 2 * 512], F32, ctx)
    psr = K.psum(f"R{call}_psr", [8, 128], F32, ctx)
    cwtile = K.sbuf(f"R{call}_cw", [128, 8], F32, ctx)
    cwb16 = K.sbuf(f"R{call}_cwb", [128, 8], BF16, ctx)
    scr = K.sbuf(f"R{call}_scr", [128, 8 * 6], F32, ctx)
    col = K.sbuf(f"R{call}_col", [128, 4], F32, ctx)
    ones8 = K.sbuf(f"R{call}_o8", [128, 8], F32, ctx)
    cws = K.sbuf(f"R{call}_cws", [8, T], F32, ctx)
    P.op("ve", lambda e: e.memset(ones8[:], 1.0))
    idb = K.sb["idb"]
    lg = scr[:, 0:8]
    c1 = scr[:, 8:16]
    s1 = scr[:, 16:24]
    m1 = scr[:, 24:32]
    l2 = scr[:, 32:40]
    m2 = scr[:, 40:48]
    v1 = col[:, 0:1]
    v2 = col[:, 1:2]
    w1 = col[:, 2:3]
    onec = K.sb["consts"][:, CONSTS.index(1.0):CONSTS.index(1.0) + 1]
    zeroc = K.sb["consts"][:, CONSTS.index(0.0):CONSTS.index(0.0) + 1]
    ev_tt = {}
    for tt in range(T // 128):
        pb = tt % 2

        def lmm(e, tt=tt, pb=pb):
            r = None
            for i in range(NCH):
                r = e.matmul(psl[:, pb * 512:pb * 512 + 8],
                             FLR[:, i * T + tt * 128:i * T + (tt + 1) * 128],
                             rwt[:, i * 8:(i + 1) * 8],
                             start=(i == 0), stop=(i == NCH - 1))
            return r
        eml = P.op("pe", lmm, deps=(erw, ev_flr, ev_tt.get(tt - 2)))
        P.op("ve", lambda e, pb=pb: e.tensor_tensor(lg, psl[:, pb * 512:pb * 512 + 8],
                                                    K.sb["RBB"][:], AO.add),
             deps=(eml, K._ev_rbb, ev_tt.get(tt - 1)))
        K.fill()
        P.op("ve", lambda e: e.tensor_reduce(v1, lg, AX.X, AO.max))
        K.fill()
        P.op("ve", lambda e: e.tensor_scalar(c1, lg, v1, None, AO.is_ge))
        K.fill()
        P.op("ve", lambda e: e.tensor_tensor_scan(s1, c1, c1, zeroc, AO.add, AO.bypass))
        K.fill()
        P.op("ve", lambda e: e.tensor_scalar(m1, s1, onec, None, AO.is_le))
        K.fill()
        P.op("ve", lambda e: e.tensor_tensor(m1, m1, c1, AO.mult))
        K.fill()
        P.op("ve", lambda e: e.scalar_tensor_tensor(l2, m1, -1e30, lg, AO.mult, AO.add))
        K.fill()
        P.op("ve", lambda e: e.tensor_reduce(v2, l2, AX.X, AO.max))
        K.fill()
        P.op("ve", lambda e: e.tensor_scalar(m2, l2, v2, None, AO.is_ge))
        K.fill()
        P.op("ve", lambda e: e.tensor_tensor_scan(s1, m2, m2, zeroc, AO.add, AO.bypass))
        K.fill()
        P.op("ve", lambda e: e.tensor_scalar(s1, s1, onec, None, AO.is_le))
        K.fill()
        P.op("ve", lambda e: e.tensor_tensor(m2, m2, s1, AO.mult))
        ed = P.op("ve", lambda e: e.tensor_tensor(col[:, 3:4], v1, v2, AO.subtract))
        ew = P.op("sc", lambda e: e.activation(w1, col[:, 3:4], AF.Sigmoid), deps=(ed,))
        P.op("ve", lambda e: e.tensor_scalar(cwtile[:], m1, w1, None, AO.mult), deps=(ew,))
        P.op("ve", lambda e: e.tensor_scalar(col[:, 3:4], w1, -1.0, 1.0, AO.mult, AO.add))
        K.fill()
        P.op("ve", lambda e: e.scalar_tensor_tensor(cwtile[:], m2, col[:, 3:4], cwtile[:],
                                                    AO.mult, AO.add))
        K.fill()
        ecast = P.op("ve", lambda e: e.tensor_copy(cwb16[:], cwtile[:]))
        emt = P.op("pe", lambda e: e.matmul(psr[:], cwb16[:], idb[:], start=True, stop=True),
                   deps=(ecast, ev_tt.get(tt - 1)))
        ev_tt[tt] = P.op("ve", lambda e, tt=tt: e.tensor_copy(
            cws[:, tt * 128:(tt + 1) * 128], psr[:]), deps=(emt,))
    K._ev_cwd[call] = P.op("gd", lambda e: e.dma_start(out=io["CWD"], in_=cws[:]),
                           deps=(ev_tt[T // 128 - 1],))


def phase_attn_b(K):
    nc, P, io = K.nc, K.P, K.io
    CH = 1024
    NC4 = T // CH
    TE = T + 1
    with ExitStack() as ctx:
        XBL = K.sbuf("B_XBL", [128, 8], F32, ctx)
        XBP = K.sbuf("B_XBP", [128, 8], F32, ctx)
        XBPB = K.sbuf("B_XBPB", [128, 8], BF16, ctx)
        KB = K.sbuf("B_KB", [128, 8], F32, ctx)
        GSR = K.sbuf("B_GSR", [1, T], F32, ctx)
        BW0 = K.sbuf("B_BW0", [1, T], BF16, ctx)
        BW1 = K.sbuf("B_BW1", [1, T], BF16, ctx)
        CARR = K.sbuf("B_CARR", [128, 34], F32, ctx)
        PCAR = K.sbuf("B_PCAR", [128, 34], F32, ctx)

        with ExitStack() as c0:
            xin = K.sbuf("B_xin", [128, T], F32, c0)
            ld = {}

            def src(i):
                ev = P.op("sy", lambda e, i=i: e.dma_start(
                    out=xin[:], in_=io["XRES"][i * 128:(i + 1) * 128, :]), deps=(ld.get("p"),))
                ld["p"] = ev
                return xin[:], ev
            mu3, r3, _v3, ev_st = ln_stats(K, c0, "B", src, NCH, T, 1e-5, out_dram_row=4)
            P.barrier()

        cscan = ctx.enter_context(ExitStack())
        FLATB = K.sbuf("B_FLATB", [128, NCH * TE], BF16, cscan)
        with ExitStack() as c0b:
            (m3b, r3b), ev_bc = bcast_dram_rows(K, c0b, "B0", [4, 5], T, 0)
            xin2 = K.sbuf("B_xin2", [128, T], F32, c0b)
            evp = None
            for i in range(NCH):
                el = P.op("sy", lambda e, i=i: e.dma_start(
                    out=xin2[:], in_=io["XRES"][i * 128:(i + 1) * 128, :]),
                    deps=(ev_bc, evp))
                P.op("ve", lambda e: e.tensor_tensor(xin2[:], xin2[:], m3b, AO.subtract),
                     deps=(el, ev_bc))
                P.op("ve", lambda e: e.tensor_tensor(xin2[:], xin2[:], r3b, AO.mult))
                P.op("ve", lambda e, i=i: e.tensor_copy(
                    FLATB[:, i * TE + 1:(i + 1) * TE], xin2[:]))
                evp = P.op("ve", lambda e, i=i: e.tensor_copy(
                    XBL[:, i:i + 1], xin2[:, T - 1:T]))
            ev_flatb = evp
            P.barrier()

        exs = P.op("gd", lambda e: e.dma_start(
            out=io["xsm"][bass.ds(K.r_mysm, 128), 40:48], in_=XBL[:]), deps=(ev_flatb,))
        ev_r1 = flag_and_poll(K, 1, (exs,))
        exr = P.op("gd", lambda e: e.dma_start(
            out=XBP[:], in_=io["xsm"][bass.ds(K.r_peersm, 128), 40:48]), deps=(ev_r1,))
        emk = P.op("ve", lambda e: e.tensor_scalar(XBP[:], XBP[:], K.sb["mprev_t"][:, 0:1],
                                                   None, AO.mult), deps=(exr,))
        K.fill()
        exb = P.op("ve", lambda e: e.tensor_copy(XBPB[:], XBP[:]), deps=(emk,))
        ev_bnd = None
        for i in range(NCH):
            ev_bnd = P.op("ve", lambda e, i=i: e.tensor_copy(
                FLATB[:, i * TE:i * TE + 1], XBP[:, i:i + 1]), deps=(emk, ev_flatb))

        wb = K.sbuf("B_wb", [128, 4 * NCH * 128], BF16, cscan)
        KEYPH = K.sbuf("B_KEYPH", [128, NCH * (CH + 1)], F32, cscan)
        GH = K.sbuf("B_GH", [128, NCH * CH], BF16, cscan)
        BH = K.sbuf("B_BH", [128, 4 * CH], BF16, cscan)
        VGR = K.sbuf("B_VGR", [1, CH], F32, cscan)
        VGB = K.sbuf("B_VGB", [128, CH], F32, cscan)
        t_a = K.sbuf("B_ta", [128, CH], F32, cscan)
        t_b = K.sbuf("B_tb", [128, CH], F32, cscan)
        t_c = K.sbuf("B_tc", [128, CH], F32, cscan)
        t_d = K.sbuf("B_td", [128, CH], F32, cscan)
        rowsc = K.sbuf("B_rowsc", [1, 3 * CH], F32, cscan)
        psA = K.psum("B_psA", [128, 2 * 512], F32, cscan)
        psR = K.psum("B_psR", [1, 2 * 512], F32, cscan)
        psB = K.psum("B_psB", [128, 1024], F32, cscan)

        ev_init = P.op("ve", lambda e: e.memset(CARR[:], 0.0))
        P.op("ve", lambda e: e.memset(GSR[:], 0.0))

        def stream_och(wsrc, o, slot, deps):
            return stream_w(K, wb[:, slot * NCH * 128:(slot + 1) * NCH * 128],
                            lambda: wsrc[:, o * 128:(o + 1) * 128], NCH, deps=deps)

        def frhs(i, c0_, width, shifted=False):
            base = i * TE + (0 if shifted else 1)
            return FLATB[:, base + c0_:base + c0_ + width]

        ev_kb = {}
        ev_sp_chunk = {}
        for cs in range(NC4):
            c0_ = cs * CH
            ev_o = {}
            for o in range(NCH):
                slot = o % 2
                ew = stream_och(io["Wk"], o, slot, deps=(ev_o.get(o - 2), ev_init,
                                                         ev_sp_chunk.get(cs - 1)))
                for s in range(2):
                    bank = s % 2

                    def kmm(e, o=o, s=s, slot=slot, bank=bank, c0_=c0_):
                        r = None
                        for i in range(NCH):
                            r = e.matmul(psA[:, bank * 512:(bank + 1) * 512],
                                         wb[:, slot * NCH * 128 + i * 128:
                                            slot * NCH * 128 + (i + 1) * 128],
                                         frhs(i, c0_ + s * 512, 512),
                                         start=(i == 0), stop=(i == NCH - 1))
                        return r
                    prevb = (ev_kb.get(o - 1) if cs == 0 and s == 0
                             else ev_o.get((o - 1, s)))
                    em = P.op("pe", kmm, deps=(ew, ev_flatb, ev_bnd, prevb))
                    ev_o[(o, s)] = P.op("sc", lambda e, o=o, s=s, bank=bank: e.activation(
                        KEYPH[:, o * (CH + 1) + 1 + s * 512:o * (CH + 1) + 1 + (s + 1) * 512],
                        psA[:, bank * 512:(bank + 1) * 512],
                        AF.Tanh, bias=K.bcol("bk", o), scale=1.0), deps=(em,))
                ev_o[o] = ev_o[(o, 1)]
                if cs == 0:
                    def kbm(e, o=o, slot=slot):
                        r = None
                        for i in range(NCH):
                            r = e.matmul(psA[:, 0:1],
                                         wb[:, slot * NCH * 128 + i * 128:
                                            slot * NCH * 128 + (i + 1) * 128],
                                         XBPB[:, i:i + 1],
                                         start=(i == 0), stop=(i == NCH - 1))
                        return r
                    ekb = P.op("pe", kbm, deps=(ev_o[(o, 1)], exb))
                    ev_kb[o] = P.op("sc", lambda e, o=o: e.activation(
                        KB[:, o:o + 1], psA[:, 0:1], AF.Tanh, bias=K.bcol("bk", o), scale=1.0),
                        deps=(ekb,))
            ev_kcol = None
            for o in range(NCH):
                dep = ev_kb.get(o) if cs == 0 else ev_kb.get((cs, o))
                ev_kcol = P.op("ve", lambda e, o=o: e.tensor_copy(
                    KEYPH[:, o * (CH + 1):o * (CH + 1) + 1], KB[:, o:o + 1]),
                    deps=(dep, ev_o[o]))
            for o in range(NCH):
                ev_kb[(cs + 1, o)] = P.op("ve", lambda e, o=o: e.tensor_copy(
                    KB[:, o:o + 1], KEYPH[:, o * (CH + 1) + CH:o * (CH + 1) + CH + 1]),
                    deps=(ev_o[o], ev_kcol))
            ev_keyph = ev_kb[(cs + 1, NCH - 1)]

            ev_g = {}
            for o in range(NCH):
                slot = o % 2
                ewg = stream_w(K, wb[:, (2 + slot) * NCH * 128:(3 + slot) * NCH * 128],
                               lambda o=o: io["Wg1"][0:D, o * 128:(o + 1) * 128], NCH,
                               deps=(ev_g.get(o - 2),))
                ewg2 = stream_w(K, wb[:, slot * NCH * 128:(slot + 1) * NCH * 128],
                                lambda o=o: io["Wg1"][D:2 * D, o * 128:(o + 1) * 128], NCH,
                                deps=(ev_g.get(o - 2), ev_keyph))
                for s in range(2):
                    bank = s % 2

                    def gmm(e, o=o, s=s, slot=slot, bank=bank, c0_=c0_):
                        r = None
                        for i in range(NCH):
                            r = e.matmul(psA[:, bank * 512:(bank + 1) * 512],
                                         wb[:, (2 + slot) * NCH * 128 + i * 128:
                                            (2 + slot) * NCH * 128 + (i + 1) * 128],
                                         frhs(i, c0_ + s * 512, 512),
                                         start=(i == 0), stop=False)
                        for i in range(NCH):
                            r = e.matmul(psA[:, bank * 512:(bank + 1) * 512],
                                         wb[:, slot * NCH * 128 + i * 128:
                                            slot * NCH * 128 + (i + 1) * 128],
                                         frhs(i, c0_ + s * 512, 512, shifted=True),
                                         start=False, stop=(i == NCH - 1))
                        return r
                    em = P.op("pe", gmm, deps=(ewg, ewg2, ev_bnd, ev_g.get((o - 1, s))))
                    ev_g[(o, s)] = P.op("sc", lambda e, o=o, s=s, bank=bank: e.activation(
                        GH[:, o * CH + s * 512:o * CH + (s + 1) * 512],
                        psA[:, bank * 512:(bank + 1) * 512],
                        AF.Gelu, bias=K.bcol("bg1", o), scale=1.0), deps=(em,))
                ev_g[o] = ev_g[(o, 1)]

            def vgmm(e, s):
                r = None
                for i in range(NCH):
                    r = e.matmul(psR[0:1, s * 512:(s + 1) * 512],
                                 K.sb["WG2C"][:, i:i + 1],
                                 GH[:, i * CH + s * 512:i * CH + (s + 1) * 512],
                                 start=(i == 0), stop=(i == NCH - 1))
                return r
            emv = P.op("pe", lambda e, vgmm=vgmm: [vgmm(e, 0), vgmm(e, 1)][-1],
                       deps=(ev_g[NCH - 1], K._ev_wg2))
            evg = P.op("sc", lambda e: e.activation(VGR[:], psR[0:1, 0:CH], AF.Sigmoid,
                                                    bias=K.sb["bb2g"][0:1, 0:1], scale=1.0),
                       deps=(emv, K._ev_bg2g))
            egs = P.op("ve", lambda e, c0_=c0_: e.tensor_tensor_scan(
                GSR[0:1, c0_:c0_ + CH], VGR[:], VGR[:], CARR[0:1, 32:33], AO.add, AO.bypass),
                deps=(evg, ev_init))
            K.fill()
            P.op("ve", lambda e, c0_=c0_: e.tensor_copy(
                CARR[0:1, 32:33], GSR[0:1, c0_ + CH - 1:c0_ + CH]), deps=(egs,))
            emb = P.op("pe", lambda e: [e.matmul(psB[:, s * 512:(s + 1) * 512],
                                                 K.sb["ones1f"][:],
                                                 VGR[0:1, s * 512:(s + 1) * 512],
                                                 start=True, stop=True) for s in range(2)][-1],
                       deps=(evg,))
            evgb = P.op("sc", lambda e: e.copy(out=VGB[:], in_=psB[:, 0:CH]), deps=(emb,))

            ev_bh = {}
            for o in range(4):
                slot = o % 2
                ewb = stream_och(io["Wb1"], o, slot, deps=(ev_bh.get(o - 2), evgb))
                for s in range(2):
                    bank = s % 2

                    def bmm(e, o=o, s=s, slot=slot, bank=bank, c0_=c0_):
                        r = None
                        for i in range(NCH):
                            r = e.matmul(psA[:, bank * 512:(bank + 1) * 512],
                                         wb[:, slot * NCH * 128 + i * 128:
                                            slot * NCH * 128 + (i + 1) * 128],
                                         frhs(i, c0_ + s * 512, 512),
                                         start=(i == 0), stop=(i == NCH - 1))
                        return r
                    em = P.op("pe", bmm, deps=(ewb, ev_bh.get((o - 1, s))))
                    ev_bh[(o, s)] = P.op("sc", lambda e, o=o, s=s, bank=bank: e.activation(
                        BH[:, o * CH + s * 512:o * CH + (s + 1) * 512],
                        psA[:, bank * 512:(bank + 1) * 512],
                        AF.Gelu, bias=K.bcol("bb1", o), scale=1.0), deps=(em,))
                ev_bh[o] = ev_bh[(o, 1)]

            def bwmm(e, colx, s):
                r = None
                for i in range(4):
                    r = e.matmul(psR[0:1, s * 512:(s + 1) * 512],
                                 K.sb["WB2C"][:, colx * 4 + i:colx * 4 + i + 1],
                                 BH[:, i * CH + s * 512:i * CH + (s + 1) * 512],
                                 start=(i == 0), stop=(i == 3))
                return r
            emw0 = P.op("pe", lambda e, bwmm=bwmm: [bwmm(e, 0, 0), bwmm(e, 0, 1)][-1],
                        deps=(ev_bh[3], K._ev_wb2, evg, evgb))
            ecp0 = P.op("ve", lambda e: e.tensor_copy(rowsc[0:1, 0:CH], psR[0:1, 0:CH]),
                        deps=(emw0,))
            emw1 = P.op("pe", lambda e, bwmm=bwmm: [bwmm(e, 1, 0), bwmm(e, 1, 1)][-1],
                        deps=(ecp0,))
            ecp1 = P.op("ve", lambda e: e.tensor_copy(rowsc[0:1, CH:2 * CH], psR[0:1, 0:CH]),
                        deps=(emw1,))
            ed = P.op("ve", lambda e: e.tensor_tensor(
                rowsc[0:1, 2 * CH:3 * CH], rowsc[0:1, 0:CH], rowsc[0:1, CH:2 * CH],
                AO.subtract), deps=(ecp1,))
            P.op("sc", lambda e, c0_=c0_: e.activation(
                BW0[0:1, c0_:c0_ + CH], rowsc[0:1, 2 * CH:3 * CH], AF.Sigmoid,
                bias=K.sb["bb2diff"][0:1, 0:1], scale=1.0), deps=(ed, K._ev_bb2))
            P.op("sc", lambda e, c0_=c0_: e.activation(
                BW1[0:1, c0_:c0_ + CH], rowsc[0:1, 2 * CH:3 * CH], AF.Sigmoid,
                bias=K.sb["bb2diff"][0:1, 1:2], scale=-1.0), deps=(ed, K._ev_bb2))
            ev_bw = P.ev("sc")

            ev_sp = None
            for i in range(NCH):
                slot = i % 2
                ewp = stream_och(io["Wpv"], i, slot, deps=(ev_sp, ev_bw))

                def pvm2(e, i=i, slot=slot, c0_=c0_):
                    r = None
                    for s in range(2):
                        for j in range(NCH):
                            r = e.matmul(psA[:, (s % 2) * 512:((s % 2) + 1) * 512],
                                         wb[:, slot * NCH * 128 + j * 128:
                                            slot * NCH * 128 + (j + 1) * 128],
                                         frhs(j, c0_ + s * 512, 512),
                                         start=(j == 0), stop=(j == NCH - 1))
                    return r
                em = P.op("pe", pvm2, deps=(ewp, ev_sp))
                epv = P.op("sc", lambda e, i=i: [
                    e.activation(t_a[:, s * 512:(s + 1) * 512],
                                 psA[:, (s % 2) * 512:((s % 2) + 1) * 512],
                                 AF.Identity, bias=K.bcol("bpv", i), scale=1.0)
                    for s in range(2)][-1], deps=(em,))
                epp = P.op("sy", lambda e, i=i, c0_=c0_: e.dma_start(
                    out=t_b[:], in_=io["ppT"][i * 128:(i + 1) * 128, c0_:c0_ + CH]),
                    deps=(ev_sp,))
                P.op("ve", lambda e: e.tensor_scalar(t_c[:], t_b[:], 1.0 / (2 * PI),
                                                     MAGIC, AO.mult, AO.add), deps=(epp, ev_sp))
                P.op("ve", lambda e: e.tensor_scalar(t_c[:], t_c[:], MAGIC, None, AO.subtract))
                P.op("ve", lambda e: e.scalar_tensor_tensor(t_b[:], t_c[:], -2.0 * PI, t_b[:],
                                                            AO.mult, AO.add))
                esin = P.op("sc", lambda e: e.activation(t_c[:], t_b[:], AF.Sin),
                            deps=(P.ev("ve"), epv))
                eabs = P.op("ve", lambda e: e.scalar_tensor_tensor(t_b[:], t_b[:], -1.0, t_b[:],
                                                                   AO.mult, AO.max),
                            deps=(esin,))
                ecos = P.op("sc", lambda e: e.activation(t_b[:], t_b[:], AF.Sin,
                                                         bias=0.5 * PI, scale=-1.0),
                            deps=(eabs,))
                P.op("ve", lambda e: e.tensor_tensor(t_d[:], t_b[:], t_a[:], AO.mult),
                     deps=(ecos, epv))
                P.op("ve", lambda e, i=i: e.tensor_tensor_scan(
                    t_d[:], t_d[:], t_d[:], CARR[:, 0 * 8 + i:0 * 8 + i + 1],
                    AO.add, AO.bypass))
                K.fill()
                P.op("ve", lambda e, i=i: e.tensor_copy(CARR[:, 0 * 8 + i:0 * 8 + i + 1],
                                                        t_d[:, CH - 1:CH]))
                es1 = P.op("sy", lambda e, i=i, c0_=c0_: e.dma_start(
                    out=io["MEMS"][0 * D + i * 128:0 * D + (i + 1) * 128, c0_:c0_ + CH],
                    in_=t_d[:]), deps=(P.ev("ve"),))
                P.op("ve", lambda e: e.tensor_tensor(t_d[:], t_c[:], t_a[:], AO.mult),
                     deps=(esin, es1))
                P.op("ve", lambda e, i=i: e.tensor_tensor_scan(
                    t_d[:], t_d[:], t_d[:], CARR[:, 1 * 8 + i:1 * 8 + i + 1],
                    AO.add, AO.bypass))
                K.fill()
                P.op("ve", lambda e, i=i: e.tensor_copy(CARR[:, 1 * 8 + i:1 * 8 + i + 1],
                                                        t_d[:, CH - 1:CH]))
                es2 = P.op("sy", lambda e, i=i, c0_=c0_: e.dma_start(
                    out=io["MEMS"][1 * D + i * 128:1 * D + (i + 1) * 128, c0_:c0_ + CH],
                    in_=t_d[:]), deps=(P.ev("ve"),))
                ewk = stream_w(K, wb[:, (2 + slot) * NCH * 128:(3 + slot) * NCH * 128],
                               lambda i=i: io["Wkv"][:, i * 128:(i + 1) * 128], NCH,
                               deps=(ev_sp, ev_g[NCH - 1]))

                def kvm2(e, i=i, slot=slot, c0_=c0_):
                    r = None
                    for s in range(2):
                        for j in range(NCH):
                            r = e.matmul(psA[:, (s % 2) * 512:((s % 2) + 1) * 512],
                                         wb[:, (2 + slot) * NCH * 128 + j * 128:
                                            (2 + slot) * NCH * 128 + (j + 1) * 128],
                                         frhs(j, c0_ + s * 512, 512),
                                         start=(j == 0), stop=(j == NCH - 1))
                    return r
                emk2_ = P.op("pe", kvm2, deps=(ewk, epv))
                ekv = P.op("sc", lambda e, i=i: [
                    e.activation(t_a[:, s * 512:(s + 1) * 512],
                                 psA[:, (s % 2) * 512:((s % 2) + 1) * 512],
                                 AF.Identity, bias=K.bcol("bkv", i), scale=1.0)
                    for s in range(2)][-1], deps=(emk2_, P.ev("ve")))
                P.op("ve", lambda e: e.tensor_tensor(t_a[:], t_a[:], VGB[:], AO.mult),
                     deps=(ekv, evgb))
                ksh = KEYPH[:, i * (CH + 1):i * (CH + 1) + CH]
                esn = P.op("sc", lambda e, ksh=ksh: e.activation(t_b[:], ksh, AF.Sin, scale=PI),
                           deps=(ev_keyph, P.ev("ve")))
                eab = P.op("ve", lambda e, ksh=ksh: e.scalar_tensor_tensor(
                    t_c[:], ksh, -1.0, ksh, AO.mult, AO.max), deps=(ev_keyph, esn))
                ecn = P.op("sc", lambda e: e.activation(t_c[:], t_c[:], AF.Sin,
                                                        bias=0.5 * PI, scale=-PI), deps=(eab,))
                if cs == 0:
                    P.op("ve", lambda e: e.tensor_scalar(
                        t_b[:, 0:1], t_b[:, 0:1], K.sb["mprev_t"][:, 0:1], None, AO.mult),
                        deps=(esn,))
                    ecn = P.op("ve", lambda e: e.tensor_scalar(
                        t_c[:, 0:1], t_c[:, 0:1], K.sb["mprev_t"][:, 0:1], None, AO.mult),
                        deps=(ecn,))
                    K.fill()
                P.op("ve", lambda e: e.tensor_tensor(t_d[:], t_a[:], t_c[:], AO.mult),
                     deps=(ecn,))
                P.op("ve", lambda e, i=i: e.tensor_tensor_scan(
                    t_d[:], t_d[:], t_d[:], CARR[:, 2 * 8 + i:2 * 8 + i + 1],
                    AO.add, AO.bypass))
                K.fill()
                P.op("ve", lambda e, i=i: e.tensor_copy(CARR[:, 2 * 8 + i:2 * 8 + i + 1],
                                                        t_d[:, CH - 1:CH]))
                es3 = P.op("sy", lambda e, i=i, c0_=c0_: e.dma_start(
                    out=io["MEMS"][2 * D + i * 128:2 * D + (i + 1) * 128, c0_:c0_ + CH],
                    in_=t_d[:]), deps=(P.ev("ve"),))
                P.op("ve", lambda e: e.tensor_tensor(t_d[:], t_a[:], t_b[:], AO.mult),
                     deps=(esn, es3))
                P.op("ve", lambda e, i=i: e.tensor_tensor_scan(
                    t_d[:], t_d[:], t_d[:], CARR[:, 3 * 8 + i:3 * 8 + i + 1],
                    AO.add, AO.bypass))
                K.fill()
                P.op("ve", lambda e, i=i: e.tensor_copy(CARR[:, 3 * 8 + i:3 * 8 + i + 1],
                                                        t_d[:, CH - 1:CH]))
                es4 = P.op("sy", lambda e, i=i, c0_=c0_: e.dma_start(
                    out=io["MEMS"][3 * D + i * 128:3 * D + (i + 1) * 128, c0_:c0_ + CH],
                    in_=t_d[:]), deps=(P.ev("ve"),))
                ev_sp = P.op("sy", lambda e, i=i, c0_=c0_: e.dma_start(
                    out=io["MEMS"][4 * D + i * 128:4 * D + (i + 1) * 128, c0_:c0_ + CH],
                    in_=KEYPH[:, i * (CH + 1) + 1:i * (CH + 1) + 1 + CH]),
                    deps=(ev_keyph, es4))
            ev_sp_chunk[cs] = ev_sp
        P.barrier()

        ex2 = P.op("gd", lambda e: e.dma_start(
            out=io["xsm"][bass.ds(K.r_mysm, 128), 0:34], in_=CARR[:]),
            deps=(ev_sp_chunk[NC4 - 1], P.ev("ve")))
        ev_r2 = flag_and_poll(K, 2, (ex2,))
        ex2r = P.op("gd", lambda e: e.dma_start(
            out=PCAR[:], in_=io["xsm"][bass.ds(K.r_peersm, 128), 0:34]), deps=(ev_r2,))
        emc = P.op("ve", lambda e: e.tensor_scalar(PCAR[:], PCAR[:], K.sb["mprev_t"][:, 0:1],
                                                   None, AO.mult), deps=(ex2r,))
        P.barrier()
        cscan.close()

        cret = ctx.enter_context(ExitStack())
        BLD = K.sbuf("B_BLD", [128, NCH * 1024], F32, cret)
        BLN = K.sbuf("B_BLN", [128, NCH * 1024], BF16, cret)
        rt_a = K.sbuf("B_rta", [128, 1024], F32, cret)
        rt_b = K.sbuf("B_rtb", [128, 1024], F32, cret)
        rt_c = K.sbuf("B_rtc", [128, 1024], F32, cret)
        rt_d = K.sbuf("B_rtd", [128, 1024], F32, cret)
        rwb = K.sbuf("B_rwb", [128, 2 * NCH * 128], BF16, cret)
        rpsA = K.psum("B_rpsA", [128, 2 * 512], F32, cret)
        rpsB = K.psum("B_rpsB", [128, 1024], F32, cret)

        for cs in range(NC4):
            c0_ = cs * CH
            with ExitStack() as cr:
                g = K.sbuf(f"B_g{cs}", [1, CH], F32, cr)
                P.op("ve", lambda e, c0_=c0_, g=g: e.tensor_scalar(
                    g[:], GSR[0:1, c0_:c0_ + CH], PCAR[0:1, 32:33], None, AO.add),
                    deps=(emc,))
                P.op("ve", lambda e, g=g: e.tensor_scalar(g[:], g[:], 1.0, None, AO.max))
                esq = P.op("sc", lambda e, g=g: e.activation(g[:], g[:], AF.Sqrt, bias=1e-12,
                                                             scale=1.0), deps=(P.ev("ve"),))
                P.op("ve", lambda e, g=g: e.reciprocal(g[:], g[:]), deps=(esq,))
                erg = P.op("ve", lambda e, g=g: e.tensor_scalar(g[:], g[:], 1.0 / 32.0, None,
                                                                AO.mult))
                b0r = K.sbuf(f"B_b0r{cs}", [1, 2 * CH], F32, cr)
                P.op("ve", lambda e, c0_=c0_, b0r=b0r: e.tensor_scalar(
                    b0r[:, 0:CH], BW0[0:1, c0_:c0_ + CH], 1.0 / 32.0, None, AO.mult))
                ebwr = P.op("ve", lambda e, c0_=c0_, b0r=b0r: e.tensor_copy(
                    b0r[:, CH:2 * CH], BW1[0:1, c0_:c0_ + CH]))
                (grorwb, bw0b, bw1b), ev_brwb = bcast_rows(
                    K, cr, f"Br{cs}", [(g[:], erg), (b0r[:, 0:CH], ebwr),
                                       (b0r[:, CH:2 * CH], ebwr)], CH, ps=rpsB)
                (m3cb, r3cb), ev_b3 = bcast_dram_rows(
                    K, cr, f"Bc{cs}", [4, 5], CH, c0_, ps=rpsB, deps=(ev_brwb,))

                ev_t = None
                for i in range(NCH):
                    e1_ = P.op("sy", lambda e, i=i, c0_=c0_: e.dma_start(
                        out=rt_a[:], in_=io["MEMS"][0 * D + i * 128:0 * D + (i + 1) * 128,
                                                   c0_:c0_ + CH]), deps=(ev_t,))
                    e2_ = P.op("sy", lambda e, i=i, c0_=c0_: e.dma_start(
                        out=rt_b[:], in_=io["MEMS"][1 * D + i * 128:1 * D + (i + 1) * 128,
                                                   c0_:c0_ + CH]))
                    epp = P.op("sy", lambda e, i=i, c0_=c0_: e.dma_start(
                        out=rt_c[:], in_=io["ppT"][i * 128:(i + 1) * 128, c0_:c0_ + CH]))
                    P.op("ve", lambda e: e.tensor_scalar(rt_d[:], rt_c[:], 1.0 / (2 * PI),
                                                         MAGIC, AO.mult, AO.add),
                         deps=(epp, ev_t))
                    P.op("ve", lambda e: e.tensor_scalar(rt_d[:], rt_d[:], MAGIC, None,
                                                         AO.subtract))
                    P.op("ve", lambda e: e.scalar_tensor_tensor(rt_c[:], rt_d[:], -2.0 * PI,
                                                                rt_c[:], AO.mult, AO.add))
                    esin = P.op("sc", lambda e: e.activation(rt_d[:], rt_c[:], AF.Sin),
                                deps=(P.ev("ve"),))
                    eabs = P.op("ve", lambda e: e.scalar_tensor_tensor(
                        rt_c[:], rt_c[:], -1.0, rt_c[:], AO.mult, AO.max), deps=(esin,))
                    ecos = P.op("sc", lambda e: e.activation(rt_c[:], rt_c[:], AF.Sin,
                                                             bias=0.5 * PI, scale=-1.0),
                                deps=(eabs,))
                    P.op("ve", lambda e, i=i: e.scalar_tensor_tensor(
                        rt_a[:], rt_a[:], PCAR[:, 0 * 8 + i:0 * 8 + i + 1], rt_c[:],
                        AO.add, AO.mult), deps=(e1_, ecos, emc))
                    P.op("ve", lambda e, i=i: e.scalar_tensor_tensor(
                        rt_b[:], rt_b[:], PCAR[:, 1 * 8 + i:1 * 8 + i + 1], rt_d[:],
                        AO.add, AO.mult), deps=(e2_, esin))
                    P.op("ve", lambda e: e.tensor_tensor(rt_a[:], rt_a[:], rt_b[:], AO.add))
                    eposp = P.op("ve", lambda e, i=i, bw0b=bw0b: e.tensor_tensor(
                        BLD[:, i * CH:(i + 1) * CH], rt_a[:], bw0b, AO.mult), deps=(ev_brwb,))
                    e3_ = P.op("sy", lambda e, i=i, c0_=c0_: e.dma_start(
                        out=rt_a[:], in_=io["MEMS"][2 * D + i * 128:2 * D + (i + 1) * 128,
                                                   c0_:c0_ + CH]), deps=(eposp,))
                    e4_ = P.op("sy", lambda e, i=i, c0_=c0_: e.dma_start(
                        out=rt_b[:], in_=io["MEMS"][3 * D + i * 128:3 * D + (i + 1) * 128,
                                                   c0_:c0_ + CH]))
                    e5_ = P.op("sy", lambda e, i=i, c0_=c0_: e.dma_start(
                        out=rt_d[:], in_=io["MEMS"][4 * D + i * 128:4 * D + (i + 1) * 128,
                                                   c0_:c0_ + CH]))
                    esn2 = P.op("sc", lambda e: e.activation(rt_c[:], rt_d[:], AF.Sin, scale=PI),
                                deps=(e5_, eposp))
                    eab2 = P.op("ve", lambda e: e.scalar_tensor_tensor(
                        rt_d[:], rt_d[:], -1.0, rt_d[:], AO.mult, AO.max), deps=(esn2,))
                    ecs2 = P.op("sc", lambda e: e.activation(rt_d[:], rt_d[:], AF.Sin,
                                                             bias=0.5 * PI, scale=-PI),
                                deps=(eab2,))
                    P.op("ve", lambda e, i=i: e.scalar_tensor_tensor(
                        rt_a[:], rt_a[:], PCAR[:, 2 * 8 + i:2 * 8 + i + 1], rt_d[:],
                        AO.add, AO.mult), deps=(e3_, ecs2))
                    P.op("ve", lambda e, i=i: e.scalar_tensor_tensor(
                        rt_b[:], rt_b[:], PCAR[:, 3 * 8 + i:3 * 8 + i + 1], rt_c[:],
                        AO.add, AO.mult), deps=(e4_,))
                    P.op("ve", lambda e: e.tensor_tensor(rt_a[:], rt_a[:], rt_b[:], AO.add))
                    P.op("ve", lambda e, grorwb=grorwb: e.tensor_tensor(
                        rt_a[:], rt_a[:], grorwb, AO.mult))
                    P.op("ve", lambda e, bw1b=bw1b: e.tensor_tensor(
                        rt_a[:], rt_a[:], bw1b, AO.mult))
                    ev_t = P.op("ve", lambda e, i=i: e.tensor_tensor(
                        BLD[:, i * CH:(i + 1) * CH], BLD[:, i * CH:(i + 1) * CH], rt_a[:],
                        AO.add))
                ev_bld = ev_t

                def srcb(i):
                    return BLD[:, i * CH:(i + 1) * CH], ev_bld
                mu4, r4, _v4, ev_st4 = ln_stats(K, cr, f"B{cs}", srcb, NCH, CH, 1e-5)
                (m4b, r4b), ev_bc4 = bcast_rows(K, cr, f"B{cs}b",
                                                [(mu4, ev_st4), (r4, ev_st4)], CH, ps=rpsB)
                evn = None
                for i in range(NCH):
                    P.op("ve", lambda e, i=i, m4b=m4b: e.tensor_tensor(
                        BLD[:, i * CH:(i + 1) * CH], BLD[:, i * CH:(i + 1) * CH], m4b,
                        AO.subtract), deps=(ev_bc4, evn))
                    P.op("ve", lambda e, i=i, r4b=r4b: e.tensor_tensor(
                        BLD[:, i * CH:(i + 1) * CH], BLD[:, i * CH:(i + 1) * CH], r4b,
                        AO.mult))
                    evn = P.op("ve", lambda e, i=i: e.tensor_scalar(
                        BLN[:, i * CH:(i + 1) * CH], BLD[:, i * CH:(i + 1) * CH],
                        K.bcol("bln_g", i), K.bcol("bln_b", i), AO.mult, AO.add))
                ev_bln = evn

                ev_o = {}
                for o in range(NCH):
                    slot = o % 2
                    ewo = stream_w(K, rwb[:, slot * NCH * 128:(slot + 1) * NCH * 128],
                                   lambda o=o: io["Wo"][:, o * 128:(o + 1) * 128], NCH,
                                   deps=(ev_o.get(o - 2), ev_bln))
                    exl = P.op("sy", lambda e, o=o, c0_=c0_: e.dma_start(
                        out=rt_c[:], in_=io["XRES"][o * 128:(o + 1) * 128, c0_:c0_ + CH]),
                        deps=(ev_o.get(o - 1), ev_bld))
                    P.op("ve", lambda e, m3cb=m3cb: e.tensor_tensor(
                        rt_d[:], rt_c[:], m3cb, AO.subtract), deps=(exl, ev_b3))
                    ezz = P.op("ve", lambda e, r3cb=r3cb: e.tensor_tensor(
                        rt_d[:], rt_d[:], r3cb, AO.mult))

                    def omm(e, o=o, slot=slot):
                        r = None
                        for s in range(2):
                            for i in range(NCH):
                                r = e.matmul(rpsA[:, (s % 2) * 512:((s % 2) + 1) * 512],
                                             rwb[:, slot * NCH * 128 + i * 128:
                                                slot * NCH * 128 + (i + 1) * 128],
                                             BLN[:, i * CH + s * 512:i * CH + (s + 1) * 512],
                                             start=(i == 0), stop=(i == NCH - 1))
                        return r
                    emo = P.op("pe", omm, deps=(ewo, ev_bln, ev_o.get((o - 1, "m"))))
                    P.op("ve", lambda e, o=o: [
                        e.scalar_tensor_tensor(rt_d[:, s * 512:(s + 1) * 512],
                                               rpsA[:, (s % 2) * 512:((s % 2) + 1) * 512],
                                               K.bcol("bo", o), rt_d[:, s * 512:(s + 1) * 512],
                                               AO.add, AO.add) for s in range(2)][-1],
                        deps=(emo, ezz))
                    eadd = P.op("ve", lambda e: e.tensor_tensor(rt_d[:], rt_d[:], rt_c[:], AO.add))
                    ev_o[(o, "m")] = eadd
                    ev_o[o] = P.op("sy", lambda e, o=o, c0_=c0_: e.dma_start(
                        out=io["XRES"][o * 128:(o + 1) * 128, c0_:c0_ + CH], in_=rt_d[:]),
                        deps=(eadd,))
                P.barrier()
        cret.close()
        P.barrier()


import os
PHASES = tuple(os.environ.get("KPHASES", "A,F1,B,F2").split(","))


def build_body(K):
    nc, P, io = K.nc, K.P, K.io
    K._ev_cwd = {}
    setup(K)
    bb2g = K.sbuf("bb2g", [1, 1], F32)
    K._ev_bg2g = P.op("sy", lambda e: e.dma_start(out=bb2g[:], in_=K.io["bg2v"]))
    P.barrier()
    def dbg_snap(nm):
        if not os.environ.get("KPHDBG"):
            return
        P.barrier()
        ev = P.op("sy", lambda e, nm=nm: e.dma_start(out=io[nm], in_=io["XRES"]))
        P.op("ve", None, deps=(ev,), inc=0)
        P.barrier()

    if "A" in PHASES:
        phase_attn_a(K)
        dbg_snap("dbgA")
    if "F1" in PHASES:
        phase_ffn(K, 1)
        dbg_snap("dbgF1")
        if os.environ.get("KDBG"):
            P.barrier()
            e1 = P.op("sy", lambda e: e.dma_start(out=io["dbgCWD"], in_=io["CWD"]))
            e2 = P.op("sy", lambda e: e.dma_start(out=io["dbgOUT"],
                                                  in_=io["MEMS"][0:D, :]))
            P.op("ve", None, deps=(e1, e2), inc=0)
            P.barrier()
    if "B" in PHASES:
        phase_attn_b(K)
        dbg_snap("dbgB")
    if "F2" in PHASES:
        phase_ffn(K, 2)
    if "F2" not in PHASES:
        fin = K.sbuf("fin_dbg", [128, T], F32)
        ev = None
        for i in range(NCH):
            e1 = P.op("sy", lambda e, i=i: e.dma_start(
                out=fin[:], in_=io["XRES"][i * 128:(i + 1) * 128, :]), deps=(ev,))
            ev = P.op("sy", lambda e, i=i: e.dma_start(
                out=io["out"][i * 128:(i + 1) * 128, :], in_=fin[:]), deps=(e1,))
        P.op("ve", None, deps=(ev,), inc=0)
    P.barrier()


def host_bias_table(inp):
    tbl = np.zeros((BTOT, 128), np.float32)

    def put(name, vec):
        v = np.asarray(vec, np.float32).reshape(-1)
        n = (v.size + 127) // 128
        pad = np.zeros(n * 128, np.float32)
        pad[:v.size] = v
        tbl[BROWS[name]:BROWS[name] + n] = pad.reshape(n, 128)
    put("aln_w", inp["L0_attn_a_ln_w"]); put("aln_b", inp["L0_attn_a_ln_b"])
    put("bqkv", inp["L0_attn_a_b_qkv"]); put("bproj", inp["L0_attn_a_b_proj"])
    put("rb", inp["L0_ffn_router_b"])
    put("fb1", inp["L0_ffn_b1"]); put("fb2", inp["L0_ffn_b2"])
    put("fln_w", inp["L0_ffn_ln_w"]); put("fln_b", inp["L0_ffn_ln_b"])
    put("bpv", inp["L0_attn_b_bpv"]); put("bk", inp["L0_attn_b_bk"])
    put("bkv", inp["L0_attn_b_bkv"]); put("bg1", inp["L0_attn_b_bg1"])
    put("bb1", inp["L0_attn_b_bb1"])
    put("bmisc", np.concatenate([np.asarray(inp["L0_attn_b_bg2"], np.float32).reshape(-1),
                                 np.asarray(inp["L0_attn_b_bb2"], np.float32).reshape(-1)]))
    put("bln_g", inp["L0_attn_b_ln_g"]); put("bln_b", inp["L0_attn_b_ln_b"])
    put("bo", inp["L0_attn_b_bo"])
    return tbl


_NC_CACHE = {}


def build_program():
    if "nc" in _NC_CACHE:
        return _NC_CACHE["nc"]
    nc = bass.Bass("TRN2", target_bir_lowering=False, detect_race_conditions=False,
                   num_devices=8)
    io = {}

    def din(name, shape, dt=F32):
        io[name] = nc.dram_tensor(name, list(shape), dt, kind="ExternalInput").ap()
    din("xT", [D, T])
    din("Wqkv", [D, 3 * D]); din("Wproj", [D, D])
    din("ppT", [D, T])
    din("Wpv", [D, D]); din("Wk", [D, D]); din("Wkv", [D, D])
    din("Wg1", [2 * D, D]); din("Wg2", [D, 1]); din("Wb1", [D, 512]); din("Wb2", [512, 2])
    din("Wo", [D, D])
    din("w1T", [NE, D, FF]); din("w2T", [NE, FF, D]); din("rwT", [D, NE])
    din("btab", [BTOT, 128])
    din("rbrow", [1, 8])
    din("b2m", [NE, D])
    din("bb2d", [1, 2])
    din("bg2v", [1, 1])
    din("slot", [1, 8], I32)
    din("nonces", [N_ROUNDS, 16], I32)
    din("mprev", [128, 1])
    io["out"] = nc.dram_tensor("out", [D, T], F32, kind="ExternalOutput").ap()
    io["XRES"] = nc.dram_tensor("XRES", [D, T], F32, kind="Internal").ap()
    io["MSAR"] = nc.dram_tensor("MSAR", [D, T], BF16, kind="Internal").ap()
    io["DEN"] = nc.dram_tensor("DEN", [16, T], F32, kind="Internal").ap()
    io["CWD"] = nc.dram_tensor("CWD", [NE, T], BF16, kind="Internal").ap()
    io["ROWSD"] = nc.dram_tensor("ROWSD", [8, T], F32, kind="Internal").ap()
    io["MEMS"] = nc.dram_tensor("MEMS", [5 * D, T], F32, kind="Internal").ap()
    io["KVSTG"] = nc.dram_tensor("KVSTG", [2 * D, T], BF16, kind="Internal").ap()
    io["xkv"] = nc.dram_tensor("xkv", [2 * 2 * D, T], BF16, kind="Internal",
                               addr_space="Shared").ap()
    io["xsm"] = nc.dram_tensor("xsm", [256, 48], F32, kind="Internal",
                               addr_space="Shared").ap()
    io["xfl"] = nc.dram_tensor("xfl", [2, 16], I32, kind="Internal",
                               addr_space="Shared").ap()
    if os.environ.get("KPHDBG"):
        for nm in ("dbgA", "dbgF1", "dbgB"):
            io[nm] = nc.dram_tensor(nm, [D, T], F32, kind="ExternalOutput").ap()
    if os.environ.get("KDBG"):
        io["dbgCWD"] = nc.dram_tensor("dbgCWD", [NE, T], BF16, kind="ExternalOutput").ap()
        io["dbgOUT"] = nc.dram_tensor("dbgOUT", [D, T], F32, kind="ExternalOutput").ap()

    with ExitStack() as ctx:
        P = Prog(nc, ctx)
        K = Kern(nc, ctx, P, io)
        build_body(K)
        block = ctx.enter_context(nc.Block())
        nbad = P.check_closures()
        assert nbad == 0, f"{nbad} lambdas with drifted closures (late-binding bug)"
        P.emit(block)
    _NC_CACHE["nc"] = nc
    _NC_CACHE["P"] = P
    return nc


def make_inmaps(inputs):
    x = np.asarray(inputs["L0_attn_a_z"], np.float32)
    pp = np.asarray(inputs["L0_attn_b_pos_phases"], np.float32)[:L]
    w1 = np.asarray(inputs["L0_ffn_w1"], np.float32)
    w2 = np.asarray(inputs["L0_ffn_w2"], np.float32)
    rw = np.asarray(inputs["L0_ffn_router_w"], np.float32)
    btab = host_bias_table(inputs)
    w1T = np.ascontiguousarray(np.transpose(w1, (0, 2, 1)))
    w2T = np.ascontiguousarray(np.transpose(w2, (0, 2, 1)))
    rwT = np.ascontiguousarray(rw.T)
    rng = np.random.default_rng()
    nonces = rng.integers(1, 2**31 - 1, size=(N_ROUNDS, 16)).astype(np.int32)
    bb2 = np.asarray(inputs["L0_attn_b_bb2"], np.float32).reshape(-1)
    bb2d = np.array([[bb2[0] - bb2[1], bb2[1] - bb2[0]]], np.float32)
    rbrow = np.asarray(inputs["L0_ffn_router_b"], np.float32).reshape(1, 8)
    bg2v = np.asarray(inputs["L0_attn_b_bg2"], np.float32).reshape(1, 1)

    common = {
        "Wqkv": np.ascontiguousarray(inputs["L0_attn_a_W_qkv"], dtype=np.float32),
        "Wproj": np.ascontiguousarray(inputs["L0_attn_a_W_proj"], dtype=np.float32),
        "Wpv": np.ascontiguousarray(inputs["L0_attn_b_Wpv"], dtype=np.float32),
        "Wk": np.ascontiguousarray(inputs["L0_attn_b_Wk"], dtype=np.float32),
        "Wkv": np.ascontiguousarray(inputs["L0_attn_b_Wkv"], dtype=np.float32),
        "Wg1": np.ascontiguousarray(inputs["L0_attn_b_Wg1"], dtype=np.float32),
        "Wg2": np.ascontiguousarray(inputs["L0_attn_b_Wg2"], dtype=np.float32),
        "Wb1": np.ascontiguousarray(inputs["L0_attn_b_Wb1"], dtype=np.float32),
        "Wb2": np.ascontiguousarray(inputs["L0_attn_b_Wb2"], dtype=np.float32),
        "Wo": np.ascontiguousarray(inputs["L0_attn_b_Wo"], dtype=np.float32),
        "w1T": w1T, "w2T": w2T, "rwT": rwT, "btab": btab,
        "rbrow": rbrow, "bb2d": bb2d, "bg2v": bg2v, "nonces": nonces,
        "b2m": np.ascontiguousarray(inputs["L0_ffn_b2"], dtype=np.float32),
    }
    in_maps = []
    for c in range(8):
        b = (c // 2) % 2
        h = c % 2
        m = dict(common)
        m["xT"] = np.ascontiguousarray(x[b, h * T:(h + 1) * T, :].T)
        m["ppT"] = np.ascontiguousarray(pp[h * T:(h + 1) * T, :].T)
        m["slot"] = np.array([[h * 2048, h, h * 128, (1 - h) * 128, 0, 0, 0, 0]], np.int32)
        m["mprev"] = np.full((128, 1), float(h), np.float32)
        in_maps.append(m)
    return in_maps


def kernel(**inputs):
    nc = build_program()
    in_maps = make_inmaps(inputs)
    res = run_bass_kernel_spmd(nc, in_maps, core_ids=list(range(8)))
    out = np.empty((2, L, D), np.float32)
    for c in range(4):
        b = c // 2
        h = c % 2
        out[b, h * T:(h + 1) * T, :] = np.asarray(res.results[c]["out"]).T
    return out


# version 04
# speedup vs baseline: 1.2411x; 1.2411x over previous
import numpy as np
import ml_dtypes
import concourse.bass as bass
import concourse.mybir as mybir
from concourse.alu_op_type import AluOpType
from concourse.bass_utils import run_bass_kernel_spmd
from contextlib import ExitStack

F32 = mybir.dt.float32
BF16 = mybir.dt.bfloat16
I32 = mybir.dt.int32
AF = mybir.ActivationFunctionType
AO = AluOpType
AX = mybir.AxisListType
PI = float(np.pi)
MAGIC = 12582912.0

D = 1024
NCH = 8
T = 4096
L = 8192
FF = 4096
NE = 8
NPAIR = 8
N_ROUNDS = 4
NS = T // 512

ENGS = ("sy", "sc", "ve", "pe", "gp", "gd")
BLOCK_OF = {"sy": "sync", "sc": "scalar", "ve": "vector", "pe": "tensor",
            "gp": "gpsimd", "gd": "gpsimd"}
INC_OF = {"sy": 16, "sc": 1, "ve": 1, "pe": 1, "gp": 1, "gd": 16}
ENG_OF_BLOCK = {"sync": "sy", "scalar": "sc", "vector": "ve", "tensor": "pe",
                "gpsimd": "gp"}


class Prog:
    def __init__(self, nc, ctx):
        self.nc = nc
        self.sems = {n: ctx.enter_context(nc.semaphore("S_" + n)) for n in ENGS}
        self.count = {n: 0 for n in ENGS}
        self.blocks = {b: [] for b in ("sync", "scalar", "vector", "tensor", "gpsimd")}
        self.last_wait = {b: {} for b in self.blocks}
        self.snaps = []

    def op(self, eng, fn, deps=(), inc=None):
        blk = BLOCK_OF[eng]
        if inc is None:
            inc = INC_OF[eng]
        real = []
        for d in deps:
            if d is None:
                continue
            dsem, dval = d
            if BLOCK_OF[dsem] == blk:
                continue
            prev = self.last_wait[blk].get(dsem, 0)
            if dval > prev:
                self.last_wait[blk][dsem] = dval
                real.append((dsem, dval))
        import traceback as _tb
        fr = _tb.extract_stack(limit=2)[0]
        tag = f"{fr.filename.split('/')[-1]}:{fr.lineno}"
        if fn is not None and getattr(fn, "__closure__", None):
            snap = tuple(id(c.cell_contents) for c in fn.__closure__)
            self.snaps.append((fn, snap, tag))
        self.blocks[blk].append((eng, fn, real, inc, tag))
        self.count[eng] += inc
        return (eng, self.count[eng])

    def check_closures(self):
        bad = {}
        for fn, snap, tag in self.snaps:
            cur = tuple(id(c.cell_contents) for c in fn.__closure__)
            if cur != snap:
                names = [n for n, a, b in zip(fn.__code__.co_freevars, snap, cur) if a != b]
                bad.setdefault((tag, tuple(names)), [0])[0] += 1
        for (tag, names), cnt in sorted(bad.items()):
            print(f"CLOSURE DRIFT {tag} vars={names} x{cnt[0]}")
        return len(bad)

    def ev(self, eng):
        return (eng, self.count[eng])

    def barrier(self):
        evs = [(n, self.count[n]) for n in ENGS if self.count[n] > 0]
        for b in self.blocks:
            self.op(ENG_OF_BLOCK[b], None, deps=evs, inc=0)

    def emit(self, block):
        for blk_name, lst in self.blocks.items():
            if not lst:
                continue

            def body(engine, lst=lst):
                for eng, fn, deps, inc, tag in lst:
                    for dsem, dval in deps:
                        engine.wait_ge(self.sems[dsem], dval)
                    if fn is None:
                        continue
                    r = fn(engine)
                    if inc > 0:
                        assert r is not None, f"op on {eng} returned None, inc={inc}"
                        r.then_inc(self.sems[eng], inc)
            getattr(block, blk_name)(body)


def _bias_spec():
    spec = [("aln_w", 8), ("aln_b", 8), ("bqkv", 24), ("bproj", 8), ("rb", 1),
            ("fb1", NE * FF // 128), ("fb2", NE * D // 128),
            ("fln_w", 8), ("fln_b", 8),
            ("bpv", 8), ("bk", 8), ("bkv", 8), ("bg1", 8), ("bb1", 4),
            ("bmisc", 1), ("bln_g", 8), ("bln_b", 8), ("bo", 8)]
    rows, off = {}, 0
    for name, n in spec:
        rows[name] = off
        off += n
    return rows, ((off + 127) // 128) * 128


BROWS, BTOT = _bias_spec()

CONSTS = [1e-5, 1e-12, 0.125, 1.0, -1.0, 0.5, -PI, 0.5 * PI, PI, -2.0 * PI,
          1.0 / (2.0 * PI), MAGIC, 1.0 / D, 1.0 / 32.0, -1e30, 2.0, -0.125, 0.0]


class Kern:
    def __init__(self, nc, ctx, P, io):
        self.nc, self.ctx, self.P, self.io = nc, ctx, P, io
        self.sb = {}
        self._b3 = {}

    def sbuf(self, name, shape, dt, ctx=None):
        t = (ctx or self.ctx).enter_context(self.nc.sbuf_tensor(name, list(shape), dt))
        self.sb[name] = t
        return t

    def psum(self, name, shape, dt=F32, ctx=None):
        return (ctx or self.ctx).enter_context(self.nc.psum_tensor(name, list(shape), dt))

    def bcol(self, name, i=0):
        c = BROWS[name] + i
        return self.sb["BCOLS"][:, c:c + 1]

    def slotv(self, e, idx, mx):
        return e.value_load(self.sb["slot_t"][0:1, idx:idx + 1], min_val=0, max_val=mx)


def setup(K):
    nc, P, io = K.nc, K.P, K.io
    consts = K.sbuf("consts", [128, len(CONSTS)], F32)
    for i, v in enumerate(CONSTS):
        nc.const_aps.register(F32, v, consts[:, i:i + 1])
    for i, v in enumerate(CONSTS):
        P.op("ve", lambda e, i=i, v=v: e.memset(consts[:, i:i + 1], v))
    ones1f = K.sbuf("ones1f", [1, 128], F32)
    onescb = K.sbuf("onescb", [128, 1], BF16)
    P.op("ve", lambda e: e.memset(ones1f[:], 1.0))
    P.op("ve", lambda e: e.memset(onescb[:], 1.0))
    jnk = K.sbuf("jnk", [128, 1], F32)

    def fill(n=6):
        for _ in range(n):
            P.op("ve", lambda e: e.memset(jnk[:], 0.0))
    K.fill = fill

    idf = K.sbuf("idf", [128, 128], F32)
    idb = K.sbuf("idb", [128, 128], BF16)
    bc = K.sbuf("BCOLS", [128, BTOT], F32)
    with ExitStack() as cset:
        iota_r = K.sbuf("iota_r", [1, 128], F32, cset)
        iota_c = K.sbuf("iota_c", [128, 1], F32, cset)
        e1 = P.op("gp", lambda e: e.iota(iota_r[:], [[1, 128]], channel_multiplier=0, allow_small_or_imprecise_dtypes=True))
        e2 = P.op("gp", lambda e: e.iota(iota_c[:], [[0, 1]], channel_multiplier=1, allow_small_or_imprecise_dtypes=True))
        psb = K.psum("ps_id", [128, 128], F32, cset)
        e3 = P.op("pe", lambda e: e.matmul(psb[:], ones1f[:], iota_r[:], start=True, stop=True),
                  deps=(e1, P.ev("ve")))
        idt = K.sbuf("idt", [128, 128], F32, cset)
        P.op("ve", lambda e: e.tensor_scalar(idf[:], psb[:], iota_c[:], None, AO.is_ge),
             deps=(e3, e2))
        P.op("ve", lambda e: e.tensor_scalar(idt[:], psb[:], iota_c[:], None, AO.is_le))
        K.fill()
        e4 = P.op("ve", lambda e: e.tensor_tensor(idf[:], idf[:], idt[:], AO.mult))
        K.fill()
        P.op("ve", lambda e: e.tensor_copy(idb[:], idf[:]))

        braw = K.sbuf("braw", [128, BTOT], F32, cset)
        ld = P.op("sy", lambda e: e.dma_start(
            out=braw[:].rearrange("p (b c) -> p b c", b=BTOT // 128),
            in_=io["btab"].rearrange("(b p) c -> p b c", p=128)))
        evc = None
        for b in range(BTOT // 128):
            et = P.op("pe", lambda e, b=b: e.transpose(psb[:], braw[:, b * 128:(b + 1) * 128], idf[:]),
                      deps=(ld, e4, evc))
            evc = P.op("sc", lambda e, b=b: e.copy(out=bc[:, b * 128:(b + 1) * 128], in_=psb[:]),
                      deps=(et,))
        P.barrier()

    slot_t = K.sbuf("slot_t", [1, 8], I32)
    non_t = K.sbuf("non_t", [N_ROUNDS, 16], I32)
    mprev_t = K.sbuf("mprev_t", [128, 1], F32)
    P.op("sy", lambda e: e.dma_start(out=slot_t[:], in_=io["slot"]))
    P.op("sy", lambda e: e.dma_start(out=non_t[:], in_=io["nonces"]))
    P.op("sy", lambda e: e.dma_start(out=mprev_t[:], in_=io["mprev"]))
    P.barrier()

    def f(e):
        K.r_myfc = e.value_load(slot_t[0:1, 1:2])
        K.r_mysm = e.value_load(slot_t[0:1, 2:3])
        K.r_peersm = e.value_load(slot_t[0:1, 3:4])
        return e.engine_nop()
    P.op("gp", f)

    wg2c = K.sbuf("WG2C", [128, 8], BF16)
    wb2c = K.sbuf("WB2C", [128, 8], BF16)
    rbb = K.sbuf("RBB", [128, 8], F32)
    bb2d = K.sbuf("bb2diff", [1, 2], F32)
    K._ev_wg2 = P.op("gd", lambda e: e.dma_start(
        out=wg2c[:].rearrange("p (i c) -> p i c", i=NCH),
        in_=io["Wg2"].rearrange("(i p) c -> p i c", p=128),
        allow_slow_non_contiguous=True))
    K._ev_wb2 = P.op("gd", lambda e: e.dma_start(
        out=wb2c[:].rearrange("p (c i) -> p c i", c=2),
        in_=io["Wb2"].rearrange("(i p) c -> p c i", p=128),
        allow_slow_non_contiguous=True))
    K._ev_rbb = P.op("sy", lambda e: e.dma_start(
        out=rbb[:], in_=io["rbrow"].broadcast_to((128, 8))))
    K._ev_bb2 = P.op("sy", lambda e: e.dma_start(out=bb2d[:], in_=io["bb2d"]))
    b2b = K.sbuf("B2B", [NE, D], BF16)
    K._ev_b2m = P.op("gd", lambda e: e.dma_start(out=b2b[:], in_=io["b2m"]))
    P.barrier()


def flag_and_poll(K, rnd, deps):
    nc, P, io = K.nc, K.P, K.io
    non_t = K.sb["non_t"]
    ef = P.op("gd", lambda e: e.dma_start(
        out=io["xfl"][bass.ds(K.r_myfc, 1), :], in_=non_t[rnd:rnd + 1, :]), deps=deps)

    def poll(e):
        e0 = e.value_load(non_t[rnd:rnd + 1, 0:1])
        e1 = e.value_load(non_t[rnd:rnd + 1, 1:2])
        with e.register(f"pc{rnd}") as cond, e.register(f"pv{rnd}") as v, \
             e.register(f"p2{rnd}") as c2:
            e.reg_mov(cond, 1)
            with e.While(cond):
                e.load(v, io["xfl"][0:1, 0:1])
                e.reg_alu(cond, v, e0, AO.not_equal)
                e.load(v, io["xfl"][0:1, 1:2])
                e.reg_alu(c2, v, e1, AO.not_equal)
                e.reg_alu(cond, cond, c2, AO.add)
                e.load(v, io["xfl"][1:2, 0:1])
                e.reg_alu(c2, v, e0, AO.not_equal)
                e.reg_alu(cond, cond, c2, AO.add)
                e.load(v, io["xfl"][1:2, 1:2])
                e.reg_alu(c2, v, e1, AO.not_equal)
                e.reg_alu(cond, cond, c2, AO.add)
        return e.engine_nop()
    return P.op("gp", poll, deps=(ef,))


def ln_stats(K, ctx, tag, src, n_tiles, tok, eps, out_dram_row=None, keep_xb=False):
    nc, P, io = K.nc, K.P, K.io
    rows = K.sbuf(f"rows_{tag}", [1, 3 * tok], F32, ctx)
    XB = K.sbuf(f"XB_{tag}", [128, n_tiles * tok], BF16, ctx)
    x2 = K.sbuf(f"x2_{tag}", [128, 2 * 512], BF16, ctx)
    ps = K.psum(f"ps_{tag}", [1, 2 * 512], F32, ctx)
    onescb = K.sb["onescb"]
    ev_cast = None
    for i in range(n_tiles):
        ap, ev_src = src(i)
        ev_cast = P.op("ve", lambda e, ap=ap, i=i: e.tensor_copy(
            XB[:, i * tok:(i + 1) * tok], ap), deps=(ev_src,))
    ev1 = {}
    ev2 = {}
    nsl = tok // 512
    for s in range(nsl):
        for i in range(n_tiles):
            ev1[(s, i)] = P.op("pe", lambda e, s=s, i=i: e.matmul(
                ps[0:1, 0:512], onescb[:], XB[:, i * tok + s * 512:i * tok + (s + 1) * 512],
                start=(i == 0), stop=(i == n_tiles - 1)),
                deps=(ev_cast, ev2.get(s - 1)))
        evr1 = P.op("ve", lambda e, s=s: e.tensor_scalar(
            rows[0:1, s * 512:(s + 1) * 512], ps[0:1, 0:512], 1.0 / D, None, AO.mult),
            deps=(ev1[(s, n_tiles - 1)],))
        ev1[s] = evr1
        for i in range(n_tiles):
            b2 = i % 2
            eq = P.op("gp", lambda e, s=s, i=i, b2=b2: e.tensor_tensor(
                x2[:, b2 * 512:(b2 + 1) * 512],
                XB[:, i * tok + s * 512:i * tok + (s + 1) * 512],
                XB[:, i * tok + s * 512:i * tok + (s + 1) * 512], AO.mult),
                deps=(ev_cast, ev2.get((s, i - 2)) if i >= 2
                      else ev2.get((s - 1, n_tiles - 2 + i))))
            ev2[(s, i)] = P.op("pe", lambda e, s=s, i=i, b2=b2: e.matmul(
                ps[0:1, 512:1024], onescb[:], x2[:, b2 * 512:(b2 + 1) * 512],
                start=(i == 0), stop=(i == n_tiles - 1)),
                deps=(eq, evr1 if i == 0 else None))
        ev2[s] = P.op("ve", lambda e, s=s: e.tensor_scalar(
            rows[0:1, tok + s * 512:tok + (s + 1) * 512], ps[0:1, 512:1024],
            1.0 / D, None, AO.mult), deps=(ev2[(s, n_tiles - 1)],))
    mu, var, rstd = rows[:, 0:tok], rows[:, tok:2 * tok], rows[:, 2 * tok:3 * tok]
    edone = ev2[nsl - 1]
    P.op("ve", lambda e: e.scalar_tensor_tensor(rstd, mu, -1.0, mu, AO.mult, AO.mult),
         deps=(edone,))
    e4 = P.op("ve", lambda e: e.tensor_tensor(var, var, rstd, AO.add))
    e5 = P.op("sc", lambda e: e.activation(rstd, var, AF.Sqrt, bias=eps, scale=1.0), deps=(e4,))
    e6 = P.op("ve", lambda e: e.reciprocal(rstd, rstd), deps=(e5,))
    ev = e6
    if out_dram_row is not None:
        r = out_dram_row
        P.op("sy", lambda e: e.dma_start(out=io["ROWSD"][r:r + 1, 0:tok], in_=mu), deps=(e6,))
        ev = P.op("sy", lambda e: e.dma_start(out=io["ROWSD"][r + 1:r + 2, 0:tok], in_=rstd),
                  deps=(e6,))
    if keep_xb:
        return mu, rstd, var, ev, XB
    return mu, rstd, var, ev


def bcast_rows(K, ctx, tag, rows_evs, tok, ps=None):
    nc, P = K.nc, K.P
    n = len(rows_evs)
    out = K.sbuf(f"bc_{tag}", [128, n * tok], F32, ctx)
    if ps is None:
        ps = K.psum(f"psbc_{tag}", [128, 1024], F32, ctx)
    ones1f = K.sb["ones1f"]
    ev = None
    for j, (row, evr) in enumerate(rows_evs):
        for s in range(0, tok, 512):
            w = min(512, tok - s)
            bank = (s // 512) % 2
            em = P.op("pe", lambda e, row=row, s=s, w=w, bank=bank: e.matmul(
                ps[:, bank * 512:bank * 512 + w], ones1f[:], row[:, s:s + w],
                start=True, stop=True), deps=(evr, ev))
            ev = P.op("sc", lambda e, j=j, s=s, w=w, bank=bank: e.copy(
                out=out[:, j * tok + s:j * tok + s + w],
                in_=ps[:, bank * 512:bank * 512 + w]), deps=(em,))
    return [out[:, j * tok:(j + 1) * tok] for j in range(n)], ev


def bcast_dram_rows(K, ctx, tag, row_idx, tok, tok0, deps=(), ps=None):
    nc, P, io = K.nc, K.P, K.io
    n = len(row_idx)
    out = K.sbuf(f"bc_{tag}", [128, n * tok], F32, ctx)
    scr = K.sbuf(f"bcs_{tag}", [1, 2 * 512], F32, ctx)
    if ps is None:
        ps = K.psum(f"psbc_{tag}", [128, 1024], F32, ctx)
    ones1f = K.sb["ones1f"]
    ev = None
    evl_prev = {}
    for j, r in enumerate(row_idx):
        for s_ in range(0, tok, 512):
            w = min(512, tok - s_)
            bank = (s_ // 512) % 2
            evl = P.op("sy", lambda e, r=r, s_=s_, w=w, bank=bank: e.dma_start(
                out=scr[0:1, bank * 512:bank * 512 + w],
                in_=io["ROWSD"][r:r + 1, tok0 + s_:tok0 + s_ + w]),
                deps=deps + (evl_prev.get(bank),))
            em = P.op("pe", lambda e, s_=s_, w=w, bank=bank: e.matmul(
                ps[:, bank * 512:bank * 512 + w], ones1f[:],
                scr[0:1, bank * 512:bank * 512 + w], start=True, stop=True),
                deps=(evl, ev))
            ev = P.op("sc", lambda e, j=j, s_=s_, w=w, bank=bank: e.copy(
                out=out[:, j * tok + s_:j * tok + s_ + w],
                in_=ps[:, bank * 512:bank * 512 + w]), deps=(em,))
            evl_prev[bank] = ev
    return [out[:, j * tok:(j + 1) * tok] for j in range(n)], ev


def stream_w(K, wbuf, src_fn, n_in_tiles, deps):
    P = K.P

    def f(e):
        src = src_fn()
        return e.dma_start(
            out=wbuf.rearrange("p (i c) -> p i c", i=n_in_tiles),
            in_=src.rearrange("(i p) c -> p i c", p=128))
    return P.op("gd", f, deps=deps)


def phase_attn_a(K):
    nc, P, io = K.nc, K.P, K.io
    with ExitStack() as ctx:
        with ExitStack() as c0:
            xin = K.sbuf("A_xin", [128, T], F32, c0)
            ld = {}

            def src(i):
                ev = P.op("sy", lambda e, i=i: e.dma_start(
                    out=xin[:], in_=io["xT"][i * 128:(i + 1) * 128, :]), deps=(ld.get("p"),))
                ld["p"] = ev
                return xin[:], ev
            mu1, r1, var1, ev_st = ln_stats(K, c0, "A1", src, NCH, T, 1e-5, out_dram_row=0)
            rr = K.sbuf("A_rr", [1, 2 * T], F32, c0)
            R_row, A_row = rr[:, 0:T], rr[:, T:2 * T]
            P.op("ve", lambda e: e.tensor_tensor(R_row, r1, r1, AO.mult), deps=(ev_st,))
            e2 = P.op("ve", lambda e: e.tensor_tensor(R_row, R_row, var1, AO.mult))
            e3 = P.op("sc", lambda e: e.activation(R_row, R_row, AF.Sqrt, bias=1e-5, scale=1.0),
                      deps=(e2,))
            P.op("ve", lambda e: e.reciprocal(R_row, R_row), deps=(e3,))
            e5 = P.op("ve", lambda e: e.tensor_tensor(R_row, R_row, r1, AO.mult))
            e6 = P.op("ve", lambda e: e.tensor_tensor(A_row, mu1, R_row, AO.mult))
            P.op("sy", lambda e: e.dma_start(out=io["ROWSD"][6:7, :], in_=R_row), deps=(e5,))
            P.op("sy", lambda e: e.dma_start(out=io["ROWSD"][7:8, :], in_=A_row), deps=(e6,))
            P.barrier()

        cqa = ctx.enter_context(ExitStack())
        Q = K.sbuf("A_Q", [128, NCH * T], BF16, cqa)
        with ExitStack() as c2a:
            ZN = K.sbuf("A_ZN", [128, NCH * T], BF16, c2a)
            with ExitStack() as cz:
                (Rb, Ab), ev_bc = bcast_dram_rows(K, cz, "A", [6, 7], T, 0)
                xin2 = K.sbuf("A_xin2", [128, T], F32, cz)
                evp = None
                for i in range(NCH):
                    el = P.op("sy", lambda e, i=i: e.dma_start(
                        out=xin2[:], in_=io["xT"][i * 128:(i + 1) * 128, :]),
                        deps=(ev_bc, evp))
                    P.op("ve", lambda e: e.tensor_tensor(xin2[:], xin2[:], Rb, AO.mult),
                         deps=(el, ev_bc))
                    P.op("ve", lambda e: e.tensor_tensor(xin2[:], xin2[:], Ab, AO.subtract))
                    evp = P.op("ve", lambda e, i=i: e.tensor_scalar(
                        ZN[:, i * T:(i + 1) * T], xin2[:], K.bcol("aln_w", i),
                        K.bcol("aln_b", i), AO.mult, AO.add))
                ev_zn = evp
                P.barrier()

            with ExitStack() as cq:
                kvst = K.sbuf("A_kvst", [128, 2 * 512], BF16, cq)
                wq = K.sbuf("A_wq", [128, 2 * NCH * 128], BF16, cq)
                psq = K.psum("A_psq", [128, 2 * 512], F32, cq)
                ev_ev, kvw = {}, []
                for o in range(24):
                    buf = o % 2
                    ew = stream_w(K, wq[:, buf * NCH * 128:(buf + 1) * NCH * 128],
                                  lambda o=o: io["Wqkv"][:, o * 128:(o + 1) * 128], NCH,
                                  deps=(ev_ev.get((o - 2, NS - 1)),))
                    for s in range(NS):
                        bank = s % 2

                        def mmf(e, o=o, s=s, buf=buf, bank=bank):
                            r = None
                            for i in range(NCH):
                                r = e.matmul(
                                    psq[:, bank * 512:(bank + 1) * 512],
                                    wq[:, buf * NCH * 128 + i * 128:buf * NCH * 128 + (i + 1) * 128],
                                    ZN[:, i * T + s * 512:i * T + s * 512 + 512],
                                    start=(i == 0), stop=(i == NCH - 1))
                            return r
                        em = P.op("pe", mmf, deps=(ew, ev_zn,
                                                   ev_ev.get((o, s - 2)) if s >= 2
                                                   else ev_ev.get((o - 1, NS - 2 + s))))
                        if o < 8:
                            ee = P.op("sc", lambda e, o=o, s=s, bank=bank: e.activation(
                                Q[:, o * T + s * 512:o * T + s * 512 + 512],
                                psq[:, bank * 512:(bank + 1) * 512],
                                AF.Identity, bias=K.bcol("bqkv", o), scale=1.0), deps=(em,))
                        else:
                            kb = s % 2
                            dep_st = kvw[-2] if len(kvw) >= 2 else None
                            ee = P.op("sc", lambda e, o=o, s=s, bank=bank, kb=kb: e.activation(
                                kvst[:, kb * 512:(kb + 1) * 512],
                                psq[:, bank * 512:(bank + 1) * 512],
                                AF.Identity, bias=K.bcol("bqkv", o), scale=1.0),
                                deps=(em, dep_st))
                            ekv = P.op("gd", lambda e, o=o, s=s, kb=kb: e.dma_start(
                                out=io["KVSTG"][(o - 8) * 128:(o - 7) * 128,
                                                s * 512:(s + 1) * 512],
                                in_=kvst[:, kb * 512:(kb + 1) * 512]), deps=(ee,))
                            kvw.append(ekv)
                        ev_ev[(o, s)] = ee
                epub = P.op("gd", lambda e: e.dma_start(
                    out=io["xkv"].rearrange("(z r) t -> z r t", z=2)[
                        bass.ds(K.r_myfc, 1), :, :],
                    in_=io["KVSTG"][:, :]), deps=tuple(kvw[-2:]))
                ev_ready = flag_and_poll(K, 0, (epub,))
                P.barrier()

        with ExitStack() as ca:
            KG = K.sbuf("A_KG", [128, L], BF16, ca)
            VG = K.sbuf("A_VG", [128, L], BF16, ca)
            VT = K.sbuf("A_VT", [128, 64 * 130], BF16, ca)
            PT = K.sbuf("A_PT", [128, 2 * 1024], BF16, ca)
            mst = K.sbuf("A_mst", [64, 1024], BF16, ca)
            dent = K.sbuf("A_dent", [1, 1024], F32, ca)
            psS = K.psum("A_psS", [128, 2 * 1024], F32, ca)
            psM = K.psum("A_psM", [65, 1024], F32, ca)
            pstr = K.psum("A_pstr", [128, 128], BF16, ca)
            idb = K.sb["idb"]

            ev_pair = None
            for p in range(NPAIR):
                ekg1 = P.op("sy", lambda e, p=p: e.dma_start(
                    out=KG[:, 0:T], in_=io["xkv"][p * 128:(p + 1) * 128, :]),
                    deps=(ev_ready, ev_pair))
                P.op("sy", lambda e, p=p: e.dma_start(
                    out=KG[:, T:L], in_=io["xkv"][2048 + p * 128:2048 + (p + 1) * 128, :]))
                P.op("sy", lambda e, p=p: e.dma_start(
                    out=VG[:, 0:T], in_=io["xkv"][1024 + p * 128:1024 + (p + 1) * 128, :]))
                evg2 = P.op("sy", lambda e, p=p: e.dma_start(
                    out=VG[:, T:L],
                    in_=io["xkv"][3072 + p * 128:3072 + (p + 1) * 128, :]))
                evm = P.op("ve", lambda e: e.memset(VT[:], 1.0), deps=(ev_pair,))
                evt = None
                for k in range(64):
                    et = P.op("pe", lambda e, k=k: e.transpose(
                        pstr[:], VG[:, k * 128:(k + 1) * 128], idb[:]),
                        deps=(evg2, evm, evt, ev_pair))
                    P.op("ve", lambda e, k=k: e.tensor_copy(
                        VT[:, k * 130:k * 130 + 64], pstr[:, 0:64]), deps=(et,))
                    evt = P.op("ve", lambda e, k=k: e.tensor_copy(
                        VT[:, k * 130 + 65:k * 130 + 129], pstr[:, 64:128]))
                ev_vt = evt

                ev_q = None
                ev_evac1 = None
                ev_evac2 = None
                for h in range(2):
                    for qc in range(4):
                        q0 = qc * 1024
                        ev_exp, ev_pv = {}, {}
                        for k in range(64):
                            sb_ = k % 2

                            def smm(e, p=p, h=h, q0=q0, k=k, sb_=sb_):
                                r = None
                                for qs in range(2):
                                    r = e.matmul(
                                        psS[:, sb_ * 1024 + qs * 512:sb_ * 1024 + (qs + 1) * 512],
                                        KG[h * 64:(h + 1) * 64, k * 128:(k + 1) * 128],
                                        Q[h * 64:(h + 1) * 64,
                                          p * T + q0 + qs * 512:p * T + q0 + (qs + 1) * 512],
                                        start=True, stop=True)
                                return r
                            es = P.op("pe", smm, deps=(ekg1, evg2, ev_vt, ev_exp.get(k - 2)))
                            ee = P.op("sc", lambda e, k=k, sb_=sb_: e.activation(
                                PT[:, sb_ * 1024:(sb_ + 1) * 1024],
                                psS[:, sb_ * 1024:(sb_ + 1) * 1024],
                                AF.Exp, scale=0.125), deps=(es, ev_pv.get(k - 2)))
                            ev_exp[k] = ee

                            def pvm(e, h=h, k=k, sb_=sb_):
                                r = None
                                for qs in range(2):
                                    r = e.matmul(
                                        psM[:, qs * 512:(qs + 1) * 512],
                                        VT[:, k * 130 + h * 65:k * 130 + h * 65 + 65],
                                        PT[:, sb_ * 1024 + qs * 512:sb_ * 1024 + (qs + 1) * 512],
                                        start=(k == 0), stop=(k == 63))
                                return r
                            ev_pv[k] = P.op("pe", pvm,
                                            deps=(ee,) + ((ev_evac1, ev_evac2) if k == 0 else ()))
                        ed = P.op("ve", lambda e: e.reciprocal(dent[:], psM[64:65, :]),
                                  deps=(ev_pv[63], ev_q))
                        em = P.op("sc", lambda e: e.copy(out=mst[:], in_=psM[0:64, :]),
                                  deps=(ev_pv[63], ev_q))
                        P.op("sy", lambda e, p=p, h=h, qc=qc: e.dma_start(
                            out=io["DEN"][2 * p + h:2 * p + h + 1, qc * 1024:(qc + 1) * 1024],
                            in_=dent[:]), deps=(ed,))
                        ems = P.op("sy", lambda e, p=p, h=h, qc=qc: e.dma_start(
                            out=io["MSAR"][(2 * p + h) * 64:(2 * p + h + 1) * 64,
                                           qc * 1024:(qc + 1) * 1024],
                            in_=mst[:]), deps=(em,))
                        ev_q = ems
                        ev_evac1, ev_evac2 = ed, em
                ev_pair = ev_q
            ev_attn = ev_pair
            P.barrier()
        cqa.close()

        with ExitStack() as c3:
            MSA = K.sbuf("A_MSA", [128, NCH * T], BF16, c3)
            dro = K.sbuf("A_dro", [1, 2 * 2048], F32, c3)
            mraw = K.sbuf("A_mraw", [128, 2048], BF16, c3)
            mA = K.sbuf("A_mA", [1, 128], F32, c3)
            mB = K.sbuf("A_mB", [1, 128], F32, c3)
            psD = K.psum("A_psD", [128, 2048], F32, c3)
            P.op("ve", lambda e: e.memset(mA[:], 0.0), deps=(ev_attn,))
            P.op("ve", lambda e: e.memset(mA[:, 0:64], 1.0))
            P.op("ve", lambda e: e.memset(mB[:], 0.0))
            em4 = P.op("ve", lambda e: e.memset(mB[:, 64:128], 1.0))
            evn = None
            for p in range(NPAIR):
                for qc in range(2):
                    eda = P.op("sy", lambda e, p=p, qc=qc: e.dma_start(
                        out=dro[:, 0:2048],
                        in_=io["DEN"][2 * p:2 * p + 1, qc * 2048:(qc + 1) * 2048]),
                        deps=(evn, ev_attn))
                    edb = P.op("sy", lambda e, p=p, qc=qc: e.dma_start(
                        out=dro[:, 2048:4096],
                        in_=io["DEN"][2 * p + 1:2 * p + 2, qc * 2048:(qc + 1) * 2048]))
                    erm = P.op("sy", lambda e, p=p, qc=qc: e.dma_start(
                        out=mraw[:],
                        in_=io["MSAR"][2 * p * 64:(2 * p + 2) * 64,
                                       qc * 2048:(qc + 1) * 2048]), deps=(edb, evn))
                    eb = None
                    for half in range(4):
                        h0 = half * 512
                        P.op("pe", lambda e, h0=h0: e.matmul(
                            psD[:, h0:h0 + 512], mA[:], dro[:, h0:h0 + 512],
                            start=True, stop=False), deps=(eda, edb, em4, evn))
                        eb = P.op("pe", lambda e, h0=h0: e.matmul(
                            psD[:, h0:h0 + 512], mB[:], dro[:, 2048 + h0:2048 + h0 + 512],
                            start=False, stop=True))
                    evn = P.op("ve", lambda e, p=p, qc=qc: e.tensor_tensor(
                        MSA[:, p * T + qc * 2048:p * T + (qc + 1) * 2048], mraw[:], psD[:],
                        AO.mult), deps=(erm, eb))
            ev_msa = evn

            (m1b, r1b), ev_bc2 = bcast_dram_rows(K, c3, "A2", [0, 1], T, 0)
            wpj = K.sbuf("A_wpj", [128, 2 * NCH * 128], BF16, c3)
            xr1 = K.sbuf("A_xr1", [128, T], F32, c3)
            xr2 = K.sbuf("A_xr2", [128, T], F32, c3)
            pso = K.psum("A_pso", [128, 2 * 512], F32, c3)
            ev_o = {}
            for o in range(NCH):
                buf = o % 2
                ewp = stream_w(K, wpj[:, buf * NCH * 128:(buf + 1) * NCH * 128],
                               lambda o=o: io["Wproj"][:, o * 128:(o + 1) * 128], NCH,
                               deps=(ev_o.get(o - 2),))
                exl = P.op("sy", lambda e, o=o: e.dma_start(
                    out=xr1[:], in_=io["xT"][o * 128:(o + 1) * 128, :]),
                    deps=(ev_o.get(o - 1),))
                P.op("ve", lambda e: e.tensor_tensor(xr2[:], xr1[:], m1b, AO.subtract),
                     deps=(exl, ev_bc2))
                P.op("ve", lambda e: e.tensor_tensor(xr2[:], xr2[:], r1b, AO.mult))
                ez3 = P.op("ve", lambda e: e.tensor_tensor(xr2[:], xr2[:], xr1[:], AO.add))
                evo = None
                for s in range(NS):
                    bank = s % 2

                    def pmm(e, o=o, s=s, buf=buf, bank=bank):
                        r = None
                        for i in range(NCH):
                            r = e.matmul(
                                pso[:, bank * 512:(bank + 1) * 512],
                                wpj[:, buf * NCH * 128 + i * 128:buf * NCH * 128 + (i + 1) * 128],
                                MSA[:, i * T + s * 512:i * T + s * 512 + 512],
                                start=(i == 0), stop=(i == NCH - 1))
                        return r
                    emm = P.op("pe", pmm, deps=(ewp, ev_msa,
                                                ev_o.get((o, s - 2)) if s >= 2
                                                else ev_o.get((o - 1, NS - 2 + s))))
                    evo = P.op("ve", lambda e, o=o, s=s, bank=bank: e.scalar_tensor_tensor(
                        xr1[:, s * 512:(s + 1) * 512], pso[:, bank * 512:(bank + 1) * 512],
                        K.bcol("bproj", o), xr2[:, s * 512:(s + 1) * 512], AO.add, AO.add),
                        deps=(emm, ez3))
                    ev_o[(o, s)] = evo
                est = P.op("sy", lambda e, o=o: e.dma_start(
                    out=io["XRES"][o * 128:(o + 1) * 128, :], in_=xr1[:]), deps=(evo,))
                ev_o[o] = est
            P.barrier()


def phase_ffn(K, call):
    nc, P, io = K.nc, K.P, K.io
    last = (call == 2)
    with ExitStack() as ctx:
        with ExitStack() as c0:
            xin = K.sbuf(f"F{call}_xin", [128, T], F32, c0)
            ld = {}

            def src(i):
                ev = P.op("sy", lambda e, i=i: e.dma_start(
                    out=xin[:], in_=io["XRES"][i * 128:(i + 1) * 128, :]), deps=(ld.get("p"),))
                ld["p"] = ev
                return xin[:], ev
            mu, rstd, _v, ev_st = ln_stats(K, c0, f"F{call}", src, NCH, T, 1e-5,
                                           out_dram_row=2)
            P.barrier()

        with ExitStack() as c1:
            _ffn_routing(K, c1, call)
            P.barrier()

        for th in range(2):
            t0 = th * 2048
            with ExitStack() as c2:
                cps = c2.enter_context(ExitStack())
                ps1 = K.psum(f"F{call}t{th}_p1", [128, 2 * 1024], F32, cps)
                ps2 = K.psum(f"F{call}t{th}_p2", [128, 2 * 1024], F32, cps)
                (mub, rb), ev_bc = bcast_dram_rows(
                    K, c2, f"F{call}t{th}", [2, 3], 2048, t0, ps=ps1[:, 0:1024])
                FLAT = K.sbuf(f"F{call}t{th}_FLAT", [128, NCH * 2048], BF16, c2)
                ft = K.sbuf(f"F{call}t{th}_ft", [128, 2048], F32, c2)
                cwtb = K.sbuf(f"F{call}t{th}_cwt", [NE, 2048], BF16, c2)
                ecwt = P.op("sy", lambda e, cwtb=cwtb, t0=t0: e.dma_start(
                    out=cwtb[:], in_=io["CWD"][:, t0:t0 + 2048]),
                    deps=(K._ev_cwd[call],))
                evp = None
                for i in range(NCH):
                    el = P.op("sy", lambda e, i=i, ft=ft, t0=t0: e.dma_start(
                        out=ft[:], in_=io["XRES"][i * 128:(i + 1) * 128, t0:t0 + 2048]),
                        deps=(ev_bc, evp))
                    P.op("ve", lambda e, ft=ft, mub=mub: e.tensor_tensor(
                        ft[:], ft[:], mub, AO.subtract), deps=(el, ev_bc))
                    evp = P.op("ve", lambda e, i=i, FLAT=FLAT, ft=ft, rb=rb: e.tensor_tensor(
                        FLAT[:, i * 2048:(i + 1) * 2048], ft[:], rb, AO.mult))
                ev_flat = evp
                OUT = K.sbuf(f"F{call}t{th}_OUT", [128, NCH * 2048], F32, c2)
                emz = None
                for o in range(NCH):
                    emz = P.op("ve", lambda e, o=o, OUT=OUT: e.memset(
                        OUT[:, o * 2048:(o + 1) * 2048], 0.0))

                with ExitStack() as c3:
                    W1C = K.sbuf(f"F{call}t{th}_W1", [128, 2 * NCH * 1024], BF16, c3)
                    W2C = K.sbuf(f"F{call}t{th}_W2", [128, 8 * 1024], BF16, c3)
                    CWB = K.sbuf(f"F{call}t{th}_CWB", [128, 2048], BF16, c3)
                    h1c = K.sbuf(f"F{call}t{th}_h1", [128, 8 * 1024], BF16, c3)
                    stg = K.sbuf(f"F{call}t{th}_stg", [128, 2 * 1024], BF16, c3)
                    ev_wl = {}
                    last_mm2 = None
                    last_cw_reader = None
                    for e_i in range(NE):
                        ecb = P.op("sy", lambda e, e_i=e_i, CWB=CWB, t0=t0: e.dma_start(
                            out=CWB[:],
                            in_=io["CWD"][e_i:e_i + 1, t0:t0 + 2048].broadcast_to((128, 2048))),
                            deps=(K._ev_cwd[call], last_cw_reader))
                        for c in range(4):
                            jj = e_i * 4 + c
                            widx = jj % 2
                            dep_w = ev_wl.get(jj - 2)
                            if K._fw_cached:
                                ew1 = P.op("sy", lambda e, jj=jj, widx=widx, W1C=W1C:
                                           e.dma_start(
                                    out=W1C[:, widx * NCH * 1024:(widx + 1) * NCH * 1024],
                                    in_=io["W1BD"][jj * 128:(jj + 1) * 128, :]),
                                    deps=(dep_w,))
                                ew2 = P.op("sy", lambda e, jj=jj, W2C=W2C: e.dma_start(
                                    out=W2C[:], in_=io["W2BD"][jj * 128:(jj + 1) * 128, :]),
                                    deps=(ev_wl.get(jj - 1),))
                            else:
                                ew1 = stream_w(
                                    K, W1C[:, widx * NCH * 1024:(widx + 1) * NCH * 1024],
                                    lambda e_i=e_i, c=c:
                                        io["w1T"][e_i, :, c * 1024:(c + 1) * 1024],
                                    NCH, deps=(dep_w, K._fw_st1.get(jj - 2)))
                                ew2 = P.op("gd", lambda e, e_i=e_i, c=c, W2C=W2C: e.dma_start(
                                    out=W2C[:].rearrange("p (i o) -> p i o", i=8),
                                    in_=io["w2T"][e_i, c * 1024:(c + 1) * 1024, :].rearrange(
                                        "(i p) o -> p i o", p=128)),
                                    deps=(ev_wl.get(jj - 1), K._fw_st2.get(jj - 1)))
                                K._fw_st1[jj] = P.op(
                                    "sy", lambda e, jj=jj, widx=widx, W1C=W1C: e.dma_start(
                                        out=io["W1BD"][jj * 128:(jj + 1) * 128, :],
                                        in_=W1C[:, widx * NCH * 1024:
                                                (widx + 1) * NCH * 1024]), deps=(ew1,))
                                K._fw_st2[jj] = P.op(
                                    "sy", lambda e, jj=jj, W2C=W2C: e.dma_start(
                                        out=io["W2BD"][jj * 128:(jj + 1) * 128, :],
                                        in_=W2C[:]), deps=(ew2,))
                            for ts in range(2):
                                s0 = ts * 1024
                                prev_mm2_of_ts = last_mm2
                                ev_h = {}
                                for f in range(8):
                                    pb = f % 2

                                    def m1(e, e_i=e_i, c=c, f=f, widx=widx, s0=s0, pb=pb,
                                           ps1=ps1, W1C=W1C, FLAT=FLAT):
                                        r = None
                                        for half in range(2):
                                            for i in range(NCH):
                                                r = e.matmul(
                                                    ps1[:, pb * 1024 + half * 512:
                                                        pb * 1024 + (half + 1) * 512],
                                                    W1C[:, widx * NCH * 1024 + i * 1024 + f * 128:
                                                        widx * NCH * 1024 + i * 1024 + (f + 1) * 128],
                                                    FLAT[:, i * 2048 + s0 + half * 512:
                                                         i * 2048 + s0 + (half + 1) * 512],
                                                    start=(i == 0), stop=(i == NCH - 1))
                                        return r
                                    em1 = P.op("pe", m1, deps=(ew1, ev_flat,
                                                               ev_h.get(f - 2),
                                                               prev_mm2_of_ts if f < 2 else None))
                                    eg = P.op("sc", lambda e, f=f, pb=pb, e_i=e_i, c=c,
                                              stg=stg, ps1=ps1: e.activation(
                                        stg[:, pb * 1024:(pb + 1) * 1024],
                                        ps1[:, pb * 1024:(pb + 1) * 1024],
                                        AF.Gelu, bias=K.bcol("fb1", e_i * 32 + c * 8 + f),
                                        scale=1.0), deps=(em1, ev_h.get(f - 1)))
                                    eh = P.op("ve", lambda e, f=f, pb=pb, s0=s0, e_i=e_i,
                                              h1c=h1c, stg=stg, CWB=CWB: e.tensor_tensor(
                                        h1c[:, f * 1024:(f + 1) * 1024],
                                        stg[:, pb * 1024:(pb + 1) * 1024],
                                        CWB[:, s0:s0 + 1024], AO.mult),
                                        deps=(eg, ecb, prev_mm2_of_ts if f == 0 else None))
                                    ev_h[f] = eh
                                edbg = None
                                if (os.environ.get("KDBG2") and call == 1 and th == 0
                                        and e_i == 0 and c == 0 and ts == 0):
                                    P.op("sy", lambda e, h1c=h1c: e.dma_start(
                                        out=io["KVSTG"][1024:1152, :], in_=h1c[:, 0:4096]),
                                        deps=(ev_h[7],))
                                    edbg = P.op("sy", lambda e, h1c=h1c: e.dma_start(
                                        out=io["KVSTG"][1152:1280, :], in_=h1c[:, 4096:8192]),
                                        deps=(ev_h[7],))
                                ev_m2 = {}
                                for o in range(NCH):
                                    ob = o % 2

                                    addb2 = (e_i == NE - 1 and c == 3)

                                    def m2(e, o=o, ob=ob, ps2=ps2, W2C=W2C, h1c=h1c,
                                           addb2=addb2, s0=s0, cwtb=cwtb):
                                        r = None
                                        for half in range(2):
                                            for f in range(8):
                                                r = e.matmul(
                                                    ps2[:, ob * 1024 + half * 512:
                                                        ob * 1024 + (half + 1) * 512],
                                                    W2C[:, f * 1024 + o * 128:
                                                        f * 1024 + (o + 1) * 128],
                                                    h1c[:, f * 1024 + half * 512:
                                                        f * 1024 + (half + 1) * 512],
                                                    start=(f == 0),
                                                    stop=(f == 7 and not addb2))
                                            if addb2:
                                                r = e.matmul(
                                                    ps2[:, ob * 1024 + half * 512:
                                                        ob * 1024 + (half + 1) * 512],
                                                    K.sb["B2B"][:, o * 128:(o + 1) * 128],
                                                    cwtb[:, s0 + half * 512:
                                                         s0 + (half + 1) * 512],
                                                    start=False, stop=True)
                                        return r
                                    em2 = P.op("pe", m2, deps=(ew2, ev_h[7], ev_m2.get(o - 2),
                                                               ecwt, K._ev_b2m, edbg))
                                    ea = P.op("ve", lambda e, o=o, ob=ob, s0=s0,
                                              OUT=OUT, ps2=ps2: e.tensor_tensor(
                                        OUT[:, o * 2048 + s0:o * 2048 + s0 + 1024],
                                        OUT[:, o * 2048 + s0:o * 2048 + s0 + 1024],
                                        ps2[:, ob * 1024:(ob + 1) * 1024], AO.add),
                                        deps=(em2, emz))
                                    ev_m2[o] = ea
                                last_mm2 = ev_m2[NCH - 1]
                            ev_wl[e_i * 4 + c] = last_mm2
                        last_cw_reader = last_mm2
                    ev_experts = last_mm2
                    P.barrier()
                cps.close()

                evp = None
                for i in range(NCH):
                    el = P.op("sy", lambda e, i=i, ft=ft, t0=t0: e.dma_start(
                        out=ft[:], in_=io["XRES"][i * 128:(i + 1) * 128, t0:t0 + 2048]),
                        deps=(ev_experts, evp))
                    P.op("ve", lambda e, ft=ft, mub=mub: e.tensor_tensor(
                        ft[:], ft[:], mub, AO.subtract), deps=(el,))
                    P.op("ve", lambda e, ft=ft, rb=rb: e.tensor_tensor(
                        ft[:], ft[:], rb, AO.mult))
                    evp = P.op("ve", lambda e, i=i, OUT=OUT, ft=ft: e.tensor_tensor(
                        OUT[:, i * 2048:(i + 1) * 2048], OUT[:, i * 2048:(i + 1) * 2048],
                        ft[:], AO.add))
                ev_sum = P.ev("ve")
                edmp = None
                if os.environ.get("KDBG") and call == 1:
                    for i in range(NCH):
                        P.op("sy", lambda e, i=i, OUT=OUT, t0=t0: e.dma_start(
                            out=io["MEMS"][i * 128:(i + 1) * 128, t0:t0 + 2048],
                            in_=OUT[:, i * 2048:(i + 1) * 2048]), deps=(ev_sum,))
                    for i in range(NCH):
                        edmp = P.op("sy", lambda e, i=i, FLAT=FLAT, t0=t0: e.dma_start(
                            out=io["KVSTG"][i * 128:(i + 1) * 128, t0:t0 + 2048],
                            in_=FLAT[:, i * 2048:(i + 1) * 2048]), deps=(ev_sum,))

                with ExitStack() as c4:
                    def src2(i):
                        return OUT[:, i * 2048:(i + 1) * 2048], ev_sum
                    mu2, rstd2, _v2, ev_st2 = ln_stats(K, c4, f"F{call}e{th}", src2, NCH,
                                                       2048, 1e-12)
                    (mu2b, r2b), ev_bc2 = bcast_rows(
                        K, c4, f"F{call}e{th}b", [(mu2, ev_st2), (rstd2, ev_st2)], 2048)
                    evw = None
                    for i in range(NCH):
                        el = P.op("sy", lambda e, i=i, ft=ft, t0=t0: e.dma_start(
                            out=ft[:], in_=io["XRES"][i * 128:(i + 1) * 128, t0:t0 + 2048]),
                            deps=(evw, ev_bc2))
                        P.op("ve", lambda e, i=i, OUT=OUT, mu2b=mu2b: e.tensor_tensor(
                            OUT[:, i * 2048:(i + 1) * 2048], OUT[:, i * 2048:(i + 1) * 2048],
                            mu2b, AO.subtract), deps=(ev_bc2, edmp))
                        P.op("ve", lambda e, i=i, OUT=OUT, r2b=r2b: e.tensor_tensor(
                            OUT[:, i * 2048:(i + 1) * 2048], OUT[:, i * 2048:(i + 1) * 2048],
                            r2b, AO.mult))
                        P.op("ve", lambda e, i=i, OUT=OUT: e.tensor_scalar(
                            OUT[:, i * 2048:(i + 1) * 2048], OUT[:, i * 2048:(i + 1) * 2048],
                            K.bcol("fln_w", i), K.bcol("fln_b", i), AO.mult, AO.add))
                        ex = P.op("ve", lambda e, i=i, OUT=OUT, ft=ft: e.tensor_tensor(
                            OUT[:, i * 2048:(i + 1) * 2048], OUT[:, i * 2048:(i + 1) * 2048],
                            ft[:], AO.add), deps=(el,))
                        dst = io["out"] if last else io["XRES"]
                        evw = P.op("sy", lambda e, i=i, dst=dst, OUT=OUT, t0=t0: e.dma_start(
                            out=dst[i * 128:(i + 1) * 128, t0:t0 + 2048],
                            in_=OUT[:, i * 2048:(i + 1) * 2048]), deps=(ex,))
                    P.barrier()
                K._fw_cached = True


def _ffn_routing(K, ctx, call):
    nc, P, io = K.nc, K.P, K.io
    FLR = K.sbuf(f"R{call}_FLR", [128, NCH * T], BF16, ctx)
    rrow = K.sbuf(f"R{call}_rr", [1, 2 * T], F32, ctx)
    er1 = P.op("sy", lambda e: e.dma_start(out=rrow[:, 0:T], in_=io["ROWSD"][2:3, :]))
    er2 = P.op("sy", lambda e: e.dma_start(out=rrow[:, T:2 * T], in_=io["ROWSD"][3:4, :]))
    with ExitStack() as cb:
        (mub, rb), ev_bc = bcast_rows(K, cb, f"R{call}",
                                      [(rrow[:, 0:T], er1), (rrow[:, T:2 * T], er2)], T)
        ft = K.sbuf(f"R{call}_ft", [128, T], F32, cb)
        evp = None
        for i in range(NCH):
            el = P.op("sy", lambda e, i=i: e.dma_start(
                out=ft[:], in_=io["XRES"][i * 128:(i + 1) * 128, :]), deps=(ev_bc, evp))
            P.op("ve", lambda e: e.tensor_tensor(ft[:], ft[:], mub, AO.subtract),
                 deps=(el, ev_bc))
            evp = P.op("ve", lambda e, i=i: e.tensor_tensor(
                FLR[:, i * T:(i + 1) * T], ft[:], rb, AO.mult))
        ev_flr = evp
        P.barrier()

    rwt = K.sbuf(f"R{call}_rwt", [128, NCH * 8], BF16, ctx)
    erw = stream_w(K, rwt[:], lambda: io["rwT"], NCH, deps=())
    psl = K.psum(f"R{call}_psl", [128, 2 * 512], F32, ctx)
    psr = K.psum(f"R{call}_psr", [8, 128], F32, ctx)
    cwtile = K.sbuf(f"R{call}_cw", [128, 8], F32, ctx)
    cwb16 = K.sbuf(f"R{call}_cwb", [128, 8], BF16, ctx)
    scr = K.sbuf(f"R{call}_scr", [128, 8 * 6], F32, ctx)
    col = K.sbuf(f"R{call}_col", [128, 4], F32, ctx)
    ones8 = K.sbuf(f"R{call}_o8", [128, 8], F32, ctx)
    cws = K.sbuf(f"R{call}_cws", [8, T], F32, ctx)
    P.op("ve", lambda e: e.memset(ones8[:], 1.0))
    idb = K.sb["idb"]
    lg = scr[:, 0:8]
    c1 = scr[:, 8:16]
    s1 = scr[:, 16:24]
    m1 = scr[:, 24:32]
    l2 = scr[:, 32:40]
    m2 = scr[:, 40:48]
    v1 = col[:, 0:1]
    v2 = col[:, 1:2]
    w1 = col[:, 2:3]
    onec = K.sb["consts"][:, CONSTS.index(1.0):CONSTS.index(1.0) + 1]
    zeroc = K.sb["consts"][:, CONSTS.index(0.0):CONSTS.index(0.0) + 1]
    ev_tt = {}
    for tt in range(T // 128):
        pb = tt % 2

        def lmm(e, tt=tt, pb=pb):
            r = None
            for i in range(NCH):
                r = e.matmul(psl[:, pb * 512:pb * 512 + 8],
                             FLR[:, i * T + tt * 128:i * T + (tt + 1) * 128],
                             rwt[:, i * 8:(i + 1) * 8],
                             start=(i == 0), stop=(i == NCH - 1))
            return r
        eml = P.op("pe", lmm, deps=(erw, ev_flr, ev_tt.get(tt - 2)))
        P.op("ve", lambda e, pb=pb: e.tensor_tensor(lg, psl[:, pb * 512:pb * 512 + 8],
                                                    K.sb["RBB"][:], AO.add),
             deps=(eml, K._ev_rbb, ev_tt.get(tt - 1)))
        K.fill()
        P.op("ve", lambda e: e.tensor_reduce(v1, lg, AX.X, AO.max))
        K.fill()
        P.op("ve", lambda e: e.tensor_scalar(c1, lg, v1, None, AO.is_ge))
        K.fill()
        P.op("ve", lambda e: e.tensor_tensor_scan(s1, c1, c1, zeroc, AO.add, AO.bypass))
        K.fill()
        P.op("ve", lambda e: e.tensor_scalar(m1, s1, onec, None, AO.is_le))
        K.fill()
        P.op("ve", lambda e: e.tensor_tensor(m1, m1, c1, AO.mult))
        K.fill()
        P.op("ve", lambda e: e.scalar_tensor_tensor(l2, m1, -1e30, lg, AO.mult, AO.add))
        K.fill()
        P.op("ve", lambda e: e.tensor_reduce(v2, l2, AX.X, AO.max))
        K.fill()
        P.op("ve", lambda e: e.tensor_scalar(m2, l2, v2, None, AO.is_ge))
        K.fill()
        P.op("ve", lambda e: e.tensor_tensor_scan(s1, m2, m2, zeroc, AO.add, AO.bypass))
        K.fill()
        P.op("ve", lambda e: e.tensor_scalar(s1, s1, onec, None, AO.is_le))
        K.fill()
        P.op("ve", lambda e: e.tensor_tensor(m2, m2, s1, AO.mult))
        ed = P.op("ve", lambda e: e.tensor_tensor(col[:, 3:4], v1, v2, AO.subtract))
        ew = P.op("sc", lambda e: e.activation(w1, col[:, 3:4], AF.Sigmoid), deps=(ed,))
        P.op("ve", lambda e: e.tensor_scalar(cwtile[:], m1, w1, None, AO.mult), deps=(ew,))
        P.op("ve", lambda e: e.tensor_scalar(col[:, 3:4], w1, -1.0, 1.0, AO.mult, AO.add))
        K.fill()
        P.op("ve", lambda e: e.scalar_tensor_tensor(cwtile[:], m2, col[:, 3:4], cwtile[:],
                                                    AO.mult, AO.add))
        K.fill()
        ecast = P.op("ve", lambda e: e.tensor_copy(cwb16[:], cwtile[:]))
        emt = P.op("pe", lambda e: e.matmul(psr[:], cwb16[:], idb[:], start=True, stop=True),
                   deps=(ecast, ev_tt.get(tt - 1)))
        ev_tt[tt] = P.op("ve", lambda e, tt=tt: e.tensor_copy(
            cws[:, tt * 128:(tt + 1) * 128], psr[:]), deps=(emt,))
    K._ev_cwd[call] = P.op("gd", lambda e: e.dma_start(out=io["CWD"], in_=cws[:]),
                           deps=(ev_tt[T // 128 - 1],))


def phase_attn_b(K):
    nc, P, io = K.nc, K.P, K.io
    CH = 1024
    NC4 = T // CH
    TE = T + 1
    with ExitStack() as ctx:
        XBL = K.sbuf("B_XBL", [128, 8], F32, ctx)
        XBP = K.sbuf("B_XBP", [128, 8], F32, ctx)
        XBPB = K.sbuf("B_XBPB", [128, 8], BF16, ctx)
        KB = K.sbuf("B_KB", [128, 8], F32, ctx)
        GSR = K.sbuf("B_GSR", [1, T], F32, ctx)
        BW0 = K.sbuf("B_BW0", [1, T], BF16, ctx)
        BW1 = K.sbuf("B_BW1", [1, T], BF16, ctx)
        CARR = K.sbuf("B_CARR", [128, 34], F32, ctx)
        PCAR = K.sbuf("B_PCAR", [128, 34], F32, ctx)

        with ExitStack() as c0:
            xin = K.sbuf("B_xin", [128, T], F32, c0)
            ld = {}

            def src(i):
                ev = P.op("sy", lambda e, i=i: e.dma_start(
                    out=xin[:], in_=io["XRES"][i * 128:(i + 1) * 128, :]), deps=(ld.get("p"),))
                ld["p"] = ev
                return xin[:], ev
            mu3, r3, _v3, ev_st = ln_stats(K, c0, "B", src, NCH, T, 1e-5, out_dram_row=4)
            P.barrier()

        cscan = ctx.enter_context(ExitStack())
        FLATB = K.sbuf("B_FLATB", [128, NCH * TE], BF16, cscan)
        with ExitStack() as c0b:
            (m3b, r3b), ev_bc = bcast_dram_rows(K, c0b, "B0", [4, 5], T, 0)
            xin2 = K.sbuf("B_xin2", [128, T], F32, c0b)
            evp = None
            for i in range(NCH):
                el = P.op("sy", lambda e, i=i: e.dma_start(
                    out=xin2[:], in_=io["XRES"][i * 128:(i + 1) * 128, :]),
                    deps=(ev_bc, evp))
                P.op("ve", lambda e: e.tensor_tensor(xin2[:], xin2[:], m3b, AO.subtract),
                     deps=(el, ev_bc))
                P.op("ve", lambda e: e.tensor_tensor(xin2[:], xin2[:], r3b, AO.mult))
                P.op("ve", lambda e, i=i: e.tensor_copy(
                    FLATB[:, i * TE + 1:(i + 1) * TE], xin2[:]))
                evp = P.op("ve", lambda e, i=i: e.tensor_copy(
                    XBL[:, i:i + 1], xin2[:, T - 1:T]))
            ev_flatb = evp
            P.barrier()

        exs = P.op("gd", lambda e: e.dma_start(
            out=io["xsm"][bass.ds(K.r_mysm, 128), 40:48], in_=XBL[:]), deps=(ev_flatb,))
        ev_r1 = flag_and_poll(K, 1, (exs,))
        exr = P.op("gd", lambda e: e.dma_start(
            out=XBP[:], in_=io["xsm"][bass.ds(K.r_peersm, 128), 40:48]), deps=(ev_r1,))
        emk = P.op("ve", lambda e: e.tensor_scalar(XBP[:], XBP[:], K.sb["mprev_t"][:, 0:1],
                                                   None, AO.mult), deps=(exr,))
        K.fill()
        exb = P.op("ve", lambda e: e.tensor_copy(XBPB[:], XBP[:]), deps=(emk,))
        ev_bnd = None
        for i in range(NCH):
            ev_bnd = P.op("ve", lambda e, i=i: e.tensor_copy(
                FLATB[:, i * TE:i * TE + 1], XBP[:, i:i + 1]), deps=(emk, ev_flatb))

        wb = K.sbuf("B_wb", [128, 4 * NCH * 128], BF16, cscan)
        KEYPH = K.sbuf("B_KEYPH", [128, NCH * (CH + 1)], F32, cscan)
        GH = K.sbuf("B_GH", [128, NCH * CH], BF16, cscan)
        BH = K.sbuf("B_BH", [128, 4 * CH], BF16, cscan)
        VGR = K.sbuf("B_VGR", [1, CH], F32, cscan)
        VGB = K.sbuf("B_VGB", [128, CH], F32, cscan)
        t_a = K.sbuf("B_ta", [128, CH], F32, cscan)
        t_b = K.sbuf("B_tb", [128, CH], F32, cscan)
        t_c = K.sbuf("B_tc", [128, CH], F32, cscan)
        t_d = K.sbuf("B_td", [128, CH], F32, cscan)
        rowsc = K.sbuf("B_rowsc", [1, 3 * CH], F32, cscan)
        psA = K.psum("B_psA", [128, 2 * 512], F32, cscan)
        psR = K.psum("B_psR", [1, 2 * 512], F32, cscan)
        psB = K.psum("B_psB", [128, 1024], F32, cscan)

        ev_init = P.op("ve", lambda e: e.memset(CARR[:], 0.0))
        P.op("ve", lambda e: e.memset(GSR[:], 0.0))

        def stream_och(wsrc, o, slot, deps):
            return stream_w(K, wb[:, slot * NCH * 128:(slot + 1) * NCH * 128],
                            lambda: wsrc[:, o * 128:(o + 1) * 128], NCH, deps=deps)

        def frhs(i, c0_, width, shifted=False):
            base = i * TE + (0 if shifted else 1)
            return FLATB[:, base + c0_:base + c0_ + width]

        ev_kb = {}
        ev_sp_chunk = {}
        for cs in range(NC4):
            c0_ = cs * CH
            ev_o = {}
            for o in range(NCH):
                slot = o % 2
                ew = stream_och(io["Wk"], o, slot, deps=(ev_o.get(o - 2), ev_init,
                                                         ev_sp_chunk.get(cs - 1)))
                for s in range(2):
                    bank = s % 2

                    def kmm(e, o=o, s=s, slot=slot, bank=bank, c0_=c0_):
                        r = None
                        for i in range(NCH):
                            r = e.matmul(psA[:, bank * 512:(bank + 1) * 512],
                                         wb[:, slot * NCH * 128 + i * 128:
                                            slot * NCH * 128 + (i + 1) * 128],
                                         frhs(i, c0_ + s * 512, 512),
                                         start=(i == 0), stop=(i == NCH - 1))
                        return r
                    prevb = (ev_kb.get(o - 1) if cs == 0 and s == 0
                             else ev_o.get((o - 1, s)))
                    em = P.op("pe", kmm, deps=(ew, ev_flatb, ev_bnd, prevb))
                    ev_o[(o, s)] = P.op("sc", lambda e, o=o, s=s, bank=bank: e.activation(
                        KEYPH[:, o * (CH + 1) + 1 + s * 512:o * (CH + 1) + 1 + (s + 1) * 512],
                        psA[:, bank * 512:(bank + 1) * 512],
                        AF.Tanh, bias=K.bcol("bk", o), scale=1.0), deps=(em,))
                ev_o[o] = ev_o[(o, 1)]
                if cs == 0:
                    def kbm(e, o=o, slot=slot):
                        r = None
                        for i in range(NCH):
                            r = e.matmul(psA[:, 0:1],
                                         wb[:, slot * NCH * 128 + i * 128:
                                            slot * NCH * 128 + (i + 1) * 128],
                                         XBPB[:, i:i + 1],
                                         start=(i == 0), stop=(i == NCH - 1))
                        return r
                    ekb = P.op("pe", kbm, deps=(ev_o[(o, 1)], exb))
                    ev_kb[o] = P.op("sc", lambda e, o=o: e.activation(
                        KB[:, o:o + 1], psA[:, 0:1], AF.Tanh, bias=K.bcol("bk", o), scale=1.0),
                        deps=(ekb,))
            ev_kcol = None
            for o in range(NCH):
                dep = ev_kb.get(o) if cs == 0 else ev_kb.get((cs, o))
                ev_kcol = P.op("ve", lambda e, o=o: e.tensor_copy(
                    KEYPH[:, o * (CH + 1):o * (CH + 1) + 1], KB[:, o:o + 1]),
                    deps=(dep, ev_o[o]))
            for o in range(NCH):
                ev_kb[(cs + 1, o)] = P.op("ve", lambda e, o=o: e.tensor_copy(
                    KB[:, o:o + 1], KEYPH[:, o * (CH + 1) + CH:o * (CH + 1) + CH + 1]),
                    deps=(ev_o[o], ev_kcol))
            ev_keyph = ev_kb[(cs + 1, NCH - 1)]

            ev_g = {}
            for o in range(NCH):
                slot = o % 2
                ewg = stream_w(K, wb[:, (2 + slot) * NCH * 128:(3 + slot) * NCH * 128],
                               lambda o=o: io["Wg1"][0:D, o * 128:(o + 1) * 128], NCH,
                               deps=(ev_g.get(o - 2),))
                ewg2 = stream_w(K, wb[:, slot * NCH * 128:(slot + 1) * NCH * 128],
                                lambda o=o: io["Wg1"][D:2 * D, o * 128:(o + 1) * 128], NCH,
                                deps=(ev_g.get(o - 2), ev_keyph))
                for s in range(2):
                    bank = s % 2

                    def gmm(e, o=o, s=s, slot=slot, bank=bank, c0_=c0_):
                        r = None
                        for i in range(NCH):
                            r = e.matmul(psA[:, bank * 512:(bank + 1) * 512],
                                         wb[:, (2 + slot) * NCH * 128 + i * 128:
                                            (2 + slot) * NCH * 128 + (i + 1) * 128],
                                         frhs(i, c0_ + s * 512, 512),
                                         start=(i == 0), stop=False)
                        for i in range(NCH):
                            r = e.matmul(psA[:, bank * 512:(bank + 1) * 512],
                                         wb[:, slot * NCH * 128 + i * 128:
                                            slot * NCH * 128 + (i + 1) * 128],
                                         frhs(i, c0_ + s * 512, 512, shifted=True),
                                         start=False, stop=(i == NCH - 1))
                        return r
                    em = P.op("pe", gmm, deps=(ewg, ewg2, ev_bnd, ev_g.get((o - 1, s))))
                    ev_g[(o, s)] = P.op("sc", lambda e, o=o, s=s, bank=bank: e.activation(
                        GH[:, o * CH + s * 512:o * CH + (s + 1) * 512],
                        psA[:, bank * 512:(bank + 1) * 512],
                        AF.Gelu, bias=K.bcol("bg1", o), scale=1.0), deps=(em,))
                ev_g[o] = ev_g[(o, 1)]

            def vgmm(e, s):
                r = None
                for i in range(NCH):
                    r = e.matmul(psR[0:1, s * 512:(s + 1) * 512],
                                 K.sb["WG2C"][:, i:i + 1],
                                 GH[:, i * CH + s * 512:i * CH + (s + 1) * 512],
                                 start=(i == 0), stop=(i == NCH - 1))
                return r
            emv = P.op("pe", lambda e, vgmm=vgmm: [vgmm(e, 0), vgmm(e, 1)][-1],
                       deps=(ev_g[NCH - 1], K._ev_wg2))
            evg = P.op("sc", lambda e: e.activation(VGR[:], psR[0:1, 0:CH], AF.Sigmoid,
                                                    bias=K.sb["bb2g"][0:1, 0:1], scale=1.0),
                       deps=(emv, K._ev_bg2g))
            egs = P.op("ve", lambda e, c0_=c0_: e.tensor_tensor_scan(
                GSR[0:1, c0_:c0_ + CH], VGR[:], VGR[:], CARR[0:1, 32:33], AO.add, AO.bypass),
                deps=(evg, ev_init))
            K.fill()
            P.op("ve", lambda e, c0_=c0_: e.tensor_copy(
                CARR[0:1, 32:33], GSR[0:1, c0_ + CH - 1:c0_ + CH]), deps=(egs,))
            emb = P.op("pe", lambda e: [e.matmul(psB[:, s * 512:(s + 1) * 512],
                                                 K.sb["ones1f"][:],
                                                 VGR[0:1, s * 512:(s + 1) * 512],
                                                 start=True, stop=True) for s in range(2)][-1],
                       deps=(evg,))
            evgb = P.op("sc", lambda e: e.copy(out=VGB[:], in_=psB[:, 0:CH]), deps=(emb,))

            ev_bh = {}
            for o in range(4):
                slot = o % 2
                ewb = stream_och(io["Wb1"], o, slot, deps=(ev_bh.get(o - 2), evgb))
                for s in range(2):
                    bank = s % 2

                    def bmm(e, o=o, s=s, slot=slot, bank=bank, c0_=c0_):
                        r = None
                        for i in range(NCH):
                            r = e.matmul(psA[:, bank * 512:(bank + 1) * 512],
                                         wb[:, slot * NCH * 128 + i * 128:
                                            slot * NCH * 128 + (i + 1) * 128],
                                         frhs(i, c0_ + s * 512, 512),
                                         start=(i == 0), stop=(i == NCH - 1))
                        return r
                    em = P.op("pe", bmm, deps=(ewb, ev_bh.get((o - 1, s))))
                    ev_bh[(o, s)] = P.op("sc", lambda e, o=o, s=s, bank=bank: e.activation(
                        BH[:, o * CH + s * 512:o * CH + (s + 1) * 512],
                        psA[:, bank * 512:(bank + 1) * 512],
                        AF.Gelu, bias=K.bcol("bb1", o), scale=1.0), deps=(em,))
                ev_bh[o] = ev_bh[(o, 1)]

            def bwmm(e, colx, s):
                r = None
                for i in range(4):
                    r = e.matmul(psR[0:1, s * 512:(s + 1) * 512],
                                 K.sb["WB2C"][:, colx * 4 + i:colx * 4 + i + 1],
                                 BH[:, i * CH + s * 512:i * CH + (s + 1) * 512],
                                 start=(i == 0), stop=(i == 3))
                return r
            emw0 = P.op("pe", lambda e, bwmm=bwmm: [bwmm(e, 0, 0), bwmm(e, 0, 1)][-1],
                        deps=(ev_bh[3], K._ev_wb2, evg, evgb))
            ecp0 = P.op("ve", lambda e: e.tensor_copy(rowsc[0:1, 0:CH], psR[0:1, 0:CH]),
                        deps=(emw0,))
            emw1 = P.op("pe", lambda e, bwmm=bwmm: [bwmm(e, 1, 0), bwmm(e, 1, 1)][-1],
                        deps=(ecp0,))
            ecp1 = P.op("ve", lambda e: e.tensor_copy(rowsc[0:1, CH:2 * CH], psR[0:1, 0:CH]),
                        deps=(emw1,))
            ed = P.op("ve", lambda e: e.tensor_tensor(
                rowsc[0:1, 2 * CH:3 * CH], rowsc[0:1, 0:CH], rowsc[0:1, CH:2 * CH],
                AO.subtract), deps=(ecp1,))
            P.op("sc", lambda e, c0_=c0_: e.activation(
                BW0[0:1, c0_:c0_ + CH], rowsc[0:1, 2 * CH:3 * CH], AF.Sigmoid,
                bias=K.sb["bb2diff"][0:1, 0:1], scale=1.0), deps=(ed, K._ev_bb2))
            P.op("sc", lambda e, c0_=c0_: e.activation(
                BW1[0:1, c0_:c0_ + CH], rowsc[0:1, 2 * CH:3 * CH], AF.Sigmoid,
                bias=K.sb["bb2diff"][0:1, 1:2], scale=-1.0), deps=(ed, K._ev_bb2))
            ev_bw = P.ev("sc")

            ev_sp = None
            for i in range(NCH):
                slot = i % 2
                ewp = stream_och(io["Wpv"], i, slot, deps=(ev_sp, ev_bw))

                def pvm2(e, i=i, slot=slot, c0_=c0_):
                    r = None
                    for s in range(2):
                        for j in range(NCH):
                            r = e.matmul(psA[:, (s % 2) * 512:((s % 2) + 1) * 512],
                                         wb[:, slot * NCH * 128 + j * 128:
                                            slot * NCH * 128 + (j + 1) * 128],
                                         frhs(j, c0_ + s * 512, 512),
                                         start=(j == 0), stop=(j == NCH - 1))
                    return r
                em = P.op("pe", pvm2, deps=(ewp, ev_sp))
                epv = P.op("sc", lambda e, i=i: [
                    e.activation(t_a[:, s * 512:(s + 1) * 512],
                                 psA[:, (s % 2) * 512:((s % 2) + 1) * 512],
                                 AF.Identity, bias=K.bcol("bpv", i), scale=1.0)
                    for s in range(2)][-1], deps=(em,))
                epp = P.op("sy", lambda e, i=i, c0_=c0_: e.dma_start(
                    out=t_b[:], in_=io["ppT"][i * 128:(i + 1) * 128, c0_:c0_ + CH]),
                    deps=(ev_sp,))
                P.op("ve", lambda e: e.tensor_scalar(t_c[:], t_b[:], 1.0 / (2 * PI),
                                                     MAGIC, AO.mult, AO.add), deps=(epp, ev_sp))
                P.op("ve", lambda e: e.tensor_scalar(t_c[:], t_c[:], MAGIC, None, AO.subtract))
                P.op("ve", lambda e: e.scalar_tensor_tensor(t_b[:], t_c[:], -2.0 * PI, t_b[:],
                                                            AO.mult, AO.add))
                esin = P.op("sc", lambda e: e.activation(t_c[:], t_b[:], AF.Sin),
                            deps=(P.ev("ve"), epv))
                eabs = P.op("ve", lambda e: e.scalar_tensor_tensor(t_b[:], t_b[:], -1.0, t_b[:],
                                                                   AO.mult, AO.max),
                            deps=(esin,))
                ecos = P.op("sc", lambda e: e.activation(t_b[:], t_b[:], AF.Sin,
                                                         bias=0.5 * PI, scale=-1.0),
                            deps=(eabs,))
                P.op("ve", lambda e: e.tensor_tensor(t_d[:], t_b[:], t_a[:], AO.mult),
                     deps=(ecos, epv))
                P.op("ve", lambda e, i=i: e.tensor_tensor_scan(
                    t_d[:], t_d[:], t_d[:], CARR[:, 0 * 8 + i:0 * 8 + i + 1],
                    AO.add, AO.bypass))
                K.fill()
                P.op("ve", lambda e, i=i: e.tensor_copy(CARR[:, 0 * 8 + i:0 * 8 + i + 1],
                                                        t_d[:, CH - 1:CH]))
                es1 = P.op("sy", lambda e, i=i, c0_=c0_: e.dma_start(
                    out=io["MEMS"][0 * D + i * 128:0 * D + (i + 1) * 128, c0_:c0_ + CH],
                    in_=t_d[:]), deps=(P.ev("ve"),))
                P.op("ve", lambda e: e.tensor_tensor(t_d[:], t_c[:], t_a[:], AO.mult),
                     deps=(esin, es1))
                P.op("ve", lambda e, i=i: e.tensor_tensor_scan(
                    t_d[:], t_d[:], t_d[:], CARR[:, 1 * 8 + i:1 * 8 + i + 1],
                    AO.add, AO.bypass))
                K.fill()
                P.op("ve", lambda e, i=i: e.tensor_copy(CARR[:, 1 * 8 + i:1 * 8 + i + 1],
                                                        t_d[:, CH - 1:CH]))
                es2 = P.op("sy", lambda e, i=i, c0_=c0_: e.dma_start(
                    out=io["MEMS"][1 * D + i * 128:1 * D + (i + 1) * 128, c0_:c0_ + CH],
                    in_=t_d[:]), deps=(P.ev("ve"),))
                ewk = stream_w(K, wb[:, (2 + slot) * NCH * 128:(3 + slot) * NCH * 128],
                               lambda i=i: io["Wkv"][:, i * 128:(i + 1) * 128], NCH,
                               deps=(ev_sp, ev_g[NCH - 1]))

                def kvm2(e, i=i, slot=slot, c0_=c0_):
                    r = None
                    for s in range(2):
                        for j in range(NCH):
                            r = e.matmul(psA[:, (s % 2) * 512:((s % 2) + 1) * 512],
                                         wb[:, (2 + slot) * NCH * 128 + j * 128:
                                            (2 + slot) * NCH * 128 + (j + 1) * 128],
                                         frhs(j, c0_ + s * 512, 512),
                                         start=(j == 0), stop=(j == NCH - 1))
                    return r
                emk2_ = P.op("pe", kvm2, deps=(ewk, epv))
                ekv = P.op("sc", lambda e, i=i: [
                    e.activation(t_a[:, s * 512:(s + 1) * 512],
                                 psA[:, (s % 2) * 512:((s % 2) + 1) * 512],
                                 AF.Identity, bias=K.bcol("bkv", i), scale=1.0)
                    for s in range(2)][-1], deps=(emk2_, P.ev("ve")))
                P.op("ve", lambda e: e.tensor_tensor(t_a[:], t_a[:], VGB[:], AO.mult),
                     deps=(ekv, evgb))
                ksh = KEYPH[:, i * (CH + 1):i * (CH + 1) + CH]
                esn = P.op("sc", lambda e, ksh=ksh: e.activation(t_b[:], ksh, AF.Sin, scale=PI),
                           deps=(ev_keyph, P.ev("ve")))
                eab = P.op("ve", lambda e, ksh=ksh: e.scalar_tensor_tensor(
                    t_c[:], ksh, -1.0, ksh, AO.mult, AO.max), deps=(ev_keyph, esn))
                ecn = P.op("sc", lambda e: e.activation(t_c[:], t_c[:], AF.Sin,
                                                        bias=0.5 * PI, scale=-PI), deps=(eab,))
                if cs == 0:
                    P.op("ve", lambda e: e.tensor_scalar(
                        t_b[:, 0:1], t_b[:, 0:1], K.sb["mprev_t"][:, 0:1], None, AO.mult),
                        deps=(esn,))
                    ecn = P.op("ve", lambda e: e.tensor_scalar(
                        t_c[:, 0:1], t_c[:, 0:1], K.sb["mprev_t"][:, 0:1], None, AO.mult),
                        deps=(ecn,))
                    K.fill()
                P.op("ve", lambda e: e.tensor_tensor(t_d[:], t_a[:], t_c[:], AO.mult),
                     deps=(ecn,))
                P.op("ve", lambda e, i=i: e.tensor_tensor_scan(
                    t_d[:], t_d[:], t_d[:], CARR[:, 2 * 8 + i:2 * 8 + i + 1],
                    AO.add, AO.bypass))
                K.fill()
                P.op("ve", lambda e, i=i: e.tensor_copy(CARR[:, 2 * 8 + i:2 * 8 + i + 1],
                                                        t_d[:, CH - 1:CH]))
                es3 = P.op("sy", lambda e, i=i, c0_=c0_: e.dma_start(
                    out=io["MEMS"][2 * D + i * 128:2 * D + (i + 1) * 128, c0_:c0_ + CH],
                    in_=t_d[:]), deps=(P.ev("ve"),))
                P.op("ve", lambda e: e.tensor_tensor(t_d[:], t_a[:], t_b[:], AO.mult),
                     deps=(esn, es3))
                P.op("ve", lambda e, i=i: e.tensor_tensor_scan(
                    t_d[:], t_d[:], t_d[:], CARR[:, 3 * 8 + i:3 * 8 + i + 1],
                    AO.add, AO.bypass))
                K.fill()
                P.op("ve", lambda e, i=i: e.tensor_copy(CARR[:, 3 * 8 + i:3 * 8 + i + 1],
                                                        t_d[:, CH - 1:CH]))
                es4 = P.op("sy", lambda e, i=i, c0_=c0_: e.dma_start(
                    out=io["MEMS"][3 * D + i * 128:3 * D + (i + 1) * 128, c0_:c0_ + CH],
                    in_=t_d[:]), deps=(P.ev("ve"),))
                ev_sp = P.op("sy", lambda e, i=i, c0_=c0_: e.dma_start(
                    out=io["MEMS"][4 * D + i * 128:4 * D + (i + 1) * 128, c0_:c0_ + CH],
                    in_=KEYPH[:, i * (CH + 1) + 1:i * (CH + 1) + 1 + CH]),
                    deps=(ev_keyph, es4))
            ev_sp_chunk[cs] = ev_sp
        P.barrier()

        ex2 = P.op("gd", lambda e: e.dma_start(
            out=io["xsm"][bass.ds(K.r_mysm, 128), 0:34], in_=CARR[:]),
            deps=(ev_sp_chunk[NC4 - 1], P.ev("ve")))
        ev_r2 = flag_and_poll(K, 2, (ex2,))
        ex2r = P.op("gd", lambda e: e.dma_start(
            out=PCAR[:], in_=io["xsm"][bass.ds(K.r_peersm, 128), 0:34]), deps=(ev_r2,))
        emc = P.op("ve", lambda e: e.tensor_scalar(PCAR[:], PCAR[:], K.sb["mprev_t"][:, 0:1],
                                                   None, AO.mult), deps=(ex2r,))
        P.barrier()
        cscan.close()

        cret = ctx.enter_context(ExitStack())
        BLD = K.sbuf("B_BLD", [128, NCH * 1024], F32, cret)
        BLN = K.sbuf("B_BLN", [128, NCH * 1024], BF16, cret)
        rt_a = K.sbuf("B_rta", [128, 1024], F32, cret)
        rt_b = K.sbuf("B_rtb", [128, 1024], F32, cret)
        rt_c = K.sbuf("B_rtc", [128, 1024], F32, cret)
        rt_d = K.sbuf("B_rtd", [128, 1024], F32, cret)
        rwb = K.sbuf("B_rwb", [128, 2 * NCH * 128], BF16, cret)
        rpsA = K.psum("B_rpsA", [128, 2 * 512], F32, cret)
        rpsB = K.psum("B_rpsB", [128, 1024], F32, cret)

        for cs in range(NC4):
            c0_ = cs * CH
            with ExitStack() as cr:
                g = K.sbuf(f"B_g{cs}", [1, CH], F32, cr)
                P.op("ve", lambda e, c0_=c0_, g=g: e.tensor_scalar(
                    g[:], GSR[0:1, c0_:c0_ + CH], PCAR[0:1, 32:33], None, AO.add),
                    deps=(emc,))
                P.op("ve", lambda e, g=g: e.tensor_scalar(g[:], g[:], 1.0, None, AO.max))
                esq = P.op("sc", lambda e, g=g: e.activation(g[:], g[:], AF.Sqrt, bias=1e-12,
                                                             scale=1.0), deps=(P.ev("ve"),))
                P.op("ve", lambda e, g=g: e.reciprocal(g[:], g[:]), deps=(esq,))
                erg = P.op("ve", lambda e, g=g: e.tensor_scalar(g[:], g[:], 1.0 / 32.0, None,
                                                                AO.mult))
                b0r = K.sbuf(f"B_b0r{cs}", [1, 2 * CH], F32, cr)
                P.op("ve", lambda e, c0_=c0_, b0r=b0r: e.tensor_scalar(
                    b0r[:, 0:CH], BW0[0:1, c0_:c0_ + CH], 1.0 / 32.0, None, AO.mult))
                ebwr = P.op("ve", lambda e, c0_=c0_, b0r=b0r: e.tensor_copy(
                    b0r[:, CH:2 * CH], BW1[0:1, c0_:c0_ + CH]))
                (grorwb, bw0b, bw1b), ev_brwb = bcast_rows(
                    K, cr, f"Br{cs}", [(g[:], erg), (b0r[:, 0:CH], ebwr),
                                       (b0r[:, CH:2 * CH], ebwr)], CH, ps=rpsB)
                (m3cb, r3cb), ev_b3 = bcast_dram_rows(
                    K, cr, f"Bc{cs}", [4, 5], CH, c0_, ps=rpsB, deps=(ev_brwb,))

                ev_t = None
                for i in range(NCH):
                    e1_ = P.op("sy", lambda e, i=i, c0_=c0_: e.dma_start(
                        out=rt_a[:], in_=io["MEMS"][0 * D + i * 128:0 * D + (i + 1) * 128,
                                                   c0_:c0_ + CH]), deps=(ev_t,))
                    e2_ = P.op("sy", lambda e, i=i, c0_=c0_: e.dma_start(
                        out=rt_b[:], in_=io["MEMS"][1 * D + i * 128:1 * D + (i + 1) * 128,
                                                   c0_:c0_ + CH]))
                    epp = P.op("sy", lambda e, i=i, c0_=c0_: e.dma_start(
                        out=rt_c[:], in_=io["ppT"][i * 128:(i + 1) * 128, c0_:c0_ + CH]))
                    P.op("ve", lambda e: e.tensor_scalar(rt_d[:], rt_c[:], 1.0 / (2 * PI),
                                                         MAGIC, AO.mult, AO.add),
                         deps=(epp, ev_t))
                    P.op("ve", lambda e: e.tensor_scalar(rt_d[:], rt_d[:], MAGIC, None,
                                                         AO.subtract))
                    P.op("ve", lambda e: e.scalar_tensor_tensor(rt_c[:], rt_d[:], -2.0 * PI,
                                                                rt_c[:], AO.mult, AO.add))
                    esin = P.op("sc", lambda e: e.activation(rt_d[:], rt_c[:], AF.Sin),
                                deps=(P.ev("ve"),))
                    eabs = P.op("ve", lambda e: e.scalar_tensor_tensor(
                        rt_c[:], rt_c[:], -1.0, rt_c[:], AO.mult, AO.max), deps=(esin,))
                    ecos = P.op("sc", lambda e: e.activation(rt_c[:], rt_c[:], AF.Sin,
                                                             bias=0.5 * PI, scale=-1.0),
                                deps=(eabs,))
                    P.op("ve", lambda e, i=i: e.scalar_tensor_tensor(
                        rt_a[:], rt_a[:], PCAR[:, 0 * 8 + i:0 * 8 + i + 1], rt_c[:],
                        AO.add, AO.mult), deps=(e1_, ecos, emc))
                    P.op("ve", lambda e, i=i: e.scalar_tensor_tensor(
                        rt_b[:], rt_b[:], PCAR[:, 1 * 8 + i:1 * 8 + i + 1], rt_d[:],
                        AO.add, AO.mult), deps=(e2_, esin))
                    P.op("ve", lambda e: e.tensor_tensor(rt_a[:], rt_a[:], rt_b[:], AO.add))
                    eposp = P.op("ve", lambda e, i=i, bw0b=bw0b: e.tensor_tensor(
                        BLD[:, i * CH:(i + 1) * CH], rt_a[:], bw0b, AO.mult), deps=(ev_brwb,))
                    e3_ = P.op("sy", lambda e, i=i, c0_=c0_: e.dma_start(
                        out=rt_a[:], in_=io["MEMS"][2 * D + i * 128:2 * D + (i + 1) * 128,
                                                   c0_:c0_ + CH]), deps=(eposp,))
                    e4_ = P.op("sy", lambda e, i=i, c0_=c0_: e.dma_start(
                        out=rt_b[:], in_=io["MEMS"][3 * D + i * 128:3 * D + (i + 1) * 128,
                                                   c0_:c0_ + CH]))
                    e5_ = P.op("sy", lambda e, i=i, c0_=c0_: e.dma_start(
                        out=rt_d[:], in_=io["MEMS"][4 * D + i * 128:4 * D + (i + 1) * 128,
                                                   c0_:c0_ + CH]))
                    esn2 = P.op("sc", lambda e: e.activation(rt_c[:], rt_d[:], AF.Sin, scale=PI),
                                deps=(e5_, eposp))
                    eab2 = P.op("ve", lambda e: e.scalar_tensor_tensor(
                        rt_d[:], rt_d[:], -1.0, rt_d[:], AO.mult, AO.max), deps=(esn2,))
                    ecs2 = P.op("sc", lambda e: e.activation(rt_d[:], rt_d[:], AF.Sin,
                                                             bias=0.5 * PI, scale=-PI),
                                deps=(eab2,))
                    P.op("ve", lambda e, i=i: e.scalar_tensor_tensor(
                        rt_a[:], rt_a[:], PCAR[:, 2 * 8 + i:2 * 8 + i + 1], rt_d[:],
                        AO.add, AO.mult), deps=(e3_, ecs2))
                    P.op("ve", lambda e, i=i: e.scalar_tensor_tensor(
                        rt_b[:], rt_b[:], PCAR[:, 3 * 8 + i:3 * 8 + i + 1], rt_c[:],
                        AO.add, AO.mult), deps=(e4_,))
                    P.op("ve", lambda e: e.tensor_tensor(rt_a[:], rt_a[:], rt_b[:], AO.add))
                    P.op("ve", lambda e, grorwb=grorwb: e.tensor_tensor(
                        rt_a[:], rt_a[:], grorwb, AO.mult))
                    P.op("ve", lambda e, bw1b=bw1b: e.tensor_tensor(
                        rt_a[:], rt_a[:], bw1b, AO.mult))
                    ev_t = P.op("ve", lambda e, i=i: e.tensor_tensor(
                        BLD[:, i * CH:(i + 1) * CH], BLD[:, i * CH:(i + 1) * CH], rt_a[:],
                        AO.add))
                ev_bld = ev_t

                def srcb(i):
                    return BLD[:, i * CH:(i + 1) * CH], ev_bld
                mu4, r4, _v4, ev_st4 = ln_stats(K, cr, f"B{cs}", srcb, NCH, CH, 1e-5)
                (m4b, r4b), ev_bc4 = bcast_rows(K, cr, f"B{cs}b",
                                                [(mu4, ev_st4), (r4, ev_st4)], CH, ps=rpsB)
                evn = None
                for i in range(NCH):
                    P.op("ve", lambda e, i=i, m4b=m4b: e.tensor_tensor(
                        BLD[:, i * CH:(i + 1) * CH], BLD[:, i * CH:(i + 1) * CH], m4b,
                        AO.subtract), deps=(ev_bc4, evn))
                    P.op("ve", lambda e, i=i, r4b=r4b: e.tensor_tensor(
                        BLD[:, i * CH:(i + 1) * CH], BLD[:, i * CH:(i + 1) * CH], r4b,
                        AO.mult))
                    evn = P.op("ve", lambda e, i=i: e.tensor_scalar(
                        BLN[:, i * CH:(i + 1) * CH], BLD[:, i * CH:(i + 1) * CH],
                        K.bcol("bln_g", i), K.bcol("bln_b", i), AO.mult, AO.add))
                ev_bln = evn

                ev_o = {}
                for o in range(NCH):
                    slot = o % 2
                    ewo = stream_w(K, rwb[:, slot * NCH * 128:(slot + 1) * NCH * 128],
                                   lambda o=o: io["Wo"][:, o * 128:(o + 1) * 128], NCH,
                                   deps=(ev_o.get(o - 2), ev_bln))
                    exl = P.op("sy", lambda e, o=o, c0_=c0_: e.dma_start(
                        out=rt_c[:], in_=io["XRES"][o * 128:(o + 1) * 128, c0_:c0_ + CH]),
                        deps=(ev_o.get(o - 1), ev_bld))
                    P.op("ve", lambda e, m3cb=m3cb: e.tensor_tensor(
                        rt_d[:], rt_c[:], m3cb, AO.subtract), deps=(exl, ev_b3))
                    ezz = P.op("ve", lambda e, r3cb=r3cb: e.tensor_tensor(
                        rt_d[:], rt_d[:], r3cb, AO.mult))

                    def omm(e, o=o, slot=slot):
                        r = None
                        for s in range(2):
                            for i in range(NCH):
                                r = e.matmul(rpsA[:, (s % 2) * 512:((s % 2) + 1) * 512],
                                             rwb[:, slot * NCH * 128 + i * 128:
                                                slot * NCH * 128 + (i + 1) * 128],
                                             BLN[:, i * CH + s * 512:i * CH + (s + 1) * 512],
                                             start=(i == 0), stop=(i == NCH - 1))
                        return r
                    emo = P.op("pe", omm, deps=(ewo, ev_bln, ev_o.get((o - 1, "m"))))
                    P.op("ve", lambda e, o=o: [
                        e.scalar_tensor_tensor(rt_d[:, s * 512:(s + 1) * 512],
                                               rpsA[:, (s % 2) * 512:((s % 2) + 1) * 512],
                                               K.bcol("bo", o), rt_d[:, s * 512:(s + 1) * 512],
                                               AO.add, AO.add) for s in range(2)][-1],
                        deps=(emo, ezz))
                    eadd = P.op("ve", lambda e: e.tensor_tensor(rt_d[:], rt_d[:], rt_c[:], AO.add))
                    ev_o[(o, "m")] = eadd
                    ev_o[o] = P.op("sy", lambda e, o=o, c0_=c0_: e.dma_start(
                        out=io["XRES"][o * 128:(o + 1) * 128, c0_:c0_ + CH], in_=rt_d[:]),
                        deps=(eadd,))
                P.barrier()
        cret.close()
        P.barrier()


import os
PHASES = tuple(os.environ.get("KPHASES", "A,F1,B,F2").split(","))


def build_body(K):
    nc, P, io = K.nc, K.P, K.io
    K._ev_cwd = {}
    K._fw_cached = False
    K._fw_st1 = {}
    K._fw_st2 = {}
    setup(K)
    bb2g = K.sbuf("bb2g", [1, 1], F32)
    K._ev_bg2g = P.op("sy", lambda e: e.dma_start(out=bb2g[:], in_=K.io["bg2v"]))
    P.barrier()
    def dbg_snap(nm):
        if not os.environ.get("KPHDBG"):
            return
        P.barrier()
        ev = P.op("sy", lambda e, nm=nm: e.dma_start(out=io[nm], in_=io["XRES"]))
        P.op("ve", None, deps=(ev,), inc=0)
        P.barrier()

    if "A" in PHASES:
        phase_attn_a(K)
        dbg_snap("dbgA")
    if "F1" in PHASES:
        phase_ffn(K, 1)
        dbg_snap("dbgF1")
        if os.environ.get("KDBG"):
            P.barrier()
            e1 = P.op("sy", lambda e: e.dma_start(out=io["dbgCWD"], in_=io["CWD"]))
            e2 = P.op("sy", lambda e: e.dma_start(out=io["dbgOUT"],
                                                  in_=io["MEMS"][0:D, :]))
            P.op("ve", None, deps=(e1, e2), inc=0)
            P.barrier()
    if "B" in PHASES:
        phase_attn_b(K)
        dbg_snap("dbgB")
    if "F2" in PHASES:
        phase_ffn(K, 2)
    if "F2" not in PHASES:
        fin = K.sbuf("fin_dbg", [128, T], F32)
        ev = None
        for i in range(NCH):
            e1 = P.op("sy", lambda e, i=i: e.dma_start(
                out=fin[:], in_=io["XRES"][i * 128:(i + 1) * 128, :]), deps=(ev,))
            ev = P.op("sy", lambda e, i=i: e.dma_start(
                out=io["out"][i * 128:(i + 1) * 128, :], in_=fin[:]), deps=(e1,))
        P.op("ve", None, deps=(ev,), inc=0)
    P.barrier()


def host_bias_table(inp):
    tbl = np.zeros((BTOT, 128), np.float32)

    def put(name, vec):
        v = np.asarray(vec, np.float32).reshape(-1)
        n = (v.size + 127) // 128
        pad = np.zeros(n * 128, np.float32)
        pad[:v.size] = v
        tbl[BROWS[name]:BROWS[name] + n] = pad.reshape(n, 128)
    put("aln_w", inp["L0_attn_a_ln_w"]); put("aln_b", inp["L0_attn_a_ln_b"])
    put("bqkv", inp["L0_attn_a_b_qkv"]); put("bproj", inp["L0_attn_a_b_proj"])
    put("rb", inp["L0_ffn_router_b"])
    put("fb1", inp["L0_ffn_b1"]); put("fb2", inp["L0_ffn_b2"])
    put("fln_w", inp["L0_ffn_ln_w"]); put("fln_b", inp["L0_ffn_ln_b"])
    put("bpv", inp["L0_attn_b_bpv"]); put("bk", inp["L0_attn_b_bk"])
    put("bkv", inp["L0_attn_b_bkv"]); put("bg1", inp["L0_attn_b_bg1"])
    put("bb1", inp["L0_attn_b_bb1"])
    put("bmisc", np.concatenate([np.asarray(inp["L0_attn_b_bg2"], np.float32).reshape(-1),
                                 np.asarray(inp["L0_attn_b_bb2"], np.float32).reshape(-1)]))
    put("bln_g", inp["L0_attn_b_ln_g"]); put("bln_b", inp["L0_attn_b_ln_b"])
    put("bo", inp["L0_attn_b_bo"])
    return tbl


_NC_CACHE = {}


def build_program():
    if "nc" in _NC_CACHE:
        return _NC_CACHE["nc"]
    nc = bass.Bass("TRN2", target_bir_lowering=False, detect_race_conditions=False,
                   num_devices=8)
    io = {}

    def din(name, shape, dt=F32):
        io[name] = nc.dram_tensor(name, list(shape), dt, kind="ExternalInput").ap()
    din("xT", [D, T])
    din("Wqkv", [D, 3 * D]); din("Wproj", [D, D])
    din("ppT", [D, T])
    din("Wpv", [D, D]); din("Wk", [D, D]); din("Wkv", [D, D])
    din("Wg1", [2 * D, D]); din("Wg2", [D, 1]); din("Wb1", [D, 512]); din("Wb2", [512, 2])
    din("Wo", [D, D])
    din("w1T", [NE, D, FF]); din("w2T", [NE, FF, D]); din("rwT", [D, NE])
    din("btab", [BTOT, 128])
    din("rbrow", [1, 8])
    din("b2m", [NE, D])
    din("bb2d", [1, 2])
    din("bg2v", [1, 1])
    din("slot", [1, 8], I32)
    din("nonces", [N_ROUNDS, 16], I32)
    din("mprev", [128, 1])
    io["out"] = nc.dram_tensor("out", [D, T], F32, kind="ExternalOutput").ap()
    io["XRES"] = nc.dram_tensor("XRES", [D, T], F32, kind="Internal").ap()
    io["MSAR"] = nc.dram_tensor("MSAR", [D, T], BF16, kind="Internal").ap()
    io["DEN"] = nc.dram_tensor("DEN", [16, T], F32, kind="Internal").ap()
    io["CWD"] = nc.dram_tensor("CWD", [NE, T], BF16, kind="Internal").ap()
    io["ROWSD"] = nc.dram_tensor("ROWSD", [8, T], F32, kind="Internal").ap()
    io["MEMS"] = nc.dram_tensor("MEMS", [5 * D, T], F32, kind="Internal").ap()
    io["W1BD"] = nc.dram_tensor("W1BD", [32 * 128, NCH * 1024], BF16, kind="Internal").ap()
    io["W2BD"] = nc.dram_tensor("W2BD", [32 * 128, 8 * 1024], BF16, kind="Internal").ap()
    io["KVSTG"] = nc.dram_tensor("KVSTG", [2 * D, T], BF16, kind="Internal").ap()
    io["xkv"] = nc.dram_tensor("xkv", [2 * 2 * D, T], BF16, kind="Internal",
                               addr_space="Shared").ap()
    io["xsm"] = nc.dram_tensor("xsm", [256, 48], F32, kind="Internal",
                               addr_space="Shared").ap()
    io["xfl"] = nc.dram_tensor("xfl", [2, 16], I32, kind="Internal",
                               addr_space="Shared").ap()
    if os.environ.get("KPHDBG"):
        for nm in ("dbgA", "dbgF1", "dbgB"):
            io[nm] = nc.dram_tensor(nm, [D, T], F32, kind="ExternalOutput").ap()
    if os.environ.get("KDBG"):
        io["dbgCWD"] = nc.dram_tensor("dbgCWD", [NE, T], BF16, kind="ExternalOutput").ap()
        io["dbgOUT"] = nc.dram_tensor("dbgOUT", [D, T], F32, kind="ExternalOutput").ap()

    with ExitStack() as ctx:
        P = Prog(nc, ctx)
        K = Kern(nc, ctx, P, io)
        build_body(K)
        block = ctx.enter_context(nc.Block())
        nbad = P.check_closures()
        assert nbad == 0, f"{nbad} lambdas with drifted closures (late-binding bug)"
        P.emit(block)
    _NC_CACHE["nc"] = nc
    _NC_CACHE["P"] = P
    return nc


def make_inmaps(inputs):
    x = np.asarray(inputs["L0_attn_a_z"], np.float32)
    pp = np.asarray(inputs["L0_attn_b_pos_phases"], np.float32)[:L]
    w1 = np.asarray(inputs["L0_ffn_w1"], np.float32)
    w2 = np.asarray(inputs["L0_ffn_w2"], np.float32)
    rw = np.asarray(inputs["L0_ffn_router_w"], np.float32)
    btab = host_bias_table(inputs)
    w1T = np.ascontiguousarray(np.transpose(w1, (0, 2, 1)))
    w2T = np.ascontiguousarray(np.transpose(w2, (0, 2, 1)))
    rwT = np.ascontiguousarray(rw.T)
    rng = np.random.default_rng()
    nonces = rng.integers(1, 2**31 - 1, size=(N_ROUNDS, 16)).astype(np.int32)
    bb2 = np.asarray(inputs["L0_attn_b_bb2"], np.float32).reshape(-1)
    bb2d = np.array([[bb2[0] - bb2[1], bb2[1] - bb2[0]]], np.float32)
    rbrow = np.asarray(inputs["L0_ffn_router_b"], np.float32).reshape(1, 8)
    bg2v = np.asarray(inputs["L0_attn_b_bg2"], np.float32).reshape(1, 1)

    common = {
        "Wqkv": np.ascontiguousarray(inputs["L0_attn_a_W_qkv"], dtype=np.float32),
        "Wproj": np.ascontiguousarray(inputs["L0_attn_a_W_proj"], dtype=np.float32),
        "Wpv": np.ascontiguousarray(inputs["L0_attn_b_Wpv"], dtype=np.float32),
        "Wk": np.ascontiguousarray(inputs["L0_attn_b_Wk"], dtype=np.float32),
        "Wkv": np.ascontiguousarray(inputs["L0_attn_b_Wkv"], dtype=np.float32),
        "Wg1": np.ascontiguousarray(inputs["L0_attn_b_Wg1"], dtype=np.float32),
        "Wg2": np.ascontiguousarray(inputs["L0_attn_b_Wg2"], dtype=np.float32),
        "Wb1": np.ascontiguousarray(inputs["L0_attn_b_Wb1"], dtype=np.float32),
        "Wb2": np.ascontiguousarray(inputs["L0_attn_b_Wb2"], dtype=np.float32),
        "Wo": np.ascontiguousarray(inputs["L0_attn_b_Wo"], dtype=np.float32),
        "w1T": w1T, "w2T": w2T, "rwT": rwT, "btab": btab,
        "rbrow": rbrow, "bb2d": bb2d, "bg2v": bg2v, "nonces": nonces,
        "b2m": np.ascontiguousarray(inputs["L0_ffn_b2"], dtype=np.float32),
    }
    in_maps = []
    for c in range(8):
        b = (c // 2) % 2
        h = c % 2
        m = dict(common)
        m["xT"] = np.ascontiguousarray(x[b, h * T:(h + 1) * T, :].T)
        m["ppT"] = np.ascontiguousarray(pp[h * T:(h + 1) * T, :].T)
        m["slot"] = np.array([[h * 2048, h, h * 128, (1 - h) * 128, 0, 0, 0, 0]], np.int32)
        m["mprev"] = np.full((128, 1), float(h), np.float32)
        in_maps.append(m)
    return in_maps


def kernel(**inputs):
    nc = build_program()
    in_maps = make_inmaps(inputs)
    res = run_bass_kernel_spmd(nc, in_maps, core_ids=list(range(8)))
    out = np.empty((2, L, D), np.float32)
    for c in range(4):
        b = c // 2
        h = c % 2
        out[b, h * T:(h + 1) * T, :] = np.asarray(res.results[c]["out"]).T
    return out


# version 05
# speedup vs baseline: 1.3729x; 1.1063x over previous
import numpy as np
import ml_dtypes
import concourse.bass as bass
import concourse.mybir as mybir
from concourse.alu_op_type import AluOpType
from concourse.bass_utils import run_bass_kernel_spmd
from contextlib import ExitStack

F32 = mybir.dt.float32
BF16 = mybir.dt.bfloat16
I32 = mybir.dt.int32
AF = mybir.ActivationFunctionType
AO = AluOpType
AX = mybir.AxisListType
PI = float(np.pi)
MAGIC = 12582912.0

D = 1024
NCH = 8
T = 4096
L = 8192
FF = 4096
NE = 8
NPAIR = 8
N_ROUNDS = 4
NS = T // 512

ENGS = ("sy", "sc", "ve", "pe", "gp", "gd")
BLOCK_OF = {"sy": "sync", "sc": "scalar", "ve": "vector", "pe": "tensor",
            "gp": "gpsimd", "gd": "gpsimd"}
INC_OF = {"sy": 16, "sc": 1, "ve": 1, "pe": 1, "gp": 1, "gd": 16}
ENG_OF_BLOCK = {"sync": "sy", "scalar": "sc", "vector": "ve", "tensor": "pe",
                "gpsimd": "gp"}


class Prog:
    def __init__(self, nc, ctx):
        self.nc = nc
        self.sems = {n: ctx.enter_context(nc.semaphore("S_" + n)) for n in ENGS}
        self.count = {n: 0 for n in ENGS}
        self.blocks = {b: [] for b in ("sync", "scalar", "vector", "tensor", "gpsimd")}
        self.last_wait = {b: {} for b in self.blocks}
        self.snaps = []

    def op(self, eng, fn, deps=(), inc=None):
        blk = BLOCK_OF[eng]
        if inc is None:
            inc = INC_OF[eng]
        real = []
        for d in deps:
            if d is None:
                continue
            dsem, dval = d
            if BLOCK_OF[dsem] == blk:
                continue
            prev = self.last_wait[blk].get(dsem, 0)
            if dval > prev:
                self.last_wait[blk][dsem] = dval
                real.append((dsem, dval))
        import traceback as _tb
        fr = _tb.extract_stack(limit=2)[0]
        tag = f"{fr.filename.split('/')[-1]}:{fr.lineno}"
        if fn is not None and getattr(fn, "__closure__", None):
            snap = tuple(id(c.cell_contents) for c in fn.__closure__)
            self.snaps.append((fn, snap, tag))
        self.blocks[blk].append((eng, fn, real, inc, tag))
        self.count[eng] += inc
        return (eng, self.count[eng])

    def check_closures(self):
        bad = {}
        for fn, snap, tag in self.snaps:
            cur = tuple(id(c.cell_contents) for c in fn.__closure__)
            if cur != snap:
                names = [n for n, a, b in zip(fn.__code__.co_freevars, snap, cur) if a != b]
                bad.setdefault((tag, tuple(names)), [0])[0] += 1
        for (tag, names), cnt in sorted(bad.items()):
            print(f"CLOSURE DRIFT {tag} vars={names} x{cnt[0]}")
        return len(bad)

    def ev(self, eng):
        return (eng, self.count[eng])

    def barrier(self):
        evs = [(n, self.count[n]) for n in ENGS if self.count[n] > 0]
        for b in self.blocks:
            self.op(ENG_OF_BLOCK[b], None, deps=evs, inc=0)

    def emit(self, block):
        for blk_name, lst in self.blocks.items():
            if not lst:
                continue

            def body(engine, lst=lst):
                for eng, fn, deps, inc, tag in lst:
                    for dsem, dval in deps:
                        engine.wait_ge(self.sems[dsem], dval)
                    if fn is None:
                        continue
                    r = fn(engine)
                    if inc > 0:
                        assert r is not None, f"op on {eng} returned None, inc={inc}"
                        r.then_inc(self.sems[eng], inc)
            getattr(block, blk_name)(body)


def _bias_spec():
    spec = [("aln_w", 8), ("aln_b", 8), ("bqkv", 24), ("bproj", 8), ("rb", 1),
            ("fb1", NE * FF // 128), ("fb2", NE * D // 128),
            ("fln_w", 8), ("fln_b", 8),
            ("bpv", 8), ("bk", 8), ("bkv", 8), ("bg1", 8), ("bb1", 4),
            ("bmisc", 1), ("bln_g", 8), ("bln_b", 8), ("bo", 8)]
    rows, off = {}, 0
    for name, n in spec:
        rows[name] = off
        off += n
    return rows, ((off + 127) // 128) * 128


BROWS, BTOT = _bias_spec()

CONSTS = [1e-5, 1e-12, 0.125, 1.0, -1.0, 0.5, -PI, 0.5 * PI, PI, -2.0 * PI,
          1.0 / (2.0 * PI), MAGIC, 1.0 / D, 1.0 / 32.0, -1e30, 2.0, -0.125, 0.0]


class Kern:
    def __init__(self, nc, ctx, P, io):
        self.nc, self.ctx, self.P, self.io = nc, ctx, P, io
        self.sb = {}
        self._b3 = {}

    def sbuf(self, name, shape, dt, ctx=None):
        t = (ctx or self.ctx).enter_context(self.nc.sbuf_tensor(name, list(shape), dt))
        self.sb[name] = t
        return t

    def psum(self, name, shape, dt=F32, ctx=None):
        return (ctx or self.ctx).enter_context(self.nc.psum_tensor(name, list(shape), dt))

    def bcol(self, name, i=0):
        c = BROWS[name] + i
        return self.sb["BCOLS"][:, c:c + 1]

    def slotv(self, e, idx, mx):
        return e.value_load(self.sb["slot_t"][0:1, idx:idx + 1], min_val=0, max_val=mx)


def setup(K):
    nc, P, io = K.nc, K.P, K.io
    consts = K.sbuf("consts", [128, len(CONSTS)], F32)
    for i, v in enumerate(CONSTS):
        nc.const_aps.register(F32, v, consts[:, i:i + 1])
    for i, v in enumerate(CONSTS):
        P.op("ve", lambda e, i=i, v=v: e.memset(consts[:, i:i + 1], v))
    ones1f = K.sbuf("ones1f", [1, 128], F32)
    onescb = K.sbuf("onescb", [128, 1], BF16)
    P.op("ve", lambda e: e.memset(ones1f[:], 1.0))
    P.op("ve", lambda e: e.memset(onescb[:], 1.0))
    jnk = K.sbuf("jnk", [128, 1], F32)

    def fill(n=6):
        for _ in range(n):
            P.op("ve", lambda e: e.memset(jnk[:], 0.0))
    K.fill = fill

    idf = K.sbuf("idf", [128, 128], F32)
    idb = K.sbuf("idb", [128, 128], BF16)
    bc = K.sbuf("BCOLS", [128, BTOT], F32)
    with ExitStack() as cset:
        iota_r = K.sbuf("iota_r", [1, 128], F32, cset)
        iota_c = K.sbuf("iota_c", [128, 1], F32, cset)
        e1 = P.op("gp", lambda e: e.iota(iota_r[:], [[1, 128]], channel_multiplier=0, allow_small_or_imprecise_dtypes=True))
        e2 = P.op("gp", lambda e: e.iota(iota_c[:], [[0, 1]], channel_multiplier=1, allow_small_or_imprecise_dtypes=True))
        psb = K.psum("ps_id", [128, 128], F32, cset)
        e3 = P.op("pe", lambda e: e.matmul(psb[:], ones1f[:], iota_r[:], start=True, stop=True),
                  deps=(e1, P.ev("ve")))
        idt = K.sbuf("idt", [128, 128], F32, cset)
        P.op("ve", lambda e: e.tensor_scalar(idf[:], psb[:], iota_c[:], None, AO.is_ge),
             deps=(e3, e2))
        P.op("ve", lambda e: e.tensor_scalar(idt[:], psb[:], iota_c[:], None, AO.is_le))
        K.fill()
        e4 = P.op("ve", lambda e: e.tensor_tensor(idf[:], idf[:], idt[:], AO.mult))
        K.fill()
        P.op("ve", lambda e: e.tensor_copy(idb[:], idf[:]))

        braw = K.sbuf("braw", [128, BTOT], F32, cset)
        ld = P.op("sy", lambda e: e.dma_start(
            out=braw[:].rearrange("p (b c) -> p b c", b=BTOT // 128),
            in_=io["btab"].rearrange("(b p) c -> p b c", p=128)))
        evc = None
        for b in range(BTOT // 128):
            et = P.op("pe", lambda e, b=b: e.transpose(psb[:], braw[:, b * 128:(b + 1) * 128], idf[:]),
                      deps=(ld, e4, evc))
            evc = P.op("sc", lambda e, b=b: e.copy(out=bc[:, b * 128:(b + 1) * 128], in_=psb[:]),
                      deps=(et,))
        P.barrier()

    slot_t = K.sbuf("slot_t", [1, 8], I32)
    non_t = K.sbuf("non_t", [N_ROUNDS, 16], I32)
    mprev_t = K.sbuf("mprev_t", [128, 1], F32)
    P.op("sy", lambda e: e.dma_start(out=slot_t[:], in_=io["slot"]))
    P.op("sy", lambda e: e.dma_start(out=non_t[:], in_=io["nonces"]))
    P.op("sy", lambda e: e.dma_start(out=mprev_t[:], in_=io["mprev"]))
    P.barrier()

    def f(e):
        K.r_myfc = e.value_load(slot_t[0:1, 1:2])
        K.r_mysm = e.value_load(slot_t[0:1, 2:3])
        K.r_peersm = e.value_load(slot_t[0:1, 3:4])
        return e.engine_nop()
    P.op("gp", f)

    wg2c = K.sbuf("WG2C", [128, 8], BF16)
    wb2c = K.sbuf("WB2C", [128, 8], BF16)
    rbb = K.sbuf("RBB", [128, 8], F32)
    bb2d = K.sbuf("bb2diff", [1, 2], F32)
    K._ev_wg2 = P.op("gd", lambda e: e.dma_start(
        out=wg2c[:].rearrange("p (i c) -> p i c", i=NCH),
        in_=io["Wg2"].rearrange("(i p) c -> p i c", p=128),
        allow_slow_non_contiguous=True))
    K._ev_wb2 = P.op("gd", lambda e: e.dma_start(
        out=wb2c[:].rearrange("p (c i) -> p c i", c=2),
        in_=io["Wb2"].rearrange("(i p) c -> p c i", p=128),
        allow_slow_non_contiguous=True))
    K._ev_rbb = P.op("sy", lambda e: e.dma_start(
        out=rbb[:], in_=io["rbrow"].broadcast_to((128, 8))))
    K._ev_bb2 = P.op("sy", lambda e: e.dma_start(out=bb2d[:], in_=io["bb2d"]))
    b2b = K.sbuf("B2B", [NE, D], BF16)
    K._ev_b2m = P.op("gd", lambda e: e.dma_start(out=b2b[:], in_=io["b2m"]))
    P.barrier()


def flag_and_poll(K, rnd, deps):
    nc, P, io = K.nc, K.P, K.io
    non_t = K.sb["non_t"]
    ef = P.op("gd", lambda e: e.dma_start(
        out=io["xfl"][bass.ds(K.r_myfc, 1), :], in_=non_t[rnd:rnd + 1, :]), deps=deps)

    def poll(e):
        e0 = e.value_load(non_t[rnd:rnd + 1, 0:1])
        e1 = e.value_load(non_t[rnd:rnd + 1, 1:2])
        with e.register(f"pc{rnd}") as cond, e.register(f"pv{rnd}") as v, \
             e.register(f"p2{rnd}") as c2:
            e.reg_mov(cond, 1)
            with e.While(cond):
                e.load(v, io["xfl"][0:1, 0:1])
                e.reg_alu(cond, v, e0, AO.not_equal)
                e.load(v, io["xfl"][0:1, 1:2])
                e.reg_alu(c2, v, e1, AO.not_equal)
                e.reg_alu(cond, cond, c2, AO.add)
                e.load(v, io["xfl"][1:2, 0:1])
                e.reg_alu(c2, v, e0, AO.not_equal)
                e.reg_alu(cond, cond, c2, AO.add)
                e.load(v, io["xfl"][1:2, 1:2])
                e.reg_alu(c2, v, e1, AO.not_equal)
                e.reg_alu(cond, cond, c2, AO.add)
        return e.engine_nop()
    return P.op("gp", poll, deps=(ef,))


def ln_stats(K, ctx, tag, src, n_tiles, tok, eps, out_dram_row=None, keep_xb=False):
    nc, P, io = K.nc, K.P, K.io
    rows = K.sbuf(f"rows_{tag}", [1, 3 * tok], F32, ctx)
    XB = K.sbuf(f"XB_{tag}", [128, n_tiles * tok], BF16, ctx)
    x2 = K.sbuf(f"x2_{tag}", [128, 2 * 512], BF16, ctx)
    ps = K.psum(f"ps_{tag}", [1, 2 * 512], F32, ctx)
    onescb = K.sb["onescb"]
    ev_cast = None
    for i in range(n_tiles):
        ap, ev_src = src(i)
        ev_cast = P.op("ve", lambda e, ap=ap, i=i: e.tensor_copy(
            XB[:, i * tok:(i + 1) * tok], ap), deps=(ev_src,))
    ev1 = {}
    ev2 = {}
    nsl = tok // 512
    for s in range(nsl):
        for i in range(n_tiles):
            ev1[(s, i)] = P.op("pe", lambda e, s=s, i=i: e.matmul(
                ps[0:1, 0:512], onescb[:], XB[:, i * tok + s * 512:i * tok + (s + 1) * 512],
                start=(i == 0), stop=(i == n_tiles - 1)),
                deps=(ev_cast, ev2.get(s - 1)))
        evr1 = P.op("ve", lambda e, s=s: e.tensor_scalar(
            rows[0:1, s * 512:(s + 1) * 512], ps[0:1, 0:512], 1.0 / D, None, AO.mult),
            deps=(ev1[(s, n_tiles - 1)],))
        ev1[s] = evr1
        for i in range(n_tiles):
            b2 = i % 2
            eq = P.op("gp", lambda e, s=s, i=i, b2=b2: e.tensor_tensor(
                x2[:, b2 * 512:(b2 + 1) * 512],
                XB[:, i * tok + s * 512:i * tok + (s + 1) * 512],
                XB[:, i * tok + s * 512:i * tok + (s + 1) * 512], AO.mult),
                deps=(ev_cast, ev2.get((s, i - 2)) if i >= 2
                      else ev2.get((s - 1, n_tiles - 2 + i))))
            ev2[(s, i)] = P.op("pe", lambda e, s=s, i=i, b2=b2: e.matmul(
                ps[0:1, 512:1024], onescb[:], x2[:, b2 * 512:(b2 + 1) * 512],
                start=(i == 0), stop=(i == n_tiles - 1)),
                deps=(eq, evr1 if i == 0 else None))
        ev2[s] = P.op("ve", lambda e, s=s: e.tensor_scalar(
            rows[0:1, tok + s * 512:tok + (s + 1) * 512], ps[0:1, 512:1024],
            1.0 / D, None, AO.mult), deps=(ev2[(s, n_tiles - 1)],))
    mu, var, rstd = rows[:, 0:tok], rows[:, tok:2 * tok], rows[:, 2 * tok:3 * tok]
    edone = ev2[nsl - 1]
    P.op("ve", lambda e: e.scalar_tensor_tensor(rstd, mu, -1.0, mu, AO.mult, AO.mult),
         deps=(edone,))
    e4 = P.op("ve", lambda e: e.tensor_tensor(var, var, rstd, AO.add))
    e5 = P.op("sc", lambda e: e.activation(rstd, var, AF.Sqrt, bias=eps, scale=1.0), deps=(e4,))
    e6 = P.op("ve", lambda e: e.reciprocal(rstd, rstd), deps=(e5,))
    ev = e6
    if out_dram_row is not None:
        r = out_dram_row
        P.op("sy", lambda e: e.dma_start(out=io["ROWSD"][r:r + 1, 0:tok], in_=mu), deps=(e6,))
        ev = P.op("sy", lambda e: e.dma_start(out=io["ROWSD"][r + 1:r + 2, 0:tok], in_=rstd),
                  deps=(e6,))
    if keep_xb:
        return mu, rstd, var, ev, XB
    return mu, rstd, var, ev


def bcast_rows(K, ctx, tag, rows_evs, tok, ps=None):
    nc, P = K.nc, K.P
    n = len(rows_evs)
    out = K.sbuf(f"bc_{tag}", [128, n * tok], F32, ctx)
    if ps is None:
        ps = K.psum(f"psbc_{tag}", [128, 1024], F32, ctx)
    ones1f = K.sb["ones1f"]
    ev = None
    for j, (row, evr) in enumerate(rows_evs):
        for s in range(0, tok, 512):
            w = min(512, tok - s)
            bank = (s // 512) % 2
            em = P.op("pe", lambda e, row=row, s=s, w=w, bank=bank: e.matmul(
                ps[:, bank * 512:bank * 512 + w], ones1f[:], row[:, s:s + w],
                start=True, stop=True), deps=(evr, ev))
            ev = P.op("sc", lambda e, j=j, s=s, w=w, bank=bank: e.copy(
                out=out[:, j * tok + s:j * tok + s + w],
                in_=ps[:, bank * 512:bank * 512 + w]), deps=(em,))
    return [out[:, j * tok:(j + 1) * tok] for j in range(n)], ev


def bcast_dram_rows(K, ctx, tag, row_idx, tok, tok0, deps=(), ps=None):
    nc, P, io = K.nc, K.P, K.io
    n = len(row_idx)
    out = K.sbuf(f"bc_{tag}", [128, n * tok], F32, ctx)
    scr = K.sbuf(f"bcs_{tag}", [1, 2 * 512], F32, ctx)
    if ps is None:
        ps = K.psum(f"psbc_{tag}", [128, 1024], F32, ctx)
    ones1f = K.sb["ones1f"]
    ev = None
    evl_prev = {}
    for j, r in enumerate(row_idx):
        for s_ in range(0, tok, 512):
            w = min(512, tok - s_)
            bank = (s_ // 512) % 2
            evl = P.op("sy", lambda e, r=r, s_=s_, w=w, bank=bank: e.dma_start(
                out=scr[0:1, bank * 512:bank * 512 + w],
                in_=io["ROWSD"][r:r + 1, tok0 + s_:tok0 + s_ + w]),
                deps=deps + (evl_prev.get(bank),))
            em = P.op("pe", lambda e, s_=s_, w=w, bank=bank: e.matmul(
                ps[:, bank * 512:bank * 512 + w], ones1f[:],
                scr[0:1, bank * 512:bank * 512 + w], start=True, stop=True),
                deps=(evl, ev))
            ev = P.op("sc", lambda e, j=j, s_=s_, w=w, bank=bank: e.copy(
                out=out[:, j * tok + s_:j * tok + s_ + w],
                in_=ps[:, bank * 512:bank * 512 + w]), deps=(em,))
            evl_prev[bank] = ev
    return [out[:, j * tok:(j + 1) * tok] for j in range(n)], ev


def stream_w(K, wbuf, src_fn, n_in_tiles, deps):
    P = K.P

    def f(e):
        src = src_fn()
        return e.dma_start(
            out=wbuf.rearrange("p (i c) -> p i c", i=n_in_tiles),
            in_=src.rearrange("(i p) c -> p i c", p=128))
    return P.op("gd", f, deps=deps)


def phase_attn_a(K):
    nc, P, io = K.nc, K.P, K.io
    with ExitStack() as ctx:
        with ExitStack() as c0:
            xin = K.sbuf("A_xin", [128, T], F32, c0)
            ld = {}

            def src(i):
                ev = P.op("sy", lambda e, i=i: e.dma_start(
                    out=xin[:], in_=io["xT"][i * 128:(i + 1) * 128, :]), deps=(ld.get("p"),))
                ld["p"] = ev
                return xin[:], ev
            mu1, r1, var1, ev_st = ln_stats(K, c0, "A1", src, NCH, T, 1e-5, out_dram_row=0)
            rr = K.sbuf("A_rr", [1, 2 * T], F32, c0)
            R_row, A_row = rr[:, 0:T], rr[:, T:2 * T]
            P.op("ve", lambda e: e.tensor_tensor(R_row, r1, r1, AO.mult), deps=(ev_st,))
            e2 = P.op("ve", lambda e: e.tensor_tensor(R_row, R_row, var1, AO.mult))
            e3 = P.op("sc", lambda e: e.activation(R_row, R_row, AF.Sqrt, bias=1e-5, scale=1.0),
                      deps=(e2,))
            P.op("ve", lambda e: e.reciprocal(R_row, R_row), deps=(e3,))
            e5 = P.op("ve", lambda e: e.tensor_tensor(R_row, R_row, r1, AO.mult))
            e6 = P.op("ve", lambda e: e.tensor_tensor(A_row, mu1, R_row, AO.mult))
            P.op("sy", lambda e: e.dma_start(out=io["ROWSD"][6:7, :], in_=R_row), deps=(e5,))
            P.op("sy", lambda e: e.dma_start(out=io["ROWSD"][7:8, :], in_=A_row), deps=(e6,))
            P.barrier()

        cqa = ctx.enter_context(ExitStack())
        Q = K.sbuf("A_Q", [128, NCH * T], BF16, cqa)
        with ExitStack() as c2a:
            ZN = K.sbuf("A_ZN", [128, NCH * T], BF16, c2a)
            with ExitStack() as cz:
                (Rb, Ab), ev_bc = bcast_dram_rows(K, cz, "A", [6, 7], T, 0)
                xin2 = K.sbuf("A_xin2", [128, T], F32, cz)
                evp = None
                for i in range(NCH):
                    el = P.op("sy", lambda e, i=i: e.dma_start(
                        out=xin2[:], in_=io["xT"][i * 128:(i + 1) * 128, :]),
                        deps=(ev_bc, evp))
                    P.op("ve", lambda e: e.tensor_tensor(xin2[:], xin2[:], Rb, AO.mult),
                         deps=(el, ev_bc))
                    P.op("ve", lambda e: e.tensor_tensor(xin2[:], xin2[:], Ab, AO.subtract))
                    evp = P.op("ve", lambda e, i=i: e.tensor_scalar(
                        ZN[:, i * T:(i + 1) * T], xin2[:], K.bcol("aln_w", i),
                        K.bcol("aln_b", i), AO.mult, AO.add))
                ev_zn = evp
                P.barrier()

            with ExitStack() as cq:
                kvst = K.sbuf("A_kvst", [128, 2 * 512], BF16, cq)
                wq = K.sbuf("A_wq", [128, 2 * NCH * 128], BF16, cq)
                psq = K.psum("A_psq", [128, 2 * 512], F32, cq)
                ev_ev, kvw = {}, []
                for o in range(24):
                    buf = o % 2
                    ew = stream_w(K, wq[:, buf * NCH * 128:(buf + 1) * NCH * 128],
                                  lambda o=o: io["Wqkv"][:, o * 128:(o + 1) * 128], NCH,
                                  deps=(ev_ev.get((o - 2, NS - 1)),))
                    for s in range(NS):
                        bank = s % 2

                        def mmf(e, o=o, s=s, buf=buf, bank=bank):
                            r = None
                            for i in range(NCH):
                                r = e.matmul(
                                    psq[:, bank * 512:(bank + 1) * 512],
                                    wq[:, buf * NCH * 128 + i * 128:buf * NCH * 128 + (i + 1) * 128],
                                    ZN[:, i * T + s * 512:i * T + s * 512 + 512],
                                    start=(i == 0), stop=(i == NCH - 1))
                            return r
                        em = P.op("pe", mmf, deps=(ew, ev_zn,
                                                   ev_ev.get((o, s - 2)) if s >= 2
                                                   else ev_ev.get((o - 1, NS - 2 + s))))
                        if o < 8:
                            ee = P.op("sc", lambda e, o=o, s=s, bank=bank: e.activation(
                                Q[:, o * T + s * 512:o * T + s * 512 + 512],
                                psq[:, bank * 512:(bank + 1) * 512],
                                AF.Identity, bias=K.bcol("bqkv", o), scale=1.0), deps=(em,))
                        else:
                            kb = s % 2
                            dep_st = kvw[-2] if len(kvw) >= 2 else None
                            ee = P.op("sc", lambda e, o=o, s=s, bank=bank, kb=kb: e.activation(
                                kvst[:, kb * 512:(kb + 1) * 512],
                                psq[:, bank * 512:(bank + 1) * 512],
                                AF.Identity, bias=K.bcol("bqkv", o), scale=1.0),
                                deps=(em, dep_st))
                            ekv = P.op("gd", lambda e, o=o, s=s, kb=kb: e.dma_start(
                                out=io["KVSTG"][(o - 8) * 128:(o - 7) * 128,
                                                s * 512:(s + 1) * 512],
                                in_=kvst[:, kb * 512:(kb + 1) * 512]), deps=(ee,))
                            kvw.append(ekv)
                        ev_ev[(o, s)] = ee
                epub = P.op("gd", lambda e: e.dma_start(
                    out=io["xkv"].rearrange("(z r) t -> z r t", z=2)[
                        bass.ds(K.r_myfc, 1), :, :],
                    in_=io["KVSTG"][:, :]), deps=tuple(kvw[-2:]))
                ev_ready = flag_and_poll(K, 0, (epub,))
                P.barrier()

        with ExitStack() as ca:
            KG = K.sbuf("A_KG", [128, L], BF16, ca)
            VG = K.sbuf("A_VG", [128, L], BF16, ca)
            VT = K.sbuf("A_VT", [128, 64 * 130], BF16, ca)
            PT = K.sbuf("A_PT", [128, 2 * 1024], BF16, ca)
            mst = K.sbuf("A_mst", [64, 1024], BF16, ca)
            dent = K.sbuf("A_dent", [1, 1024], F32, ca)
            psS = K.psum("A_psS", [128, 2 * 1024], F32, ca)
            psM = K.psum("A_psM", [65, 1024], F32, ca)
            pstr = K.psum("A_pstr", [128, 128], BF16, ca)
            idb = K.sb["idb"]

            ev_pair = None
            for p in range(NPAIR):
                ekg1 = P.op("sy", lambda e, p=p: e.dma_start(
                    out=KG[:, 0:T], in_=io["xkv"][p * 128:(p + 1) * 128, :]),
                    deps=(ev_ready, ev_pair))
                P.op("sy", lambda e, p=p: e.dma_start(
                    out=KG[:, T:L], in_=io["xkv"][2048 + p * 128:2048 + (p + 1) * 128, :]))
                P.op("sy", lambda e, p=p: e.dma_start(
                    out=VG[:, 0:T], in_=io["xkv"][1024 + p * 128:1024 + (p + 1) * 128, :]))
                evg2 = P.op("sy", lambda e, p=p: e.dma_start(
                    out=VG[:, T:L],
                    in_=io["xkv"][3072 + p * 128:3072 + (p + 1) * 128, :]))
                evm = P.op("ve", lambda e: e.memset(VT[:], 1.0), deps=(ev_pair,))
                evt = None
                for k in range(64):
                    et = P.op("pe", lambda e, k=k: e.transpose(
                        pstr[:], VG[:, k * 128:(k + 1) * 128], idb[:]),
                        deps=(evg2, evm, evt, ev_pair))
                    P.op("ve", lambda e, k=k: e.tensor_copy(
                        VT[:, k * 130:k * 130 + 64], pstr[:, 0:64]), deps=(et,))
                    evt = P.op("ve", lambda e, k=k: e.tensor_copy(
                        VT[:, k * 130 + 65:k * 130 + 129], pstr[:, 64:128]))
                ev_vt = evt

                ev_q = None
                ev_evac1 = None
                ev_evac2 = None
                for h in range(2):
                    for qc in range(4):
                        q0 = qc * 1024
                        ev_exp, ev_pv = {}, {}

                        def mk_pvm(h, k, sb_):
                            def pvm(e, h=h, k=k, sb_=sb_):
                                r = None
                                for qs in range(2):
                                    r = e.matmul(
                                        psM[:, qs * 512:(qs + 1) * 512],
                                        VT[:, k * 130 + h * 65:k * 130 + h * 65 + 65],
                                        PT[:, sb_ * 1024 + qs * 512:sb_ * 1024 + (qs + 1) * 512],
                                        start=(k == 0), stop=(k == 63))
                                return r
                            return pvm
                        for k in range(64):
                            sb_ = k % 2

                            def smm(e, p=p, h=h, q0=q0, k=k, sb_=sb_):
                                r = None
                                for qs in range(2):
                                    r = e.matmul(
                                        psS[:, sb_ * 1024 + qs * 512:sb_ * 1024 + (qs + 1) * 512],
                                        KG[h * 64:(h + 1) * 64, k * 128:(k + 1) * 128],
                                        Q[h * 64:(h + 1) * 64,
                                          p * T + q0 + qs * 512:p * T + q0 + (qs + 1) * 512],
                                        start=True, stop=True)
                                return r
                            es = P.op("pe", smm, deps=(ekg1, evg2, ev_vt, ev_exp.get(k - 2)))
                            ee = P.op("sc", lambda e, k=k, sb_=sb_: e.activation(
                                PT[:, sb_ * 1024:(sb_ + 1) * 1024],
                                psS[:, sb_ * 1024:(sb_ + 1) * 1024],
                                AF.Exp, scale=0.125), deps=(es, ev_pv.get(k - 2)))
                            ev_exp[k] = ee
                            if k >= 1:
                                ev_pv[k - 1] = P.op(
                                    "pe", mk_pvm(h, k - 1, (k - 1) % 2),
                                    deps=(ev_exp[k - 1],)
                                    + ((ev_evac1, ev_evac2) if k - 1 == 0 else ()))
                        ev_pv[63] = P.op("pe", mk_pvm(h, 63, 63 % 2), deps=(ev_exp[63],))
                        ed = P.op("ve", lambda e: e.reciprocal(dent[:], psM[64:65, :]),
                                  deps=(ev_pv[63], ev_q))
                        em = P.op("sc", lambda e: e.copy(out=mst[:], in_=psM[0:64, :]),
                                  deps=(ev_pv[63], ev_q))
                        P.op("sy", lambda e, p=p, h=h, qc=qc: e.dma_start(
                            out=io["DEN"][2 * p + h:2 * p + h + 1, qc * 1024:(qc + 1) * 1024],
                            in_=dent[:]), deps=(ed,))
                        ems = P.op("sy", lambda e, p=p, h=h, qc=qc: e.dma_start(
                            out=io["MSAR"][(2 * p + h) * 64:(2 * p + h + 1) * 64,
                                           qc * 1024:(qc + 1) * 1024],
                            in_=mst[:]), deps=(em,))
                        ev_q = ems
                        ev_evac1, ev_evac2 = ed, em
                ev_pair = ev_q
            ev_attn = ev_pair
            P.barrier()
        cqa.close()

        with ExitStack() as c3:
            MSA = K.sbuf("A_MSA", [128, NCH * T], BF16, c3)
            dro = K.sbuf("A_dro", [1, 2 * 2048], F32, c3)
            mraw = K.sbuf("A_mraw", [128, 2048], BF16, c3)
            mA = K.sbuf("A_mA", [1, 128], F32, c3)
            mB = K.sbuf("A_mB", [1, 128], F32, c3)
            psD = K.psum("A_psD", [128, 2048], F32, c3)
            P.op("ve", lambda e: e.memset(mA[:], 0.0), deps=(ev_attn,))
            P.op("ve", lambda e: e.memset(mA[:, 0:64], 1.0))
            P.op("ve", lambda e: e.memset(mB[:], 0.0))
            em4 = P.op("ve", lambda e: e.memset(mB[:, 64:128], 1.0))
            evn = None
            for p in range(NPAIR):
                for qc in range(2):
                    eda = P.op("sy", lambda e, p=p, qc=qc: e.dma_start(
                        out=dro[:, 0:2048],
                        in_=io["DEN"][2 * p:2 * p + 1, qc * 2048:(qc + 1) * 2048]),
                        deps=(evn, ev_attn))
                    edb = P.op("sy", lambda e, p=p, qc=qc: e.dma_start(
                        out=dro[:, 2048:4096],
                        in_=io["DEN"][2 * p + 1:2 * p + 2, qc * 2048:(qc + 1) * 2048]))
                    erm = P.op("sy", lambda e, p=p, qc=qc: e.dma_start(
                        out=mraw[:],
                        in_=io["MSAR"][2 * p * 64:(2 * p + 2) * 64,
                                       qc * 2048:(qc + 1) * 2048]), deps=(edb, evn))
                    eb = None
                    for half in range(4):
                        h0 = half * 512
                        P.op("pe", lambda e, h0=h0: e.matmul(
                            psD[:, h0:h0 + 512], mA[:], dro[:, h0:h0 + 512],
                            start=True, stop=False), deps=(eda, edb, em4, evn))
                        eb = P.op("pe", lambda e, h0=h0: e.matmul(
                            psD[:, h0:h0 + 512], mB[:], dro[:, 2048 + h0:2048 + h0 + 512],
                            start=False, stop=True))
                    evn = P.op("ve", lambda e, p=p, qc=qc: e.tensor_tensor(
                        MSA[:, p * T + qc * 2048:p * T + (qc + 1) * 2048], mraw[:], psD[:],
                        AO.mult), deps=(erm, eb))
            ev_msa = evn

            (m1b, r1b), ev_bc2 = bcast_dram_rows(K, c3, "A2", [0, 1], T, 0)
            wpj = K.sbuf("A_wpj", [128, 2 * NCH * 128], BF16, c3)
            xr1 = K.sbuf("A_xr1", [128, T], F32, c3)
            xr2 = K.sbuf("A_xr2", [128, T], F32, c3)
            pso = K.psum("A_pso", [128, 2 * 512], F32, c3)
            ev_o = {}
            for o in range(NCH):
                buf = o % 2
                ewp = stream_w(K, wpj[:, buf * NCH * 128:(buf + 1) * NCH * 128],
                               lambda o=o: io["Wproj"][:, o * 128:(o + 1) * 128], NCH,
                               deps=(ev_o.get(o - 2),))
                exl = P.op("sy", lambda e, o=o: e.dma_start(
                    out=xr1[:], in_=io["xT"][o * 128:(o + 1) * 128, :]),
                    deps=(ev_o.get(o - 1),))
                P.op("ve", lambda e: e.tensor_tensor(xr2[:], xr1[:], m1b, AO.subtract),
                     deps=(exl, ev_bc2))
                P.op("ve", lambda e: e.tensor_tensor(xr2[:], xr2[:], r1b, AO.mult))
                ez3 = P.op("ve", lambda e: e.tensor_tensor(xr2[:], xr2[:], xr1[:], AO.add))
                evo = None
                for s in range(NS):
                    bank = s % 2

                    def pmm(e, o=o, s=s, buf=buf, bank=bank):
                        r = None
                        for i in range(NCH):
                            r = e.matmul(
                                pso[:, bank * 512:(bank + 1) * 512],
                                wpj[:, buf * NCH * 128 + i * 128:buf * NCH * 128 + (i + 1) * 128],
                                MSA[:, i * T + s * 512:i * T + s * 512 + 512],
                                start=(i == 0), stop=(i == NCH - 1))
                        return r
                    emm = P.op("pe", pmm, deps=(ewp, ev_msa,
                                                ev_o.get((o, s - 2)) if s >= 2
                                                else ev_o.get((o - 1, NS - 2 + s))))
                    evo = P.op("ve", lambda e, o=o, s=s, bank=bank: e.scalar_tensor_tensor(
                        xr1[:, s * 512:(s + 1) * 512], pso[:, bank * 512:(bank + 1) * 512],
                        K.bcol("bproj", o), xr2[:, s * 512:(s + 1) * 512], AO.add, AO.add),
                        deps=(emm, ez3))
                    ev_o[(o, s)] = evo
                est = P.op("sy", lambda e, o=o: e.dma_start(
                    out=io["XRES"][o * 128:(o + 1) * 128, :], in_=xr1[:]), deps=(evo,))
                ev_o[o] = est
            P.barrier()


def phase_ffn(K, call):
    nc, P, io = K.nc, K.P, K.io
    last = (call == 2)
    with ExitStack() as ctx:
        with ExitStack() as c0:
            xin = K.sbuf(f"F{call}_xin", [128, T], F32, c0)
            ld = {}

            def src(i):
                ev = P.op("sy", lambda e, i=i: e.dma_start(
                    out=xin[:], in_=io["XRES"][i * 128:(i + 1) * 128, :]), deps=(ld.get("p"),))
                ld["p"] = ev
                return xin[:], ev
            mu, rstd, _v, ev_st = ln_stats(K, c0, f"F{call}", src, NCH, T, 1e-5,
                                           out_dram_row=2)
            P.barrier()

        with ExitStack() as c1:
            _ffn_routing(K, c1, call)
            P.barrier()

        for th in range(2):
            t0 = th * 2048
            with ExitStack() as c2:
                cps = c2.enter_context(ExitStack())
                ps1 = K.psum(f"F{call}t{th}_p1", [128, 2 * 1024], F32, cps)
                ps2 = K.psum(f"F{call}t{th}_p2", [128, 2 * 1024], F32, cps)
                (mub, rb), ev_bc = bcast_dram_rows(
                    K, c2, f"F{call}t{th}", [2, 3], 2048, t0, ps=ps1[:, 0:1024])
                FLAT = K.sbuf(f"F{call}t{th}_FLAT", [128, NCH * 2048], BF16, c2)
                ft = K.sbuf(f"F{call}t{th}_ft", [128, 2048], F32, c2)
                cwtb = K.sbuf(f"F{call}t{th}_cwt", [NE, 2048], BF16, c2)
                ecwt = P.op("sy", lambda e, cwtb=cwtb, t0=t0: e.dma_start(
                    out=cwtb[:], in_=io["CWD"][:, t0:t0 + 2048]),
                    deps=(K._ev_cwd[call],))
                evp = None
                for i in range(NCH):
                    el = P.op("sy", lambda e, i=i, ft=ft, t0=t0: e.dma_start(
                        out=ft[:], in_=io["XRES"][i * 128:(i + 1) * 128, t0:t0 + 2048]),
                        deps=(ev_bc, evp))
                    P.op("ve", lambda e, ft=ft, mub=mub: e.tensor_tensor(
                        ft[:], ft[:], mub, AO.subtract), deps=(el, ev_bc))
                    evp = P.op("ve", lambda e, i=i, FLAT=FLAT, ft=ft, rb=rb: e.tensor_tensor(
                        FLAT[:, i * 2048:(i + 1) * 2048], ft[:], rb, AO.mult))
                ev_flat = evp
                OUT = K.sbuf(f"F{call}t{th}_OUT", [128, NCH * 2048], F32, c2)
                emz = None
                for o in range(NCH):
                    emz = P.op("ve", lambda e, o=o, OUT=OUT: e.memset(
                        OUT[:, o * 2048:(o + 1) * 2048], 0.0))

                with ExitStack() as c3:
                    W1C = K.sbuf(f"F{call}t{th}_W1", [128, 2 * NCH * 1024], BF16, c3)
                    W2C = K.sbuf(f"F{call}t{th}_W2", [128, 8 * 1024], BF16, c3)
                    CWB = K.sbuf(f"F{call}t{th}_CWB", [128, 2048], BF16, c3)
                    h1c = K.sbuf(f"F{call}t{th}_h1", [128, 8 * 1024], BF16, c3)
                    stg = K.sbuf(f"F{call}t{th}_stg", [128, 2 * 1024], BF16, c3)
                    ev_wl = {}
                    last_mm2 = None
                    last_cw_reader = None
                    for e_i in range(NE):
                        ecb = P.op("sy", lambda e, e_i=e_i, CWB=CWB, t0=t0: e.dma_start(
                            out=CWB[:],
                            in_=io["CWD"][e_i:e_i + 1, t0:t0 + 2048].broadcast_to((128, 2048))),
                            deps=(K._ev_cwd[call], last_cw_reader))
                        for c in range(4):
                            jj = e_i * 4 + c
                            widx = jj % 2
                            dep_w = ev_wl.get(jj - 2)
                            if K._fw_cached:
                                ew1 = P.op("sy", lambda e, jj=jj, widx=widx, W1C=W1C:
                                           e.dma_start(
                                    out=W1C[:, widx * NCH * 1024:(widx + 1) * NCH * 1024],
                                    in_=io["W1BD"][jj * 128:(jj + 1) * 128, :]),
                                    deps=(dep_w,))
                                ew2 = P.op("sy", lambda e, jj=jj, W2C=W2C: e.dma_start(
                                    out=W2C[:], in_=io["W2BD"][jj * 128:(jj + 1) * 128, :]),
                                    deps=(ev_wl.get(jj - 1),))
                            else:
                                ew1 = stream_w(
                                    K, W1C[:, widx * NCH * 1024:(widx + 1) * NCH * 1024],
                                    lambda e_i=e_i, c=c:
                                        io["w1T"][e_i, :, c * 1024:(c + 1) * 1024],
                                    NCH, deps=(dep_w, K._fw_st1.get(jj - 2)))
                                ew2 = P.op("gd", lambda e, e_i=e_i, c=c, W2C=W2C: e.dma_start(
                                    out=W2C[:].rearrange("p (i o) -> p i o", i=8),
                                    in_=io["w2T"][e_i, c * 1024:(c + 1) * 1024, :].rearrange(
                                        "(i p) o -> p i o", p=128)),
                                    deps=(ev_wl.get(jj - 1), K._fw_st2.get(jj - 1)))
                                K._fw_st1[jj] = P.op(
                                    "sy", lambda e, jj=jj, widx=widx, W1C=W1C: e.dma_start(
                                        out=io["W1BD"][jj * 128:(jj + 1) * 128, :],
                                        in_=W1C[:, widx * NCH * 1024:
                                                (widx + 1) * NCH * 1024]), deps=(ew1,))
                                K._fw_st2[jj] = P.op(
                                    "sy", lambda e, jj=jj, W2C=W2C: e.dma_start(
                                        out=io["W2BD"][jj * 128:(jj + 1) * 128, :],
                                        in_=W2C[:]), deps=(ew2,))
                            for ts in range(2):
                                s0 = ts * 1024
                                prev_mm2_of_ts = last_mm2
                                ev_h = {}
                                for f in range(8):
                                    pb = f % 2

                                    def m1(e, e_i=e_i, c=c, f=f, widx=widx, s0=s0, pb=pb,
                                           ps1=ps1, W1C=W1C, FLAT=FLAT):
                                        r = None
                                        for half in range(2):
                                            for i in range(NCH):
                                                r = e.matmul(
                                                    ps1[:, pb * 1024 + half * 512:
                                                        pb * 1024 + (half + 1) * 512],
                                                    W1C[:, widx * NCH * 1024 + i * 1024 + f * 128:
                                                        widx * NCH * 1024 + i * 1024 + (f + 1) * 128],
                                                    FLAT[:, i * 2048 + s0 + half * 512:
                                                         i * 2048 + s0 + (half + 1) * 512],
                                                    start=(i == 0), stop=(i == NCH - 1))
                                        return r
                                    em1 = P.op("pe", m1, deps=(ew1, ev_flat,
                                                               ev_h.get(f - 2),
                                                               prev_mm2_of_ts if f < 2 else None))
                                    eg = P.op("sc", lambda e, f=f, pb=pb, e_i=e_i, c=c,
                                              stg=stg, ps1=ps1: e.activation(
                                        stg[:, pb * 1024:(pb + 1) * 1024],
                                        ps1[:, pb * 1024:(pb + 1) * 1024],
                                        AF.Gelu, bias=K.bcol("fb1", e_i * 32 + c * 8 + f),
                                        scale=1.0), deps=(em1, ev_h.get(f - 1)))
                                    eh = P.op("ve", lambda e, f=f, pb=pb, s0=s0, e_i=e_i,
                                              h1c=h1c, stg=stg, CWB=CWB: e.tensor_tensor(
                                        h1c[:, f * 1024:(f + 1) * 1024],
                                        stg[:, pb * 1024:(pb + 1) * 1024],
                                        CWB[:, s0:s0 + 1024], AO.mult),
                                        deps=(eg, ecb, prev_mm2_of_ts if f == 0 else None))
                                    ev_h[f] = eh
                                edbg = None
                                if (os.environ.get("KDBG2") and call == 1 and th == 0
                                        and e_i == 0 and c == 0 and ts == 0):
                                    P.op("sy", lambda e, h1c=h1c: e.dma_start(
                                        out=io["KVSTG"][1024:1152, :], in_=h1c[:, 0:4096]),
                                        deps=(ev_h[7],))
                                    edbg = P.op("sy", lambda e, h1c=h1c: e.dma_start(
                                        out=io["KVSTG"][1152:1280, :], in_=h1c[:, 4096:8192]),
                                        deps=(ev_h[7],))
                                ev_m2 = {}
                                for o in range(NCH):
                                    ob = o % 2

                                    addb2 = (e_i == NE - 1 and c == 3)

                                    def m2(e, o=o, ob=ob, ps2=ps2, W2C=W2C, h1c=h1c,
                                           addb2=addb2, s0=s0, cwtb=cwtb):
                                        r = None
                                        for half in range(2):
                                            for f in range(8):
                                                r = e.matmul(
                                                    ps2[:, ob * 1024 + half * 512:
                                                        ob * 1024 + (half + 1) * 512],
                                                    W2C[:, f * 1024 + o * 128:
                                                        f * 1024 + (o + 1) * 128],
                                                    h1c[:, f * 1024 + half * 512:
                                                        f * 1024 + (half + 1) * 512],
                                                    start=(f == 0),
                                                    stop=(f == 7 and not addb2))
                                            if addb2:
                                                r = e.matmul(
                                                    ps2[:, ob * 1024 + half * 512:
                                                        ob * 1024 + (half + 1) * 512],
                                                    K.sb["B2B"][:, o * 128:(o + 1) * 128],
                                                    cwtb[:, s0 + half * 512:
                                                         s0 + (half + 1) * 512],
                                                    start=False, stop=True)
                                        return r
                                    em2 = P.op("pe", m2, deps=(ew2, ev_h[7], ev_m2.get(o - 2),
                                                               ecwt, K._ev_b2m, edbg))
                                    ea = P.op("ve", lambda e, o=o, ob=ob, s0=s0,
                                              OUT=OUT, ps2=ps2: e.tensor_tensor(
                                        OUT[:, o * 2048 + s0:o * 2048 + s0 + 1024],
                                        OUT[:, o * 2048 + s0:o * 2048 + s0 + 1024],
                                        ps2[:, ob * 1024:(ob + 1) * 1024], AO.add),
                                        deps=(em2, emz))
                                    ev_m2[o] = ea
                                last_mm2 = ev_m2[NCH - 1]
                            ev_wl[e_i * 4 + c] = last_mm2
                        last_cw_reader = last_mm2
                    ev_experts = last_mm2
                    P.barrier()
                cps.close()

                evp = None
                for i in range(NCH):
                    el = P.op("sy", lambda e, i=i, ft=ft, t0=t0: e.dma_start(
                        out=ft[:], in_=io["XRES"][i * 128:(i + 1) * 128, t0:t0 + 2048]),
                        deps=(ev_experts, evp))
                    P.op("ve", lambda e, ft=ft, mub=mub: e.tensor_tensor(
                        ft[:], ft[:], mub, AO.subtract), deps=(el,))
                    P.op("ve", lambda e, ft=ft, rb=rb: e.tensor_tensor(
                        ft[:], ft[:], rb, AO.mult))
                    evp = P.op("ve", lambda e, i=i, OUT=OUT, ft=ft: e.tensor_tensor(
                        OUT[:, i * 2048:(i + 1) * 2048], OUT[:, i * 2048:(i + 1) * 2048],
                        ft[:], AO.add))
                ev_sum = P.ev("ve")
                edmp = None
                if os.environ.get("KDBG") and call == 1:
                    for i in range(NCH):
                        P.op("sy", lambda e, i=i, OUT=OUT, t0=t0: e.dma_start(
                            out=io["MEMS"][i * 128:(i + 1) * 128, t0:t0 + 2048],
                            in_=OUT[:, i * 2048:(i + 1) * 2048]), deps=(ev_sum,))
                    for i in range(NCH):
                        edmp = P.op("sy", lambda e, i=i, FLAT=FLAT, t0=t0: e.dma_start(
                            out=io["KVSTG"][i * 128:(i + 1) * 128, t0:t0 + 2048],
                            in_=FLAT[:, i * 2048:(i + 1) * 2048]), deps=(ev_sum,))

                with ExitStack() as c4:
                    def src2(i):
                        return OUT[:, i * 2048:(i + 1) * 2048], ev_sum
                    mu2, rstd2, _v2, ev_st2 = ln_stats(K, c4, f"F{call}e{th}", src2, NCH,
                                                       2048, 1e-12)
                    (mu2b, r2b), ev_bc2 = bcast_rows(
                        K, c4, f"F{call}e{th}b", [(mu2, ev_st2), (rstd2, ev_st2)], 2048)
                    evw = None
                    for i in range(NCH):
                        el = P.op("sy", lambda e, i=i, ft=ft, t0=t0: e.dma_start(
                            out=ft[:], in_=io["XRES"][i * 128:(i + 1) * 128, t0:t0 + 2048]),
                            deps=(evw, ev_bc2))
                        P.op("ve", lambda e, i=i, OUT=OUT, mu2b=mu2b: e.tensor_tensor(
                            OUT[:, i * 2048:(i + 1) * 2048], OUT[:, i * 2048:(i + 1) * 2048],
                            mu2b, AO.subtract), deps=(ev_bc2, edmp))
                        P.op("ve", lambda e, i=i, OUT=OUT, r2b=r2b: e.tensor_tensor(
                            OUT[:, i * 2048:(i + 1) * 2048], OUT[:, i * 2048:(i + 1) * 2048],
                            r2b, AO.mult))
                        P.op("ve", lambda e, i=i, OUT=OUT: e.tensor_scalar(
                            OUT[:, i * 2048:(i + 1) * 2048], OUT[:, i * 2048:(i + 1) * 2048],
                            K.bcol("fln_w", i), K.bcol("fln_b", i), AO.mult, AO.add))
                        ex = P.op("ve", lambda e, i=i, OUT=OUT, ft=ft: e.tensor_tensor(
                            OUT[:, i * 2048:(i + 1) * 2048], OUT[:, i * 2048:(i + 1) * 2048],
                            ft[:], AO.add), deps=(el,))
                        dst = io["out"] if last else io["XRES"]
                        evw = P.op("sy", lambda e, i=i, dst=dst, OUT=OUT, t0=t0: e.dma_start(
                            out=dst[i * 128:(i + 1) * 128, t0:t0 + 2048],
                            in_=OUT[:, i * 2048:(i + 1) * 2048]), deps=(ex,))
                    P.barrier()
                K._fw_cached = True


def _ffn_routing(K, ctx, call):
    nc, P, io = K.nc, K.P, K.io
    FLR = K.sbuf(f"R{call}_FLR", [128, NCH * T], BF16, ctx)
    rrow = K.sbuf(f"R{call}_rr", [1, 2 * T], F32, ctx)
    er1 = P.op("sy", lambda e: e.dma_start(out=rrow[:, 0:T], in_=io["ROWSD"][2:3, :]))
    er2 = P.op("sy", lambda e: e.dma_start(out=rrow[:, T:2 * T], in_=io["ROWSD"][3:4, :]))
    with ExitStack() as cb:
        (mub, rb), ev_bc = bcast_rows(K, cb, f"R{call}",
                                      [(rrow[:, 0:T], er1), (rrow[:, T:2 * T], er2)], T)
        ft = K.sbuf(f"R{call}_ft", [128, T], F32, cb)
        evp = None
        for i in range(NCH):
            el = P.op("sy", lambda e, i=i: e.dma_start(
                out=ft[:], in_=io["XRES"][i * 128:(i + 1) * 128, :]), deps=(ev_bc, evp))
            P.op("ve", lambda e: e.tensor_tensor(ft[:], ft[:], mub, AO.subtract),
                 deps=(el, ev_bc))
            evp = P.op("ve", lambda e, i=i: e.tensor_tensor(
                FLR[:, i * T:(i + 1) * T], ft[:], rb, AO.mult))
        ev_flr = evp
        P.barrier()

    rwt = K.sbuf(f"R{call}_rwt", [128, NCH * 8], BF16, ctx)
    erw = stream_w(K, rwt[:], lambda: io["rwT"], NCH, deps=())
    psl = K.psum(f"R{call}_psl", [128, 2 * 512], F32, ctx)
    psr = K.psum(f"R{call}_psr", [8, 128], F32, ctx)
    cwtile = K.sbuf(f"R{call}_cw", [128, 8], F32, ctx)
    cwb16 = K.sbuf(f"R{call}_cwb", [128, 8], BF16, ctx)
    scr = K.sbuf(f"R{call}_scr", [128, 8 * 6], F32, ctx)
    col = K.sbuf(f"R{call}_col", [128, 4], F32, ctx)
    ones8 = K.sbuf(f"R{call}_o8", [128, 8], F32, ctx)
    cws = K.sbuf(f"R{call}_cws", [8, T], F32, ctx)
    P.op("ve", lambda e: e.memset(ones8[:], 1.0))
    idb = K.sb["idb"]
    lg = scr[:, 0:8]
    c1 = scr[:, 8:16]
    s1 = scr[:, 16:24]
    m1 = scr[:, 24:32]
    l2 = scr[:, 32:40]
    m2 = scr[:, 40:48]
    v1 = col[:, 0:1]
    v2 = col[:, 1:2]
    w1 = col[:, 2:3]
    onec = K.sb["consts"][:, CONSTS.index(1.0):CONSTS.index(1.0) + 1]
    zeroc = K.sb["consts"][:, CONSTS.index(0.0):CONSTS.index(0.0) + 1]
    ev_tt = {}
    for tt in range(T // 128):
        pb = tt % 2

        def lmm(e, tt=tt, pb=pb):
            r = None
            for i in range(NCH):
                r = e.matmul(psl[:, pb * 512:pb * 512 + 8],
                             FLR[:, i * T + tt * 128:i * T + (tt + 1) * 128],
                             rwt[:, i * 8:(i + 1) * 8],
                             start=(i == 0), stop=(i == NCH - 1))
            return r
        eml = P.op("pe", lmm, deps=(erw, ev_flr, ev_tt.get(tt - 2)))
        P.op("ve", lambda e, pb=pb: e.tensor_tensor(lg, psl[:, pb * 512:pb * 512 + 8],
                                                    K.sb["RBB"][:], AO.add),
             deps=(eml, K._ev_rbb, ev_tt.get(tt - 1)))
        K.fill()
        P.op("ve", lambda e: e.tensor_reduce(v1, lg, AX.X, AO.max))
        K.fill()
        P.op("ve", lambda e: e.tensor_scalar(c1, lg, v1, None, AO.is_ge))
        K.fill()
        P.op("ve", lambda e: e.tensor_tensor_scan(s1, c1, c1, zeroc, AO.add, AO.bypass))
        K.fill()
        P.op("ve", lambda e: e.tensor_scalar(m1, s1, onec, None, AO.is_le))
        K.fill()
        P.op("ve", lambda e: e.tensor_tensor(m1, m1, c1, AO.mult))
        K.fill()
        P.op("ve", lambda e: e.scalar_tensor_tensor(l2, m1, -1e30, lg, AO.mult, AO.add))
        K.fill()
        P.op("ve", lambda e: e.tensor_reduce(v2, l2, AX.X, AO.max))
        K.fill()
        P.op("ve", lambda e: e.tensor_scalar(m2, l2, v2, None, AO.is_ge))
        K.fill()
        P.op("ve", lambda e: e.tensor_tensor_scan(s1, m2, m2, zeroc, AO.add, AO.bypass))
        K.fill()
        P.op("ve", lambda e: e.tensor_scalar(s1, s1, onec, None, AO.is_le))
        K.fill()
        P.op("ve", lambda e: e.tensor_tensor(m2, m2, s1, AO.mult))
        ed = P.op("ve", lambda e: e.tensor_tensor(col[:, 3:4], v1, v2, AO.subtract))
        ew = P.op("sc", lambda e: e.activation(w1, col[:, 3:4], AF.Sigmoid), deps=(ed,))
        P.op("ve", lambda e: e.tensor_scalar(cwtile[:], m1, w1, None, AO.mult), deps=(ew,))
        P.op("ve", lambda e: e.tensor_scalar(col[:, 3:4], w1, -1.0, 1.0, AO.mult, AO.add))
        K.fill()
        P.op("ve", lambda e: e.scalar_tensor_tensor(cwtile[:], m2, col[:, 3:4], cwtile[:],
                                                    AO.mult, AO.add))
        K.fill()
        ecast = P.op("ve", lambda e: e.tensor_copy(cwb16[:], cwtile[:]))
        emt = P.op("pe", lambda e: e.matmul(psr[:], cwb16[:], idb[:], start=True, stop=True),
                   deps=(ecast, ev_tt.get(tt - 1)))
        ev_tt[tt] = P.op("ve", lambda e, tt=tt: e.tensor_copy(
            cws[:, tt * 128:(tt + 1) * 128], psr[:]), deps=(emt,))
    K._ev_cwd[call] = P.op("gd", lambda e: e.dma_start(out=io["CWD"], in_=cws[:]),
                           deps=(ev_tt[T // 128 - 1],))


def phase_attn_b(K):
    nc, P, io = K.nc, K.P, K.io
    CH = 1024
    NC4 = T // CH
    TE = T + 1
    with ExitStack() as ctx:
        XBL = K.sbuf("B_XBL", [128, 8], F32, ctx)
        XBP = K.sbuf("B_XBP", [128, 8], F32, ctx)
        XBPB = K.sbuf("B_XBPB", [128, 8], BF16, ctx)
        KB = K.sbuf("B_KB", [128, 8], F32, ctx)
        GSR = K.sbuf("B_GSR", [1, T], F32, ctx)
        BW0 = K.sbuf("B_BW0", [1, T], BF16, ctx)
        BW1 = K.sbuf("B_BW1", [1, T], BF16, ctx)
        CARR = K.sbuf("B_CARR", [128, 34], F32, ctx)
        PCAR = K.sbuf("B_PCAR", [128, 34], F32, ctx)

        with ExitStack() as c0:
            xin = K.sbuf("B_xin", [128, T], F32, c0)
            ld = {}

            def src(i):
                ev = P.op("sy", lambda e, i=i: e.dma_start(
                    out=xin[:], in_=io["XRES"][i * 128:(i + 1) * 128, :]), deps=(ld.get("p"),))
                ld["p"] = ev
                return xin[:], ev
            mu3, r3, _v3, ev_st = ln_stats(K, c0, "B", src, NCH, T, 1e-5, out_dram_row=4)
            P.barrier()

        cscan = ctx.enter_context(ExitStack())
        FLATB = K.sbuf("B_FLATB", [128, NCH * TE], BF16, cscan)
        with ExitStack() as c0b:
            (m3b, r3b), ev_bc = bcast_dram_rows(K, c0b, "B0", [4, 5], T, 0)
            xin2 = K.sbuf("B_xin2", [128, T], F32, c0b)
            evp = None
            for i in range(NCH):
                el = P.op("sy", lambda e, i=i: e.dma_start(
                    out=xin2[:], in_=io["XRES"][i * 128:(i + 1) * 128, :]),
                    deps=(ev_bc, evp))
                P.op("ve", lambda e: e.tensor_tensor(xin2[:], xin2[:], m3b, AO.subtract),
                     deps=(el, ev_bc))
                P.op("ve", lambda e: e.tensor_tensor(xin2[:], xin2[:], r3b, AO.mult))
                P.op("ve", lambda e, i=i: e.tensor_copy(
                    FLATB[:, i * TE + 1:(i + 1) * TE], xin2[:]))
                evp = P.op("ve", lambda e, i=i: e.tensor_copy(
                    XBL[:, i:i + 1], xin2[:, T - 1:T]))
            ev_flatb = evp
            P.barrier()

        exs = P.op("gd", lambda e: e.dma_start(
            out=io["xsm"][bass.ds(K.r_mysm, 128), 40:48], in_=XBL[:]), deps=(ev_flatb,))
        ev_r1 = flag_and_poll(K, 1, (exs,))
        exr = P.op("gd", lambda e: e.dma_start(
            out=XBP[:], in_=io["xsm"][bass.ds(K.r_peersm, 128), 40:48]), deps=(ev_r1,))
        emk = P.op("ve", lambda e: e.tensor_scalar(XBP[:], XBP[:], K.sb["mprev_t"][:, 0:1],
                                                   None, AO.mult), deps=(exr,))
        K.fill()
        exb = P.op("ve", lambda e: e.tensor_copy(XBPB[:], XBP[:]), deps=(emk,))
        ev_bnd = None
        for i in range(NCH):
            ev_bnd = P.op("ve", lambda e, i=i: e.tensor_copy(
                FLATB[:, i * TE:i * TE + 1], XBP[:, i:i + 1]), deps=(emk, ev_flatb))

        wb = K.sbuf("B_wb", [128, 4 * NCH * 128], BF16, cscan)
        KEYPH = K.sbuf("B_KEYPH", [128, NCH * (CH + 1)], F32, cscan)
        GH = K.sbuf("B_GH", [128, NCH * CH], BF16, cscan)
        BH = K.sbuf("B_BH", [128, 4 * CH], BF16, cscan)
        VGR = K.sbuf("B_VGR", [1, CH], F32, cscan)
        VGB = K.sbuf("B_VGB", [128, CH], F32, cscan)
        t_a = K.sbuf("B_ta", [128, CH], F32, cscan)
        t_b = K.sbuf("B_tb", [128, CH], F32, cscan)
        t_c = K.sbuf("B_tc", [128, CH], F32, cscan)
        t_d = K.sbuf("B_td", [128, CH], F32, cscan)
        rowsc = K.sbuf("B_rowsc", [1, 3 * CH], F32, cscan)
        psA = K.psum("B_psA", [128, 2 * 512], F32, cscan)
        psR = K.psum("B_psR", [1, 2 * 512], F32, cscan)
        psB = K.psum("B_psB", [128, 1024], F32, cscan)

        ev_init = P.op("ve", lambda e: e.memset(CARR[:], 0.0))
        P.op("ve", lambda e: e.memset(GSR[:], 0.0))

        def stream_och(wsrc, o, slot, deps):
            return stream_w(K, wb[:, slot * NCH * 128:(slot + 1) * NCH * 128],
                            lambda: wsrc[:, o * 128:(o + 1) * 128], NCH, deps=deps)

        def frhs(i, c0_, width, shifted=False):
            base = i * TE + (0 if shifted else 1)
            return FLATB[:, base + c0_:base + c0_ + width]

        ev_kb = {}
        ev_sp_chunk = {}
        for cs in range(NC4):
            c0_ = cs * CH
            ev_o = {}
            for o in range(NCH):
                slot = o % 2
                ew = stream_och(io["Wk"], o, slot, deps=(ev_o.get(o - 2), ev_init,
                                                         ev_sp_chunk.get(cs - 1)))
                for s in range(2):
                    bank = s % 2

                    def kmm(e, o=o, s=s, slot=slot, bank=bank, c0_=c0_):
                        r = None
                        for i in range(NCH):
                            r = e.matmul(psA[:, bank * 512:(bank + 1) * 512],
                                         wb[:, slot * NCH * 128 + i * 128:
                                            slot * NCH * 128 + (i + 1) * 128],
                                         frhs(i, c0_ + s * 512, 512),
                                         start=(i == 0), stop=(i == NCH - 1))
                        return r
                    prevb = (ev_kb.get(o - 1) if cs == 0 and s == 0
                             else ev_o.get((o - 1, s)))
                    em = P.op("pe", kmm, deps=(ew, ev_flatb, ev_bnd, prevb))
                    ev_o[(o, s)] = P.op("sc", lambda e, o=o, s=s, bank=bank: e.activation(
                        KEYPH[:, o * (CH + 1) + 1 + s * 512:o * (CH + 1) + 1 + (s + 1) * 512],
                        psA[:, bank * 512:(bank + 1) * 512],
                        AF.Tanh, bias=K.bcol("bk", o), scale=1.0), deps=(em,))
                ev_o[o] = ev_o[(o, 1)]
                if cs == 0:
                    def kbm(e, o=o, slot=slot):
                        r = None
                        for i in range(NCH):
                            r = e.matmul(psA[:, 0:1],
                                         wb[:, slot * NCH * 128 + i * 128:
                                            slot * NCH * 128 + (i + 1) * 128],
                                         XBPB[:, i:i + 1],
                                         start=(i == 0), stop=(i == NCH - 1))
                        return r
                    ekb = P.op("pe", kbm, deps=(ev_o[(o, 1)], exb))
                    ev_kb[o] = P.op("sc", lambda e, o=o: e.activation(
                        KB[:, o:o + 1], psA[:, 0:1], AF.Tanh, bias=K.bcol("bk", o), scale=1.0),
                        deps=(ekb,))
            ev_kcol = None
            for o in range(NCH):
                dep = ev_kb.get(o) if cs == 0 else ev_kb.get((cs, o))
                ev_kcol = P.op("ve", lambda e, o=o: e.tensor_copy(
                    KEYPH[:, o * (CH + 1):o * (CH + 1) + 1], KB[:, o:o + 1]),
                    deps=(dep, ev_o[o]))
            for o in range(NCH):
                ev_kb[(cs + 1, o)] = P.op("ve", lambda e, o=o: e.tensor_copy(
                    KB[:, o:o + 1], KEYPH[:, o * (CH + 1) + CH:o * (CH + 1) + CH + 1]),
                    deps=(ev_o[o], ev_kcol))
            ev_keyph = ev_kb[(cs + 1, NCH - 1)]

            ev_g = {}
            for o in range(NCH):
                slot = o % 2
                ewg = stream_w(K, wb[:, (2 + slot) * NCH * 128:(3 + slot) * NCH * 128],
                               lambda o=o: io["Wg1"][0:D, o * 128:(o + 1) * 128], NCH,
                               deps=(ev_g.get(o - 2),))
                ewg2 = stream_w(K, wb[:, slot * NCH * 128:(slot + 1) * NCH * 128],
                                lambda o=o: io["Wg1"][D:2 * D, o * 128:(o + 1) * 128], NCH,
                                deps=(ev_g.get(o - 2), ev_keyph))
                for s in range(2):
                    bank = s % 2

                    def gmm(e, o=o, s=s, slot=slot, bank=bank, c0_=c0_):
                        r = None
                        for i in range(NCH):
                            r = e.matmul(psA[:, bank * 512:(bank + 1) * 512],
                                         wb[:, (2 + slot) * NCH * 128 + i * 128:
                                            (2 + slot) * NCH * 128 + (i + 1) * 128],
                                         frhs(i, c0_ + s * 512, 512),
                                         start=(i == 0), stop=False)
                        for i in range(NCH):
                            r = e.matmul(psA[:, bank * 512:(bank + 1) * 512],
                                         wb[:, slot * NCH * 128 + i * 128:
                                            slot * NCH * 128 + (i + 1) * 128],
                                         frhs(i, c0_ + s * 512, 512, shifted=True),
                                         start=False, stop=(i == NCH - 1))
                        return r
                    em = P.op("pe", gmm, deps=(ewg, ewg2, ev_bnd, ev_g.get((o - 1, s))))
                    ev_g[(o, s)] = P.op("sc", lambda e, o=o, s=s, bank=bank: e.activation(
                        GH[:, o * CH + s * 512:o * CH + (s + 1) * 512],
                        psA[:, bank * 512:(bank + 1) * 512],
                        AF.Gelu, bias=K.bcol("bg1", o), scale=1.0), deps=(em,))
                ev_g[o] = ev_g[(o, 1)]

            def vgmm(e, s):
                r = None
                for i in range(NCH):
                    r = e.matmul(psR[0:1, s * 512:(s + 1) * 512],
                                 K.sb["WG2C"][:, i:i + 1],
                                 GH[:, i * CH + s * 512:i * CH + (s + 1) * 512],
                                 start=(i == 0), stop=(i == NCH - 1))
                return r
            emv = P.op("pe", lambda e, vgmm=vgmm: [vgmm(e, 0), vgmm(e, 1)][-1],
                       deps=(ev_g[NCH - 1], K._ev_wg2))
            evg = P.op("sc", lambda e: e.activation(VGR[:], psR[0:1, 0:CH], AF.Sigmoid,
                                                    bias=K.sb["bb2g"][0:1, 0:1], scale=1.0),
                       deps=(emv, K._ev_bg2g))
            egs = P.op("ve", lambda e, c0_=c0_: e.tensor_tensor_scan(
                GSR[0:1, c0_:c0_ + CH], VGR[:], VGR[:], CARR[0:1, 32:33], AO.add, AO.bypass),
                deps=(evg, ev_init))
            K.fill()
            P.op("ve", lambda e, c0_=c0_: e.tensor_copy(
                CARR[0:1, 32:33], GSR[0:1, c0_ + CH - 1:c0_ + CH]), deps=(egs,))
            emb = P.op("pe", lambda e: [e.matmul(psB[:, s * 512:(s + 1) * 512],
                                                 K.sb["ones1f"][:],
                                                 VGR[0:1, s * 512:(s + 1) * 512],
                                                 start=True, stop=True) for s in range(2)][-1],
                       deps=(evg,))
            evgb = P.op("sc", lambda e: e.copy(out=VGB[:], in_=psB[:, 0:CH]), deps=(emb,))

            ev_bh = {}
            for o in range(4):
                slot = o % 2
                ewb = stream_och(io["Wb1"], o, slot, deps=(ev_bh.get(o - 2), evgb))
                for s in range(2):
                    bank = s % 2

                    def bmm(e, o=o, s=s, slot=slot, bank=bank, c0_=c0_):
                        r = None
                        for i in range(NCH):
                            r = e.matmul(psA[:, bank * 512:(bank + 1) * 512],
                                         wb[:, slot * NCH * 128 + i * 128:
                                            slot * NCH * 128 + (i + 1) * 128],
                                         frhs(i, c0_ + s * 512, 512),
                                         start=(i == 0), stop=(i == NCH - 1))
                        return r
                    em = P.op("pe", bmm, deps=(ewb, ev_bh.get((o - 1, s))))
                    ev_bh[(o, s)] = P.op("sc", lambda e, o=o, s=s, bank=bank: e.activation(
                        BH[:, o * CH + s * 512:o * CH + (s + 1) * 512],
                        psA[:, bank * 512:(bank + 1) * 512],
                        AF.Gelu, bias=K.bcol("bb1", o), scale=1.0), deps=(em,))
                ev_bh[o] = ev_bh[(o, 1)]

            def bwmm(e, colx, s):
                r = None
                for i in range(4):
                    r = e.matmul(psR[0:1, s * 512:(s + 1) * 512],
                                 K.sb["WB2C"][:, colx * 4 + i:colx * 4 + i + 1],
                                 BH[:, i * CH + s * 512:i * CH + (s + 1) * 512],
                                 start=(i == 0), stop=(i == 3))
                return r
            emw0 = P.op("pe", lambda e, bwmm=bwmm: [bwmm(e, 0, 0), bwmm(e, 0, 1)][-1],
                        deps=(ev_bh[3], K._ev_wb2, evg, evgb))
            ecp0 = P.op("ve", lambda e: e.tensor_copy(rowsc[0:1, 0:CH], psR[0:1, 0:CH]),
                        deps=(emw0,))
            emw1 = P.op("pe", lambda e, bwmm=bwmm: [bwmm(e, 1, 0), bwmm(e, 1, 1)][-1],
                        deps=(ecp0,))
            ecp1 = P.op("ve", lambda e: e.tensor_copy(rowsc[0:1, CH:2 * CH], psR[0:1, 0:CH]),
                        deps=(emw1,))
            ed = P.op("ve", lambda e: e.tensor_tensor(
                rowsc[0:1, 2 * CH:3 * CH], rowsc[0:1, 0:CH], rowsc[0:1, CH:2 * CH],
                AO.subtract), deps=(ecp1,))
            P.op("sc", lambda e, c0_=c0_: e.activation(
                BW0[0:1, c0_:c0_ + CH], rowsc[0:1, 2 * CH:3 * CH], AF.Sigmoid,
                bias=K.sb["bb2diff"][0:1, 0:1], scale=1.0), deps=(ed, K._ev_bb2))
            P.op("sc", lambda e, c0_=c0_: e.activation(
                BW1[0:1, c0_:c0_ + CH], rowsc[0:1, 2 * CH:3 * CH], AF.Sigmoid,
                bias=K.sb["bb2diff"][0:1, 1:2], scale=-1.0), deps=(ed, K._ev_bb2))
            ev_bw = P.ev("sc")

            ev_sp = None
            for i in range(NCH):
                slot = i % 2
                ewp = stream_och(io["Wpv"], i, slot, deps=(ev_sp, ev_bw))

                def pvm2(e, i=i, slot=slot, c0_=c0_):
                    r = None
                    for s in range(2):
                        for j in range(NCH):
                            r = e.matmul(psA[:, (s % 2) * 512:((s % 2) + 1) * 512],
                                         wb[:, slot * NCH * 128 + j * 128:
                                            slot * NCH * 128 + (j + 1) * 128],
                                         frhs(j, c0_ + s * 512, 512),
                                         start=(j == 0), stop=(j == NCH - 1))
                    return r
                em = P.op("pe", pvm2, deps=(ewp, ev_sp))
                epv = P.op("sc", lambda e, i=i: [
                    e.activation(t_a[:, s * 512:(s + 1) * 512],
                                 psA[:, (s % 2) * 512:((s % 2) + 1) * 512],
                                 AF.Identity, bias=K.bcol("bpv", i), scale=1.0)
                    for s in range(2)][-1], deps=(em,))
                epp = P.op("sy", lambda e, i=i, c0_=c0_: e.dma_start(
                    out=t_b[:], in_=io["ppT"][i * 128:(i + 1) * 128, c0_:c0_ + CH]),
                    deps=(ev_sp,))
                P.op("ve", lambda e: e.tensor_scalar(t_c[:], t_b[:], 1.0 / (2 * PI),
                                                     MAGIC, AO.mult, AO.add), deps=(epp, ev_sp))
                P.op("ve", lambda e: e.tensor_scalar(t_c[:], t_c[:], MAGIC, None, AO.subtract))
                P.op("ve", lambda e: e.scalar_tensor_tensor(t_b[:], t_c[:], -2.0 * PI, t_b[:],
                                                            AO.mult, AO.add))
                esin = P.op("sc", lambda e: e.activation(t_c[:], t_b[:], AF.Sin),
                            deps=(P.ev("ve"), epv))
                eabs = P.op("ve", lambda e: e.scalar_tensor_tensor(t_b[:], t_b[:], -1.0, t_b[:],
                                                                   AO.mult, AO.max),
                            deps=(esin,))
                ecos = P.op("sc", lambda e: e.activation(t_b[:], t_b[:], AF.Sin,
                                                         bias=0.5 * PI, scale=-1.0),
                            deps=(eabs,))
                P.op("ve", lambda e: e.tensor_tensor(t_d[:], t_b[:], t_a[:], AO.mult),
                     deps=(ecos, epv))
                P.op("ve", lambda e, i=i: e.tensor_tensor_scan(
                    t_d[:], t_d[:], t_d[:], CARR[:, 0 * 8 + i:0 * 8 + i + 1],
                    AO.add, AO.bypass))
                K.fill()
                P.op("ve", lambda e, i=i: e.tensor_copy(CARR[:, 0 * 8 + i:0 * 8 + i + 1],
                                                        t_d[:, CH - 1:CH]))
                es1 = P.op("sy", lambda e, i=i, c0_=c0_: e.dma_start(
                    out=io["MEMS"][0 * D + i * 128:0 * D + (i + 1) * 128, c0_:c0_ + CH],
                    in_=t_d[:]), deps=(P.ev("ve"),))
                P.op("ve", lambda e: e.tensor_tensor(t_d[:], t_c[:], t_a[:], AO.mult),
                     deps=(esin, es1))
                P.op("ve", lambda e, i=i: e.tensor_tensor_scan(
                    t_d[:], t_d[:], t_d[:], CARR[:, 1 * 8 + i:1 * 8 + i + 1],
                    AO.add, AO.bypass))
                K.fill()
                P.op("ve", lambda e, i=i: e.tensor_copy(CARR[:, 1 * 8 + i:1 * 8 + i + 1],
                                                        t_d[:, CH - 1:CH]))
                es2 = P.op("sy", lambda e, i=i, c0_=c0_: e.dma_start(
                    out=io["MEMS"][1 * D + i * 128:1 * D + (i + 1) * 128, c0_:c0_ + CH],
                    in_=t_d[:]), deps=(P.ev("ve"),))
                ewk = stream_w(K, wb[:, (2 + slot) * NCH * 128:(3 + slot) * NCH * 128],
                               lambda i=i: io["Wkv"][:, i * 128:(i + 1) * 128], NCH,
                               deps=(ev_sp, ev_g[NCH - 1]))

                def kvm2(e, i=i, slot=slot, c0_=c0_):
                    r = None
                    for s in range(2):
                        for j in range(NCH):
                            r = e.matmul(psA[:, (s % 2) * 512:((s % 2) + 1) * 512],
                                         wb[:, (2 + slot) * NCH * 128 + j * 128:
                                            (2 + slot) * NCH * 128 + (j + 1) * 128],
                                         frhs(j, c0_ + s * 512, 512),
                                         start=(j == 0), stop=(j == NCH - 1))
                    return r
                emk2_ = P.op("pe", kvm2, deps=(ewk, epv))
                ekv = P.op("sc", lambda e, i=i: [
                    e.activation(t_a[:, s * 512:(s + 1) * 512],
                                 psA[:, (s % 2) * 512:((s % 2) + 1) * 512],
                                 AF.Identity, bias=K.bcol("bkv", i), scale=1.0)
                    for s in range(2)][-1], deps=(emk2_, P.ev("ve")))
                P.op("ve", lambda e: e.tensor_tensor(t_a[:], t_a[:], VGB[:], AO.mult),
                     deps=(ekv, evgb))
                ksh = KEYPH[:, i * (CH + 1):i * (CH + 1) + CH]
                esn = P.op("sc", lambda e, ksh=ksh: e.activation(t_b[:], ksh, AF.Sin, scale=PI),
                           deps=(ev_keyph, P.ev("ve")))
                eab = P.op("ve", lambda e, ksh=ksh: e.scalar_tensor_tensor(
                    t_c[:], ksh, -1.0, ksh, AO.mult, AO.max), deps=(ev_keyph, esn))
                ecn = P.op("sc", lambda e: e.activation(t_c[:], t_c[:], AF.Sin,
                                                        bias=0.5 * PI, scale=-PI), deps=(eab,))
                if cs == 0:
                    P.op("ve", lambda e: e.tensor_scalar(
                        t_b[:, 0:1], t_b[:, 0:1], K.sb["mprev_t"][:, 0:1], None, AO.mult),
                        deps=(esn,))
                    ecn = P.op("ve", lambda e: e.tensor_scalar(
                        t_c[:, 0:1], t_c[:, 0:1], K.sb["mprev_t"][:, 0:1], None, AO.mult),
                        deps=(ecn,))
                    K.fill()
                P.op("ve", lambda e: e.tensor_tensor(t_d[:], t_a[:], t_c[:], AO.mult),
                     deps=(ecn,))
                P.op("ve", lambda e, i=i: e.tensor_tensor_scan(
                    t_d[:], t_d[:], t_d[:], CARR[:, 2 * 8 + i:2 * 8 + i + 1],
                    AO.add, AO.bypass))
                K.fill()
                P.op("ve", lambda e, i=i: e.tensor_copy(CARR[:, 2 * 8 + i:2 * 8 + i + 1],
                                                        t_d[:, CH - 1:CH]))
                es3 = P.op("sy", lambda e, i=i, c0_=c0_: e.dma_start(
                    out=io["MEMS"][2 * D + i * 128:2 * D + (i + 1) * 128, c0_:c0_ + CH],
                    in_=t_d[:]), deps=(P.ev("ve"),))
                P.op("ve", lambda e: e.tensor_tensor(t_d[:], t_a[:], t_b[:], AO.mult),
                     deps=(esn, es3))
                P.op("ve", lambda e, i=i: e.tensor_tensor_scan(
                    t_d[:], t_d[:], t_d[:], CARR[:, 3 * 8 + i:3 * 8 + i + 1],
                    AO.add, AO.bypass))
                K.fill()
                P.op("ve", lambda e, i=i: e.tensor_copy(CARR[:, 3 * 8 + i:3 * 8 + i + 1],
                                                        t_d[:, CH - 1:CH]))
                es4 = P.op("sy", lambda e, i=i, c0_=c0_: e.dma_start(
                    out=io["MEMS"][3 * D + i * 128:3 * D + (i + 1) * 128, c0_:c0_ + CH],
                    in_=t_d[:]), deps=(P.ev("ve"),))
                ev_sp = P.op("sy", lambda e, i=i, c0_=c0_: e.dma_start(
                    out=io["MEMS"][4 * D + i * 128:4 * D + (i + 1) * 128, c0_:c0_ + CH],
                    in_=KEYPH[:, i * (CH + 1) + 1:i * (CH + 1) + 1 + CH]),
                    deps=(ev_keyph, es4))
            ev_sp_chunk[cs] = ev_sp
        P.barrier()

        ex2 = P.op("gd", lambda e: e.dma_start(
            out=io["xsm"][bass.ds(K.r_mysm, 128), 0:34], in_=CARR[:]),
            deps=(ev_sp_chunk[NC4 - 1], P.ev("ve")))
        ev_r2 = flag_and_poll(K, 2, (ex2,))
        ex2r = P.op("gd", lambda e: e.dma_start(
            out=PCAR[:], in_=io["xsm"][bass.ds(K.r_peersm, 128), 0:34]), deps=(ev_r2,))
        emc = P.op("ve", lambda e: e.tensor_scalar(PCAR[:], PCAR[:], K.sb["mprev_t"][:, 0:1],
                                                   None, AO.mult), deps=(ex2r,))
        P.barrier()
        cscan.close()

        cret = ctx.enter_context(ExitStack())
        BLD = K.sbuf("B_BLD", [128, NCH * 1024], F32, cret)
        BLN = K.sbuf("B_BLN", [128, NCH * 1024], BF16, cret)
        rt_a = K.sbuf("B_rta", [128, 1024], F32, cret)
        rt_b = K.sbuf("B_rtb", [128, 1024], F32, cret)
        rt_c = K.sbuf("B_rtc", [128, 1024], F32, cret)
        rt_d = K.sbuf("B_rtd", [128, 1024], F32, cret)
        rwb = K.sbuf("B_rwb", [128, 2 * NCH * 128], BF16, cret)
        rpsA = K.psum("B_rpsA", [128, 2 * 512], F32, cret)
        rpsB = K.psum("B_rpsB", [128, 1024], F32, cret)

        for cs in range(NC4):
            c0_ = cs * CH
            with ExitStack() as cr:
                g = K.sbuf(f"B_g{cs}", [1, CH], F32, cr)
                P.op("ve", lambda e, c0_=c0_, g=g: e.tensor_scalar(
                    g[:], GSR[0:1, c0_:c0_ + CH], PCAR[0:1, 32:33], None, AO.add),
                    deps=(emc,))
                P.op("ve", lambda e, g=g: e.tensor_scalar(g[:], g[:], 1.0, None, AO.max))
                esq = P.op("sc", lambda e, g=g: e.activation(g[:], g[:], AF.Sqrt, bias=1e-12,
                                                             scale=1.0), deps=(P.ev("ve"),))
                P.op("ve", lambda e, g=g: e.reciprocal(g[:], g[:]), deps=(esq,))
                erg = P.op("ve", lambda e, g=g: e.tensor_scalar(g[:], g[:], 1.0 / 32.0, None,
                                                                AO.mult))
                b0r = K.sbuf(f"B_b0r{cs}", [1, 2 * CH], F32, cr)
                P.op("ve", lambda e, c0_=c0_, b0r=b0r: e.tensor_scalar(
                    b0r[:, 0:CH], BW0[0:1, c0_:c0_ + CH], 1.0 / 32.0, None, AO.mult))
                ebwr = P.op("ve", lambda e, c0_=c0_, b0r=b0r: e.tensor_copy(
                    b0r[:, CH:2 * CH], BW1[0:1, c0_:c0_ + CH]))
                (grorwb, bw0b, bw1b), ev_brwb = bcast_rows(
                    K, cr, f"Br{cs}", [(g[:], erg), (b0r[:, 0:CH], ebwr),
                                       (b0r[:, CH:2 * CH], ebwr)], CH, ps=rpsB)
                (m3cb, r3cb), ev_b3 = bcast_dram_rows(
                    K, cr, f"Bc{cs}", [4, 5], CH, c0_, ps=rpsB, deps=(ev_brwb,))

                ev_t = None
                for i in range(NCH):
                    e1_ = P.op("sy", lambda e, i=i, c0_=c0_: e.dma_start(
                        out=rt_a[:], in_=io["MEMS"][0 * D + i * 128:0 * D + (i + 1) * 128,
                                                   c0_:c0_ + CH]), deps=(ev_t,))
                    e2_ = P.op("sy", lambda e, i=i, c0_=c0_: e.dma_start(
                        out=rt_b[:], in_=io["MEMS"][1 * D + i * 128:1 * D + (i + 1) * 128,
                                                   c0_:c0_ + CH]))
                    epp = P.op("sy", lambda e, i=i, c0_=c0_: e.dma_start(
                        out=rt_c[:], in_=io["ppT"][i * 128:(i + 1) * 128, c0_:c0_ + CH]))
                    P.op("ve", lambda e: e.tensor_scalar(rt_d[:], rt_c[:], 1.0 / (2 * PI),
                                                         MAGIC, AO.mult, AO.add),
                         deps=(epp, ev_t))
                    P.op("ve", lambda e: e.tensor_scalar(rt_d[:], rt_d[:], MAGIC, None,
                                                         AO.subtract))
                    P.op("ve", lambda e: e.scalar_tensor_tensor(rt_c[:], rt_d[:], -2.0 * PI,
                                                                rt_c[:], AO.mult, AO.add))
                    esin = P.op("sc", lambda e: e.activation(rt_d[:], rt_c[:], AF.Sin),
                                deps=(P.ev("ve"),))
                    eabs = P.op("ve", lambda e: e.scalar_tensor_tensor(
                        rt_c[:], rt_c[:], -1.0, rt_c[:], AO.mult, AO.max), deps=(esin,))
                    ecos = P.op("sc", lambda e: e.activation(rt_c[:], rt_c[:], AF.Sin,
                                                             bias=0.5 * PI, scale=-1.0),
                                deps=(eabs,))
                    P.op("ve", lambda e, i=i: e.scalar_tensor_tensor(
                        rt_a[:], rt_a[:], PCAR[:, 0 * 8 + i:0 * 8 + i + 1], rt_c[:],
                        AO.add, AO.mult), deps=(e1_, ecos, emc))
                    P.op("ve", lambda e, i=i: e.scalar_tensor_tensor(
                        rt_b[:], rt_b[:], PCAR[:, 1 * 8 + i:1 * 8 + i + 1], rt_d[:],
                        AO.add, AO.mult), deps=(e2_, esin))
                    P.op("ve", lambda e: e.tensor_tensor(rt_a[:], rt_a[:], rt_b[:], AO.add))
                    eposp = P.op("ve", lambda e, i=i, bw0b=bw0b: e.tensor_tensor(
                        BLD[:, i * CH:(i + 1) * CH], rt_a[:], bw0b, AO.mult), deps=(ev_brwb,))
                    e3_ = P.op("sy", lambda e, i=i, c0_=c0_: e.dma_start(
                        out=rt_a[:], in_=io["MEMS"][2 * D + i * 128:2 * D + (i + 1) * 128,
                                                   c0_:c0_ + CH]), deps=(eposp,))
                    e4_ = P.op("sy", lambda e, i=i, c0_=c0_: e.dma_start(
                        out=rt_b[:], in_=io["MEMS"][3 * D + i * 128:3 * D + (i + 1) * 128,
                                                   c0_:c0_ + CH]))
                    e5_ = P.op("sy", lambda e, i=i, c0_=c0_: e.dma_start(
                        out=rt_d[:], in_=io["MEMS"][4 * D + i * 128:4 * D + (i + 1) * 128,
                                                   c0_:c0_ + CH]))
                    esn2 = P.op("sc", lambda e: e.activation(rt_c[:], rt_d[:], AF.Sin, scale=PI),
                                deps=(e5_, eposp))
                    eab2 = P.op("ve", lambda e: e.scalar_tensor_tensor(
                        rt_d[:], rt_d[:], -1.0, rt_d[:], AO.mult, AO.max), deps=(esn2,))
                    ecs2 = P.op("sc", lambda e: e.activation(rt_d[:], rt_d[:], AF.Sin,
                                                             bias=0.5 * PI, scale=-PI),
                                deps=(eab2,))
                    P.op("ve", lambda e, i=i: e.scalar_tensor_tensor(
                        rt_a[:], rt_a[:], PCAR[:, 2 * 8 + i:2 * 8 + i + 1], rt_d[:],
                        AO.add, AO.mult), deps=(e3_, ecs2))
                    P.op("ve", lambda e, i=i: e.scalar_tensor_tensor(
                        rt_b[:], rt_b[:], PCAR[:, 3 * 8 + i:3 * 8 + i + 1], rt_c[:],
                        AO.add, AO.mult), deps=(e4_,))
                    P.op("ve", lambda e: e.tensor_tensor(rt_a[:], rt_a[:], rt_b[:], AO.add))
                    P.op("ve", lambda e, grorwb=grorwb: e.tensor_tensor(
                        rt_a[:], rt_a[:], grorwb, AO.mult))
                    P.op("ve", lambda e, bw1b=bw1b: e.tensor_tensor(
                        rt_a[:], rt_a[:], bw1b, AO.mult))
                    ev_t = P.op("ve", lambda e, i=i: e.tensor_tensor(
                        BLD[:, i * CH:(i + 1) * CH], BLD[:, i * CH:(i + 1) * CH], rt_a[:],
                        AO.add))
                ev_bld = ev_t

                def srcb(i):
                    return BLD[:, i * CH:(i + 1) * CH], ev_bld
                mu4, r4, _v4, ev_st4 = ln_stats(K, cr, f"B{cs}", srcb, NCH, CH, 1e-5)
                (m4b, r4b), ev_bc4 = bcast_rows(K, cr, f"B{cs}b",
                                                [(mu4, ev_st4), (r4, ev_st4)], CH, ps=rpsB)
                evn = None
                for i in range(NCH):
                    P.op("ve", lambda e, i=i, m4b=m4b: e.tensor_tensor(
                        BLD[:, i * CH:(i + 1) * CH], BLD[:, i * CH:(i + 1) * CH], m4b,
                        AO.subtract), deps=(ev_bc4, evn))
                    P.op("ve", lambda e, i=i, r4b=r4b: e.tensor_tensor(
                        BLD[:, i * CH:(i + 1) * CH], BLD[:, i * CH:(i + 1) * CH], r4b,
                        AO.mult))
                    evn = P.op("ve", lambda e, i=i: e.tensor_scalar(
                        BLN[:, i * CH:(i + 1) * CH], BLD[:, i * CH:(i + 1) * CH],
                        K.bcol("bln_g", i), K.bcol("bln_b", i), AO.mult, AO.add))
                ev_bln = evn

                ev_o = {}
                for o in range(NCH):
                    slot = o % 2
                    ewo = stream_w(K, rwb[:, slot * NCH * 128:(slot + 1) * NCH * 128],
                                   lambda o=o: io["Wo"][:, o * 128:(o + 1) * 128], NCH,
                                   deps=(ev_o.get(o - 2), ev_bln))
                    exl = P.op("sy", lambda e, o=o, c0_=c0_: e.dma_start(
                        out=rt_c[:], in_=io["XRES"][o * 128:(o + 1) * 128, c0_:c0_ + CH]),
                        deps=(ev_o.get(o - 1), ev_bld))
                    P.op("ve", lambda e, m3cb=m3cb: e.tensor_tensor(
                        rt_d[:], rt_c[:], m3cb, AO.subtract), deps=(exl, ev_b3))
                    ezz = P.op("ve", lambda e, r3cb=r3cb: e.tensor_tensor(
                        rt_d[:], rt_d[:], r3cb, AO.mult))

                    def omm(e, o=o, slot=slot):
                        r = None
                        for s in range(2):
                            for i in range(NCH):
                                r = e.matmul(rpsA[:, (s % 2) * 512:((s % 2) + 1) * 512],
                                             rwb[:, slot * NCH * 128 + i * 128:
                                                slot * NCH * 128 + (i + 1) * 128],
                                             BLN[:, i * CH + s * 512:i * CH + (s + 1) * 512],
                                             start=(i == 0), stop=(i == NCH - 1))
                        return r
                    emo = P.op("pe", omm, deps=(ewo, ev_bln, ev_o.get((o - 1, "m"))))
                    P.op("ve", lambda e, o=o: [
                        e.scalar_tensor_tensor(rt_d[:, s * 512:(s + 1) * 512],
                                               rpsA[:, (s % 2) * 512:((s % 2) + 1) * 512],
                                               K.bcol("bo", o), rt_d[:, s * 512:(s + 1) * 512],
                                               AO.add, AO.add) for s in range(2)][-1],
                        deps=(emo, ezz))
                    eadd = P.op("ve", lambda e: e.tensor_tensor(rt_d[:], rt_d[:], rt_c[:], AO.add))
                    ev_o[(o, "m")] = eadd
                    ev_o[o] = P.op("sy", lambda e, o=o, c0_=c0_: e.dma_start(
                        out=io["XRES"][o * 128:(o + 1) * 128, c0_:c0_ + CH], in_=rt_d[:]),
                        deps=(eadd,))
                P.barrier()
        cret.close()
        P.barrier()


import os
PHASES = tuple(os.environ.get("KPHASES", "A,F1,B,F2").split(","))


def build_body(K):
    nc, P, io = K.nc, K.P, K.io
    K._ev_cwd = {}
    K._fw_cached = False
    K._fw_st1 = {}
    K._fw_st2 = {}
    setup(K)
    bb2g = K.sbuf("bb2g", [1, 1], F32)
    K._ev_bg2g = P.op("sy", lambda e: e.dma_start(out=bb2g[:], in_=K.io["bg2v"]))
    P.barrier()
    def dbg_snap(nm):
        if not os.environ.get("KPHDBG"):
            return
        P.barrier()
        ev = P.op("sy", lambda e, nm=nm: e.dma_start(out=io[nm], in_=io["XRES"]))
        P.op("ve", None, deps=(ev,), inc=0)
        P.barrier()

    if "A" in PHASES:
        phase_attn_a(K)
        dbg_snap("dbgA")
    if "F1" in PHASES:
        phase_ffn(K, 1)
        dbg_snap("dbgF1")
        if os.environ.get("KDBG"):
            P.barrier()
            e1 = P.op("sy", lambda e: e.dma_start(out=io["dbgCWD"], in_=io["CWD"]))
            e2 = P.op("sy", lambda e: e.dma_start(out=io["dbgOUT"],
                                                  in_=io["MEMS"][0:D, :]))
            P.op("ve", None, deps=(e1, e2), inc=0)
            P.barrier()
    if "B" in PHASES:
        phase_attn_b(K)
        dbg_snap("dbgB")
    if "F2" in PHASES:
        phase_ffn(K, 2)
    if "F2" not in PHASES:
        fin = K.sbuf("fin_dbg", [128, T], F32)
        ev = None
        for i in range(NCH):
            e1 = P.op("sy", lambda e, i=i: e.dma_start(
                out=fin[:], in_=io["XRES"][i * 128:(i + 1) * 128, :]), deps=(ev,))
            ev = P.op("sy", lambda e, i=i: e.dma_start(
                out=io["out"][i * 128:(i + 1) * 128, :], in_=fin[:]), deps=(e1,))
        P.op("ve", None, deps=(ev,), inc=0)
    P.barrier()


def host_bias_table(inp):
    tbl = np.zeros((BTOT, 128), np.float32)

    def put(name, vec):
        v = np.asarray(vec, np.float32).reshape(-1)
        n = (v.size + 127) // 128
        pad = np.zeros(n * 128, np.float32)
        pad[:v.size] = v
        tbl[BROWS[name]:BROWS[name] + n] = pad.reshape(n, 128)
    put("aln_w", inp["L0_attn_a_ln_w"]); put("aln_b", inp["L0_attn_a_ln_b"])
    put("bqkv", inp["L0_attn_a_b_qkv"]); put("bproj", inp["L0_attn_a_b_proj"])
    put("rb", inp["L0_ffn_router_b"])
    put("fb1", inp["L0_ffn_b1"]); put("fb2", inp["L0_ffn_b2"])
    put("fln_w", inp["L0_ffn_ln_w"]); put("fln_b", inp["L0_ffn_ln_b"])
    put("bpv", inp["L0_attn_b_bpv"]); put("bk", inp["L0_attn_b_bk"])
    put("bkv", inp["L0_attn_b_bkv"]); put("bg1", inp["L0_attn_b_bg1"])
    put("bb1", inp["L0_attn_b_bb1"])
    put("bmisc", np.concatenate([np.asarray(inp["L0_attn_b_bg2"], np.float32).reshape(-1),
                                 np.asarray(inp["L0_attn_b_bb2"], np.float32).reshape(-1)]))
    put("bln_g", inp["L0_attn_b_ln_g"]); put("bln_b", inp["L0_attn_b_ln_b"])
    put("bo", inp["L0_attn_b_bo"])
    return tbl


_NC_CACHE = {}


def build_program():
    if "nc" in _NC_CACHE:
        return _NC_CACHE["nc"]
    nc = bass.Bass("TRN2", target_bir_lowering=False, detect_race_conditions=False,
                   num_devices=8)
    io = {}

    def din(name, shape, dt=F32):
        io[name] = nc.dram_tensor(name, list(shape), dt, kind="ExternalInput").ap()
    din("xT", [D, T])
    din("Wqkv", [D, 3 * D]); din("Wproj", [D, D])
    din("ppT", [D, T])
    din("Wpv", [D, D]); din("Wk", [D, D]); din("Wkv", [D, D])
    din("Wg1", [2 * D, D]); din("Wg2", [D, 1]); din("Wb1", [D, 512]); din("Wb2", [512, 2])
    din("Wo", [D, D])
    din("w1T", [NE, D, FF]); din("w2T", [NE, FF, D]); din("rwT", [D, NE])
    din("btab", [BTOT, 128])
    din("rbrow", [1, 8])
    din("b2m", [NE, D])
    din("bb2d", [1, 2])
    din("bg2v", [1, 1])
    din("slot", [1, 8], I32)
    din("nonces", [N_ROUNDS, 16], I32)
    din("mprev", [128, 1])
    io["out"] = nc.dram_tensor("out", [D, T], F32, kind="ExternalOutput").ap()
    io["XRES"] = nc.dram_tensor("XRES", [D, T], F32, kind="Internal").ap()
    io["MSAR"] = nc.dram_tensor("MSAR", [D, T], BF16, kind="Internal").ap()
    io["DEN"] = nc.dram_tensor("DEN", [16, T], F32, kind="Internal").ap()
    io["CWD"] = nc.dram_tensor("CWD", [NE, T], BF16, kind="Internal").ap()
    io["ROWSD"] = nc.dram_tensor("ROWSD", [8, T], F32, kind="Internal").ap()
    io["MEMS"] = nc.dram_tensor("MEMS", [5 * D, T], F32, kind="Internal").ap()
    io["W1BD"] = nc.dram_tensor("W1BD", [32 * 128, NCH * 1024], BF16, kind="Internal").ap()
    io["W2BD"] = nc.dram_tensor("W2BD", [32 * 128, 8 * 1024], BF16, kind="Internal").ap()
    io["KVSTG"] = nc.dram_tensor("KVSTG", [2 * D, T], BF16, kind="Internal").ap()
    io["xkv"] = nc.dram_tensor("xkv", [2 * 2 * D, T], BF16, kind="Internal",
                               addr_space="Shared").ap()
    io["xsm"] = nc.dram_tensor("xsm", [256, 48], F32, kind="Internal",
                               addr_space="Shared").ap()
    io["xfl"] = nc.dram_tensor("xfl", [2, 16], I32, kind="Internal",
                               addr_space="Shared").ap()
    if os.environ.get("KPHDBG"):
        for nm in ("dbgA", "dbgF1", "dbgB"):
            io[nm] = nc.dram_tensor(nm, [D, T], F32, kind="ExternalOutput").ap()
    if os.environ.get("KDBG"):
        io["dbgCWD"] = nc.dram_tensor("dbgCWD", [NE, T], BF16, kind="ExternalOutput").ap()
        io["dbgOUT"] = nc.dram_tensor("dbgOUT", [D, T], F32, kind="ExternalOutput").ap()

    with ExitStack() as ctx:
        P = Prog(nc, ctx)
        K = Kern(nc, ctx, P, io)
        build_body(K)
        block = ctx.enter_context(nc.Block())
        nbad = P.check_closures()
        assert nbad == 0, f"{nbad} lambdas with drifted closures (late-binding bug)"
        P.emit(block)
    _NC_CACHE["nc"] = nc
    _NC_CACHE["P"] = P
    return nc


def make_inmaps(inputs):
    x = np.asarray(inputs["L0_attn_a_z"], np.float32)
    pp = np.asarray(inputs["L0_attn_b_pos_phases"], np.float32)[:L]
    w1 = np.asarray(inputs["L0_ffn_w1"], np.float32)
    w2 = np.asarray(inputs["L0_ffn_w2"], np.float32)
    rw = np.asarray(inputs["L0_ffn_router_w"], np.float32)
    btab = host_bias_table(inputs)
    w1T = np.ascontiguousarray(np.transpose(w1, (0, 2, 1)))
    w2T = np.ascontiguousarray(np.transpose(w2, (0, 2, 1)))
    rwT = np.ascontiguousarray(rw.T)
    rng = np.random.default_rng()
    nonces = rng.integers(1, 2**31 - 1, size=(N_ROUNDS, 16)).astype(np.int32)
    bb2 = np.asarray(inputs["L0_attn_b_bb2"], np.float32).reshape(-1)
    bb2d = np.array([[bb2[0] - bb2[1], bb2[1] - bb2[0]]], np.float32)
    rbrow = np.asarray(inputs["L0_ffn_router_b"], np.float32).reshape(1, 8)
    bg2v = np.asarray(inputs["L0_attn_b_bg2"], np.float32).reshape(1, 1)

    common = {
        "Wqkv": np.ascontiguousarray(inputs["L0_attn_a_W_qkv"], dtype=np.float32),
        "Wproj": np.ascontiguousarray(inputs["L0_attn_a_W_proj"], dtype=np.float32),
        "Wpv": np.ascontiguousarray(inputs["L0_attn_b_Wpv"], dtype=np.float32),
        "Wk": np.ascontiguousarray(inputs["L0_attn_b_Wk"], dtype=np.float32),
        "Wkv": np.ascontiguousarray(inputs["L0_attn_b_Wkv"], dtype=np.float32),
        "Wg1": np.ascontiguousarray(inputs["L0_attn_b_Wg1"], dtype=np.float32),
        "Wg2": np.ascontiguousarray(inputs["L0_attn_b_Wg2"], dtype=np.float32),
        "Wb1": np.ascontiguousarray(inputs["L0_attn_b_Wb1"], dtype=np.float32),
        "Wb2": np.ascontiguousarray(inputs["L0_attn_b_Wb2"], dtype=np.float32),
        "Wo": np.ascontiguousarray(inputs["L0_attn_b_Wo"], dtype=np.float32),
        "w1T": w1T, "w2T": w2T, "rwT": rwT, "btab": btab,
        "rbrow": rbrow, "bb2d": bb2d, "bg2v": bg2v, "nonces": nonces,
        "b2m": np.ascontiguousarray(inputs["L0_ffn_b2"], dtype=np.float32),
    }
    in_maps = []
    for c in range(8):
        b = (c // 2) % 2
        h = c % 2
        m = dict(common)
        m["xT"] = np.ascontiguousarray(x[b, h * T:(h + 1) * T, :].T)
        m["ppT"] = np.ascontiguousarray(pp[h * T:(h + 1) * T, :].T)
        m["slot"] = np.array([[h * 2048, h, h * 128, (1 - h) * 128, 0, 0, 0, 0]], np.int32)
        m["mprev"] = np.full((128, 1), float(h), np.float32)
        in_maps.append(m)
    return in_maps


def kernel(**inputs):
    nc = build_program()
    in_maps = make_inmaps(inputs)
    res = run_bass_kernel_spmd(nc, in_maps, core_ids=list(range(8)))
    out = np.empty((2, L, D), np.float32)
    for c in range(4):
        b = c // 2
        h = c % 2
        out[b, h * T:(h + 1) * T, :] = np.asarray(res.results[c]["out"]).T
    return out


# version 06
# speedup vs baseline: 1.4747x; 1.0741x over previous
import numpy as np
import ml_dtypes
import concourse.bass as bass
import concourse.mybir as mybir
from concourse.alu_op_type import AluOpType
from concourse.bass_utils import run_bass_kernel_spmd
from contextlib import ExitStack

F32 = mybir.dt.float32
BF16 = mybir.dt.bfloat16
I32 = mybir.dt.int32
AF = mybir.ActivationFunctionType
AO = AluOpType
AX = mybir.AxisListType
PI = float(np.pi)
MAGIC = 12582912.0

D = 1024
NCH = 8
T = 4096
L = 8192
FF = 4096
NE = 8
NPAIR = 8
N_ROUNDS = 4
NS = T // 512

ENGS = ("sy", "sc", "ve", "pe", "gp", "gd")
BLOCK_OF = {"sy": "sync", "sc": "scalar", "ve": "vector", "pe": "tensor",
            "gp": "gpsimd", "gd": "gpsimd"}
INC_OF = {"sy": 16, "sc": 1, "ve": 1, "pe": 1, "gp": 1, "gd": 16}
ENG_OF_BLOCK = {"sync": "sy", "scalar": "sc", "vector": "ve", "tensor": "pe",
                "gpsimd": "gp"}


class Prog:
    def __init__(self, nc, ctx):
        self.nc = nc
        self.sems = {n: ctx.enter_context(nc.semaphore("S_" + n)) for n in ENGS}
        self.count = {n: 0 for n in ENGS}
        self.blocks = {b: [] for b in ("sync", "scalar", "vector", "tensor", "gpsimd")}
        self.last_wait = {b: {} for b in self.blocks}
        self.snaps = []

    def op(self, eng, fn, deps=(), inc=None):
        blk = BLOCK_OF[eng]
        if inc is None:
            inc = INC_OF[eng]
        real = []
        for d in deps:
            if d is None:
                continue
            dsem, dval = d
            if BLOCK_OF[dsem] == blk:
                continue
            prev = self.last_wait[blk].get(dsem, 0)
            if dval > prev:
                self.last_wait[blk][dsem] = dval
                real.append((dsem, dval))
        import traceback as _tb
        fr = _tb.extract_stack(limit=2)[0]
        tag = f"{fr.filename.split('/')[-1]}:{fr.lineno}"
        if fn is not None and getattr(fn, "__closure__", None):
            snap = tuple(id(c.cell_contents) for c in fn.__closure__)
            self.snaps.append((fn, snap, tag))
        self.blocks[blk].append((eng, fn, real, inc, tag))
        self.count[eng] += inc
        return (eng, self.count[eng])

    def check_closures(self):
        bad = {}
        for fn, snap, tag in self.snaps:
            cur = tuple(id(c.cell_contents) for c in fn.__closure__)
            if cur != snap:
                names = [n for n, a, b in zip(fn.__code__.co_freevars, snap, cur) if a != b]
                bad.setdefault((tag, tuple(names)), [0])[0] += 1
        for (tag, names), cnt in sorted(bad.items()):
            print(f"CLOSURE DRIFT {tag} vars={names} x{cnt[0]}")
        return len(bad)

    def ev(self, eng):
        return (eng, self.count[eng])

    def barrier(self):
        evs = [(n, self.count[n]) for n in ENGS if self.count[n] > 0]
        for b in self.blocks:
            self.op(ENG_OF_BLOCK[b], None, deps=evs, inc=0)

    def emit(self, block):
        for blk_name, lst in self.blocks.items():
            if not lst:
                continue

            def body(engine, lst=lst):
                for eng, fn, deps, inc, tag in lst:
                    for dsem, dval in deps:
                        engine.wait_ge(self.sems[dsem], dval)
                    if fn is None:
                        continue
                    r = fn(engine)
                    if inc > 0:
                        assert r is not None, f"op on {eng} returned None, inc={inc}"
                        r.then_inc(self.sems[eng], inc)
            getattr(block, blk_name)(body)


def _bias_spec():
    spec = [("aln_w", 8), ("aln_b", 8), ("bqkv", 24), ("bproj", 8), ("rb", 1),
            ("fb1", NE * FF // 128), ("fb2", NE * D // 128),
            ("fln_w", 8), ("fln_b", 8),
            ("bpv", 8), ("bk", 8), ("bkv", 8), ("bg1", 8), ("bb1", 4),
            ("bmisc", 1), ("bln_g", 8), ("bln_b", 8), ("bo", 8)]
    rows, off = {}, 0
    for name, n in spec:
        rows[name] = off
        off += n
    return rows, ((off + 127) // 128) * 128


BROWS, BTOT = _bias_spec()

CONSTS = [1e-5, 1e-12, 0.125, 1.0, -1.0, 0.5, -PI, 0.5 * PI, PI, -2.0 * PI,
          1.0 / (2.0 * PI), MAGIC, 1.0 / D, 1.0 / 32.0, -1e30, 2.0, -0.125, 0.0]


class Kern:
    def __init__(self, nc, ctx, P, io):
        self.nc, self.ctx, self.P, self.io = nc, ctx, P, io
        self.sb = {}
        self._b3 = {}

    def sbuf(self, name, shape, dt, ctx=None):
        t = (ctx or self.ctx).enter_context(self.nc.sbuf_tensor(name, list(shape), dt))
        self.sb[name] = t
        return t

    def psum(self, name, shape, dt=F32, ctx=None):
        return (ctx or self.ctx).enter_context(self.nc.psum_tensor(name, list(shape), dt))

    def bcol(self, name, i=0):
        c = BROWS[name] + i
        return self.sb["BCOLS"][:, c:c + 1]

    def slotv(self, e, idx, mx):
        return e.value_load(self.sb["slot_t"][0:1, idx:idx + 1], min_val=0, max_val=mx)


def setup(K):
    nc, P, io = K.nc, K.P, K.io
    consts = K.sbuf("consts", [128, len(CONSTS)], F32)
    for i, v in enumerate(CONSTS):
        nc.const_aps.register(F32, v, consts[:, i:i + 1])
    for i, v in enumerate(CONSTS):
        P.op("ve", lambda e, i=i, v=v: e.memset(consts[:, i:i + 1], v))
    ones1f = K.sbuf("ones1f", [1, 128], F32)
    onescb = K.sbuf("onescb", [128, 1], BF16)
    P.op("ve", lambda e: e.memset(ones1f[:], 1.0))
    P.op("ve", lambda e: e.memset(onescb[:], 1.0))
    jnk = K.sbuf("jnk", [128, 1], F32)

    def fill(n=6):
        for _ in range(n):
            P.op("ve", lambda e: e.memset(jnk[:], 0.0))
    K.fill = fill

    idf = K.sbuf("idf", [128, 128], F32)
    idb = K.sbuf("idb", [128, 128], BF16)
    bc = K.sbuf("BCOLS", [128, BTOT], F32)
    with ExitStack() as cset:
        iota_r = K.sbuf("iota_r", [1, 128], F32, cset)
        iota_c = K.sbuf("iota_c", [128, 1], F32, cset)
        e1 = P.op("gp", lambda e: e.iota(iota_r[:], [[1, 128]], channel_multiplier=0, allow_small_or_imprecise_dtypes=True))
        e2 = P.op("gp", lambda e: e.iota(iota_c[:], [[0, 1]], channel_multiplier=1, allow_small_or_imprecise_dtypes=True))
        psb = K.psum("ps_id", [128, 128], F32, cset)
        e3 = P.op("pe", lambda e: e.matmul(psb[:], ones1f[:], iota_r[:], start=True, stop=True),
                  deps=(e1, P.ev("ve")))
        idt = K.sbuf("idt", [128, 128], F32, cset)
        P.op("ve", lambda e: e.tensor_scalar(idf[:], psb[:], iota_c[:], None, AO.is_ge),
             deps=(e3, e2))
        P.op("ve", lambda e: e.tensor_scalar(idt[:], psb[:], iota_c[:], None, AO.is_le))
        K.fill()
        e4 = P.op("ve", lambda e: e.tensor_tensor(idf[:], idf[:], idt[:], AO.mult))
        K.fill()
        P.op("ve", lambda e: e.tensor_copy(idb[:], idf[:]))

        braw = K.sbuf("braw", [128, BTOT], F32, cset)
        ld = P.op("sy", lambda e: e.dma_start(
            out=braw[:].rearrange("p (b c) -> p b c", b=BTOT // 128),
            in_=io["btab"].rearrange("(b p) c -> p b c", p=128)))
        evc = None
        for b in range(BTOT // 128):
            et = P.op("pe", lambda e, b=b: e.transpose(psb[:], braw[:, b * 128:(b + 1) * 128], idf[:]),
                      deps=(ld, e4, evc))
            evc = P.op("sc", lambda e, b=b: e.copy(out=bc[:, b * 128:(b + 1) * 128], in_=psb[:]),
                      deps=(et,))
        P.barrier()

    slot_t = K.sbuf("slot_t", [1, 8], I32)
    non_t = K.sbuf("non_t", [N_ROUNDS, 16], I32)
    mprev_t = K.sbuf("mprev_t", [128, 1], F32)
    P.op("sy", lambda e: e.dma_start(out=slot_t[:], in_=io["slot"]))
    P.op("sy", lambda e: e.dma_start(out=non_t[:], in_=io["nonces"]))
    P.op("sy", lambda e: e.dma_start(out=mprev_t[:], in_=io["mprev"]))
    P.barrier()

    def f(e):
        K.r_myfc = e.value_load(slot_t[0:1, 1:2])
        K.r_mysm = e.value_load(slot_t[0:1, 2:3])
        K.r_peersm = e.value_load(slot_t[0:1, 3:4])
        return e.engine_nop()
    P.op("gp", f)

    wg2c = K.sbuf("WG2C", [128, 8], BF16)
    wb2c = K.sbuf("WB2C", [128, 8], BF16)
    rbb = K.sbuf("RBB", [128, 8], F32)
    bb2d = K.sbuf("bb2diff", [1, 2], F32)
    K._ev_wg2 = P.op("gd", lambda e: e.dma_start(
        out=wg2c[:].rearrange("p (i c) -> p i c", i=NCH),
        in_=io["Wg2"].rearrange("(i p) c -> p i c", p=128),
        allow_slow_non_contiguous=True))
    K._ev_wb2 = P.op("gd", lambda e: e.dma_start(
        out=wb2c[:].rearrange("p (c i) -> p c i", c=2),
        in_=io["Wb2"].rearrange("(i p) c -> p c i", p=128),
        allow_slow_non_contiguous=True))
    K._ev_rbb = P.op("sy", lambda e: e.dma_start(
        out=rbb[:], in_=io["rbrow"].broadcast_to((128, 8))))
    K._ev_bb2 = P.op("sy", lambda e: e.dma_start(out=bb2d[:], in_=io["bb2d"]))
    b2b = K.sbuf("B2B", [NE, D], BF16)
    K._ev_b2m = P.op("gd", lambda e: e.dma_start(out=b2b[:], in_=io["b2m"]))
    P.barrier()


def flag_and_poll(K, rnd, deps):
    nc, P, io = K.nc, K.P, K.io
    non_t = K.sb["non_t"]
    ef = P.op("gd", lambda e: e.dma_start(
        out=io["xfl"][bass.ds(K.r_myfc, 1), :], in_=non_t[rnd:rnd + 1, :]), deps=deps)

    def poll(e):
        e0 = e.value_load(non_t[rnd:rnd + 1, 0:1])
        e1 = e.value_load(non_t[rnd:rnd + 1, 1:2])
        with e.register(f"pc{rnd}") as cond, e.register(f"pv{rnd}") as v, \
             e.register(f"p2{rnd}") as c2:
            e.reg_mov(cond, 1)
            with e.While(cond):
                e.load(v, io["xfl"][0:1, 0:1])
                e.reg_alu(cond, v, e0, AO.not_equal)
                e.load(v, io["xfl"][0:1, 1:2])
                e.reg_alu(c2, v, e1, AO.not_equal)
                e.reg_alu(cond, cond, c2, AO.add)
                e.load(v, io["xfl"][1:2, 0:1])
                e.reg_alu(c2, v, e0, AO.not_equal)
                e.reg_alu(cond, cond, c2, AO.add)
                e.load(v, io["xfl"][1:2, 1:2])
                e.reg_alu(c2, v, e1, AO.not_equal)
                e.reg_alu(cond, cond, c2, AO.add)
        return e.engine_nop()
    return P.op("gp", poll, deps=(ef,))


def ln_stats(K, ctx, tag, src, n_tiles, tok, eps, out_dram_row=None, keep_xb=False):
    nc, P, io = K.nc, K.P, K.io
    rows = K.sbuf(f"rows_{tag}", [1, 3 * tok], F32, ctx)
    XB = K.sbuf(f"XB_{tag}", [128, n_tiles * tok], BF16, ctx)
    x2 = K.sbuf(f"x2_{tag}", [128, 2 * 512], BF16, ctx)
    ps = K.psum(f"ps_{tag}", [1, 2 * 512], F32, ctx)
    onescb = K.sb["onescb"]
    ev_cast = None
    for i in range(n_tiles):
        ap, ev_src = src(i)
        ev_cast = P.op("ve", lambda e, ap=ap, i=i: e.tensor_copy(
            XB[:, i * tok:(i + 1) * tok], ap), deps=(ev_src,))
    ev1 = {}
    ev2 = {}
    nsl = tok // 512
    for s in range(nsl):
        for i in range(n_tiles):
            ev1[(s, i)] = P.op("pe", lambda e, s=s, i=i: e.matmul(
                ps[0:1, 0:512], onescb[:], XB[:, i * tok + s * 512:i * tok + (s + 1) * 512],
                start=(i == 0), stop=(i == n_tiles - 1)),
                deps=(ev_cast, ev2.get(s - 1)))
        evr1 = P.op("ve", lambda e, s=s: e.tensor_scalar(
            rows[0:1, s * 512:(s + 1) * 512], ps[0:1, 0:512], 1.0 / D, None, AO.mult),
            deps=(ev1[(s, n_tiles - 1)],))
        ev1[s] = evr1
        for i in range(n_tiles):
            b2 = i % 2
            eq = P.op("gp", lambda e, s=s, i=i, b2=b2: e.tensor_tensor(
                x2[:, b2 * 512:(b2 + 1) * 512],
                XB[:, i * tok + s * 512:i * tok + (s + 1) * 512],
                XB[:, i * tok + s * 512:i * tok + (s + 1) * 512], AO.mult),
                deps=(ev_cast, ev2.get((s, i - 2)) if i >= 2
                      else ev2.get((s - 1, n_tiles - 2 + i))))
            ev2[(s, i)] = P.op("pe", lambda e, s=s, i=i, b2=b2: e.matmul(
                ps[0:1, 512:1024], onescb[:], x2[:, b2 * 512:(b2 + 1) * 512],
                start=(i == 0), stop=(i == n_tiles - 1)),
                deps=(eq, evr1 if i == 0 else None))
        ev2[s] = P.op("ve", lambda e, s=s: e.tensor_scalar(
            rows[0:1, tok + s * 512:tok + (s + 1) * 512], ps[0:1, 512:1024],
            1.0 / D, None, AO.mult), deps=(ev2[(s, n_tiles - 1)],))
    mu, var, rstd = rows[:, 0:tok], rows[:, tok:2 * tok], rows[:, 2 * tok:3 * tok]
    edone = ev2[nsl - 1]
    P.op("ve", lambda e: e.scalar_tensor_tensor(rstd, mu, -1.0, mu, AO.mult, AO.mult),
         deps=(edone,))
    e4 = P.op("ve", lambda e: e.tensor_tensor(var, var, rstd, AO.add))
    e5 = P.op("sc", lambda e: e.activation(rstd, var, AF.Sqrt, bias=eps, scale=1.0), deps=(e4,))
    e6 = P.op("ve", lambda e: e.reciprocal(rstd, rstd), deps=(e5,))
    ev = e6
    if out_dram_row is not None:
        r = out_dram_row
        P.op("sy", lambda e: e.dma_start(out=io["ROWSD"][r:r + 1, 0:tok], in_=mu), deps=(e6,))
        ev = P.op("sy", lambda e: e.dma_start(out=io["ROWSD"][r + 1:r + 2, 0:tok], in_=rstd),
                  deps=(e6,))
    if keep_xb:
        return mu, rstd, var, ev, XB
    return mu, rstd, var, ev


def bcast_rows(K, ctx, tag, rows_evs, tok, ps=None):
    nc, P = K.nc, K.P
    n = len(rows_evs)
    out = K.sbuf(f"bc_{tag}", [128, n * tok], F32, ctx)
    if ps is None:
        ps = K.psum(f"psbc_{tag}", [128, 1024], F32, ctx)
    ones1f = K.sb["ones1f"]
    ev = None
    for j, (row, evr) in enumerate(rows_evs):
        for s in range(0, tok, 512):
            w = min(512, tok - s)
            bank = (s // 512) % 2
            em = P.op("pe", lambda e, row=row, s=s, w=w, bank=bank: e.matmul(
                ps[:, bank * 512:bank * 512 + w], ones1f[:], row[:, s:s + w],
                start=True, stop=True), deps=(evr, ev))
            ev = P.op("sc", lambda e, j=j, s=s, w=w, bank=bank: e.copy(
                out=out[:, j * tok + s:j * tok + s + w],
                in_=ps[:, bank * 512:bank * 512 + w]), deps=(em,))
    return [out[:, j * tok:(j + 1) * tok] for j in range(n)], ev


def bcast_dram_rows(K, ctx, tag, row_idx, tok, tok0, deps=(), ps=None):
    nc, P, io = K.nc, K.P, K.io
    n = len(row_idx)
    out = K.sbuf(f"bc_{tag}", [128, n * tok], F32, ctx)
    scr = K.sbuf(f"bcs_{tag}", [1, 2 * 512], F32, ctx)
    if ps is None:
        ps = K.psum(f"psbc_{tag}", [128, 1024], F32, ctx)
    ones1f = K.sb["ones1f"]
    ev = None
    evl_prev = {}
    for j, r in enumerate(row_idx):
        for s_ in range(0, tok, 512):
            w = min(512, tok - s_)
            bank = (s_ // 512) % 2
            evl = P.op("sy", lambda e, r=r, s_=s_, w=w, bank=bank: e.dma_start(
                out=scr[0:1, bank * 512:bank * 512 + w],
                in_=io["ROWSD"][r:r + 1, tok0 + s_:tok0 + s_ + w]),
                deps=deps + (evl_prev.get(bank),))
            em = P.op("pe", lambda e, s_=s_, w=w, bank=bank: e.matmul(
                ps[:, bank * 512:bank * 512 + w], ones1f[:],
                scr[0:1, bank * 512:bank * 512 + w], start=True, stop=True),
                deps=(evl, ev))
            ev = P.op("sc", lambda e, j=j, s_=s_, w=w, bank=bank: e.copy(
                out=out[:, j * tok + s_:j * tok + s_ + w],
                in_=ps[:, bank * 512:bank * 512 + w]), deps=(em,))
            evl_prev[bank] = ev
    return [out[:, j * tok:(j + 1) * tok] for j in range(n)], ev


def stream_w(K, wbuf, src_fn, n_in_tiles, deps):
    P = K.P

    def f(e):
        src = src_fn()
        return e.dma_start(
            out=wbuf.rearrange("p (i c) -> p i c", i=n_in_tiles),
            in_=src.rearrange("(i p) c -> p i c", p=128))
    return P.op("gd", f, deps=deps)


def phase_attn_a(K):
    nc, P, io = K.nc, K.P, K.io
    with ExitStack() as ctx:
        with ExitStack() as c0:
            xin = K.sbuf("A_xin", [128, T], F32, c0)
            ld = {}

            def src(i):
                ev = P.op("sy", lambda e, i=i: e.dma_start(
                    out=xin[:], in_=io["xT"][i * 128:(i + 1) * 128, :]), deps=(ld.get("p"),))
                ld["p"] = ev
                return xin[:], ev
            mu1, r1, var1, ev_st = ln_stats(K, c0, "A1", src, NCH, T, 1e-5, out_dram_row=0)
            rr = K.sbuf("A_rr", [1, 2 * T], F32, c0)
            R_row, A_row = rr[:, 0:T], rr[:, T:2 * T]
            P.op("ve", lambda e: e.tensor_tensor(R_row, r1, r1, AO.mult), deps=(ev_st,))
            e2 = P.op("ve", lambda e: e.tensor_tensor(R_row, R_row, var1, AO.mult))
            e3 = P.op("sc", lambda e: e.activation(R_row, R_row, AF.Sqrt, bias=1e-5, scale=1.0),
                      deps=(e2,))
            P.op("ve", lambda e: e.reciprocal(R_row, R_row), deps=(e3,))
            e5 = P.op("ve", lambda e: e.tensor_tensor(R_row, R_row, r1, AO.mult))
            e6 = P.op("ve", lambda e: e.tensor_tensor(A_row, mu1, R_row, AO.mult))
            P.op("sy", lambda e: e.dma_start(out=io["ROWSD"][6:7, :], in_=R_row), deps=(e5,))
            P.op("sy", lambda e: e.dma_start(out=io["ROWSD"][7:8, :], in_=A_row), deps=(e6,))
            P.barrier()

        cqa = ctx.enter_context(ExitStack())
        Q = K.sbuf("A_Q", [128, NCH * T], BF16, cqa)
        with ExitStack() as c2a:
            ZN = K.sbuf("A_ZN", [128, NCH * T], BF16, c2a)
            with ExitStack() as cz:
                (Rb, Ab), ev_bc = bcast_dram_rows(K, cz, "A", [6, 7], T, 0)
                xin2 = K.sbuf("A_xin2", [128, T], F32, cz)
                evp = None
                for i in range(NCH):
                    el = P.op("sy", lambda e, i=i: e.dma_start(
                        out=xin2[:], in_=io["xT"][i * 128:(i + 1) * 128, :]),
                        deps=(ev_bc, evp))
                    P.op("ve", lambda e: e.tensor_tensor(xin2[:], xin2[:], Rb, AO.mult),
                         deps=(el, ev_bc))
                    P.op("ve", lambda e: e.tensor_tensor(xin2[:], xin2[:], Ab, AO.subtract))
                    evp = P.op("ve", lambda e, i=i: e.tensor_scalar(
                        ZN[:, i * T:(i + 1) * T], xin2[:], K.bcol("aln_w", i),
                        K.bcol("aln_b", i), AO.mult, AO.add))
                ev_zn = evp
                P.barrier()

            with ExitStack() as cq:
                kvst = K.sbuf("A_kvst", [128, 2 * 512], BF16, cq)
                wq = K.sbuf("A_wq", [128, 2 * NCH * 128], BF16, cq)
                psq = K.psum("A_psq", [128, 2 * 512], F32, cq)
                ev_ev, kvw = {}, []
                for o in range(24):
                    buf = o % 2
                    ew = stream_w(K, wq[:, buf * NCH * 128:(buf + 1) * NCH * 128],
                                  lambda o=o: io["Wqkv"][:, o * 128:(o + 1) * 128], NCH,
                                  deps=(ev_ev.get((o - 2, NS - 1)),))
                    for s in range(NS):
                        bank = s % 2

                        def mmf(e, o=o, s=s, buf=buf, bank=bank):
                            r = None
                            for i in range(NCH):
                                r = e.matmul(
                                    psq[:, bank * 512:(bank + 1) * 512],
                                    wq[:, buf * NCH * 128 + i * 128:buf * NCH * 128 + (i + 1) * 128],
                                    ZN[:, i * T + s * 512:i * T + s * 512 + 512],
                                    start=(i == 0), stop=(i == NCH - 1))
                            return r
                        em = P.op("pe", mmf, deps=(ew, ev_zn,
                                                   ev_ev.get((o, s - 2)) if s >= 2
                                                   else ev_ev.get((o - 1, NS - 2 + s))))
                        if o < 8:
                            ee = P.op("sc", lambda e, o=o, s=s, bank=bank: e.activation(
                                Q[:, o * T + s * 512:o * T + s * 512 + 512],
                                psq[:, bank * 512:(bank + 1) * 512],
                                AF.Identity, bias=K.bcol("bqkv", o), scale=1.0), deps=(em,))
                        else:
                            kb = s % 2
                            dep_st = kvw[-2] if len(kvw) >= 2 else None
                            ee = P.op("sc", lambda e, o=o, s=s, bank=bank, kb=kb: e.activation(
                                kvst[:, kb * 512:(kb + 1) * 512],
                                psq[:, bank * 512:(bank + 1) * 512],
                                AF.Identity, bias=K.bcol("bqkv", o), scale=1.0),
                                deps=(em, dep_st))
                            ekv = P.op("gd", lambda e, o=o, s=s, kb=kb: e.dma_start(
                                out=io["KVSTG"][(o - 8) * 128:(o - 7) * 128,
                                                s * 512:(s + 1) * 512],
                                in_=kvst[:, kb * 512:(kb + 1) * 512]), deps=(ee,))
                            kvw.append(ekv)
                        ev_ev[(o, s)] = ee
                epub = P.op("gd", lambda e: e.dma_start(
                    out=io["xkv"].rearrange("(z r) t -> z r t", z=2)[
                        bass.ds(K.r_myfc, 1), :, :],
                    in_=io["KVSTG"][:, :]), deps=tuple(kvw[-2:]))
                ev_ready = flag_and_poll(K, 0, (epub,))
                P.barrier()

        with ExitStack() as ca:
            KG = K.sbuf("A_KG", [128, L], BF16, ca)
            VG = K.sbuf("A_VG", [128, L], BF16, ca)
            VT = K.sbuf("A_VT", [128, 64 * 130], BF16, ca)
            PT = K.sbuf("A_PT", [128, 2 * 1024], BF16, ca)
            mst = K.sbuf("A_mst", [64, 1024], BF16, ca)
            dent = K.sbuf("A_dent", [1, 1024], F32, ca)
            psS = K.psum("A_psS", [128, 2 * 1024], F32, ca)
            psM = K.psum("A_psM", [65, 1024], F32, ca)
            pstr = K.psum("A_pstr", [128, 2 * 1024], BF16, ca)
            idb = K.sb["idb"]

            ev_pair = None
            for p in range(NPAIR):
                ekg1 = P.op("sy", lambda e, p=p: e.dma_start(
                    out=KG[:, 0:T], in_=io["xkv"][p * 128:(p + 1) * 128, :]),
                    deps=(ev_ready, ev_pair))
                P.op("sy", lambda e, p=p: e.dma_start(
                    out=KG[:, T:L], in_=io["xkv"][2048 + p * 128:2048 + (p + 1) * 128, :]))
                P.op("sy", lambda e, p=p: e.dma_start(
                    out=VG[:, 0:T], in_=io["xkv"][1024 + p * 128:1024 + (p + 1) * 128, :]))
                evg2 = P.op("sy", lambda e, p=p: e.dma_start(
                    out=VG[:, T:L],
                    in_=io["xkv"][3072 + p * 128:3072 + (p + 1) * 128, :]))
                evm = P.op("ve", lambda e: e.memset(VT[:], 1.0), deps=(ev_pair,))
                ev_tc = {}
                for k in range(64):
                    tb_ = (k % 2) * 1024
                    et = P.op("pe", lambda e, k=k, tb_=tb_: e.transpose(
                        pstr[:, tb_:tb_ + 128], VG[:, k * 128:(k + 1) * 128], idb[:]),
                        deps=(evg2, evm, ev_tc.get(k - 2), ev_pair))
                    P.op("ve", lambda e, k=k, tb_=tb_: e.tensor_copy(
                        VT[:, k * 130:k * 130 + 64], pstr[:, tb_:tb_ + 64]), deps=(et,))
                    ev_tc[k] = P.op("ve", lambda e, k=k, tb_=tb_: e.tensor_copy(
                        VT[:, k * 130 + 65:k * 130 + 129], pstr[:, tb_ + 64:tb_ + 128]))
                ev_vt = ev_tc[63]

                ev_q = None
                ev_evac1 = None
                ev_evac2 = None
                for h in range(2):
                    for qc in range(4):
                        q0 = qc * 1024
                        ev_exp, ev_pv = {}, {}

                        def mk_pvm(h, k, sb_):
                            def pvm(e, h=h, k=k, sb_=sb_):
                                r = None
                                for qs in range(2):
                                    r = e.matmul(
                                        psM[:, qs * 512:(qs + 1) * 512],
                                        VT[:, k * 130 + h * 65:k * 130 + h * 65 + 65],
                                        PT[:, sb_ * 1024 + qs * 512:sb_ * 1024 + (qs + 1) * 512],
                                        start=(k == 0), stop=(k == 63))
                                return r
                            return pvm
                        for k in range(64):
                            sb_ = k % 2

                            def smm(e, p=p, h=h, q0=q0, k=k, sb_=sb_):
                                r = None
                                for qs in range(2):
                                    r = e.matmul(
                                        psS[:, sb_ * 1024 + qs * 512:sb_ * 1024 + (qs + 1) * 512],
                                        KG[h * 64:(h + 1) * 64, k * 128:(k + 1) * 128],
                                        Q[h * 64:(h + 1) * 64,
                                          p * T + q0 + qs * 512:p * T + q0 + (qs + 1) * 512],
                                        start=True, stop=True)
                                return r
                            es = P.op("pe", smm, deps=(ekg1, evg2, ev_vt, ev_exp.get(k - 2)))
                            ee = P.op("sc", lambda e, k=k, sb_=sb_: e.activation(
                                PT[:, sb_ * 1024:(sb_ + 1) * 1024],
                                psS[:, sb_ * 1024:(sb_ + 1) * 1024],
                                AF.Exp, scale=0.125), deps=(es, ev_pv.get(k - 2)))
                            ev_exp[k] = ee
                            if k >= 1:
                                ev_pv[k - 1] = P.op(
                                    "pe", mk_pvm(h, k - 1, (k - 1) % 2),
                                    deps=(ev_exp[k - 1],)
                                    + ((ev_evac1, ev_evac2) if k - 1 == 0 else ()))
                        ev_pv[63] = P.op("pe", mk_pvm(h, 63, 63 % 2), deps=(ev_exp[63],))
                        ed = P.op("ve", lambda e: e.reciprocal(dent[:], psM[64:65, :]),
                                  deps=(ev_pv[63], ev_q))
                        em = P.op("sc", lambda e: e.copy(out=mst[:], in_=psM[0:64, :]),
                                  deps=(ev_pv[63], ev_q))
                        P.op("sy", lambda e, p=p, h=h, qc=qc: e.dma_start(
                            out=io["DEN"][2 * p + h:2 * p + h + 1, qc * 1024:(qc + 1) * 1024],
                            in_=dent[:]), deps=(ed,))
                        ems = P.op("sy", lambda e, p=p, h=h, qc=qc: e.dma_start(
                            out=io["MSAR"][(2 * p + h) * 64:(2 * p + h + 1) * 64,
                                           qc * 1024:(qc + 1) * 1024],
                            in_=mst[:]), deps=(em,))
                        ev_q = ems
                        ev_evac1, ev_evac2 = ed, em
                ev_pair = ev_q
            ev_attn = ev_pair
            P.barrier()
        cqa.close()

        with ExitStack() as c3:
            MSA = K.sbuf("A_MSA", [128, NCH * T], BF16, c3)
            dro = K.sbuf("A_dro", [1, 2 * 2048], F32, c3)
            mraw = K.sbuf("A_mraw", [128, 2048], BF16, c3)
            mA = K.sbuf("A_mA", [1, 128], F32, c3)
            mB = K.sbuf("A_mB", [1, 128], F32, c3)
            psD = K.psum("A_psD", [128, 2048], F32, c3)
            P.op("ve", lambda e: e.memset(mA[:], 0.0), deps=(ev_attn,))
            P.op("ve", lambda e: e.memset(mA[:, 0:64], 1.0))
            P.op("ve", lambda e: e.memset(mB[:], 0.0))
            em4 = P.op("ve", lambda e: e.memset(mB[:, 64:128], 1.0))
            evn = None
            for p in range(NPAIR):
                for qc in range(2):
                    eda = P.op("sy", lambda e, p=p, qc=qc: e.dma_start(
                        out=dro[:, 0:2048],
                        in_=io["DEN"][2 * p:2 * p + 1, qc * 2048:(qc + 1) * 2048]),
                        deps=(evn, ev_attn))
                    edb = P.op("sy", lambda e, p=p, qc=qc: e.dma_start(
                        out=dro[:, 2048:4096],
                        in_=io["DEN"][2 * p + 1:2 * p + 2, qc * 2048:(qc + 1) * 2048]))
                    erm = P.op("sy", lambda e, p=p, qc=qc: e.dma_start(
                        out=mraw[:],
                        in_=io["MSAR"][2 * p * 64:(2 * p + 2) * 64,
                                       qc * 2048:(qc + 1) * 2048]), deps=(edb, evn))
                    eb = None
                    for half in range(4):
                        h0 = half * 512
                        P.op("pe", lambda e, h0=h0: e.matmul(
                            psD[:, h0:h0 + 512], mA[:], dro[:, h0:h0 + 512],
                            start=True, stop=False), deps=(eda, edb, em4, evn))
                        eb = P.op("pe", lambda e, h0=h0: e.matmul(
                            psD[:, h0:h0 + 512], mB[:], dro[:, 2048 + h0:2048 + h0 + 512],
                            start=False, stop=True))
                    evn = P.op("ve", lambda e, p=p, qc=qc: e.tensor_tensor(
                        MSA[:, p * T + qc * 2048:p * T + (qc + 1) * 2048], mraw[:], psD[:],
                        AO.mult), deps=(erm, eb))
            ev_msa = evn

            (m1b, r1b), ev_bc2 = bcast_dram_rows(K, c3, "A2", [0, 1], T, 0)
            wpj = K.sbuf("A_wpj", [128, 2 * NCH * 128], BF16, c3)
            xr1 = K.sbuf("A_xr1", [128, T], F32, c3)
            xr2 = K.sbuf("A_xr2", [128, T], F32, c3)
            pso = K.psum("A_pso", [128, 2 * 512], F32, c3)
            ev_o = {}
            for o in range(NCH):
                buf = o % 2
                ewp = stream_w(K, wpj[:, buf * NCH * 128:(buf + 1) * NCH * 128],
                               lambda o=o: io["Wproj"][:, o * 128:(o + 1) * 128], NCH,
                               deps=(ev_o.get(o - 2),))
                exl = P.op("sy", lambda e, o=o: e.dma_start(
                    out=xr1[:], in_=io["xT"][o * 128:(o + 1) * 128, :]),
                    deps=(ev_o.get(o - 1),))
                P.op("ve", lambda e: e.tensor_tensor(xr2[:], xr1[:], m1b, AO.subtract),
                     deps=(exl, ev_bc2))
                P.op("ve", lambda e: e.tensor_tensor(xr2[:], xr2[:], r1b, AO.mult))
                ez3 = P.op("ve", lambda e: e.tensor_tensor(xr2[:], xr2[:], xr1[:], AO.add))
                evo = None
                for s in range(NS):
                    bank = s % 2

                    def pmm(e, o=o, s=s, buf=buf, bank=bank):
                        r = None
                        for i in range(NCH):
                            r = e.matmul(
                                pso[:, bank * 512:(bank + 1) * 512],
                                wpj[:, buf * NCH * 128 + i * 128:buf * NCH * 128 + (i + 1) * 128],
                                MSA[:, i * T + s * 512:i * T + s * 512 + 512],
                                start=(i == 0), stop=(i == NCH - 1))
                        return r
                    emm = P.op("pe", pmm, deps=(ewp, ev_msa,
                                                ev_o.get((o, s - 2)) if s >= 2
                                                else ev_o.get((o - 1, NS - 2 + s))))
                    evo = P.op("ve", lambda e, o=o, s=s, bank=bank: e.scalar_tensor_tensor(
                        xr1[:, s * 512:(s + 1) * 512], pso[:, bank * 512:(bank + 1) * 512],
                        K.bcol("bproj", o), xr2[:, s * 512:(s + 1) * 512], AO.add, AO.add),
                        deps=(emm, ez3))
                    ev_o[(o, s)] = evo
                est = P.op("sy", lambda e, o=o: e.dma_start(
                    out=io["XRES"][o * 128:(o + 1) * 128, :], in_=xr1[:]), deps=(evo,))
                ev_o[o] = est
            P.barrier()


def phase_ffn(K, call):
    nc, P, io = K.nc, K.P, K.io
    last = (call == 2)
    with ExitStack() as ctx:
        with ExitStack() as c0:
            xin = K.sbuf(f"F{call}_xin", [128, T], F32, c0)
            ld = {}

            def src(i):
                ev = P.op("sy", lambda e, i=i: e.dma_start(
                    out=xin[:], in_=io["XRES"][i * 128:(i + 1) * 128, :]), deps=(ld.get("p"),))
                ld["p"] = ev
                return xin[:], ev
            mu, rstd, _v, ev_st = ln_stats(K, c0, f"F{call}", src, NCH, T, 1e-5,
                                           out_dram_row=2)
            P.barrier()

        with ExitStack() as c1:
            _ffn_routing(K, c1, call)
            P.barrier()

        for th in range(2):
            t0 = th * 2048
            with ExitStack() as c2:
                cps = c2.enter_context(ExitStack())
                ps1 = K.psum(f"F{call}t{th}_p1", [128, 2 * 1024], F32, cps)
                ps2 = K.psum(f"F{call}t{th}_p2", [128, 2 * 1024], F32, cps)
                (mub, rb), ev_bc = bcast_dram_rows(
                    K, c2, f"F{call}t{th}", [2, 3], 2048, t0, ps=ps1[:, 0:1024])
                FLAT = K.sbuf(f"F{call}t{th}_FLAT", [128, NCH * 2048], BF16, c2)
                ft = K.sbuf(f"F{call}t{th}_ft", [128, 2048], F32, c2)
                cwtb = K.sbuf(f"F{call}t{th}_cwt", [NE, 2048], BF16, c2)
                ecwt = P.op("sy", lambda e, cwtb=cwtb, t0=t0: e.dma_start(
                    out=cwtb[:], in_=io["CWD"][:, t0:t0 + 2048]),
                    deps=(K._ev_cwd[call],))
                evp = None
                for i in range(NCH):
                    el = P.op("sy", lambda e, i=i, ft=ft, t0=t0: e.dma_start(
                        out=ft[:], in_=io["XRES"][i * 128:(i + 1) * 128, t0:t0 + 2048]),
                        deps=(ev_bc, evp))
                    P.op("ve", lambda e, ft=ft, mub=mub: e.tensor_tensor(
                        ft[:], ft[:], mub, AO.subtract), deps=(el, ev_bc))
                    evp = P.op("ve", lambda e, i=i, FLAT=FLAT, ft=ft, rb=rb: e.tensor_tensor(
                        FLAT[:, i * 2048:(i + 1) * 2048], ft[:], rb, AO.mult))
                ev_flat = evp
                OUT = K.sbuf(f"F{call}t{th}_OUT", [128, NCH * 2048], F32, c2)
                emz = None
                for o in range(NCH):
                    emz = P.op("ve", lambda e, o=o, OUT=OUT: e.memset(
                        OUT[:, o * 2048:(o + 1) * 2048], 0.0))

                with ExitStack() as c3:
                    W1C = K.sbuf(f"F{call}t{th}_W1", [128, 2 * NCH * 1024], BF16, c3)
                    W2C = K.sbuf(f"F{call}t{th}_W2", [128, 8 * 1024], BF16, c3)
                    CWB = K.sbuf(f"F{call}t{th}_CWB", [128, 2048], BF16, c3)
                    h1c = K.sbuf(f"F{call}t{th}_h1", [128, 8 * 1024], BF16, c3)
                    stg = K.sbuf(f"F{call}t{th}_stg", [128, 2 * 1024], BF16, c3)
                    ev_wl = {}
                    last_mm2 = None
                    last_cw_reader = None
                    for e_i in range(NE):
                        ecb = P.op("sy", lambda e, e_i=e_i, CWB=CWB, t0=t0: e.dma_start(
                            out=CWB[:],
                            in_=io["CWD"][e_i:e_i + 1, t0:t0 + 2048].broadcast_to((128, 2048))),
                            deps=(K._ev_cwd[call], last_cw_reader))
                        for c in range(4):
                            jj = e_i * 4 + c
                            widx = jj % 2
                            dep_w = ev_wl.get(jj - 2)
                            if K._fw_cached:
                                ew1 = P.op("sy", lambda e, jj=jj, widx=widx, W1C=W1C:
                                           e.dma_start(
                                    out=W1C[:, widx * NCH * 1024:(widx + 1) * NCH * 1024],
                                    in_=io["W1BD"][jj * 128:(jj + 1) * 128, :]),
                                    deps=(dep_w,))
                                ew2 = P.op("sy", lambda e, jj=jj, W2C=W2C: e.dma_start(
                                    out=W2C[:], in_=io["W2BD"][jj * 128:(jj + 1) * 128, :]),
                                    deps=(ev_wl.get(jj - 1),))
                            else:
                                ew1 = stream_w(
                                    K, W1C[:, widx * NCH * 1024:(widx + 1) * NCH * 1024],
                                    lambda e_i=e_i, c=c:
                                        io["w1T"][e_i, :, c * 1024:(c + 1) * 1024],
                                    NCH, deps=(dep_w, K._fw_st1.get(jj - 2)))
                                ew2 = P.op("gd", lambda e, e_i=e_i, c=c, W2C=W2C: e.dma_start(
                                    out=W2C[:].rearrange("p (i o) -> p i o", i=8),
                                    in_=io["w2T"][e_i, c * 1024:(c + 1) * 1024, :].rearrange(
                                        "(i p) o -> p i o", p=128)),
                                    deps=(ev_wl.get(jj - 1), K._fw_st2.get(jj - 1)))
                                K._fw_st1[jj] = P.op(
                                    "sy", lambda e, jj=jj, widx=widx, W1C=W1C: e.dma_start(
                                        out=io["W1BD"][jj * 128:(jj + 1) * 128, :],
                                        in_=W1C[:, widx * NCH * 1024:
                                                (widx + 1) * NCH * 1024]), deps=(ew1,))
                                K._fw_st2[jj] = P.op(
                                    "sy", lambda e, jj=jj, W2C=W2C: e.dma_start(
                                        out=io["W2BD"][jj * 128:(jj + 1) * 128, :],
                                        in_=W2C[:]), deps=(ew2,))
                            for ts in range(2):
                                s0 = ts * 1024
                                prev_mm2_of_ts = last_mm2
                                ev_h = {}
                                for f in range(8):
                                    pb = f % 2

                                    def m1(e, e_i=e_i, c=c, f=f, widx=widx, s0=s0, pb=pb,
                                           ps1=ps1, W1C=W1C, FLAT=FLAT):
                                        r = None
                                        for half in range(2):
                                            for i in range(NCH):
                                                r = e.matmul(
                                                    ps1[:, pb * 1024 + half * 512:
                                                        pb * 1024 + (half + 1) * 512],
                                                    W1C[:, widx * NCH * 1024 + i * 1024 + f * 128:
                                                        widx * NCH * 1024 + i * 1024 + (f + 1) * 128],
                                                    FLAT[:, i * 2048 + s0 + half * 512:
                                                         i * 2048 + s0 + (half + 1) * 512],
                                                    start=(i == 0), stop=(i == NCH - 1))
                                        return r
                                    em1 = P.op("pe", m1, deps=(ew1, ev_flat,
                                                               ev_h.get(f - 2),
                                                               prev_mm2_of_ts if f < 2 else None))
                                    eg = P.op("sc", lambda e, f=f, pb=pb, e_i=e_i, c=c,
                                              stg=stg, ps1=ps1: e.activation(
                                        stg[:, pb * 1024:(pb + 1) * 1024],
                                        ps1[:, pb * 1024:(pb + 1) * 1024],
                                        AF.Gelu, bias=K.bcol("fb1", e_i * 32 + c * 8 + f),
                                        scale=1.0), deps=(em1, ev_h.get(f - 1)))
                                    eh = P.op("ve", lambda e, f=f, pb=pb, s0=s0, e_i=e_i,
                                              h1c=h1c, stg=stg, CWB=CWB: e.tensor_tensor(
                                        h1c[:, f * 1024:(f + 1) * 1024],
                                        stg[:, pb * 1024:(pb + 1) * 1024],
                                        CWB[:, s0:s0 + 1024], AO.mult),
                                        deps=(eg, ecb, prev_mm2_of_ts if f == 0 else None))
                                    ev_h[f] = eh
                                edbg = None
                                if (os.environ.get("KDBG2") and call == 1 and th == 0
                                        and e_i == 0 and c == 0 and ts == 0):
                                    P.op("sy", lambda e, h1c=h1c: e.dma_start(
                                        out=io["KVSTG"][1024:1152, :], in_=h1c[:, 0:4096]),
                                        deps=(ev_h[7],))
                                    edbg = P.op("sy", lambda e, h1c=h1c: e.dma_start(
                                        out=io["KVSTG"][1152:1280, :], in_=h1c[:, 4096:8192]),
                                        deps=(ev_h[7],))
                                ev_m2 = {}
                                for o in range(NCH):
                                    ob = o % 2

                                    addb2 = (e_i == NE - 1 and c == 3)

                                    def m2(e, o=o, ob=ob, ps2=ps2, W2C=W2C, h1c=h1c,
                                           addb2=addb2, s0=s0, cwtb=cwtb):
                                        r = None
                                        for half in range(2):
                                            for f in range(8):
                                                r = e.matmul(
                                                    ps2[:, ob * 1024 + half * 512:
                                                        ob * 1024 + (half + 1) * 512],
                                                    W2C[:, f * 1024 + o * 128:
                                                        f * 1024 + (o + 1) * 128],
                                                    h1c[:, f * 1024 + half * 512:
                                                        f * 1024 + (half + 1) * 512],
                                                    start=(f == 0),
                                                    stop=(f == 7 and not addb2))
                                            if addb2:
                                                r = e.matmul(
                                                    ps2[:, ob * 1024 + half * 512:
                                                        ob * 1024 + (half + 1) * 512],
                                                    K.sb["B2B"][:, o * 128:(o + 1) * 128],
                                                    cwtb[:, s0 + half * 512:
                                                         s0 + (half + 1) * 512],
                                                    start=False, stop=True)
                                        return r
                                    em2 = P.op("pe", m2, deps=(ew2, ev_h[7], ev_m2.get(o - 2),
                                                               ecwt, K._ev_b2m, edbg))
                                    ea = P.op("ve", lambda e, o=o, ob=ob, s0=s0,
                                              OUT=OUT, ps2=ps2: e.tensor_tensor(
                                        OUT[:, o * 2048 + s0:o * 2048 + s0 + 1024],
                                        OUT[:, o * 2048 + s0:o * 2048 + s0 + 1024],
                                        ps2[:, ob * 1024:(ob + 1) * 1024], AO.add),
                                        deps=(em2, emz))
                                    ev_m2[o] = ea
                                last_mm2 = ev_m2[NCH - 1]
                            ev_wl[e_i * 4 + c] = last_mm2
                        last_cw_reader = last_mm2
                    ev_experts = last_mm2
                    P.barrier()
                cps.close()

                evp = None
                for i in range(NCH):
                    el = P.op("sy", lambda e, i=i, ft=ft, t0=t0: e.dma_start(
                        out=ft[:], in_=io["XRES"][i * 128:(i + 1) * 128, t0:t0 + 2048]),
                        deps=(ev_experts, evp))
                    P.op("ve", lambda e, ft=ft, mub=mub: e.tensor_tensor(
                        ft[:], ft[:], mub, AO.subtract), deps=(el,))
                    P.op("ve", lambda e, ft=ft, rb=rb: e.tensor_tensor(
                        ft[:], ft[:], rb, AO.mult))
                    evp = P.op("ve", lambda e, i=i, OUT=OUT, ft=ft: e.tensor_tensor(
                        OUT[:, i * 2048:(i + 1) * 2048], OUT[:, i * 2048:(i + 1) * 2048],
                        ft[:], AO.add))
                ev_sum = P.ev("ve")
                edmp = None
                if os.environ.get("KDBG") and call == 1:
                    for i in range(NCH):
                        P.op("sy", lambda e, i=i, OUT=OUT, t0=t0: e.dma_start(
                            out=io["MEMS"][i * 128:(i + 1) * 128, t0:t0 + 2048],
                            in_=OUT[:, i * 2048:(i + 1) * 2048]), deps=(ev_sum,))
                    for i in range(NCH):
                        edmp = P.op("sy", lambda e, i=i, FLAT=FLAT, t0=t0: e.dma_start(
                            out=io["KVSTG"][i * 128:(i + 1) * 128, t0:t0 + 2048],
                            in_=FLAT[:, i * 2048:(i + 1) * 2048]), deps=(ev_sum,))

                with ExitStack() as c4:
                    def src2(i):
                        return OUT[:, i * 2048:(i + 1) * 2048], ev_sum
                    mu2, rstd2, _v2, ev_st2 = ln_stats(K, c4, f"F{call}e{th}", src2, NCH,
                                                       2048, 1e-12)
                    (mu2b, r2b), ev_bc2 = bcast_rows(
                        K, c4, f"F{call}e{th}b", [(mu2, ev_st2), (rstd2, ev_st2)], 2048)
                    evw = None
                    for i in range(NCH):
                        el = P.op("sy", lambda e, i=i, ft=ft, t0=t0: e.dma_start(
                            out=ft[:], in_=io["XRES"][i * 128:(i + 1) * 128, t0:t0 + 2048]),
                            deps=(evw, ev_bc2))
                        P.op("ve", lambda e, i=i, OUT=OUT, mu2b=mu2b: e.tensor_tensor(
                            OUT[:, i * 2048:(i + 1) * 2048], OUT[:, i * 2048:(i + 1) * 2048],
                            mu2b, AO.subtract), deps=(ev_bc2, edmp))
                        P.op("ve", lambda e, i=i, OUT=OUT, r2b=r2b: e.tensor_tensor(
                            OUT[:, i * 2048:(i + 1) * 2048], OUT[:, i * 2048:(i + 1) * 2048],
                            r2b, AO.mult))
                        P.op("ve", lambda e, i=i, OUT=OUT: e.tensor_scalar(
                            OUT[:, i * 2048:(i + 1) * 2048], OUT[:, i * 2048:(i + 1) * 2048],
                            K.bcol("fln_w", i), K.bcol("fln_b", i), AO.mult, AO.add))
                        ex = P.op("ve", lambda e, i=i, OUT=OUT, ft=ft: e.tensor_tensor(
                            OUT[:, i * 2048:(i + 1) * 2048], OUT[:, i * 2048:(i + 1) * 2048],
                            ft[:], AO.add), deps=(el,))
                        dst = io["out"] if last else io["XRES"]
                        evw = P.op("sy", lambda e, i=i, dst=dst, OUT=OUT, t0=t0: e.dma_start(
                            out=dst[i * 128:(i + 1) * 128, t0:t0 + 2048],
                            in_=OUT[:, i * 2048:(i + 1) * 2048]), deps=(ex,))
                    P.barrier()
                K._fw_cached = True


def _ffn_routing(K, ctx, call):
    nc, P, io = K.nc, K.P, K.io
    FLR = K.sbuf(f"R{call}_FLR", [128, NCH * T], BF16, ctx)
    rrow = K.sbuf(f"R{call}_rr", [1, 2 * T], F32, ctx)
    er1 = P.op("sy", lambda e: e.dma_start(out=rrow[:, 0:T], in_=io["ROWSD"][2:3, :]))
    er2 = P.op("sy", lambda e: e.dma_start(out=rrow[:, T:2 * T], in_=io["ROWSD"][3:4, :]))
    with ExitStack() as cb:
        (mub, rb), ev_bc = bcast_rows(K, cb, f"R{call}",
                                      [(rrow[:, 0:T], er1), (rrow[:, T:2 * T], er2)], T)
        ft = K.sbuf(f"R{call}_ft", [128, T], F32, cb)
        evp = None
        for i in range(NCH):
            el = P.op("sy", lambda e, i=i: e.dma_start(
                out=ft[:], in_=io["XRES"][i * 128:(i + 1) * 128, :]), deps=(ev_bc, evp))
            P.op("ve", lambda e: e.tensor_tensor(ft[:], ft[:], mub, AO.subtract),
                 deps=(el, ev_bc))
            evp = P.op("ve", lambda e, i=i: e.tensor_tensor(
                FLR[:, i * T:(i + 1) * T], ft[:], rb, AO.mult))
        ev_flr = evp
        P.barrier()

    rwt = K.sbuf(f"R{call}_rwt", [128, NCH * 8], BF16, ctx)
    erw = stream_w(K, rwt[:], lambda: io["rwT"], NCH, deps=())
    psl = K.psum(f"R{call}_psl", [128, 2 * 512], F32, ctx)
    psr = K.psum(f"R{call}_psr", [8, 128], F32, ctx)
    cwtile = K.sbuf(f"R{call}_cw", [128, 8], F32, ctx)
    cwb16 = K.sbuf(f"R{call}_cwb", [128, 8], BF16, ctx)
    scr = K.sbuf(f"R{call}_scr", [128, 8 * 6], F32, ctx)
    col = K.sbuf(f"R{call}_col", [128, 4], F32, ctx)
    ones8 = K.sbuf(f"R{call}_o8", [128, 8], F32, ctx)
    cws = K.sbuf(f"R{call}_cws", [8, T], F32, ctx)
    P.op("ve", lambda e: e.memset(ones8[:], 1.0))
    idb = K.sb["idb"]
    lg = scr[:, 0:8]
    c1 = scr[:, 8:16]
    s1 = scr[:, 16:24]
    m1 = scr[:, 24:32]
    l2 = scr[:, 32:40]
    m2 = scr[:, 40:48]
    v1 = col[:, 0:1]
    v2 = col[:, 1:2]
    w1 = col[:, 2:3]
    onec = K.sb["consts"][:, CONSTS.index(1.0):CONSTS.index(1.0) + 1]
    zeroc = K.sb["consts"][:, CONSTS.index(0.0):CONSTS.index(0.0) + 1]
    ev_tt = {}
    for tt in range(T // 128):
        pb = tt % 2

        def lmm(e, tt=tt, pb=pb):
            r = None
            for i in range(NCH):
                r = e.matmul(psl[:, pb * 512:pb * 512 + 8],
                             FLR[:, i * T + tt * 128:i * T + (tt + 1) * 128],
                             rwt[:, i * 8:(i + 1) * 8],
                             start=(i == 0), stop=(i == NCH - 1))
            return r
        eml = P.op("pe", lmm, deps=(erw, ev_flr, ev_tt.get(tt - 2)))
        P.op("ve", lambda e, pb=pb: e.tensor_tensor(lg, psl[:, pb * 512:pb * 512 + 8],
                                                    K.sb["RBB"][:], AO.add),
             deps=(eml, K._ev_rbb, ev_tt.get(tt - 1)))
        K.fill()
        P.op("ve", lambda e: e.tensor_reduce(v1, lg, AX.X, AO.max))
        K.fill()
        P.op("ve", lambda e: e.tensor_scalar(c1, lg, v1, None, AO.is_ge))
        K.fill()
        P.op("ve", lambda e: e.tensor_tensor_scan(s1, c1, c1, zeroc, AO.add, AO.bypass))
        K.fill()
        P.op("ve", lambda e: e.tensor_scalar(m1, s1, onec, None, AO.is_le))
        K.fill()
        P.op("ve", lambda e: e.tensor_tensor(m1, m1, c1, AO.mult))
        K.fill()
        P.op("ve", lambda e: e.scalar_tensor_tensor(l2, m1, -1e30, lg, AO.mult, AO.add))
        K.fill()
        P.op("ve", lambda e: e.tensor_reduce(v2, l2, AX.X, AO.max))
        K.fill()
        P.op("ve", lambda e: e.tensor_scalar(m2, l2, v2, None, AO.is_ge))
        K.fill()
        P.op("ve", lambda e: e.tensor_tensor_scan(s1, m2, m2, zeroc, AO.add, AO.bypass))
        K.fill()
        P.op("ve", lambda e: e.tensor_scalar(s1, s1, onec, None, AO.is_le))
        K.fill()
        P.op("ve", lambda e: e.tensor_tensor(m2, m2, s1, AO.mult))
        ed = P.op("ve", lambda e: e.tensor_tensor(col[:, 3:4], v1, v2, AO.subtract))
        ew = P.op("sc", lambda e: e.activation(w1, col[:, 3:4], AF.Sigmoid), deps=(ed,))
        P.op("ve", lambda e: e.tensor_scalar(cwtile[:], m1, w1, None, AO.mult), deps=(ew,))
        P.op("ve", lambda e: e.tensor_scalar(col[:, 3:4], w1, -1.0, 1.0, AO.mult, AO.add))
        K.fill()
        P.op("ve", lambda e: e.scalar_tensor_tensor(cwtile[:], m2, col[:, 3:4], cwtile[:],
                                                    AO.mult, AO.add))
        K.fill()
        ecast = P.op("ve", lambda e: e.tensor_copy(cwb16[:], cwtile[:]))
        emt = P.op("pe", lambda e: e.matmul(psr[:], cwb16[:], idb[:], start=True, stop=True),
                   deps=(ecast, ev_tt.get(tt - 1)))
        ev_tt[tt] = P.op("ve", lambda e, tt=tt: e.tensor_copy(
            cws[:, tt * 128:(tt + 1) * 128], psr[:]), deps=(emt,))
    K._ev_cwd[call] = P.op("gd", lambda e: e.dma_start(out=io["CWD"], in_=cws[:]),
                           deps=(ev_tt[T // 128 - 1],))


def phase_attn_b(K):
    nc, P, io = K.nc, K.P, K.io
    CH = 1024
    NC4 = T // CH
    TE = T + 1
    with ExitStack() as ctx:
        XBL = K.sbuf("B_XBL", [128, 8], F32, ctx)
        XBP = K.sbuf("B_XBP", [128, 8], F32, ctx)
        XBPB = K.sbuf("B_XBPB", [128, 8], BF16, ctx)
        KB = K.sbuf("B_KB", [128, 8], F32, ctx)
        GSR = K.sbuf("B_GSR", [1, T], F32, ctx)
        BW0 = K.sbuf("B_BW0", [1, T], BF16, ctx)
        BW1 = K.sbuf("B_BW1", [1, T], BF16, ctx)
        CARR = K.sbuf("B_CARR", [128, 34], F32, ctx)
        PCAR = K.sbuf("B_PCAR", [128, 34], F32, ctx)

        with ExitStack() as c0:
            xin = K.sbuf("B_xin", [128, T], F32, c0)
            ld = {}

            def src(i):
                ev = P.op("sy", lambda e, i=i: e.dma_start(
                    out=xin[:], in_=io["XRES"][i * 128:(i + 1) * 128, :]), deps=(ld.get("p"),))
                ld["p"] = ev
                return xin[:], ev
            mu3, r3, _v3, ev_st = ln_stats(K, c0, "B", src, NCH, T, 1e-5, out_dram_row=4)
            P.barrier()

        cscan = ctx.enter_context(ExitStack())
        FLATB = K.sbuf("B_FLATB", [128, NCH * TE], BF16, cscan)
        with ExitStack() as c0b:
            (m3b, r3b), ev_bc = bcast_dram_rows(K, c0b, "B0", [4, 5], T, 0)
            xin2 = K.sbuf("B_xin2", [128, T], F32, c0b)
            evp = None
            for i in range(NCH):
                el = P.op("sy", lambda e, i=i: e.dma_start(
                    out=xin2[:], in_=io["XRES"][i * 128:(i + 1) * 128, :]),
                    deps=(ev_bc, evp))
                P.op("ve", lambda e: e.tensor_tensor(xin2[:], xin2[:], m3b, AO.subtract),
                     deps=(el, ev_bc))
                P.op("ve", lambda e: e.tensor_tensor(xin2[:], xin2[:], r3b, AO.mult))
                P.op("ve", lambda e, i=i: e.tensor_copy(
                    FLATB[:, i * TE + 1:(i + 1) * TE], xin2[:]))
                evp = P.op("ve", lambda e, i=i: e.tensor_copy(
                    XBL[:, i:i + 1], xin2[:, T - 1:T]))
            ev_flatb = evp
            P.barrier()

        exs = P.op("gd", lambda e: e.dma_start(
            out=io["xsm"][bass.ds(K.r_mysm, 128), 40:48], in_=XBL[:]), deps=(ev_flatb,))
        ev_r1 = flag_and_poll(K, 1, (exs,))
        exr = P.op("gd", lambda e: e.dma_start(
            out=XBP[:], in_=io["xsm"][bass.ds(K.r_peersm, 128), 40:48]), deps=(ev_r1,))
        emk = P.op("ve", lambda e: e.tensor_scalar(XBP[:], XBP[:], K.sb["mprev_t"][:, 0:1],
                                                   None, AO.mult), deps=(exr,))
        K.fill()
        exb = P.op("ve", lambda e: e.tensor_copy(XBPB[:], XBP[:]), deps=(emk,))
        ev_bnd = None
        for i in range(NCH):
            ev_bnd = P.op("ve", lambda e, i=i: e.tensor_copy(
                FLATB[:, i * TE:i * TE + 1], XBP[:, i:i + 1]), deps=(emk, ev_flatb))

        wb = K.sbuf("B_wb", [128, 4 * NCH * 128], BF16, cscan)
        KEYPH = K.sbuf("B_KEYPH", [128, NCH * (CH + 1)], F32, cscan)
        GH = K.sbuf("B_GH", [128, NCH * CH], BF16, cscan)
        BH = K.sbuf("B_BH", [128, 4 * CH], BF16, cscan)
        VGR = K.sbuf("B_VGR", [1, CH], F32, cscan)
        VGB = K.sbuf("B_VGB", [128, CH], F32, cscan)
        t_a = K.sbuf("B_ta", [128, CH], F32, cscan)
        t_b = K.sbuf("B_tb", [128, CH], F32, cscan)
        t_c = K.sbuf("B_tc", [128, CH], F32, cscan)
        t_d = K.sbuf("B_td", [128, CH], F32, cscan)
        rowsc = K.sbuf("B_rowsc", [1, 3 * CH], F32, cscan)
        psA = K.psum("B_psA", [128, 2 * 512], F32, cscan)
        psR = K.psum("B_psR", [1, 2 * 512], F32, cscan)
        psB = K.psum("B_psB", [128, 1024], F32, cscan)

        ev_init = P.op("ve", lambda e: e.memset(CARR[:], 0.0))
        P.op("ve", lambda e: e.memset(GSR[:], 0.0))

        def stream_och(wsrc, o, slot, deps):
            return stream_w(K, wb[:, slot * NCH * 128:(slot + 1) * NCH * 128],
                            lambda: wsrc[:, o * 128:(o + 1) * 128], NCH, deps=deps)

        def frhs(i, c0_, width, shifted=False):
            base = i * TE + (0 if shifted else 1)
            return FLATB[:, base + c0_:base + c0_ + width]

        ev_kb = {}
        ev_sp_chunk = {}
        for cs in range(NC4):
            c0_ = cs * CH
            ev_o = {}
            for o in range(NCH):
                slot = o % 2
                ew = stream_och(io["Wk"], o, slot, deps=(ev_o.get(o - 2), ev_init,
                                                         ev_sp_chunk.get(cs - 1)))
                for s in range(2):
                    bank = s % 2

                    def kmm(e, o=o, s=s, slot=slot, bank=bank, c0_=c0_):
                        r = None
                        for i in range(NCH):
                            r = e.matmul(psA[:, bank * 512:(bank + 1) * 512],
                                         wb[:, slot * NCH * 128 + i * 128:
                                            slot * NCH * 128 + (i + 1) * 128],
                                         frhs(i, c0_ + s * 512, 512),
                                         start=(i == 0), stop=(i == NCH - 1))
                        return r
                    prevb = (ev_kb.get(o - 1) if cs == 0 and s == 0
                             else ev_o.get((o - 1, s)))
                    em = P.op("pe", kmm, deps=(ew, ev_flatb, ev_bnd, prevb))
                    ev_o[(o, s)] = P.op("sc", lambda e, o=o, s=s, bank=bank: e.activation(
                        KEYPH[:, o * (CH + 1) + 1 + s * 512:o * (CH + 1) + 1 + (s + 1) * 512],
                        psA[:, bank * 512:(bank + 1) * 512],
                        AF.Tanh, bias=K.bcol("bk", o), scale=1.0), deps=(em,))
                ev_o[o] = ev_o[(o, 1)]
                if cs == 0:
                    def kbm(e, o=o, slot=slot):
                        r = None
                        for i in range(NCH):
                            r = e.matmul(psA[:, 0:1],
                                         wb[:, slot * NCH * 128 + i * 128:
                                            slot * NCH * 128 + (i + 1) * 128],
                                         XBPB[:, i:i + 1],
                                         start=(i == 0), stop=(i == NCH - 1))
                        return r
                    ekb = P.op("pe", kbm, deps=(ev_o[(o, 1)], exb))
                    ev_kb[o] = P.op("sc", lambda e, o=o: e.activation(
                        KB[:, o:o + 1], psA[:, 0:1], AF.Tanh, bias=K.bcol("bk", o), scale=1.0),
                        deps=(ekb,))
            ev_kcol = None
            for o in range(NCH):
                dep = ev_kb.get(o) if cs == 0 else ev_kb.get((cs, o))
                ev_kcol = P.op("ve", lambda e, o=o: e.tensor_copy(
                    KEYPH[:, o * (CH + 1):o * (CH + 1) + 1], KB[:, o:o + 1]),
                    deps=(dep, ev_o[o]))
            for o in range(NCH):
                ev_kb[(cs + 1, o)] = P.op("ve", lambda e, o=o: e.tensor_copy(
                    KB[:, o:o + 1], KEYPH[:, o * (CH + 1) + CH:o * (CH + 1) + CH + 1]),
                    deps=(ev_o[o], ev_kcol))
            ev_keyph = ev_kb[(cs + 1, NCH - 1)]

            ev_g = {}
            for o in range(NCH):
                slot = o % 2
                ewg = stream_w(K, wb[:, (2 + slot) * NCH * 128:(3 + slot) * NCH * 128],
                               lambda o=o: io["Wg1"][0:D, o * 128:(o + 1) * 128], NCH,
                               deps=(ev_g.get(o - 2),))
                ewg2 = stream_w(K, wb[:, slot * NCH * 128:(slot + 1) * NCH * 128],
                                lambda o=o: io["Wg1"][D:2 * D, o * 128:(o + 1) * 128], NCH,
                                deps=(ev_g.get(o - 2), ev_keyph))
                for s in range(2):
                    bank = s % 2

                    def gmm(e, o=o, s=s, slot=slot, bank=bank, c0_=c0_):
                        r = None
                        for i in range(NCH):
                            r = e.matmul(psA[:, bank * 512:(bank + 1) * 512],
                                         wb[:, (2 + slot) * NCH * 128 + i * 128:
                                            (2 + slot) * NCH * 128 + (i + 1) * 128],
                                         frhs(i, c0_ + s * 512, 512),
                                         start=(i == 0), stop=False)
                        for i in range(NCH):
                            r = e.matmul(psA[:, bank * 512:(bank + 1) * 512],
                                         wb[:, slot * NCH * 128 + i * 128:
                                            slot * NCH * 128 + (i + 1) * 128],
                                         frhs(i, c0_ + s * 512, 512, shifted=True),
                                         start=False, stop=(i == NCH - 1))
                        return r
                    em = P.op("pe", gmm, deps=(ewg, ewg2, ev_bnd, ev_g.get((o - 1, s))))
                    ev_g[(o, s)] = P.op("sc", lambda e, o=o, s=s, bank=bank: e.activation(
                        GH[:, o * CH + s * 512:o * CH + (s + 1) * 512],
                        psA[:, bank * 512:(bank + 1) * 512],
                        AF.Gelu, bias=K.bcol("bg1", o), scale=1.0), deps=(em,))
                ev_g[o] = ev_g[(o, 1)]

            def vgmm(e, s):
                r = None
                for i in range(NCH):
                    r = e.matmul(psR[0:1, s * 512:(s + 1) * 512],
                                 K.sb["WG2C"][:, i:i + 1],
                                 GH[:, i * CH + s * 512:i * CH + (s + 1) * 512],
                                 start=(i == 0), stop=(i == NCH - 1))
                return r
            emv = P.op("pe", lambda e, vgmm=vgmm: [vgmm(e, 0), vgmm(e, 1)][-1],
                       deps=(ev_g[NCH - 1], K._ev_wg2))
            evg = P.op("sc", lambda e: e.activation(VGR[:], psR[0:1, 0:CH], AF.Sigmoid,
                                                    bias=K.sb["bb2g"][0:1, 0:1], scale=1.0),
                       deps=(emv, K._ev_bg2g))
            egs = P.op("ve", lambda e, c0_=c0_: e.tensor_tensor_scan(
                GSR[0:1, c0_:c0_ + CH], VGR[:], VGR[:], CARR[0:1, 32:33], AO.add, AO.bypass),
                deps=(evg, ev_init))
            K.fill()
            P.op("ve", lambda e, c0_=c0_: e.tensor_copy(
                CARR[0:1, 32:33], GSR[0:1, c0_ + CH - 1:c0_ + CH]), deps=(egs,))
            emb = P.op("pe", lambda e: [e.matmul(psB[:, s * 512:(s + 1) * 512],
                                                 K.sb["ones1f"][:],
                                                 VGR[0:1, s * 512:(s + 1) * 512],
                                                 start=True, stop=True) for s in range(2)][-1],
                       deps=(evg,))
            evgb = P.op("sc", lambda e: e.copy(out=VGB[:], in_=psB[:, 0:CH]), deps=(emb,))

            ev_bh = {}
            for o in range(4):
                slot = o % 2
                ewb = stream_och(io["Wb1"], o, slot, deps=(ev_bh.get(o - 2), evgb))
                for s in range(2):
                    bank = s % 2

                    def bmm(e, o=o, s=s, slot=slot, bank=bank, c0_=c0_):
                        r = None
                        for i in range(NCH):
                            r = e.matmul(psA[:, bank * 512:(bank + 1) * 512],
                                         wb[:, slot * NCH * 128 + i * 128:
                                            slot * NCH * 128 + (i + 1) * 128],
                                         frhs(i, c0_ + s * 512, 512),
                                         start=(i == 0), stop=(i == NCH - 1))
                        return r
                    em = P.op("pe", bmm, deps=(ewb, ev_bh.get((o - 1, s))))
                    ev_bh[(o, s)] = P.op("sc", lambda e, o=o, s=s, bank=bank: e.activation(
                        BH[:, o * CH + s * 512:o * CH + (s + 1) * 512],
                        psA[:, bank * 512:(bank + 1) * 512],
                        AF.Gelu, bias=K.bcol("bb1", o), scale=1.0), deps=(em,))
                ev_bh[o] = ev_bh[(o, 1)]

            def bwmm(e, colx, s):
                r = None
                for i in range(4):
                    r = e.matmul(psR[0:1, s * 512:(s + 1) * 512],
                                 K.sb["WB2C"][:, colx * 4 + i:colx * 4 + i + 1],
                                 BH[:, i * CH + s * 512:i * CH + (s + 1) * 512],
                                 start=(i == 0), stop=(i == 3))
                return r
            emw0 = P.op("pe", lambda e, bwmm=bwmm: [bwmm(e, 0, 0), bwmm(e, 0, 1)][-1],
                        deps=(ev_bh[3], K._ev_wb2, evg, evgb))
            ecp0 = P.op("ve", lambda e: e.tensor_copy(rowsc[0:1, 0:CH], psR[0:1, 0:CH]),
                        deps=(emw0,))
            emw1 = P.op("pe", lambda e, bwmm=bwmm: [bwmm(e, 1, 0), bwmm(e, 1, 1)][-1],
                        deps=(ecp0,))
            ecp1 = P.op("ve", lambda e: e.tensor_copy(rowsc[0:1, CH:2 * CH], psR[0:1, 0:CH]),
                        deps=(emw1,))
            ed = P.op("ve", lambda e: e.tensor_tensor(
                rowsc[0:1, 2 * CH:3 * CH], rowsc[0:1, 0:CH], rowsc[0:1, CH:2 * CH],
                AO.subtract), deps=(ecp1,))
            P.op("sc", lambda e, c0_=c0_: e.activation(
                BW0[0:1, c0_:c0_ + CH], rowsc[0:1, 2 * CH:3 * CH], AF.Sigmoid,
                bias=K.sb["bb2diff"][0:1, 0:1], scale=1.0), deps=(ed, K._ev_bb2))
            P.op("sc", lambda e, c0_=c0_: e.activation(
                BW1[0:1, c0_:c0_ + CH], rowsc[0:1, 2 * CH:3 * CH], AF.Sigmoid,
                bias=K.sb["bb2diff"][0:1, 1:2], scale=-1.0), deps=(ed, K._ev_bb2))
            ev_bw = P.ev("sc")

            ev_sp = None
            for i in range(NCH):
                slot = i % 2
                ewp = stream_och(io["Wpv"], i, slot, deps=(ev_sp, ev_bw))

                def pvm2(e, i=i, slot=slot, c0_=c0_):
                    r = None
                    for s in range(2):
                        for j in range(NCH):
                            r = e.matmul(psA[:, (s % 2) * 512:((s % 2) + 1) * 512],
                                         wb[:, slot * NCH * 128 + j * 128:
                                            slot * NCH * 128 + (j + 1) * 128],
                                         frhs(j, c0_ + s * 512, 512),
                                         start=(j == 0), stop=(j == NCH - 1))
                    return r
                em = P.op("pe", pvm2, deps=(ewp, ev_sp))
                epv = P.op("sc", lambda e, i=i: [
                    e.activation(t_a[:, s * 512:(s + 1) * 512],
                                 psA[:, (s % 2) * 512:((s % 2) + 1) * 512],
                                 AF.Identity, bias=K.bcol("bpv", i), scale=1.0)
                    for s in range(2)][-1], deps=(em,))
                epp = P.op("sy", lambda e, i=i, c0_=c0_: e.dma_start(
                    out=t_b[:], in_=io["ppT"][i * 128:(i + 1) * 128, c0_:c0_ + CH]),
                    deps=(ev_sp,))
                P.op("ve", lambda e: e.tensor_scalar(t_c[:], t_b[:], 1.0 / (2 * PI),
                                                     MAGIC, AO.mult, AO.add), deps=(epp, ev_sp))
                P.op("ve", lambda e: e.tensor_scalar(t_c[:], t_c[:], MAGIC, None, AO.subtract))
                P.op("ve", lambda e: e.scalar_tensor_tensor(t_b[:], t_c[:], -2.0 * PI, t_b[:],
                                                            AO.mult, AO.add))
                esin = P.op("sc", lambda e: e.activation(t_c[:], t_b[:], AF.Sin),
                            deps=(P.ev("ve"), epv))
                eabs = P.op("ve", lambda e: e.scalar_tensor_tensor(t_b[:], t_b[:], -1.0, t_b[:],
                                                                   AO.mult, AO.max),
                            deps=(esin,))
                ecos = P.op("sc", lambda e: e.activation(t_b[:], t_b[:], AF.Sin,
                                                         bias=0.5 * PI, scale=-1.0),
                            deps=(eabs,))
                P.op("ve", lambda e: e.tensor_tensor(t_d[:], t_b[:], t_a[:], AO.mult),
                     deps=(ecos, epv))
                P.op("ve", lambda e, i=i: e.tensor_tensor_scan(
                    t_d[:], t_d[:], t_d[:], CARR[:, 0 * 8 + i:0 * 8 + i + 1],
                    AO.add, AO.bypass))
                K.fill()
                P.op("ve", lambda e, i=i: e.tensor_copy(CARR[:, 0 * 8 + i:0 * 8 + i + 1],
                                                        t_d[:, CH - 1:CH]))
                es1 = P.op("sy", lambda e, i=i, c0_=c0_: e.dma_start(
                    out=io["MEMS"][0 * D + i * 128:0 * D + (i + 1) * 128, c0_:c0_ + CH],
                    in_=t_d[:]), deps=(P.ev("ve"),))
                P.op("ve", lambda e: e.tensor_tensor(t_d[:], t_c[:], t_a[:], AO.mult),
                     deps=(esin, es1))
                P.op("ve", lambda e, i=i: e.tensor_tensor_scan(
                    t_d[:], t_d[:], t_d[:], CARR[:, 1 * 8 + i:1 * 8 + i + 1],
                    AO.add, AO.bypass))
                K.fill()
                P.op("ve", lambda e, i=i: e.tensor_copy(CARR[:, 1 * 8 + i:1 * 8 + i + 1],
                                                        t_d[:, CH - 1:CH]))
                es2 = P.op("sy", lambda e, i=i, c0_=c0_: e.dma_start(
                    out=io["MEMS"][1 * D + i * 128:1 * D + (i + 1) * 128, c0_:c0_ + CH],
                    in_=t_d[:]), deps=(P.ev("ve"),))
                ewk = stream_w(K, wb[:, (2 + slot) * NCH * 128:(3 + slot) * NCH * 128],
                               lambda i=i: io["Wkv"][:, i * 128:(i + 1) * 128], NCH,
                               deps=(ev_sp, ev_g[NCH - 1]))

                def kvm2(e, i=i, slot=slot, c0_=c0_):
                    r = None
                    for s in range(2):
                        for j in range(NCH):
                            r = e.matmul(psA[:, (s % 2) * 512:((s % 2) + 1) * 512],
                                         wb[:, (2 + slot) * NCH * 128 + j * 128:
                                            (2 + slot) * NCH * 128 + (j + 1) * 128],
                                         frhs(j, c0_ + s * 512, 512),
                                         start=(j == 0), stop=(j == NCH - 1))
                    return r
                emk2_ = P.op("pe", kvm2, deps=(ewk, epv))
                ekv = P.op("sc", lambda e, i=i: [
                    e.activation(t_a[:, s * 512:(s + 1) * 512],
                                 psA[:, (s % 2) * 512:((s % 2) + 1) * 512],
                                 AF.Identity, bias=K.bcol("bkv", i), scale=1.0)
                    for s in range(2)][-1], deps=(emk2_, P.ev("ve")))
                P.op("ve", lambda e: e.tensor_tensor(t_a[:], t_a[:], VGB[:], AO.mult),
                     deps=(ekv, evgb))
                ksh = KEYPH[:, i * (CH + 1):i * (CH + 1) + CH]
                esn = P.op("sc", lambda e, ksh=ksh: e.activation(t_b[:], ksh, AF.Sin, scale=PI),
                           deps=(ev_keyph, P.ev("ve")))
                eab = P.op("ve", lambda e, ksh=ksh: e.scalar_tensor_tensor(
                    t_c[:], ksh, -1.0, ksh, AO.mult, AO.max), deps=(ev_keyph, esn))
                ecn = P.op("sc", lambda e: e.activation(t_c[:], t_c[:], AF.Sin,
                                                        bias=0.5 * PI, scale=-PI), deps=(eab,))
                if cs == 0:
                    P.op("ve", lambda e: e.tensor_scalar(
                        t_b[:, 0:1], t_b[:, 0:1], K.sb["mprev_t"][:, 0:1], None, AO.mult),
                        deps=(esn,))
                    ecn = P.op("ve", lambda e: e.tensor_scalar(
                        t_c[:, 0:1], t_c[:, 0:1], K.sb["mprev_t"][:, 0:1], None, AO.mult),
                        deps=(ecn,))
                    K.fill()
                P.op("ve", lambda e: e.tensor_tensor(t_d[:], t_a[:], t_c[:], AO.mult),
                     deps=(ecn,))
                P.op("ve", lambda e, i=i: e.tensor_tensor_scan(
                    t_d[:], t_d[:], t_d[:], CARR[:, 2 * 8 + i:2 * 8 + i + 1],
                    AO.add, AO.bypass))
                K.fill()
                P.op("ve", lambda e, i=i: e.tensor_copy(CARR[:, 2 * 8 + i:2 * 8 + i + 1],
                                                        t_d[:, CH - 1:CH]))
                es3 = P.op("sy", lambda e, i=i, c0_=c0_: e.dma_start(
                    out=io["MEMS"][2 * D + i * 128:2 * D + (i + 1) * 128, c0_:c0_ + CH],
                    in_=t_d[:]), deps=(P.ev("ve"),))
                P.op("ve", lambda e: e.tensor_tensor(t_d[:], t_a[:], t_b[:], AO.mult),
                     deps=(esn, es3))
                P.op("ve", lambda e, i=i: e.tensor_tensor_scan(
                    t_d[:], t_d[:], t_d[:], CARR[:, 3 * 8 + i:3 * 8 + i + 1],
                    AO.add, AO.bypass))
                K.fill()
                P.op("ve", lambda e, i=i: e.tensor_copy(CARR[:, 3 * 8 + i:3 * 8 + i + 1],
                                                        t_d[:, CH - 1:CH]))
                es4 = P.op("sy", lambda e, i=i, c0_=c0_: e.dma_start(
                    out=io["MEMS"][3 * D + i * 128:3 * D + (i + 1) * 128, c0_:c0_ + CH],
                    in_=t_d[:]), deps=(P.ev("ve"),))
                ev_sp = P.op("sy", lambda e, i=i, c0_=c0_: e.dma_start(
                    out=io["MEMS"][4 * D + i * 128:4 * D + (i + 1) * 128, c0_:c0_ + CH],
                    in_=KEYPH[:, i * (CH + 1) + 1:i * (CH + 1) + 1 + CH]),
                    deps=(ev_keyph, es4))
            ev_sp_chunk[cs] = ev_sp
        P.barrier()

        ex2 = P.op("gd", lambda e: e.dma_start(
            out=io["xsm"][bass.ds(K.r_mysm, 128), 0:34], in_=CARR[:]),
            deps=(ev_sp_chunk[NC4 - 1], P.ev("ve")))
        ev_r2 = flag_and_poll(K, 2, (ex2,))
        ex2r = P.op("gd", lambda e: e.dma_start(
            out=PCAR[:], in_=io["xsm"][bass.ds(K.r_peersm, 128), 0:34]), deps=(ev_r2,))
        emc = P.op("ve", lambda e: e.tensor_scalar(PCAR[:], PCAR[:], K.sb["mprev_t"][:, 0:1],
                                                   None, AO.mult), deps=(ex2r,))
        P.barrier()
        cscan.close()

        cret = ctx.enter_context(ExitStack())
        BLD = K.sbuf("B_BLD", [128, NCH * 1024], F32, cret)
        BLN = K.sbuf("B_BLN", [128, NCH * 1024], BF16, cret)
        rt_a = K.sbuf("B_rta", [128, 1024], F32, cret)
        rt_b = K.sbuf("B_rtb", [128, 1024], F32, cret)
        rt_c = K.sbuf("B_rtc", [128, 1024], F32, cret)
        rt_d = K.sbuf("B_rtd", [128, 1024], F32, cret)
        rwb = K.sbuf("B_rwb", [128, 2 * NCH * 128], BF16, cret)
        rpsA = K.psum("B_rpsA", [128, 2 * 512], F32, cret)
        rpsB = K.psum("B_rpsB", [128, 1024], F32, cret)

        for cs in range(NC4):
            c0_ = cs * CH
            with ExitStack() as cr:
                g = K.sbuf(f"B_g{cs}", [1, CH], F32, cr)
                P.op("ve", lambda e, c0_=c0_, g=g: e.tensor_scalar(
                    g[:], GSR[0:1, c0_:c0_ + CH], PCAR[0:1, 32:33], None, AO.add),
                    deps=(emc,))
                P.op("ve", lambda e, g=g: e.tensor_scalar(g[:], g[:], 1.0, None, AO.max))
                esq = P.op("sc", lambda e, g=g: e.activation(g[:], g[:], AF.Sqrt, bias=1e-12,
                                                             scale=1.0), deps=(P.ev("ve"),))
                P.op("ve", lambda e, g=g: e.reciprocal(g[:], g[:]), deps=(esq,))
                erg = P.op("ve", lambda e, g=g: e.tensor_scalar(g[:], g[:], 1.0 / 32.0, None,
                                                                AO.mult))
                b0r = K.sbuf(f"B_b0r{cs}", [1, 2 * CH], F32, cr)
                P.op("ve", lambda e, c0_=c0_, b0r=b0r: e.tensor_scalar(
                    b0r[:, 0:CH], BW0[0:1, c0_:c0_ + CH], 1.0 / 32.0, None, AO.mult))
                ebwr = P.op("ve", lambda e, c0_=c0_, b0r=b0r: e.tensor_copy(
                    b0r[:, CH:2 * CH], BW1[0:1, c0_:c0_ + CH]))
                (grorwb, bw0b, bw1b), ev_brwb = bcast_rows(
                    K, cr, f"Br{cs}", [(g[:], erg), (b0r[:, 0:CH], ebwr),
                                       (b0r[:, CH:2 * CH], ebwr)], CH, ps=rpsB)
                (m3cb, r3cb), ev_b3 = bcast_dram_rows(
                    K, cr, f"Bc{cs}", [4, 5], CH, c0_, ps=rpsB, deps=(ev_brwb,))

                ev_t = None
                for i in range(NCH):
                    e1_ = P.op("sy", lambda e, i=i, c0_=c0_: e.dma_start(
                        out=rt_a[:], in_=io["MEMS"][0 * D + i * 128:0 * D + (i + 1) * 128,
                                                   c0_:c0_ + CH]), deps=(ev_t,))
                    e2_ = P.op("sy", lambda e, i=i, c0_=c0_: e.dma_start(
                        out=rt_b[:], in_=io["MEMS"][1 * D + i * 128:1 * D + (i + 1) * 128,
                                                   c0_:c0_ + CH]))
                    epp = P.op("sy", lambda e, i=i, c0_=c0_: e.dma_start(
                        out=rt_c[:], in_=io["ppT"][i * 128:(i + 1) * 128, c0_:c0_ + CH]))
                    P.op("ve", lambda e: e.tensor_scalar(rt_d[:], rt_c[:], 1.0 / (2 * PI),
                                                         MAGIC, AO.mult, AO.add),
                         deps=(epp, ev_t))
                    P.op("ve", lambda e: e.tensor_scalar(rt_d[:], rt_d[:], MAGIC, None,
                                                         AO.subtract))
                    P.op("ve", lambda e: e.scalar_tensor_tensor(rt_c[:], rt_d[:], -2.0 * PI,
                                                                rt_c[:], AO.mult, AO.add))
                    esin = P.op("sc", lambda e: e.activation(rt_d[:], rt_c[:], AF.Sin),
                                deps=(P.ev("ve"),))
                    eabs = P.op("ve", lambda e: e.scalar_tensor_tensor(
                        rt_c[:], rt_c[:], -1.0, rt_c[:], AO.mult, AO.max), deps=(esin,))
                    ecos = P.op("sc", lambda e: e.activation(rt_c[:], rt_c[:], AF.Sin,
                                                             bias=0.5 * PI, scale=-1.0),
                                deps=(eabs,))
                    P.op("ve", lambda e, i=i: e.scalar_tensor_tensor(
                        rt_a[:], rt_a[:], PCAR[:, 0 * 8 + i:0 * 8 + i + 1], rt_c[:],
                        AO.add, AO.mult), deps=(e1_, ecos, emc))
                    P.op("ve", lambda e, i=i: e.scalar_tensor_tensor(
                        rt_b[:], rt_b[:], PCAR[:, 1 * 8 + i:1 * 8 + i + 1], rt_d[:],
                        AO.add, AO.mult), deps=(e2_, esin))
                    P.op("ve", lambda e: e.tensor_tensor(rt_a[:], rt_a[:], rt_b[:], AO.add))
                    eposp = P.op("ve", lambda e, i=i, bw0b=bw0b: e.tensor_tensor(
                        BLD[:, i * CH:(i + 1) * CH], rt_a[:], bw0b, AO.mult), deps=(ev_brwb,))
                    e3_ = P.op("sy", lambda e, i=i, c0_=c0_: e.dma_start(
                        out=rt_a[:], in_=io["MEMS"][2 * D + i * 128:2 * D + (i + 1) * 128,
                                                   c0_:c0_ + CH]), deps=(eposp,))
                    e4_ = P.op("sy", lambda e, i=i, c0_=c0_: e.dma_start(
                        out=rt_b[:], in_=io["MEMS"][3 * D + i * 128:3 * D + (i + 1) * 128,
                                                   c0_:c0_ + CH]))
                    e5_ = P.op("sy", lambda e, i=i, c0_=c0_: e.dma_start(
                        out=rt_d[:], in_=io["MEMS"][4 * D + i * 128:4 * D + (i + 1) * 128,
                                                   c0_:c0_ + CH]))
                    esn2 = P.op("sc", lambda e: e.activation(rt_c[:], rt_d[:], AF.Sin, scale=PI),
                                deps=(e5_, eposp))
                    eab2 = P.op("ve", lambda e: e.scalar_tensor_tensor(
                        rt_d[:], rt_d[:], -1.0, rt_d[:], AO.mult, AO.max), deps=(esn2,))
                    ecs2 = P.op("sc", lambda e: e.activation(rt_d[:], rt_d[:], AF.Sin,
                                                             bias=0.5 * PI, scale=-PI),
                                deps=(eab2,))
                    P.op("ve", lambda e, i=i: e.scalar_tensor_tensor(
                        rt_a[:], rt_a[:], PCAR[:, 2 * 8 + i:2 * 8 + i + 1], rt_d[:],
                        AO.add, AO.mult), deps=(e3_, ecs2))
                    P.op("ve", lambda e, i=i: e.scalar_tensor_tensor(
                        rt_b[:], rt_b[:], PCAR[:, 3 * 8 + i:3 * 8 + i + 1], rt_c[:],
                        AO.add, AO.mult), deps=(e4_,))
                    P.op("ve", lambda e: e.tensor_tensor(rt_a[:], rt_a[:], rt_b[:], AO.add))
                    P.op("ve", lambda e, grorwb=grorwb: e.tensor_tensor(
                        rt_a[:], rt_a[:], grorwb, AO.mult))
                    P.op("ve", lambda e, bw1b=bw1b: e.tensor_tensor(
                        rt_a[:], rt_a[:], bw1b, AO.mult))
                    ev_t = P.op("ve", lambda e, i=i: e.tensor_tensor(
                        BLD[:, i * CH:(i + 1) * CH], BLD[:, i * CH:(i + 1) * CH], rt_a[:],
                        AO.add))
                ev_bld = ev_t

                def srcb(i):
                    return BLD[:, i * CH:(i + 1) * CH], ev_bld
                mu4, r4, _v4, ev_st4 = ln_stats(K, cr, f"B{cs}", srcb, NCH, CH, 1e-5)
                (m4b, r4b), ev_bc4 = bcast_rows(K, cr, f"B{cs}b",
                                                [(mu4, ev_st4), (r4, ev_st4)], CH, ps=rpsB)
                evn = None
                for i in range(NCH):
                    P.op("ve", lambda e, i=i, m4b=m4b: e.tensor_tensor(
                        BLD[:, i * CH:(i + 1) * CH], BLD[:, i * CH:(i + 1) * CH], m4b,
                        AO.subtract), deps=(ev_bc4, evn))
                    P.op("ve", lambda e, i=i, r4b=r4b: e.tensor_tensor(
                        BLD[:, i * CH:(i + 1) * CH], BLD[:, i * CH:(i + 1) * CH], r4b,
                        AO.mult))
                    evn = P.op("ve", lambda e, i=i: e.tensor_scalar(
                        BLN[:, i * CH:(i + 1) * CH], BLD[:, i * CH:(i + 1) * CH],
                        K.bcol("bln_g", i), K.bcol("bln_b", i), AO.mult, AO.add))
                ev_bln = evn

                ev_o = {}
                for o in range(NCH):
                    slot = o % 2
                    ewo = stream_w(K, rwb[:, slot * NCH * 128:(slot + 1) * NCH * 128],
                                   lambda o=o: io["Wo"][:, o * 128:(o + 1) * 128], NCH,
                                   deps=(ev_o.get(o - 2), ev_bln))
                    exl = P.op("sy", lambda e, o=o, c0_=c0_: e.dma_start(
                        out=rt_c[:], in_=io["XRES"][o * 128:(o + 1) * 128, c0_:c0_ + CH]),
                        deps=(ev_o.get(o - 1), ev_bld))
                    P.op("ve", lambda e, m3cb=m3cb: e.tensor_tensor(
                        rt_d[:], rt_c[:], m3cb, AO.subtract), deps=(exl, ev_b3))
                    ezz = P.op("ve", lambda e, r3cb=r3cb: e.tensor_tensor(
                        rt_d[:], rt_d[:], r3cb, AO.mult))

                    def omm(e, o=o, slot=slot):
                        r = None
                        for s in range(2):
                            for i in range(NCH):
                                r = e.matmul(rpsA[:, (s % 2) * 512:((s % 2) + 1) * 512],
                                             rwb[:, slot * NCH * 128 + i * 128:
                                                slot * NCH * 128 + (i + 1) * 128],
                                             BLN[:, i * CH + s * 512:i * CH + (s + 1) * 512],
                                             start=(i == 0), stop=(i == NCH - 1))
                        return r
                    emo = P.op("pe", omm, deps=(ewo, ev_bln, ev_o.get((o - 1, "m"))))
                    P.op("ve", lambda e, o=o: [
                        e.scalar_tensor_tensor(rt_d[:, s * 512:(s + 1) * 512],
                                               rpsA[:, (s % 2) * 512:((s % 2) + 1) * 512],
                                               K.bcol("bo", o), rt_d[:, s * 512:(s + 1) * 512],
                                               AO.add, AO.add) for s in range(2)][-1],
                        deps=(emo, ezz))
                    eadd = P.op("ve", lambda e: e.tensor_tensor(rt_d[:], rt_d[:], rt_c[:], AO.add))
                    ev_o[(o, "m")] = eadd
                    ev_o[o] = P.op("sy", lambda e, o=o, c0_=c0_: e.dma_start(
                        out=io["XRES"][o * 128:(o + 1) * 128, c0_:c0_ + CH], in_=rt_d[:]),
                        deps=(eadd,))
                P.barrier()
        cret.close()
        P.barrier()


import os
PHASES = tuple(os.environ.get("KPHASES", "A,F1,B,F2").split(","))


def build_body(K):
    nc, P, io = K.nc, K.P, K.io
    K._ev_cwd = {}
    K._fw_cached = False
    K._fw_st1 = {}
    K._fw_st2 = {}
    setup(K)
    bb2g = K.sbuf("bb2g", [1, 1], F32)
    K._ev_bg2g = P.op("sy", lambda e: e.dma_start(out=bb2g[:], in_=K.io["bg2v"]))
    P.barrier()
    def dbg_snap(nm):
        if not os.environ.get("KPHDBG"):
            return
        P.barrier()
        ev = P.op("sy", lambda e, nm=nm: e.dma_start(out=io[nm], in_=io["XRES"]))
        P.op("ve", None, deps=(ev,), inc=0)
        P.barrier()

    if "A" in PHASES:
        phase_attn_a(K)
        dbg_snap("dbgA")
    if "F1" in PHASES:
        phase_ffn(K, 1)
        dbg_snap("dbgF1")
        if os.environ.get("KDBG"):
            P.barrier()
            e1 = P.op("sy", lambda e: e.dma_start(out=io["dbgCWD"], in_=io["CWD"]))
            e2 = P.op("sy", lambda e: e.dma_start(out=io["dbgOUT"],
                                                  in_=io["MEMS"][0:D, :]))
            P.op("ve", None, deps=(e1, e2), inc=0)
            P.barrier()
    if "B" in PHASES:
        phase_attn_b(K)
        dbg_snap("dbgB")
    if "F2" in PHASES:
        phase_ffn(K, 2)
    if "F2" not in PHASES:
        fin = K.sbuf("fin_dbg", [128, T], F32)
        ev = None
        for i in range(NCH):
            e1 = P.op("sy", lambda e, i=i: e.dma_start(
                out=fin[:], in_=io["XRES"][i * 128:(i + 1) * 128, :]), deps=(ev,))
            ev = P.op("sy", lambda e, i=i: e.dma_start(
                out=io["out"][i * 128:(i + 1) * 128, :], in_=fin[:]), deps=(e1,))
        P.op("ve", None, deps=(ev,), inc=0)
    P.barrier()


def host_bias_table(inp):
    tbl = np.zeros((BTOT, 128), np.float32)

    def put(name, vec):
        v = np.asarray(vec, np.float32).reshape(-1)
        n = (v.size + 127) // 128
        pad = np.zeros(n * 128, np.float32)
        pad[:v.size] = v
        tbl[BROWS[name]:BROWS[name] + n] = pad.reshape(n, 128)
    put("aln_w", inp["L0_attn_a_ln_w"]); put("aln_b", inp["L0_attn_a_ln_b"])
    put("bqkv", inp["L0_attn_a_b_qkv"]); put("bproj", inp["L0_attn_a_b_proj"])
    put("rb", inp["L0_ffn_router_b"])
    put("fb1", inp["L0_ffn_b1"]); put("fb2", inp["L0_ffn_b2"])
    put("fln_w", inp["L0_ffn_ln_w"]); put("fln_b", inp["L0_ffn_ln_b"])
    put("bpv", inp["L0_attn_b_bpv"]); put("bk", inp["L0_attn_b_bk"])
    put("bkv", inp["L0_attn_b_bkv"]); put("bg1", inp["L0_attn_b_bg1"])
    put("bb1", inp["L0_attn_b_bb1"])
    put("bmisc", np.concatenate([np.asarray(inp["L0_attn_b_bg2"], np.float32).reshape(-1),
                                 np.asarray(inp["L0_attn_b_bb2"], np.float32).reshape(-1)]))
    put("bln_g", inp["L0_attn_b_ln_g"]); put("bln_b", inp["L0_attn_b_ln_b"])
    put("bo", inp["L0_attn_b_bo"])
    return tbl


_NC_CACHE = {}


def build_program():
    if "nc" in _NC_CACHE:
        return _NC_CACHE["nc"]
    nc = bass.Bass("TRN2", target_bir_lowering=False, detect_race_conditions=False,
                   num_devices=8)
    io = {}

    def din(name, shape, dt=F32):
        io[name] = nc.dram_tensor(name, list(shape), dt, kind="ExternalInput").ap()
    din("xT", [D, T])
    din("Wqkv", [D, 3 * D]); din("Wproj", [D, D])
    din("ppT", [D, T])
    din("Wpv", [D, D]); din("Wk", [D, D]); din("Wkv", [D, D])
    din("Wg1", [2 * D, D]); din("Wg2", [D, 1]); din("Wb1", [D, 512]); din("Wb2", [512, 2])
    din("Wo", [D, D])
    din("w1T", [NE, D, FF]); din("w2T", [NE, FF, D]); din("rwT", [D, NE])
    din("btab", [BTOT, 128])
    din("rbrow", [1, 8])
    din("b2m", [NE, D])
    din("bb2d", [1, 2])
    din("bg2v", [1, 1])
    din("slot", [1, 8], I32)
    din("nonces", [N_ROUNDS, 16], I32)
    din("mprev", [128, 1])
    io["out"] = nc.dram_tensor("out", [D, T], F32, kind="ExternalOutput").ap()
    io["XRES"] = nc.dram_tensor("XRES", [D, T], F32, kind="Internal").ap()
    io["MSAR"] = nc.dram_tensor("MSAR", [D, T], BF16, kind="Internal").ap()
    io["DEN"] = nc.dram_tensor("DEN", [16, T], F32, kind="Internal").ap()
    io["CWD"] = nc.dram_tensor("CWD", [NE, T], BF16, kind="Internal").ap()
    io["ROWSD"] = nc.dram_tensor("ROWSD", [8, T], F32, kind="Internal").ap()
    io["MEMS"] = nc.dram_tensor("MEMS", [5 * D, T], F32, kind="Internal").ap()
    io["W1BD"] = nc.dram_tensor("W1BD", [32 * 128, NCH * 1024], BF16, kind="Internal").ap()
    io["W2BD"] = nc.dram_tensor("W2BD", [32 * 128, 8 * 1024], BF16, kind="Internal").ap()
    io["KVSTG"] = nc.dram_tensor("KVSTG", [2 * D, T], BF16, kind="Internal").ap()
    io["xkv"] = nc.dram_tensor("xkv", [2 * 2 * D, T], BF16, kind="Internal",
                               addr_space="Shared").ap()
    io["xsm"] = nc.dram_tensor("xsm", [256, 48], F32, kind="Internal",
                               addr_space="Shared").ap()
    io["xfl"] = nc.dram_tensor("xfl", [2, 16], I32, kind="Internal",
                               addr_space="Shared").ap()
    if os.environ.get("KPHDBG"):
        for nm in ("dbgA", "dbgF1", "dbgB"):
            io[nm] = nc.dram_tensor(nm, [D, T], F32, kind="ExternalOutput").ap()
    if os.environ.get("KDBG"):
        io["dbgCWD"] = nc.dram_tensor("dbgCWD", [NE, T], BF16, kind="ExternalOutput").ap()
        io["dbgOUT"] = nc.dram_tensor("dbgOUT", [D, T], F32, kind="ExternalOutput").ap()

    with ExitStack() as ctx:
        P = Prog(nc, ctx)
        K = Kern(nc, ctx, P, io)
        build_body(K)
        block = ctx.enter_context(nc.Block())
        nbad = P.check_closures()
        assert nbad == 0, f"{nbad} lambdas with drifted closures (late-binding bug)"
        P.emit(block)
    _NC_CACHE["nc"] = nc
    _NC_CACHE["P"] = P
    return nc


def make_inmaps(inputs):
    x = np.asarray(inputs["L0_attn_a_z"], np.float32)
    pp = np.asarray(inputs["L0_attn_b_pos_phases"], np.float32)[:L]
    w1 = np.asarray(inputs["L0_ffn_w1"], np.float32)
    w2 = np.asarray(inputs["L0_ffn_w2"], np.float32)
    rw = np.asarray(inputs["L0_ffn_router_w"], np.float32)
    btab = host_bias_table(inputs)
    w1T = np.ascontiguousarray(np.transpose(w1, (0, 2, 1)))
    w2T = np.ascontiguousarray(np.transpose(w2, (0, 2, 1)))
    rwT = np.ascontiguousarray(rw.T)
    rng = np.random.default_rng()
    nonces = rng.integers(1, 2**31 - 1, size=(N_ROUNDS, 16)).astype(np.int32)
    bb2 = np.asarray(inputs["L0_attn_b_bb2"], np.float32).reshape(-1)
    bb2d = np.array([[bb2[0] - bb2[1], bb2[1] - bb2[0]]], np.float32)
    rbrow = np.asarray(inputs["L0_ffn_router_b"], np.float32).reshape(1, 8)
    bg2v = np.asarray(inputs["L0_attn_b_bg2"], np.float32).reshape(1, 1)

    common = {
        "Wqkv": np.ascontiguousarray(inputs["L0_attn_a_W_qkv"], dtype=np.float32),
        "Wproj": np.ascontiguousarray(inputs["L0_attn_a_W_proj"], dtype=np.float32),
        "Wpv": np.ascontiguousarray(inputs["L0_attn_b_Wpv"], dtype=np.float32),
        "Wk": np.ascontiguousarray(inputs["L0_attn_b_Wk"], dtype=np.float32),
        "Wkv": np.ascontiguousarray(inputs["L0_attn_b_Wkv"], dtype=np.float32),
        "Wg1": np.ascontiguousarray(inputs["L0_attn_b_Wg1"], dtype=np.float32),
        "Wg2": np.ascontiguousarray(inputs["L0_attn_b_Wg2"], dtype=np.float32),
        "Wb1": np.ascontiguousarray(inputs["L0_attn_b_Wb1"], dtype=np.float32),
        "Wb2": np.ascontiguousarray(inputs["L0_attn_b_Wb2"], dtype=np.float32),
        "Wo": np.ascontiguousarray(inputs["L0_attn_b_Wo"], dtype=np.float32),
        "w1T": w1T, "w2T": w2T, "rwT": rwT, "btab": btab,
        "rbrow": rbrow, "bb2d": bb2d, "bg2v": bg2v, "nonces": nonces,
        "b2m": np.ascontiguousarray(inputs["L0_ffn_b2"], dtype=np.float32),
    }
    in_maps = []
    for c in range(8):
        b = (c // 2) % 2
        h = c % 2
        m = dict(common)
        m["xT"] = np.ascontiguousarray(x[b, h * T:(h + 1) * T, :].T)
        m["ppT"] = np.ascontiguousarray(pp[h * T:(h + 1) * T, :].T)
        m["slot"] = np.array([[h * 2048, h, h * 128, (1 - h) * 128, 0, 0, 0, 0]], np.int32)
        m["mprev"] = np.full((128, 1), float(h), np.float32)
        in_maps.append(m)
    return in_maps


def kernel(**inputs):
    nc = build_program()
    in_maps = make_inmaps(inputs)
    res = run_bass_kernel_spmd(nc, in_maps, core_ids=list(range(8)))
    out = np.empty((2, L, D), np.float32)
    for c in range(4):
        b = c // 2
        h = c % 2
        out[b, h * T:(h + 1) * T, :] = np.asarray(res.results[c]["out"]).T
    return out
